# version 01
# baseline (speedup 1.0000x reference)
import numpy as np
import ml_dtypes
import concourse.bass as bass
import concourse.mybir as mybir
from concourse.bass_utils import run_bass_kernel_spmd

F32 = mybir.dt.float32
BF16 = mybir.dt.bfloat16
AF = mybir.ActivationFunctionType
ALU = mybir.AluOpType
AX = mybir.AxisListType

BF = ml_dtypes.bfloat16


class Ctr:
    __slots__ = ("sem", "val")
    def __init__(self, sem):
        self.sem = sem
        self.val = 0
    def at(self, v=None):
        return (self, self.val if v is None else v)

class Prog:
    ENGINES = ("sync", "tensor", "vector", "scalar", "gpsimd")
    def __init__(self, nc):
        self.nc = nc
        self.ops = {e: [] for e in self.ENGINES}
        self.strict = {}
    def do(self, eng, fn, after=(), inc=None, n=1):
        waits = []
        sctr = self.strict.get(eng)
        if sctr is not None:
            for c in (sctr if isinstance(sctr, (list, tuple)) else [sctr]):
                if c.val > 0:
                    waits.append((c.sem, c.val))
        for a in after:
            ctr, v = a if isinstance(a, tuple) else (a, a.val)
            if v > 0:
                waits.append((ctr.sem, v))
        self.ops[eng].append((waits, fn, inc.sem if inc is not None else None, n))
        if inc is not None:
            inc.val += n
        return inc.val if inc is not None else None
    def wait(self, eng, after):
        self.do(eng, None, after=after)
    def dma(self, q, out, in_, after=(), inc=None):
        return self.do(q, lambda e, o=out, i=in_: e.dma_start(out=o, in_=i),
                       after=after, inc=inc, n=16)
    def realize(self):
        with self.nc.Block() as blk:
            for e in self.ENGINES:
                ops = self.ops[e]
                if not ops:
                    continue
                dec = getattr(blk, e)
                def body(eng, ops_=ops):
                    lastw = {}
                    for waits, fn, sem, n in ops_:
                        for (s, v) in waits:
                            k = id(s)
                            if lastw.get(k, -1) < v:
                                eng.wait_ge(s, v)
                                lastw[k] = v
                        if fn is None:
                            continue
                        inst = fn(eng)
                        if sem is not None:
                            inst.then_inc(sem, n)
                dec(body)


DM = 1024
DI = 2048
NST = 16
DTR = 64
T = 4096
TCH = 512
NCH = T // TCH

def build_phase_m(debug=False):
    nc = bass.Bass("TRN2", target_bir_lowering=False, debug=False)
    xT = nc.dram_tensor("xT", [DM, T], F32, kind="ExternalInput").ap()
    in_wT = nc.dram_tensor("in_wT", [DM, 2 * DI], F32, kind="ExternalInput").ap()
    xp_wT = nc.dram_tensor("xp_wT", [DI, 96], F32, kind="ExternalInput").ap()
    dtp_wT = nc.dram_tensor("dtp_wT", [DTR, DI], F32, kind="ExternalInput").ap()
    out_wT = nc.dram_tensor("out_wT", [DI, DM], F32, kind="ExternalInput").ap()
    conv_w_rs = nc.dram_tensor("conv_w_rs", [128, 64], F32, kind="ExternalInput").ap()
    conv_b_rs = nc.dram_tensor("conv_b_rs", [128, 16], F32, kind="ExternalInput").ap()
    dtp_b_rs = nc.dram_tensor("dtp_b_rs", [128, 16], F32, kind="ExternalInput").ap()
    A_log_rs = nc.dram_tensor("A_log_rs", [128, 256], F32, kind="ExternalInput").ap()
    Dvec_rs = nc.dram_tensor("Dvec_rs", [128, 16], F32, kind="ExternalInput").ap()
    m_out = nc.dram_tensor("m_out", [T, DM], F32, kind="ExternalOutput").ap()

    knd = "ExternalOutput" if debug else "Internal"
    xc_hbm = nc.dram_tensor("xc_hbm", [DI, T], F32, kind=knd).ap()
    sz_hbm = nc.dram_tensor("sz_hbm", [DI, T], F32, kind=knd).ap()
    bc_hbm = nc.dram_tensor("bc_hbm", [32, T], F32, kind=knd).ap()
    dbg_hbm = nc.dram_tensor("dbg_hbm", [128, 4 * TCH], F32, kind=knd).ap()

    from contextlib import ExitStack
    with ExitStack() as _stk:
        ec = _stk.enter_context
        arena_a = ec(nc.sbuf_tensor("arena_a", [128, 65536], BF16))
        arena_b = ec(nc.sbuf_tensor("arena_b", [128, 16384], F32))
        xp_sb = ec(nc.sbuf_tensor("xp_sb", [128, 16 * 96], BF16))
        dtpw_sb = ec(nc.sbuf_tensor("dtpw_sb", [64, DI], BF16))
        smalls = ec(nc.sbuf_tensor("smalls", [128, 672], F32))
        ps0 = ec(nc.psum_tensor("ps0", [128, 512], F32))
        ps1 = ec(nc.psum_tensor("ps1", [128, 512], F32))
        psx = ec(nc.psum_tensor("psx", [128, 512], F32))
        psd = ec(nc.psum_tensor("psd", [128, 512], F32))
        pso0 = ec(nc.psum_tensor("pso0", [128, 512], F32))
        pso1 = ec(nc.psum_tensor("pso1", [128, 512], F32))
        s_ld = ec(nc.semaphore("s_ld"))
        s_gp = ec(nc.semaphore("s_gp"))
        s_mm = ec(nc.semaphore("s_mm"))
        s_sc = ec(nc.semaphore("s_sc"))
        s_vv = ec(nc.semaphore("s_vv"))
        s_st = ec(nc.semaphore("s_st"))
        P = Prog(nc)
        ld = Ctr(s_ld); gp = Ctr(s_gp); mm = Ctr(s_mm)
        sc = Ctr(s_sc); vv = Ctr(s_vv); st = Ctr(s_st)

        aa = arena_a
        xT_sb = aa[:, 0:32768].rearrange("p (h t) -> p h t", h=8)
        inw_sb = aa[:, 32768:65536].rearrange("p (h c) -> p h c", h=8)
        ab = arena_b
        xi_buf = ab[:, 0:4100]
        cv_buf = ab[:, 4100:8196]
        xdbl = ab[:, 8196:12292]
        xc_bf = ab[:, 12292:14340].bitcast(BF16)
        dt_bf = ab[:, 12292:14340].bitcast(BF16)[0:64, :]

        convw_c = smalls[:, 0:64]
        convb_c = smalls[:, 64:80]
        dtpb_c = smalls[:, 80:96]
        Alog_c = smalls[:, 96:352]
        A_c = smalls[:, 352:608]
        Dvec_c = smalls[:, 608:624]

        for h in range(8):
            P.dma("gpsimd", xT_sb[:, h, :], xT[h * 128:(h + 1) * 128, :], inc=gp)
            P.dma("gpsimd", inw_sb[:, h, :], in_wT[h * 128:(h + 1) * 128, :], inc=gp)
        P.dma("gpsimd", xp_sb[:].rearrange("p (a c) -> p a c", a=16),
              xp_wT.rearrange("(a p) c -> p a c", p=128), inc=gp)
        P.dma("gpsimd", dtpw_sb[:], dtp_wT, inc=gp)
        P.dma("sync", convw_c, conv_w_rs, inc=ld)
        P.dma("sync", convb_c, conv_b_rs, inc=ld)
        P.dma("sync", dtpb_c, dtp_b_rs, inc=ld)
        P.dma("sync", Alog_c, A_log_rs, inc=ld)
        P.dma("sync", Dvec_c, Dvec_rs, inc=ld)
        ld_smalls = ld.val
        P.do("scalar", lambda e: e.activation(out=A_c, in_=Alog_c, func=AF.Exp),
             after=[ld.at(ld_smalls)], inc=sc)
        P.do("vector", lambda e: e.tensor_scalar_mul(A_c, A_c, -1.0),
             after=[sc.at()], inc=vv)

        gp_stage0 = gp.val

        conv_done_vv = 0
        xp_read_mm = 0
        last_xc_store = 0
        for ct in range(32):
            is_z = ct >= 16
            dtile = ct % 16
            if ct == 0:
                P.do("vector", lambda e: e.memset(xi_buf[:, 0:3], 0.0), inc=vv)
                P.do("vector", lambda e: e.memset(xi_buf[:, 4099:4100], 0.0), inc=vv)
            evac_sc_vals = []
            for ts in range(8):
                ps = (ps0, ps1)[ts % 2]
                deps = [gp.at(gp_stage0)]
                if ts >= 2:
                    deps.append(sc.at(evac_sc_vals[ts - 2]))
                if not is_z and ts == 0 and conv_done_vv:
                    deps.append(vv.at(conv_done_vv))
                for h in range(8):
                    P.do("tensor",
                         lambda e, ps=ps, h=h, ct=ct, ts=ts: e.matmul(
                             ps[:], inw_sb[:, h, ct * 128:(ct + 1) * 128],
                             xT_sb[:, h, ts * 512:(ts + 1) * 512],
                             start=(h == 0), stop=(h == 7)),
                         after=deps if h == 0 else (), inc=mm if h == 7 else None)
                evac_deps = [mm.at(), st.at(last_xc_store)]
                if is_z and ts == 0:
                    evac_deps.append(vv.at(conv_done_vv))
                if is_z:
                    P.do("scalar",
                         lambda e, ps=ps, ts=ts: e.activation(
                             out=cv_buf[:, ts * 512:(ts + 1) * 512], in_=ps[:], func=AF.Silu),
                         after=evac_deps, inc=sc)
                else:
                    P.do("scalar",
                         lambda e, ps=ps, ts=ts: e.copy(
                             out=xi_buf[:, 3 + ts * 512:3 + (ts + 1) * 512], in_=ps[:]),
                         after=evac_deps, inc=sc)
                evac_sc_vals.append(sc.val)
            if is_z:
                P.dma("sync", sz_hbm[dtile * 128:(dtile + 1) * 128, :], cv_buf[:, 0:4096],
                      after=[sc.at()], inc=st)
                last_xc_store = st.val
            else:
                P.do("vector",
                     lambda e, d=dtile: e.tensor_scalar(
                         out=cv_buf[:, 0:4096], in0=xi_buf[:, 0:4096],
                         scalar1=convw_c[:, d * 4:d * 4 + 1], scalar2=None, op0=ALU.mult),
                     after=[sc.at(), ld.at(ld_smalls)], inc=vv)
                for k in range(1, 4):
                    P.do("vector",
                         lambda e, d=dtile, k=k: e.scalar_tensor_tensor(
                             out=cv_buf[:, 0:4096], in0=xi_buf[:, k:k + 4096],
                             scalar=convw_c[:, d * 4 + k:d * 4 + k + 1],
                             in1=cv_buf[:, 0:4096], op0=ALU.mult, op1=ALU.add),
                         inc=vv)
                P.do("scalar",
                     lambda e, d=dtile: e.activation(
                         out=xi_buf[:, 3:4099], in_=cv_buf[:, 0:4096], func=AF.Silu,
                         bias=convb_c[:, d:d + 1]),
                     after=[vv.at()], inc=sc)
                sc_silu = sc.val
                P.dma("sync", xc_hbm[dtile * 128:(dtile + 1) * 128, :], xi_buf[:, 3:4099],
                      after=[sc.at(sc_silu)], inc=st)
                last_xc_store = st.val
                P.do("vector",
                     lambda e: e.tensor_copy(out=xc_bf[:, :], in_=xi_buf[:, 3:4099]),
                     after=[sc.at(sc_silu), mm.at(xp_read_mm)], inc=vv)
                vv_cast = vv.val
                for ts in range(8):
                    P.do("tensor",
                         lambda e, d=dtile, ts=ts: e.matmul(
                             psx[0:96, :], xp_sb[:, d * 96:(d + 1) * 96],
                             xc_bf[:, ts * 512:(ts + 1) * 512], start=True, stop=True),
                         after=[vv.at(vv_cast)] + ([vv.at()] if ts > 0 or dtile > 0 else []),
                         inc=mm)
                    if dtile == 0:
                        P.do("vector",
                             lambda e, ts=ts: e.tensor_copy(
                                 out=xdbl[0:96, ts * 512:(ts + 1) * 512], in_=psx[0:96, :]),
                             after=[mm.at()], inc=vv)
                    else:
                        P.do("vector",
                             lambda e, ts=ts: e.tensor_tensor(
                                 out=xdbl[0:96, ts * 512:(ts + 1) * 512],
                                 in0=xdbl[0:96, ts * 512:(ts + 1) * 512],
                                 in1=psx[0:96, :], op=ALU.add),
                             after=[mm.at()], inc=vv)
                xp_read_mm = mm.val
                conv_done_vv = vv.val

        P.dma("sync", bc_hbm, xdbl[64:96, :], after=[vv.at()], inc=st)
        bc_stored_st = st.val
        P.do("vector", lambda e: e.tensor_copy(out=dt_bf[:, :], in_=xdbl[0:64, :]),
             after=[vv.at(), mm.at(xp_read_mm)], inc=vv)
        dt_cast_vv = vv.val

        aaf = arena_a[:].bitcast(F32)
        bc_all = aaf[:, 0:16384].rearrange("p (j t) -> p j t", j=32)
        outw_sb = arena_a[:, 32768:49152].rearrange("p (d m) -> p d m", d=16)
        g_buf = arena_a[:, 49152:57344].rearrange("p (d t) -> p d t", d=16)
        dl_buf = ab[:, 0:512]
        w_buf = ab[:, 512:1024]
        dA_buf = ab[:, 1024:1536]
        r_buf = ab[:, 1536:2048]
        H_buf = ab[:, 2048:2560]
        y_buf = ab[:, 2560:3072]
        tm_buf = ab[:, 3072:3584]
        vb_buf = ab[:, 3584:4096]
        ab_buf = ab[:, 4096:4608]
        hcar = ab[:, 4608:4864]
        xc_c = [ab[:, 4864 + i * 512: 5376 + i * 512] for i in range(2)]
        sz_c = [ab[:, 5888 + i * 512: 6400 + i * 512] for i in range(2)]
        mo_buf = [ab[:, 6912 + i * 1024: 7936 + i * 1024] for i in range(2)]

        all_stage1_st = st.val
        for d in range(16):
            P.dma("gpsimd", outw_sb[:, d, :], out_wT[d * 128:(d + 1) * 128, :],
                  after=[st.at(all_stage1_st)], inc=gp)
        gp_outw = gp.val
        P.do("vector", lambda e: e.memset(hcar, 0.0),
             after=[st.at(all_stage1_st)], inc=vv)

        xc_dma_vals = {}
        sz_dma_vals = {}
        stream_war = {}
        sp_done_vv = 0
        def emit_stream_loads(c, dtile, slot):
            deps = [st.at(all_stage1_st)]
            P.dma("sync", xc_c[slot], xc_hbm[dtile * 128:(dtile + 1) * 128,
                                             c * TCH:(c + 1) * TCH],
                  after=deps + stream_war.get(("xc", slot), []), inc=ld)
            xc_dma_vals[(c, dtile)] = ld.val
            P.dma("sync", sz_c[slot], sz_hbm[dtile * 128:(dtile + 1) * 128,
                                             c * TCH:(c + 1) * TCH],
                  after=deps + stream_war.get(("sz", slot), []), inc=ld)
            sz_dma_vals[(c, dtile)] = ld.val

        pending = []
        for c in range(NCH):
            pair_idx = c * 16
            bc_deps = [st.at(all_stage1_st)]
            if c > 0:
                bc_deps.append(vv.at())
            for j in range(32):
                P.dma("gpsimd", bc_all[:, j, :],
                      bc_hbm[j:j + 1, c * TCH:(c + 1) * TCH].broadcast_to([128, TCH]),
                      after=bc_deps if j == 0 else (), inc=gp)
            gp_bc = gp.val

            for dtile in range(16):
                slot = dtile % 2
                emit_stream_loads(c, dtile, slot)
                xcv = xc_dma_vals[(c, dtile)]
                szv = sz_dma_vals[(c, dtile)]
                P.do("tensor",
                     lambda e, d=dtile, c=c: e.matmul(
                         psd[:], dtpw_sb[:, d * 128:(d + 1) * 128],
                         dt_bf[:, c * TCH:(c + 1) * TCH], start=True, stop=True),
                     after=[vv.at(dt_cast_vv), sc.at()], inc=mm)
                P.do("scalar",
                     lambda e, d=dtile: e.activation(
                         out=vb_buf, in_=psd[:], func=AF.Identity,
                         bias=dtpb_c[:, d:d + 1]),
                     after=[mm.at(), st.at(all_stage1_st), vv.at(sp_done_vv)], inc=sc)
                P.do("scalar", lambda e: e.activation(out=ab_buf, in_=vb_buf, func=AF.Abs),
                     after=[sc.at()], inc=sc)
                P.do("scalar", lambda e: e.activation(out=ab_buf, in_=ab_buf, func=AF.Exp,
                                                      scale=-1.0), inc=sc)
                P.do("scalar", lambda e: e.activation(out=ab_buf, in_=ab_buf, func=AF.Ln,
                                                      bias=1.0), inc=sc)
                sc_sp = sc.val
                P.do("vector",
                     lambda e: e.tensor_scalar(out=dl_buf, in0=vb_buf, scalar1=0.0,
                                               scalar2=None, op0=ALU.max),
                     after=[sc.at(sc_sp)], inc=vv)
                P.do("vector",
                     lambda e: e.tensor_tensor(out=dl_buf, in0=dl_buf, in1=ab_buf,
                                               op=ALU.add), inc=vv)
                sp_done_vv = vv.val
                P.do("vector",
                     lambda e, s=slot: e.tensor_tensor(out=w_buf, in0=dl_buf,
                                                       in1=xc_c[s], op=ALU.mult),
                     after=[ld.at(xcv)], inc=vv)
                for n in range(16):
                    col = dtile * 16 + n
                    P.do("scalar",
                         lambda e, col=col: e.activation(
                             out=dA_buf, in_=dl_buf, func=AF.Exp,
                             scale=A_c[:, col:col + 1]),
                         after=[vv.at()], inc=sc)
                    P.do("vector",
                         lambda e, n=n: e.tensor_tensor(
                             out=r_buf, in0=w_buf, in1=bc_all[:, n, :], op=ALU.mult),
                         after=[gp.at(gp_bc), sc.at()], inc=vv)
                    P.do("vector",
                         lambda e, col=col: e.tensor_tensor_scan(
                             out=H_buf, data0=dA_buf, data1=r_buf,
                             initial=hcar[:, col:col + 1], op0=ALU.mult, op1=ALU.add),
                         inc=vv)
                    P.do("vector",
                         lambda e, col=col: e.tensor_copy(
                             out=hcar[:, col:col + 1], in_=H_buf[:, TCH - 1:TCH]),
                         after=[vv.at()], inc=vv)
                    if n == 0:
                        P.do("vector",
                             lambda e, n=n: e.tensor_tensor(
                                 out=y_buf, in0=H_buf, in1=bc_all[:, 16 + n, :],
                                 op=ALU.mult), inc=vv)
                    else:
                        P.do("vector",
                             lambda e, n=n: e.tensor_tensor(
                                 out=tm_buf, in0=H_buf, in1=bc_all[:, 16 + n, :],
                                 op=ALU.mult), inc=vv)
                        P.do("vector",
                             lambda e: e.tensor_tensor(out=y_buf, in0=y_buf, in1=tm_buf,
                                                       op=ALU.add), inc=vv)
                if debug and c == 0 and dtile == 0:
                    for i, buf in enumerate([dl_buf, w_buf, H_buf, y_buf]):
                        P.dma("sync", dbg_hbm[:, i * TCH:(i + 1) * TCH], buf,
                              after=[vv.at()], inc=st)
                    P.wait("vector", [st.at()])
                P.do("vector",
                     lambda e, d=dtile, s=slot: e.scalar_tensor_tensor(
                         out=tm_buf, in0=xc_c[s], scalar=Dvec_c[:, d:d + 1],
                         in1=y_buf, op0=ALU.mult, op1=ALU.add), inc=vv)
                P.do("vector",
                     lambda e, d=dtile, s=slot: e.tensor_tensor(
                         out=g_buf[:, d, :], in0=tm_buf, in1=sz_c[s], op=ALU.mult),
                     after=[ld.at(szv), mm.at()], inc=vv)
                stream_war[("xc", slot)] = [vv.at(vv.val)]
                stream_war[("sz", slot)] = [vv.at(vv.val)]
            vv_chunk = vv.val

            for tsub in range(4):
                ob = mo_buf[tsub % 2]
                pso = (pso0, pso1)[tsub % 2]
                for ms in range(2):
                    for d in range(16):
                        P.do("tensor",
                             lambda e, d=d, ts=tsub, ms=ms, pso=pso: e.matmul(
                                 pso[:], g_buf[:, d, ts * 128:(ts + 1) * 128],
                                 outw_sb[:, d, ms * 512:(ms + 1) * 512],
                                 start=(d == 0), stop=(d == 15)),
                             after=([vv.at(vv_chunk), gp.at(gp_outw), sc.at()]
                                    if d == 0 else ()),
                             inc=mm if d == 15 else None)
                    P.do("scalar",
                         lambda e, ms=ms, ob=ob, pso=pso: e.copy(
                             out=ob[:, ms * 512:(ms + 1) * 512], in_=pso[:]),
                         after=[mm.at(), st.at(st.val)], inc=sc)
                P.dma("sync", m_out[c * TCH + tsub * 128: c * TCH + (tsub + 1) * 128, :],
                      ob, after=[sc.at()], inc=st)

        P.wait("gpsimd", [st.at(), ld.at(), gp.at()])
        P.realize()
    return nc


def prep_phase_m_inputs(inputs):
    x = inputs["L0_attn_x"]
    maps = []
    for core in range(8):
        b, d = core // 2, core % 2
        sfx = "1" if d == 0 else "2"
        xb = x[b] if d == 0 else x[b][::-1]
        g = lambda k: np.asarray(inputs["L0_attn_" + k + sfx])
        m = {
            "xT": np.ascontiguousarray(xb.T),
            "in_wT": np.ascontiguousarray(g("in_w").T),
            "xp_wT": np.ascontiguousarray(g("xp_w").T),
            "dtp_wT": np.ascontiguousarray(g("dtp_w").T),
            "out_wT": np.ascontiguousarray(g("out_w").T),
            "conv_w_rs": np.ascontiguousarray(
                g("conv_w").reshape(16, 128, 4).transpose(1, 0, 2).reshape(128, 64)),
            "conv_b_rs": np.ascontiguousarray(g("conv_b").reshape(16, 128).T),
            "dtp_b_rs": np.ascontiguousarray(g("dtp_b").reshape(16, 128).T),
            "A_log_rs": np.ascontiguousarray(
                g("A_log").reshape(16, 128, 16).transpose(1, 0, 2).reshape(128, 256)),
            "Dvec_rs": np.ascontiguousarray(g("Dvec").reshape(16, 128).T),
        }
        maps.append(m)
    return maps

def run_phase_m(inputs, debug=False):
    nc = build_phase_m(debug=debug)
    in_maps = prep_phase_m_inputs(inputs)
    res = run_bass_kernel_spmd(nc, in_maps, core_ids=list(range(8)))
    if debug:
        return res.results
    return [r["m_out"] for r in res.results]


NE0 = 32
I0 = 768
TA = 2048
NTT = TA // 128

def build_phase_a(debug=False):
    nc = bass.Bass("TRN2", target_bir_lowering=False, debug=False)
    x_in = nc.dram_tensor("x_in", [TA, DM], F32, kind="ExternalInput").ap()
    mf_in = nc.dram_tensor("mf_in", [TA, DM], F32, kind="ExternalInput").ap()
    mr_in = nc.dram_tensor("mr_in", [TA, DM], F32, kind="ExternalInput").ap()
    gate_wT = nc.dram_tensor("gate_wT", [DM, NE0], F32, kind="ExternalInput").ap()
    e_bias = nc.dram_tensor("e_bias", [1, NE0], F32, kind="ExternalInput").ap()
    w1_h = nc.dram_tensor("w1", [NE0, DM, I0], F32, kind="ExternalInput").ap()
    w3_h = nc.dram_tensor("w3", [NE0, DM, I0], F32, kind="ExternalInput").ap()
    w2_h = nc.dram_tensor("w2", [NE0, I0, DM], F32, kind="ExternalInput").ap()
    x2_out = nc.dram_tensor("x2_out", [TA, DM], F32, kind="ExternalOutput").ap()
    knd = "ExternalOutput" if debug else "Internal"
    xn_hbm = nc.dram_tensor("xn_hbm", [TA, DM], F32, kind=knd).ap()
    dbgA = nc.dram_tensor("dbgA", [128, 1024], F32, kind=knd).ap()

    from contextlib import ExitStack
    with ExitStack() as _stk:
        ec = _stk.enter_context
        xnf = ec(nc.sbuf_tensor("xnf", [128, 8 * TA], BF16))
        acc = ec(nc.sbuf_tensor("acc", [128, NTT * DM], F32))
        a_bf = ec(nc.sbuf_tensor("a_bf", [128, 6 * TA], BF16))
        wbuf = ec(nc.sbuf_tensor("wbuf", [128, 2 * 18432], BF16))
        gate_sb = ec(nc.sbuf_tensor("gate_sb", [128, 8 * NE0], F32))
        ident32 = ec(nc.sbuf_tensor("ident32", [128, 128], F32))
        comb_all = ec(nc.sbuf_tensor("comb_all", [128, NTT * NE0 + NE0], F32))
        comb = comb_all[:, 0:NTT * NE0]
        h1 = ec(nc.sbuf_tensor("h1", [128, DM], F32))
        scr2 = ec(nc.sbuf_tensor("scr2", [128, 64], F32))
        xnb = ec(nc.sbuf_tensor("xnb", [128, DM], BF16))
        sg = ec(nc.sbuf_tensor("sg", [128, 512], F32))
        sm = ec(nc.sbuf_tensor("sm", [128, 64], F32))
        ebias_sb = comb_all[:, NTT * NE0:NTT * NE0 + NE0]
        ident_bf = ec(nc.sbuf_tensor("ident_bf", [128, 128], BF16))
        pst = ec(nc.psum_tensor("pst", [128, DM], BF16))
        pst32 = ec(nc.psum_tensor("pst32", [128, DM], F32))
        psg = ec(nc.psum_tensor("psg", [128, 512], F32))
        psu = ec(nc.psum_tensor("psu", [128, 512], F32))
        pso0 = ec(nc.psum_tensor("pso0", [128, 512], F32))
        pso1 = ec(nc.psum_tensor("pso1", [128, 512], F32))
        s_ld = ec(nc.semaphore("s_ld")); s_gp = ec(nc.semaphore("s_gp"))
        s_mm = ec(nc.semaphore("s_mm")); s_sc = ec(nc.semaphore("s_sc"))
        s_vv = ec(nc.semaphore("s_vv")); s_st = ec(nc.semaphore("s_st"))
        s_gi = ec(nc.semaphore("s_gi"))
        P = Prog(nc)
        ld = Ctr(s_ld); gp = Ctr(s_gp); mm = Ctr(s_mm)
        sc = Ctr(s_sc); vv = Ctr(s_vv); st = Ctr(s_st)
        gi = Ctr(s_gi)

        route_done_vv = 0
        psr = psg[:, 0:NE0]
        xnf_v = xnf[:].rearrange("p (h t) -> p h t", h=8)
        xnt32 = a_bf[:, 0:2048].bitcast(F32)
        wv = [wbuf[:, i * 18432:(i + 1) * 18432] for i in range(2)]
        w1_v = [w[:, 0:6144].rearrange("p (h i) -> p h i", h=8) for w in wv]
        w3_v = [w[:, 6144:12288].rearrange("p (h i) -> p h i", h=8) for w in wv]
        w2_v = [w[:, 12288:18432].rearrange("p (i h) -> p i h", i=6) for w in wv]
        gate_v = gate_sb[:].rearrange("p (h e) -> p h e", h=8)
        rs_c = sm[:, 0:1]; ssq_c = sm[:, 1:2]; mean_c = sm[:, 2:3]
        var_c = sm[:, 3:4]; rstd_c = sm[:, 4:5]; msq_c = sm[:, 5:6]
        mx_c = sm[:, 8:16]; rinv_c = sm[:, 16:17]; rsum_c = sm[:, 17:18]
        eps_c = sm[:, 18:19]
        scr_c = sm[:, 32:64]

        P.dma("sync", ebias_sb, e_bias.broadcast_to([128, NE0]), inc=ld)
        P.dma("sync", gate_v, gate_wT.rearrange("(a p) e -> p a e", p=128), inc=ld)
        P.do("vector", lambda e: e.memset(eps_c, 1e-5), inc=vv)
        P.do("gpsimd", lambda e: e.memset(ident_bf[:], 0.0), inc=gi)
        P.do("gpsimd", lambda e: e.affine_select(
            out=ident_bf[:], in_=ident_bf[:], compare_op=ALU.not_equal, fill=1.0,
            base=0, pattern=[[-1, 128]], channel_multiplier=1), inc=gi)
        P.do("gpsimd", lambda e: e.memset(ident32[:], 0.0), inc=gi)
        P.do("gpsimd", lambda e: e.affine_select(
            out=ident32[:], in_=ident32[:], compare_op=ALU.not_equal, fill=1.0,
            base=0, pattern=[[-1, 128]], channel_multiplier=1), inc=gi)
        ld_init = ld.val; gp_init = gp.val

        P.strict = {"vector": [vv, sc], "scalar": [sc, vv]}
        for tt in range(NTT):
            r0, r1 = tt * 128, (tt + 1) * 128
            ldep = [vv.at(), sc.at(), mm.at()]
            P.dma("sync", h1[:], x_in[r0:r1, :], after=ldep, inc=ld)
            P.dma("sync", xnt32, mf_in[r0:r1, :], after=ldep, inc=ld)
            v_ld = ld.val
            P.do("vector", lambda e: e.tensor_tensor(out=h1[:], in0=h1[:], in1=xnt32,
                                                     op=ALU.add),
                 after=[ld.at(v_ld)], inc=vv)
            P.dma("sync", xnt32, mr_in[r0:r1, :], after=[vv.at()], inc=ld)
            P.do("vector", lambda e: e.tensor_tensor(out=h1[:], in0=h1[:], in1=xnt32,
                                                     op=ALU.add),
                 after=[ld.at()], inc=vv)
            P.do("vector", lambda e: e.reduce_sum(out=rs_c, in_=h1[:], axis=AX.X), inc=vv)
            P.do("scalar", lambda e: e.activation(out=pst32[:], in_=h1[:], func=AF.Square,
                                                  accum_out=ssq_c),
                 after=[vv.at()], inc=sc)
            P.do("vector", lambda e: e.tensor_scalar_mul(mean_c, rs_c, 1.0 / DM),
                 after=[vv.at()], inc=vv)
            P.do("vector", lambda e: e.tensor_mul(msq_c, mean_c, mean_c),
                 after=[vv.at()], inc=vv)
            P.do("vector", lambda e: e.tensor_scalar_mul(var_c, ssq_c, 1.0 / DM),
                 after=[sc.at(), vv.at()], inc=vv)
            P.do("vector", lambda e: e.tensor_sub(var_c, var_c, msq_c),
                 after=[vv.at()], inc=vv)
            P.do("scalar", lambda e: e.activation(out=rstd_c, in_=var_c, func=AF.Ln,
                                                  bias=eps_c),
                 after=[vv.at()], inc=sc)
            P.do("scalar", lambda e: e.activation(out=rstd_c, in_=rstd_c, func=AF.Exp,
                                                  scale=-0.5), inc=sc)
            P.do("vector", lambda e: e.tensor_scalar(out=h1[:], in0=h1[:], scalar1=mean_c,
                                                     scalar2=rstd_c, op0=ALU.subtract,
                                                     op1=ALU.mult),
                 after=[sc.at()], inc=vv)
            xn_done = vv.val
            if debug and tt == 0:
                P.dma("sync", dbgA[:, 0:32], sm[:, 0:32], after=[vv.at()], inc=st)
                P.wait("vector", [st.at()])
                P.wait("scalar", [st.at()])
            P.dma("sync", xn_hbm[r0:r1, :], h1[:], after=[vv.at(xn_done)], inc=st)
            P.do("vector", lambda e: e.tensor_copy(out=xnb[:], in_=h1[:]),
                 after=[vv.at(xn_done), mm.at()], inc=vv)
            cast_done = vv.val
            for ht in range(8):
                P.do("tensor",
                     lambda e, ht=ht: e.transpose(
                         pst[:, ht * 128:(ht + 1) * 128],
                         xnb[:, ht * 128:(ht + 1) * 128], ident_bf[:]),
                     after=[vv.at(cast_done), sc.at(), gi.at(4)] if ht == 0 else (),
                     inc=mm if ht == 7 else None)
            P.do("vector",
                 lambda e, tt=tt: e.tensor_copy(
                     out=xnf_v[:, :, tt * 128:(tt + 1) * 128],
                     in_=pst[:].rearrange("p (h t) -> p h t", h=8)),
                 after=[mm.at()], inc=vv)
            tr_done = vv.val
            for ht in range(8):
                P.do("tensor",
                     lambda e, ht=ht: e.transpose(
                         pst32[:, ht * 128:(ht + 1) * 128],
                         h1[:, ht * 128:(ht + 1) * 128], ident32[:]),
                     after=[vv.at(xn_done), sc.at()] if ht == 0 else (),
                     inc=mm if ht == 7 else None)
            P.do("vector", lambda e: e.tensor_copy(out=xnt32[:], in_=pst32[:]),
                 after=[mm.at()], inc=vv)
            tr32_done = vv.val
            for ht in range(8):
                P.do("tensor",
                     lambda e, ht=ht, tt=tt: e.matmul(
                         psr, xnt32[:, ht * 128:(ht + 1) * 128],
                         gate_v[:, ht, :], start=(ht == 0), stop=(ht == 7)),
                     after=[vv.at(tr32_done), ld.at(ld_init), sc.at()] if ht == 0 else (),
                     inc=mm if ht == 7 else None)
            P.do("scalar", lambda e: e.activation(out=scr_c, in_=psr, func=AF.Sigmoid),
                 after=[mm.at(), vv.at(route_done_vv)], inc=sc)
            P.do("vector", lambda e: e.tensor_tensor(out=scr2[:, 0:32], in0=scr_c,
                                                     in1=ebias_sb, op=ALU.add),
                 after=[sc.at(), ld.at(ld_init)], inc=vv)
            P.do("vector", lambda e: e.max(out=mx_c, in_=scr2[:, 0:32]), inc=vv)
            P.do("vector", lambda e: e.memset(mx_c[:, 4:8], -1e30),
                 after=[vv.at()], inc=vv)
            P.do("vector", lambda e: e.match_replace(out=scr2[:, 32:64], in_to_replace=mx_c,
                                                     in_values=scr2[:, 0:32],
                                                     imm_value=-1e30),
                 after=[vv.at()], inc=vv)
            P.do("vector", lambda e: e.tensor_single_scalar(out=scr2[:, 32:64],
                                                            in_=scr2[:, 32:64],
                                                            scalar=-1e30,
                                                            op=ALU.is_equal),
                 after=[vv.at()], inc=vv)
            P.do("vector", lambda e: e.tensor_tensor(out=scr2[:, 32:64], in0=scr2[:, 32:64],
                                                     in1=scr_c, op=ALU.mult), inc=vv)
            P.do("vector", lambda e: e.reduce_sum(out=rsum_c, in_=scr2[:, 32:64],
                                                  axis=AX.X), inc=vv)
            P.do("vector", lambda e: e.reciprocal(out=rinv_c, in_=rsum_c),
                 after=[vv.at()], inc=vv)
            P.do("vector",
                 lambda e, tt=tt: e.tensor_scalar(
                     out=comb[:, tt * NE0:(tt + 1) * NE0], in0=scr2[:, 32:64],
                     scalar1=rinv_c, scalar2=None, op0=ALU.mult),
                 after=[vv.at()], inc=vv)
            route_done_vv = vv.val
        P.strict = {}
        prep_done_vv = vv.val
        prep_done_sc = sc.val
        if debug:
            P.dma("sync", dbgA[:, 0:512], comb, after=[vv.at()], inc=st)


        wload_gp = []
        def load_expert(e_i, slot):
            deps = [mm.at(wcons_mm[slot])] if wcons_mm[slot] else []
            P.dma("gpsimd", w1_v[slot],
                  w1_h[e_i].rearrange("(a p) i -> p a i", p=128), after=deps, inc=gp)
            P.dma("gpsimd", w3_v[slot],
                  w3_h[e_i].rearrange("(a p) i -> p a i", p=128), inc=gp)
            P.dma("gpsimd", w2_v[slot],
                  w2_h[e_i].rearrange("(a p) h -> p a h", p=128), inc=gp)
            wload_gp.append(gp.val)

        wcons_mm = [0, 0]
        load_expert(0, 0)
        load_expert(1, 1)
        a_written_vv = 0
        silu_war_vv = 0
        for e_i in range(NE0):
            slot = e_i % 2
            wg = wload_gp[e_i]
            for it in range(6):
                for tsl in range(4):
                    for ht in range(8):
                        P.do("tensor",
                             lambda e, it=it, tsl=tsl, ht=ht, s=slot: e.matmul(
                                 psg[:], w1_v[s][:, ht, it * 128:(it + 1) * 128],
                                 xnf_v[:, ht, tsl * 512:(tsl + 1) * 512],
                                 start=(ht == 0), stop=(ht == 7)),
                             after=([gp.at(wg), vv.at(prep_done_vv), sc.at()]
                                    if (it == 0 and tsl == 0 and ht == 0) else
                                    ([sc.at()] if ht == 0 else ())),
                             inc=mm if ht == 7 else None)
                    for ht in range(8):
                        P.do("tensor",
                             lambda e, it=it, tsl=tsl, ht=ht, s=slot: e.matmul(
                                 psu[:], w3_v[s][:, ht, it * 128:(it + 1) * 128],
                                 xnf_v[:, ht, tsl * 512:(tsl + 1) * 512],
                                 start=(ht == 0), stop=(ht == 7)),
                             after=[vv.at()] if ht == 0 else (),
                             inc=mm if ht == 7 else None)
                    P.do("scalar",
                         lambda e: e.activation(out=sg[:], in_=psg[:], func=AF.Silu),
                         after=[mm.at(), vv.at(silu_war_vv)], inc=sc)
                    P.do("vector",
                         lambda e, it=it, tsl=tsl: e.tensor_tensor(
                             out=a_bf[:, it * TA + tsl * 512: it * TA + (tsl + 1) * 512],
                             in0=sg[:], in1=psu[:], op=ALU.mult),
                         after=[sc.at(), mm.at(w2cons_mm if e_i > 0 else 0)], inc=vv)
                    silu_war_vv = vv.val
            a_written_vv = vv.val
            for tt in range(NTT):
                pso = (pso0, pso1)[tt % 2]
                for hs in range(2):
                    for it in range(6):
                        P.do("tensor",
                             lambda e, tt=tt, hs=hs, it=it, s=slot, pso=pso: e.matmul(
                                 pso[:], a_bf[:, it * TA + tt * 128: it * TA + (tt + 1) * 128],
                                 w2_v[s][:, it, hs * 512:(hs + 1) * 512],
                                 start=(it == 0), stop=(it == 5)),
                             after=[vv.at()] if it == 0 else (),
                             inc=mm if it == 5 else None)
                    if e_i == 0:
                        P.do("vector",
                             lambda e, tt=tt, hs=hs, pso=pso: e.tensor_scalar(
                                 out=acc[:, tt * DM + hs * 512: tt * DM + (hs + 1) * 512],
                                 in0=pso[:], scalar1=comb[:, tt * NE0:tt * NE0 + 1],
                                 scalar2=None, op0=ALU.mult),
                             after=[mm.at()], inc=vv)
                    else:
                        P.do("vector",
                             lambda e, tt=tt, hs=hs, pso=pso, e_i=e_i: e.scalar_tensor_tensor(
                                 out=acc[:, tt * DM + hs * 512: tt * DM + (hs + 1) * 512],
                                 in0=pso[:], scalar=comb[:, tt * NE0 + e_i:tt * NE0 + e_i + 1],
                                 in1=acc[:, tt * DM + hs * 512: tt * DM + (hs + 1) * 512],
                                 op0=ALU.mult, op1=ALU.add),
                             after=[mm.at()], inc=vv)
            w2cons_mm = mm.val
            if e_i + 2 < NE0:
                wcons_mm[slot] = mm.val
                load_expert(e_i + 2, slot)
        moe_done_vv = vv.val
        if debug:
            P.dma("sync", dbgA[:, 512:1024], acc[:, 0:512], after=[vv.at()], inc=st)
            P.wait("vector", [st.at()])

        P.strict = {"vector": [vv, sc], "scalar": [sc, vv]}
        for tt in range(NTT):
            r0, r1 = tt * 128, (tt + 1) * 128
            P.dma("sync", h1[:], xn_hbm[r0:r1, :],
                  after=[vv.at(), sc.at(), st.at()], inc=ld)
            P.do("vector",
                 lambda e, tt=tt: e.tensor_tensor(
                     out=h1[:], in0=h1[:], in1=acc[:, tt * DM:(tt + 1) * DM], op=ALU.add),
                 after=[ld.at(), vv.at(moe_done_vv)], inc=vv)
            P.do("vector", lambda e: e.reduce_sum(out=rs_c, in_=h1[:], axis=AX.X), inc=vv)
            P.do("scalar", lambda e: e.activation(out=pst32[:], in_=h1[:], func=AF.Square,
                                                  accum_out=ssq_c),
                 after=[vv.at()], inc=sc)
            P.do("vector", lambda e: e.tensor_scalar_mul(mean_c, rs_c, 1.0 / DM),
                 after=[vv.at()], inc=vv)
            P.do("vector", lambda e: e.tensor_mul(msq_c, mean_c, mean_c),
                 after=[vv.at()], inc=vv)
            P.do("vector", lambda e: e.tensor_scalar_mul(var_c, ssq_c, 1.0 / DM),
                 after=[sc.at(), vv.at()], inc=vv)
            P.do("vector", lambda e: e.tensor_sub(var_c, var_c, msq_c),
                 after=[vv.at()], inc=vv)
            P.do("scalar", lambda e: e.activation(out=rstd_c, in_=var_c, func=AF.Ln,
                                                  bias=eps_c),
                 after=[vv.at()], inc=sc)
            P.do("scalar", lambda e: e.activation(out=rstd_c, in_=rstd_c, func=AF.Exp,
                                                  scale=-0.5), inc=sc)
            P.do("vector", lambda e: e.tensor_scalar(out=h1[:], in0=h1[:], scalar1=mean_c,
                                                     scalar2=rstd_c, op0=ALU.subtract,
                                                     op1=ALU.mult),
                 after=[sc.at()], inc=vv)
            P.dma("sync", x2_out[r0:r1, :], h1[:], after=[vv.at()], inc=st)

        P.strict = {}
        P.wait("gpsimd", [st.at(), ld.at(), gp.at()])
        P.realize()
    return nc

def prep_phase_a_inputs(inputs, m_outs):
    x = np.asarray(inputs["L0_attn_x"]).reshape(16384, DM)
    gate_wT = np.ascontiguousarray(np.asarray(inputs["L0_ffn_gate_w"]).T)
    e_bias = np.asarray(inputs["L0_ffn_e_bias"]).reshape(1, NE0)
    w1 = np.ascontiguousarray(inputs["L0_ffn_w1"])
    w3 = np.ascontiguousarray(inputs["L0_ffn_w3"])
    w2 = np.ascontiguousarray(inputs["L0_ffn_w2"])
    maps = []
    for c in range(8):
        b, half = c // 2, c % 2
        sl = slice(half * TA, (half + 1) * TA)
        maps.append({
            "x_in": np.ascontiguousarray(x[c * TA:(c + 1) * TA]),
            "mf_in": np.ascontiguousarray(np.asarray(m_outs[2 * b])[sl]),
            "mr_in": np.ascontiguousarray(np.asarray(m_outs[2 * b + 1])[sl]),
            "gate_wT": gate_wT, "e_bias": e_bias,
            "w1": w1, "w3": w3, "w2": w2,
        })
    return maps

def run_phase_a(inputs, m_outs, debug=False):
    nc = build_phase_a(debug=debug)
    in_maps = prep_phase_a_inputs(inputs, m_outs)
    res = run_bass_kernel_spmd(nc, in_maps, core_ids=list(range(8)))
    if debug:
        return res.results
    return [r["x2_out"] for r in res.results]


NH = 16
HD = 64
NTB = 17

def build_phase_b(debug=False):
    nc = bass.Bass("TRN2", target_bir_lowering=False, debug=False)
    x_in = nc.dram_tensor("x_in", [TA, DM], F32, kind="ExternalInput").ap()
    halo_in = nc.dram_tensor("halo_in", [128, DM], F32, kind="ExternalInput").ap()
    wqT = nc.dram_tensor("wqT", [DM, DM], F32, kind="ExternalInput").ap()
    wkT = nc.dram_tensor("wkT", [DM, DM], F32, kind="ExternalInput").ap()
    wvT = nc.dram_tensor("wvT", [DM, DM], F32, kind="ExternalInput").ap()
    woT = nc.dram_tensor("woT", [DM, DM], F32, kind="ExternalInput").ap()
    biases = nc.dram_tensor("biases", [4, DM], F32, kind="ExternalInput").ap()
    posv = nc.dram_tensor("posv", [128, NTB], F32, kind="ExternalInput").ap()
    masks_in = nc.dram_tensor("masks_in", [128, 512], F32, kind="ExternalInput").ap()
    x3_out = nc.dram_tensor("x3_out", [TA, DM], F32, kind="ExternalOutput").ap()
    knd = "ExternalOutput" if debug else "Internal"
    dbgB = nc.dram_tensor("dbgB", [128, 2048], F32, kind=knd).ap()

    from contextlib import ExitStack
    with ExitStack() as _stk:
        ec = _stk.enter_context
        x2f = ec(nc.sbuf_tensor("x2f", [128, 8 * NTB * 128], BF16))
        qf = ec(nc.sbuf_tensor("qf", [128, 8 * NTB * 128], BF16))
        kf = ec(nc.sbuf_tensor("kf", [128, 8 * NTB * 128], BF16))
        vtok = ec(nc.sbuf_tensor("vtok", [128, NTB * DM], BF16))
        wbuf = ec(nc.sbuf_tensor("wbuf", [128, 8 * DM], BF16))
        biasr = ec(nc.sbuf_tensor("biasr", [1, 4 * DM], BF16))
        ones_r = ec(nc.sbuf_tensor("ones_r", [1, 128], BF16))
        ident_bf = ec(nc.sbuf_tensor("identb", [128, 128], BF16))
        masks = ec(nc.sbuf_tensor("masks", [128, 512], F32))
        trig = ec(nc.sbuf_tensor("trig", [128, 256], F32))
        posv_sb = ec(nc.sbuf_tensor("posv_sb", [128, NTB], F32))
        qs = ec(nc.sbuf_tensor("qs", [128, DM], F32))
        qs2 = ec(nc.sbuf_tensor("qs2", [128, DM], F32))
        tmp5 = ec(nc.sbuf_tensor("tmp5", [128, 512], F32))
        qsb = ec(nc.sbuf_tensor("qsb", [128, DM], BF16))
        scb = ec(nc.sbuf_tensor("scb", [128, 256], F32))
        pb = ec(nc.sbuf_tensor("pb", [128, 256], BF16))
        ptb = ec(nc.sbuf_tensor("ptb", [128, 256], BF16))
        smB = ec(nc.sbuf_tensor("smB", [128, 16], F32))
        resid = ec(nc.sbuf_tensor("resid", [128, DM], F32))
        psq = ec(nc.psum_tensor("psq", [128, 512], F32))
        pstB = ec(nc.psum_tensor("pstB", [128, DM], BF16))
        pssc = ec(nc.psum_tensor("pssc", [128, 256], F32))
        psT = ec(nc.psum_tensor("psT", [128, 256], BF16))
        psctx = ec(nc.psum_tensor("psctx", [64, 128], F32))
        pso = ec(nc.psum_tensor("psoB", [128, 512], F32))
        s_ld = ec(nc.semaphore("s_ld")); s_gp = ec(nc.semaphore("s_gp"))
        s_mm = ec(nc.semaphore("s_mm")); s_sc = ec(nc.semaphore("s_sc"))
        s_vv = ec(nc.semaphore("s_vv")); s_st = ec(nc.semaphore("s_st"))
        s_gi = ec(nc.semaphore("s_gi"))
        P = Prog(nc)
        ld = Ctr(s_ld); gp = Ctr(s_gp); mm = Ctr(s_mm)
        sc = Ctr(s_sc); vv = Ctr(s_vv); st = Ctr(s_st); gi = Ctr(s_gi)

        x2f_v = x2f[:].rearrange("p (h t) -> p h t", h=8)
        qf_v = qf[:].rearrange("p (h t) -> p h t", h=8)
        kf_v = kf[:].rearrange("p (h t) -> p h t", h=8)
        vt_v = vtok[:].rearrange("p (t o) -> p t o", t=NTB)
        w_v = wbuf[:].rearrange("p (h o) -> p h o", h=8)
        mask_f = masks[:, 0:256]
        mask_n = masks[:, 256:512]
        ifr_c = trig[:, 0:32]
        ang_c = trig[:, 32:64]
        cos_c = trig[:, 64:96]
        sin_c = trig[:, 96:128]
        kk_c = trig[:, 128:160]
        rmax_c = smB[:, 0:1]; rsum_c = smB[:, 1:2]; rinv_c = smB[:, 2:3]

        M_MAGIC = 12582912.0
        INV2PI = 0.15915493667125702
        CW1 = 6.28125
        CW2 = float(np.float32(6.283185307179586 - 6.28125))
        HALFPI = float(np.float32(np.pi / 2))

        P.do("gpsimd", lambda e: e.memset(ident_bf[:], 0.0), inc=gi)
        P.do("gpsimd", lambda e: e.affine_select(
            out=ident_bf[:], in_=ident_bf[:], compare_op=ALU.not_equal, fill=1.0,
            base=0, pattern=[[-1, 128]], channel_multiplier=1), inc=gi)
        P.do("gpsimd", lambda e: e.memset(ones_r[:], 1.0), inc=gi)
        P.dma("sync", masks[:], masks_in, inc=ld)
        ifr_i32 = trig[:, 160:192].bitcast(mybir.dt.int32)
        P.do("gpsimd", lambda e: e.iota(ifr_i32, pattern=[[1, 32]], base=0,
                                        channel_multiplier=0), inc=gi)
        gi_const = gi.val
        P.do("vector", lambda e: e.tensor_copy(out=ifr_c, in_=ifr_i32),
             after=[gi.at(gi_const)], inc=vv)
        P.do("scalar", lambda e: e.activation(out=ifr_c, in_=ifr_c, func=AF.Exp,
                                              scale=-float(np.log(10000.0) / 32.0)),
             after=[vv.at()], inc=sc)
        ifr_done_sc = sc.val
        P.dma("sync", posv_sb[:], posv, inc=ld)
        P.dma("gpsimd", biasr[:].rearrange("p (a o) -> p a o", a=4),
              biases.rearrange("a o -> a o")[None, :, :] if False
              else biases.rearrange("(x a) o -> x a o", x=1), inc=gp)
        posv_ld = ld.val

        for tt in range(NTB):
            src = halo_in if tt == 0 else x_in[(tt - 1) * 128: tt * 128, :]
            P.dma("gpsimd", qsb[:], src, after=[mm.at()], inc=gp)
            for ht in range(8):
                P.do("tensor",
                     lambda e, ht=ht: e.transpose(
                         pstB[:, ht * 128:(ht + 1) * 128],
                         qsb[:, ht * 128:(ht + 1) * 128], ident_bf[:]),
                     after=[gp.at(), gi.at(gi_const), vv.at()] if ht == 0 else (),
                     inc=mm if ht == 7 else None)
            P.do("vector",
                 lambda e, tt=tt: e.tensor_copy(
                     out=x2f_v[:, :, tt * 128:(tt + 1) * 128],
                     in_=pstB[:].rearrange("p (h t) -> p h t", h=8)),
                 after=[mm.at()], inc=vv)
        x2f_done_vv = vv.val

        for w_i, wT in enumerate([wqT, wkT, wvT]):
            P.dma("gpsimd", w_v, wT.rearrange("(a p) o -> p a o", p=128),
                  after=[mm.at()], inc=gp)
            wld = gp.val
            for tt in range(NTB):
                for os_ in range(2):
                    dep0 = [gp.at(wld), vv.at(x2f_done_vv), sc.at(), vv.at()]
                    for ht in range(8):
                        P.do("tensor",
                             lambda e, tt=tt, os_=os_, ht=ht: e.matmul(
                                 psq[:], x2f_v[:, ht, tt * 128:(tt + 1) * 128],
                                 w_v[:, ht, os_ * 512:(os_ + 1) * 512],
                                 start=(ht == 0), stop=False),
                             after=dep0 if ht == 0 else (), inc=None)
                    P.do("tensor",
                         lambda e, w_i=w_i, os_=os_: e.matmul(
                             psq[:], ones_r[:],
                             biasr[0:1, w_i * DM + os_ * 512: w_i * DM + (os_ + 1) * 512],
                             start=False, stop=True),
                         after=[gp.at()], inc=mm)
                    if w_i == 2:
                        P.do("scalar",
                             lambda e, tt=tt, os_=os_: e.copy(
                                 out=vt_v[:, tt, os_ * 512:(os_ + 1) * 512], in_=psq[:]),
                             after=[mm.at()], inc=sc)
                    else:
                        P.do("scalar",
                             lambda e, os_=os_: e.copy(
                                 out=qs[:, os_ * 512:(os_ + 1) * 512], in_=psq[:]),
                             after=[mm.at()], inc=sc)
                if w_i == 2:
                    continue
                P.strict = {"vector": [vv, sc], "scalar": [sc, vv]}
                for trig_i in range(2):
                    P.do("vector", lambda e, tt=tt: e.tensor_scalar(
                        out=ang_c, in0=ifr_c, scalar1=posv_sb[:, tt:tt + 1],
                        scalar2=None, op0=ALU.mult),
                         after=[sc.at(ifr_done_sc), ld.at(posv_ld)], inc=vv)
                    if trig_i == 1:
                        P.do("vector", lambda e: e.tensor_scalar(
                            out=ang_c, in0=ang_c, scalar1=HALFPI, scalar2=None,
                            op0=ALU.add), inc=vv)
                    P.do("vector", lambda e: e.tensor_scalar(
                        out=kk_c, in0=ang_c, scalar1=INV2PI, scalar2=M_MAGIC,
                        op0=ALU.mult, op1=ALU.add), inc=vv)
                    P.do("vector", lambda e: e.tensor_scalar(
                        out=kk_c, in0=kk_c, scalar1=M_MAGIC, scalar2=None,
                        op0=ALU.subtract), inc=vv)
                    P.do("vector", lambda e: e.scalar_tensor_tensor(
                        out=ang_c, in0=kk_c, scalar=-CW1, in1=ang_c,
                        op0=ALU.mult, op1=ALU.add), inc=vv)
                    P.do("vector", lambda e: e.scalar_tensor_tensor(
                        out=ang_c, in0=kk_c, scalar=-CW2, in1=ang_c,
                        op0=ALU.mult, op1=ALU.add), inc=vv)
                    P.do("scalar", lambda e, trig_i=trig_i: e.activation(
                        out=(sin_c if trig_i == 0 else cos_c), in_=ang_c,
                        func=AF.Sin), inc=sc)
                q1 = qs[:].rearrange("p (h d) -> p h d", h=NH)[:, :, 0:32]
                q2 = qs[:].rearrange("p (h d) -> p h d", h=NH)[:, :, 32:64]
                o1 = qs2[:].rearrange("p (h d) -> p h d", h=NH)[:, :, 0:32]
                o2 = qs2[:].rearrange("p (h d) -> p h d", h=NH)[:, :, 32:64]
                tm = tmp5[:].rearrange("p (h d) -> p h d", h=NH, d=32)
                cosb = cos_c.rearrange("p (a d) -> p a d", a=1).broadcast_to([128, NH, 32])
                sinb = sin_c.rearrange("p (a d) -> p a d", a=1).broadcast_to([128, NH, 32])
                P.do("vector", lambda e, q1=q1, o1=o1, cosb=cosb: e.tensor_tensor(
                    out=o1, in0=q1, in1=cosb, op=ALU.mult), after=[sc.at()], inc=vv)
                P.do("vector", lambda e, q2=q2, tm=tm, sinb=sinb: e.tensor_tensor(
                    out=tm, in0=q2, in1=sinb, op=ALU.mult), inc=vv)
                P.do("vector", lambda e, o1=o1, tm=tm: e.tensor_tensor(
                    out=o1, in0=o1, in1=tm, op=ALU.subtract), inc=vv)
                P.do("vector", lambda e, q2=q2, o2=o2, cosb=cosb: e.tensor_tensor(
                    out=o2, in0=q2, in1=cosb, op=ALU.mult), inc=vv)
                P.do("vector", lambda e, q1=q1, tm=tm, sinb=sinb: e.tensor_tensor(
                    out=tm, in0=q1, in1=sinb, op=ALU.mult), inc=vv)
                P.do("vector", lambda e, o2=o2, tm=tm: e.tensor_tensor(
                    out=o2, in0=o2, in1=tm, op=ALU.add), inc=vv)
                P.do("vector", lambda e: e.tensor_copy(out=qsb[:], in_=qs2[:]), inc=vv)
                P.strict = {}
                cast_vv = vv.val
                dst = qf_v if w_i == 0 else kf_v
                for ht in range(8):
                    P.do("tensor",
                         lambda e, ht=ht: e.transpose(
                             pstB[:, ht * 128:(ht + 1) * 128],
                             qsb[:, ht * 128:(ht + 1) * 128], ident_bf[:]),
                         after=[vv.at(cast_vv), sc.at(), vv.at()] if ht == 0 else (),
                         inc=mm if ht == 7 else None)
                P.do("vector",
                     lambda e, tt=tt, dst=dst: e.tensor_copy(
                         out=dst[:, :, tt * 128:(tt + 1) * 128],
                         in_=pstB[:].rearrange("p (h t) -> p h t", h=8)),
                     after=[mm.at()], inc=vv)
        qkv_done_vv = vv.val
        qkv_done_sc = sc.val
        if debug:
            P.do("vector", lambda e: e.tensor_copy(out=qs[:], in_=vt_v[:, 1, :]),
                 after=[vv.at(), sc.at()], inc=vv)
            P.dma("sync", dbgB[:, 0:1024], qs[:], after=[vv.at()], inc=st)
            P.do("vector", lambda e: e.tensor_copy(
                out=qs2[:].rearrange("p (h t) -> p h t", h=8),
                in_=qf_v[:, :, 128:256]),
                 after=[st.at()], inc=vv)
            P.dma("sync", dbgB[:, 1024:2048], qs2[:], after=[vv.at()], inc=st)
            P.wait("vector", [st.at()])

        cf_v = x2f_v
        for qb in range(16):
            t_self = qb + 1
            t_prev = qb
            for h in range(NH):
                hrow = (h % 2) * 64
                htile = h // 2
                dep0 = [vv.at(qkv_done_vv), vv.at(), sc.at(), ld.at()]
                P.do("tensor",
                     lambda e, hrow=hrow, htile=htile, t_prev=t_prev, qb=qb: e.matmul(
                         pssc[:, 0:128],
                         qf_v[hrow:hrow + 64, htile, (qb + 1) * 128:(qb + 2) * 128],
                         kf_v[hrow:hrow + 64, htile, t_prev * 128:(t_prev + 1) * 128],
                         start=True, stop=True),
                     after=dep0, inc=None)
                P.do("tensor",
                     lambda e, hrow=hrow, htile=htile, t_self=t_self, qb=qb: e.matmul(
                         pssc[:, 128:256],
                         qf_v[hrow:hrow + 64, htile, (qb + 1) * 128:(qb + 2) * 128],
                         kf_v[hrow:hrow + 64, htile, t_self * 128:(t_self + 1) * 128],
                         start=True, stop=True),
                     inc=mm)
                mk_ = mask_f if (qb == 0) else mask_n
                P.strict = {"vector": [vv, sc], "scalar": [sc, vv]}
                P.do("vector", lambda e, mk_=mk_: e.scalar_tensor_tensor(
                    out=scb[:], in0=pssc[:], scalar=0.125, in1=mk_,
                    op0=ALU.mult, op1=ALU.add),
                     after=[mm.at()], inc=vv)
                P.do("vector", lambda e: e.tensor_reduce(
                    out=rmax_c, in_=scb[:], axis=AX.X, op=ALU.max, negate=True), inc=vv)
                P.do("scalar", lambda e: e.activation(
                    out=scb[:], in_=scb[:], func=AF.Exp, bias=rmax_c,
                    accum_out=rsum_c), after=[vv.at()], inc=sc)
                P.do("vector", lambda e: e.reciprocal(out=rinv_c, in_=rsum_c),
                     after=[sc.at()], inc=vv)
                P.do("vector", lambda e: e.tensor_scalar(
                    out=pb[:], in0=scb[:], scalar1=rinv_c, scalar2=None,
                    op0=ALU.mult), inc=vv)
                P.strict = {}
                pb_vv = vv.val
                for blk in range(2):
                    P.do("tensor",
                         lambda e, blk=blk: e.transpose(
                             psT[:, blk * 128:(blk + 1) * 128],
                             pb[:, blk * 128:(blk + 1) * 128], ident_bf[:]),
                         after=[vv.at(pb_vv), vv.at()] if blk == 0 else (),
                         inc=mm if blk == 1 else None)
                P.do("vector", lambda e: e.tensor_copy(out=ptb[:], in_=psT[:]),
                     after=[mm.at()], inc=vv)
                ptb_vv = vv.val
                for blk, tl in ((0, t_prev), (1, t_self)):
                    P.do("tensor",
                         lambda e, blk=blk, tl=tl, h=h: e.matmul(
                             psctx[:], vt_v[:, tl, h * 64:(h + 1) * 64],
                             ptb[:, blk * 128:(blk + 1) * 128],
                             start=(blk == 0), stop=(blk == 1)),
                         after=[vv.at(ptb_vv), sc.at()] if blk == 0 else (),
                         inc=mm if blk == 1 else None)
                P.do("scalar",
                     lambda e, hrow=hrow, htile=htile, qb=qb: e.copy(
                         out=cf_v[hrow:hrow + 64, htile, (qb + 1) * 128:(qb + 2) * 128],
                         in_=psctx[:]),
                     after=[mm.at()], inc=sc)
        attn_done_sc = sc.val

        P.dma("gpsimd", w_v, woT.rearrange("(a p) o -> p a o", p=128),
              after=[mm.at()], inc=gp)
        wo_ld = gp.val
        rs_c = smB[:, 4:5]; ssq_c = smB[:, 5:6]; mean_c = smB[:, 6:7]
        var_c = smB[:, 7:8]; rstd_c = smB[:, 8:9]; msq_c = smB[:, 9:10]
        eps_c = smB[:, 10:11]
        P.do("vector", lambda e: e.memset(eps_c, 1e-5), inc=vv)
        for tt in range(16):
            r0, r1 = tt * 128, (tt + 1) * 128
            P.dma("sync", resid[:], x_in[r0:r1, :], after=[vv.at(), sc.at()], inc=ld)
            res_ld = ld.val
            for os_ in range(2):
                for ht in range(8):
                    P.do("tensor",
                         lambda e, tt=tt, os_=os_, ht=ht: e.matmul(
                             pso[:], cf_v[:, ht, (tt + 1) * 128:(tt + 2) * 128],
                             w_v[:, ht, os_ * 512:(os_ + 1) * 512],
                             start=(ht == 0), stop=False),
                         after=[gp.at(wo_ld), sc.at(attn_done_sc), vv.at(), sc.at()]
                         if ht == 0 else (), inc=None)
                P.do("tensor",
                     lambda e, os_=os_: e.matmul(
                         pso[:], ones_r[:],
                         biasr[0:1, 3 * DM + os_ * 512: 3 * DM + (os_ + 1) * 512],
                         start=False, stop=True), inc=mm)
                P.do("scalar",
                     lambda e, os_=os_: e.copy(out=qs[:, os_ * 512:(os_ + 1) * 512],
                                               in_=pso[:]),
                     after=[mm.at()], inc=sc)
            P.strict = {"vector": [vv, sc], "scalar": [sc, vv]}
            P.do("vector", lambda e: e.tensor_tensor(out=qs[:], in0=qs[:], in1=resid[:],
                                                     op=ALU.add),
                 after=[ld.at(res_ld), sc.at()], inc=vv)
            P.do("vector", lambda e: e.reduce_sum(out=rs_c, in_=qs[:], axis=AX.X), inc=vv)
            P.do("scalar", lambda e: e.activation(out=qs2[:],
                                                  in_=qs[:], func=AF.Square,
                                                  accum_out=ssq_c), inc=sc)
            P.do("vector", lambda e: e.tensor_scalar_mul(mean_c, rs_c, 1.0 / DM), inc=vv)
            P.do("vector", lambda e: e.tensor_mul(msq_c, mean_c, mean_c), inc=vv)
            P.do("vector", lambda e: e.tensor_scalar_mul(var_c, ssq_c, 1.0 / DM),
                 after=[sc.at()], inc=vv)
            P.do("vector", lambda e: e.tensor_sub(var_c, var_c, msq_c), inc=vv)
            P.do("scalar", lambda e: e.activation(out=rstd_c, in_=var_c, func=AF.Ln,
                                                  bias=eps_c), inc=sc)
            P.do("scalar", lambda e: e.activation(out=rstd_c, in_=rstd_c, func=AF.Exp,
                                                  scale=-0.5), inc=sc)
            P.do("vector", lambda e: e.tensor_scalar(out=qs[:], in0=qs[:], scalar1=mean_c,
                                                     scalar2=rstd_c, op0=ALU.subtract,
                                                     op1=ALU.mult),
                 after=[sc.at()], inc=vv)
            P.strict = {}
            P.dma("sync", x3_out[r0:r1, :], qs[:], after=[vv.at()], inc=st)
        P.wait("gpsimd", [st.at(), ld.at(), gp.at()])
        P.realize()
    return nc

def prep_phase_b_inputs(inputs, x2_list):
    x2 = np.concatenate([np.asarray(a) for a in x2_list], axis=0)
    g = lambda k: np.asarray(inputs["L1_attn_" + k])
    shared = {
        "wqT": np.ascontiguousarray(g("Wq").T), "wkT": np.ascontiguousarray(g("Wk").T),
        "wvT": np.ascontiguousarray(g("Wv").T), "woT": np.ascontiguousarray(g("Wo").T),
        "biases": np.ascontiguousarray(np.stack([g("bq"), g("bk"), g("bv"), g("bo")])),
    }
    qi = np.arange(128)[:, None]
    kj = np.arange(256)[None, :]
    band = (kj >= qi) & (kj <= qi + 128)
    mask_n = np.where(band, 0.0, -1e30).astype(np.float32)
    mask_f = np.where(band & (kj >= 128), 0.0, -1e30).astype(np.float32)
    maps = []
    for c in range(8):
        b, half = c // 2, c % 2
        base = half * TA
        pos = (base - 128 + np.arange(NTB)[None, :] * 128
               + np.arange(128)[:, None]).astype(np.float32)
        halo = (x2[c * TA - 128: c * TA] if half == 1
                else np.zeros((128, DM), np.float32))
        m = dict(shared)
        m["x_in"] = np.ascontiguousarray(x2[c * TA:(c + 1) * TA])
        m["halo_in"] = np.ascontiguousarray(halo)
        m["posv"] = np.ascontiguousarray(pos)
        m["masks_in"] = np.ascontiguousarray(
            np.concatenate([mask_f if half == 0 else mask_n, mask_n], axis=1))
        maps.append(m)
    return maps

def run_phase_b(inputs, x2_list, debug=False):
    nc = build_phase_b(debug=debug)
    in_maps = prep_phase_b_inputs(inputs, x2_list)
    res = run_bass_kernel_spmd(nc, in_maps, core_ids=list(range(8)))
    return [r["x3_out"] for r in res.results]


def build_phase_c(debug=False):
    nc = bass.Bass("TRN2", target_bir_lowering=False, debug=False)
    x_in = nc.dram_tensor("x_in", [TA, DM], F32, kind="ExternalInput").ap()
    qwT = nc.dram_tensor("qwT", [DM, DM], F32, kind="ExternalInput").ap()
    kwT = nc.dram_tensor("kwT", [DM, DM], F32, kind="ExternalInput").ap()
    vwT = nc.dram_tensor("vwT", [DM, DM], F32, kind="ExternalInput").ap()
    biases = nc.dram_tensor("biases", [3, DM], F32, kind="ExternalInput").ap()
    gamma1 = nc.dram_tensor("gamma1", [1, DM], F32, kind="ExternalInput").ap()
    theta = nc.dram_tensor("theta", [1, 32], F32, kind="ExternalInput").ap()
    posv = nc.dram_tensor("posv", [128, 16], F32, kind="ExternalInput").ap()
    qn_out = nc.dram_tensor("qn_out", [DM, TA], BF16, kind="ExternalOutput").ap()
    kn_out = nc.dram_tensor("kn_out", [DM, TA], BF16, kind="ExternalOutput").ap()
    vn_out = nc.dram_tensor("vn_out", [TA, DM], BF16, kind="ExternalOutput").ap()
    xn_out = nc.dram_tensor("xn_out", [TA, DM], F32, kind="ExternalOutput").ap()

    from contextlib import ExitStack
    with ExitStack() as _stk:
        ec = _stk.enter_context
        xnf = ec(nc.sbuf_tensor("xnfC", [128, 8 * TA], BF16))
        wbuf = ec(nc.sbuf_tensor("wbufC", [128, 8 * DM], BF16))
        outbuf = ec(nc.sbuf_tensor("outbufC", [128, 16 * DM], BF16))
        biasr = ec(nc.sbuf_tensor("biasrC", [1, 3 * DM], BF16))
        ones_r = ec(nc.sbuf_tensor("ones_rC", [1, 128], BF16))
        ident_bf = ec(nc.sbuf_tensor("identC", [128, 128], BF16))
        gam_b = ec(nc.sbuf_tensor("gam_bC", [128, DM], F32))
        trig = ec(nc.sbuf_tensor("trigC", [128, 160], F32))
        posv_sb = ec(nc.sbuf_tensor("posvC", [128, 16], F32))
        qs = ec(nc.sbuf_tensor("qsC", [128, DM], F32))
        qs2 = ec(nc.sbuf_tensor("qs2C", [128, DM], F32))
        tmp5 = ec(nc.sbuf_tensor("tmp5C", [128, 512], F32))
        qsb = ec(nc.sbuf_tensor("qsbC", [128, DM], BF16))
        smC = ec(nc.sbuf_tensor("smC", [128, 16], F32))
        psq = ec(nc.psum_tensor("psqC", [128, 512], F32))
        pstC = ec(nc.psum_tensor("pstC", [128, DM], BF16))
        s_ld = ec(nc.semaphore("s_ld")); s_gp = ec(nc.semaphore("s_gp"))
        s_mm = ec(nc.semaphore("s_mm")); s_sc = ec(nc.semaphore("s_sc"))
        s_vv = ec(nc.semaphore("s_vv")); s_st = ec(nc.semaphore("s_st"))
        P = Prog(nc)
        ld = Ctr(s_ld); gp = Ctr(s_gp); mm = Ctr(s_mm)
        sc = Ctr(s_sc); vv = Ctr(s_vv); st = Ctr(s_st)

        xnf_v = xnf[:].rearrange("p (h t) -> p h t", h=8)
        w_v = wbuf[:].rearrange("p (h o) -> p h o", h=8)
        ob_v = outbuf[:].rearrange("p (t o) -> p t o", t=16)
        th_c = trig[:, 0:32]; ang_c = trig[:, 32:64]
        cos_c = trig[:, 64:96]; sin_c = trig[:, 96:128]; kk_c = trig[:, 128:160]
        rs_c = smC[:, 0:1]; ssq_c = smC[:, 1:2]; rstd_c = smC[:, 2:3]; eps_c = smC[:, 3:4]

        M_MAGIC = 12582912.0
        INV2PI = 0.15915493667125702
        CW1 = 6.28125
        CW2 = float(np.float32(6.283185307179586 - 6.28125))
        HALFPI = float(np.float32(np.pi / 2))

        P.do("gpsimd", lambda e: e.memset(ident_bf[:], 0.0), inc=gp, n=1)
        P.do("gpsimd", lambda e: e.affine_select(
            out=ident_bf[:], in_=ident_bf[:], compare_op=ALU.not_equal, fill=1.0,
            base=0, pattern=[[-1, 128]], channel_multiplier=1), inc=gp, n=1)
        P.do("gpsimd", lambda e: e.memset(ones_r[:], 1.0), inc=gp, n=1)
        gi_const = gp.val
        P.do("vector", lambda e: e.memset(eps_c, 1e-5), inc=vv)
        P.dma("sync", th_c, theta.broadcast_to([128, 32]), inc=ld)
        P.dma("sync", gam_b[:], gamma1.broadcast_to([128, DM]), inc=ld)
        P.dma("sync", posv_sb[:], posv, inc=ld)
        P.dma("gpsimd", biasr[:].rearrange("p (a o) -> p a o", a=3),
              biases.rearrange("(x a) o -> x a o", x=1), inc=gp)
        consts_ld = ld.val

        for tt in range(16):
            r0, r1 = tt * 128, (tt + 1) * 128
            P.dma("sync", qs[:], x_in[r0:r1, :], after=[vv.at(), sc.at(), mm.at()], inc=ld)
            P.strict = {"vector": [vv, sc], "scalar": [sc, vv]}
            P.do("scalar", lambda e: e.activation(out=qs2[:],
                                                  in_=qs[:], func=AF.Square,
                                                  accum_out=ssq_c),
                 after=[ld.at()], inc=sc)
            P.do("scalar", lambda e: e.activation(out=rstd_c, in_=ssq_c, func=AF.Ln,
                                                  scale=1.0 / DM, bias=eps_c), inc=sc)
            P.do("scalar", lambda e: e.activation(out=rstd_c, in_=rstd_c, func=AF.Exp,
                                                  scale=-0.5), inc=sc)
            P.do("vector", lambda e: e.tensor_scalar(out=qs[:], in0=qs[:], scalar1=rstd_c,
                                                     scalar2=None, op0=ALU.mult), inc=vv)
            P.do("vector", lambda e: e.tensor_tensor(out=qs[:], in0=qs[:], in1=gam_b[:],
                                                     op=ALU.mult),
                 after=[ld.at(consts_ld)], inc=vv)
            P.strict = {}
            xn_vv = vv.val
            P.dma("sync", xn_out[r0:r1, :], qs[:], after=[vv.at(xn_vv)], inc=st)
            P.do("vector", lambda e: e.tensor_copy(out=qsb[:], in_=qs[:]),
                 after=[vv.at(xn_vv)], inc=vv)
            for ht in range(8):
                P.do("tensor",
                     lambda e, ht=ht: e.transpose(
                         pstC[:, ht * 128:(ht + 1) * 128],
                         qsb[:, ht * 128:(ht + 1) * 128], ident_bf[:]),
                     after=[vv.at(), gp.at(gi_const), sc.at()] if ht == 0 else (),
                     inc=mm if ht == 7 else None)
            P.do("vector",
                 lambda e, tt=tt: e.tensor_copy(
                     out=xnf_v[:, :, tt * 128:(tt + 1) * 128],
                     in_=pstC[:].rearrange("p (h t) -> p h t", h=8)),
                 after=[mm.at()], inc=vv)
        xnf_done_vv = vv.val

        for w_i, wT in enumerate([qwT, kwT, vwT]):
            P.dma("gpsimd", w_v, wT.rearrange("(a p) o -> p a o", p=128),
                  after=[mm.at()], inc=gp)
            wld = gp.val
            for tt in range(16):
                for os_ in range(2):
                    dep0 = [gp.at(wld), vv.at(xnf_done_vv), sc.at(), vv.at()]
                    for ht in range(8):
                        P.do("tensor",
                             lambda e, tt=tt, os_=os_, ht=ht: e.matmul(
                                 psq[:], xnf_v[:, ht, tt * 128:(tt + 1) * 128],
                                 w_v[:, ht, os_ * 512:(os_ + 1) * 512],
                                 start=(ht == 0), stop=False),
                             after=dep0 if ht == 0 else (), inc=None)
                    P.do("tensor",
                         lambda e, w_i=w_i, os_=os_: e.matmul(
                             psq[:], ones_r[:],
                             biasr[0:1, w_i * DM + os_ * 512: w_i * DM + (os_ + 1) * 512],
                             start=False, stop=True),
                         after=[gp.at()], inc=mm)
                    P.do("scalar",
                         lambda e, os_=os_: e.copy(out=qs[:, os_ * 512:(os_ + 1) * 512],
                                                   in_=psq[:]),
                         after=[mm.at()], inc=sc)
                if w_i == 2:
                    P.do("vector", lambda e, tt=tt: e.tensor_copy(out=ob_v[:, tt, :],
                                                                  in_=qs[:]),
                         after=[sc.at()], inc=vv)
                    continue
                P.strict = {"vector": [vv, sc], "scalar": [sc, vv]}
                for trig_i in range(2):
                    P.do("vector", lambda e, tt=tt: e.tensor_scalar(
                        out=ang_c, in0=th_c, scalar1=posv_sb[:, tt:tt + 1],
                        scalar2=None, op0=ALU.mult),
                         after=[ld.at(consts_ld)], inc=vv)
                    if trig_i == 1:
                        P.do("vector", lambda e: e.tensor_scalar(
                            out=ang_c, in0=ang_c, scalar1=HALFPI, scalar2=None,
                            op0=ALU.add), inc=vv)
                    P.do("vector", lambda e: e.tensor_scalar(
                        out=kk_c, in0=ang_c, scalar1=INV2PI, scalar2=M_MAGIC,
                        op0=ALU.mult, op1=ALU.add), inc=vv)
                    P.do("vector", lambda e: e.tensor_scalar(
                        out=kk_c, in0=kk_c, scalar1=M_MAGIC, scalar2=None,
                        op0=ALU.subtract), inc=vv)
                    P.do("vector", lambda e: e.scalar_tensor_tensor(
                        out=ang_c, in0=kk_c, scalar=-CW1, in1=ang_c,
                        op0=ALU.mult, op1=ALU.add), inc=vv)
                    P.do("vector", lambda e: e.scalar_tensor_tensor(
                        out=ang_c, in0=kk_c, scalar=-CW2, in1=ang_c,
                        op0=ALU.mult, op1=ALU.add), inc=vv)
                    P.do("scalar", lambda e, trig_i=trig_i: e.activation(
                        out=(sin_c if trig_i == 0 else cos_c), in_=ang_c,
                        func=AF.Sin), inc=sc)
                qv = qs[:].rearrange("p (h d) -> p h d", h=NH)
                q1 = qv[:, :, 0:63:2] if False else qs[:].rearrange(
                    "p (h d2 two) -> p h d2 two", h=NH, two=2)[:, :, :, 0]
                q2 = qs[:].rearrange("p (h d2 two) -> p h d2 two", h=NH, two=2)[:, :, :, 1]
                o1 = qs2[:].rearrange("p (h d) -> p h d", h=NH)[:, :, 0:32]
                o2 = qs2[:].rearrange("p (h d) -> p h d", h=NH)[:, :, 32:64]
                tm = tmp5[:].rearrange("p (h d) -> p h d", h=NH, d=32)
                cosb = cos_c.rearrange("p (a d) -> p a d", a=1).broadcast_to([128, NH, 32])
                sinb = sin_c.rearrange("p (a d) -> p a d", a=1).broadcast_to([128, NH, 32])
                P.do("vector", lambda e, q1=q1, o1=o1, cosb=cosb: e.tensor_tensor(
                    out=o1, in0=q1, in1=cosb, op=ALU.mult), inc=vv)
                P.do("vector", lambda e, q2=q2, tm=tm, sinb=sinb: e.tensor_tensor(
                    out=tm, in0=q2, in1=sinb, op=ALU.mult), inc=vv)
                P.do("vector", lambda e, o1=o1, tm=tm: e.tensor_tensor(
                    out=o1, in0=o1, in1=tm, op=ALU.subtract), inc=vv)
                P.do("vector", lambda e, q2=q2, o2=o2, cosb=cosb: e.tensor_tensor(
                    out=o2, in0=q2, in1=cosb, op=ALU.mult), inc=vv)
                P.do("vector", lambda e, q1=q1, tm=tm, sinb=sinb: e.tensor_tensor(
                    out=tm, in0=q1, in1=sinb, op=ALU.mult), inc=vv)
                P.do("vector", lambda e, o2=o2, tm=tm: e.tensor_tensor(
                    out=o2, in0=o2, in1=tm, op=ALU.add), inc=vv)
                P.do("vector", lambda e: e.tensor_copy(out=qsb[:], in_=qs2[:]), inc=vv)
                P.strict = {}
                cast_vv = vv.val
                for ht in range(8):
                    P.do("tensor",
                         lambda e, ht=ht: e.transpose(
                             pstC[:, ht * 128:(ht + 1) * 128],
                             qsb[:, ht * 128:(ht + 1) * 128], ident_bf[:]),
                         after=[vv.at(cast_vv), sc.at(), vv.at()] if ht == 0 else (),
                         inc=mm if ht == 7 else None)
                P.do("vector",
                     lambda e, tt=tt: e.tensor_copy(
                         out=ob_v[:, tt, :].rearrange("p (h t) -> p h t", h=8)
                         if False else ob_v[:, tt, :],
                         in_=pstC[:]),
                     after=[mm.at()], inc=vv)
            dst = (qn_out, kn_out, vn_out)[w_i]
            if w_i < 2:
                P.dma("sync", dst.rearrange("(h p) (tt t) -> p tt h t", p=128, t=128),
                      ob_v[:].rearrange("p tt (h t) -> p tt h t", h=8),
                      after=[vv.at()], inc=st)
            else:
                P.dma("sync", dst.rearrange("(tt p) o -> p tt o", p=128), ob_v[:],
                      after=[vv.at()], inc=st)
        P.wait("gpsimd", [st.at(), ld.at(), gp.at()])
        P.realize()
    return nc


def build_phase_d(debug=False):
    nc = bass.Bass("TRN2", target_bir_lowering=False, debug=False)
    qn = nc.dram_tensor("qn", [512, T], BF16, kind="ExternalInput").ap()
    kn = nc.dram_tensor("kn", [512, T], BF16, kind="ExternalInput").ap()
    vn = nc.dram_tensor("vn", [T, 512], BF16, kind="ExternalInput").ap()
    ctx_out = nc.dram_tensor("ctx_out", [512, T], BF16, kind="ExternalOutput").ap()

    from contextlib import ExitStack
    with ExitStack() as _stk:
        ec = _stk.enter_context
        qsb = ec(nc.sbuf_tensor("qsbD", [128, 4 * T], BF16))
        ksb = ec(nc.sbuf_tensor("ksbD", [128, 4 * T], BF16))
        vsb = ec(nc.sbuf_tensor("vsbD", [128, 32 * 512], BF16))
        csb = ec(nc.sbuf_tensor("csbD", [128, 4 * T], BF16))
        scores = ec(nc.sbuf_tensor("scoresD", [128, T], F32))
        pbuf = ec(nc.sbuf_tensor("pbufD", [128, T], BF16))
        ptb = ec(nc.sbuf_tensor("ptbD", [128, 256], BF16))
        cmask = ec(nc.sbuf_tensor("cmaskD", [128, 128], F32))
        ident_bf = ec(nc.sbuf_tensor("identD", [128, 128], BF16))
        smD = ec(nc.sbuf_tensor("smD", [128, 16], F32))
        pss0 = ec(nc.psum_tensor("pss0", [128, 512], F32))
        pss1 = ec(nc.psum_tensor("pss1", [128, 512], F32))
        psT = ec(nc.psum_tensor("psTD", [128, 256], BF16))
        psctx = ec(nc.psum_tensor("psctxD", [64, 128], F32))
        s_ld = ec(nc.semaphore("s_ld")); s_gp = ec(nc.semaphore("s_gp"))
        s_mm = ec(nc.semaphore("s_mm")); s_sc = ec(nc.semaphore("s_sc"))
        s_vv = ec(nc.semaphore("s_vv")); s_st = ec(nc.semaphore("s_st"))
        P = Prog(nc)
        ld = Ctr(s_ld); gp = Ctr(s_gp); mm = Ctr(s_mm)
        sc = Ctr(s_sc); vv = Ctr(s_vv); st = Ctr(s_st)

        q_v = qsb[:].rearrange("p (a t) -> p a t", a=4)
        k_v = ksb[:].rearrange("p (a t) -> p a t", a=4)
        v_v = vsb[:].rearrange("p (tt o) -> p tt o", tt=32)
        c_v = csb[:].rearrange("p (a t) -> p a t", a=4)
        rmax_c = smD[:, 0:1]; rsum_c = smD[:, 1:2]; rinv_c = smD[:, 2:3]
        neg_c = smD[:, 3:4]

        P.do("gpsimd", lambda e: e.memset(ident_bf[:], 0.0), inc=gp, n=1)
        P.do("gpsimd", lambda e: e.affine_select(
            out=ident_bf[:], in_=ident_bf[:], compare_op=ALU.not_equal, fill=1.0,
            base=0, pattern=[[-1, 128]], channel_multiplier=1), inc=gp, n=1)
        P.do("gpsimd", lambda e: e.memset(cmask[:], 0.0), inc=gp, n=1)
        P.do("gpsimd", lambda e: e.affine_select(
            out=cmask[:], in_=cmask[:], compare_op=ALU.is_ge, fill=-1e30,
            base=0, pattern=[[-1, 128]], channel_multiplier=1), inc=gp, n=1)
        gi_const = gp.val
        for a in range(4):
            P.dma("sync", q_v[:, a, :], qn[a * 128:(a + 1) * 128, :], inc=ld)
            P.dma("sync", k_v[:, a, :], kn[a * 128:(a + 1) * 128, :], inc=ld)
        P.dma("sync", vsb[:].rearrange("p (tt o) -> p tt o", tt=32),
              vn.rearrange("(tt p) o -> p tt o", p=128), inc=ld)
        ld_all = ld.val

        for h in range(8):
            atile = h // 2
            hrow = (h % 2) * 64
            for qb in range(32):
                nkb = qb + 1
                nslab = (nkb + 3) // 4
                P.strict = {}
                for sl in range(nslab):
                    k0 = sl * 512
                    kw = min(512, nkb * 128 - k0)
                    ps = (pss0, pss1)[sl % 2]
                    P.do("tensor",
                         lambda e, atile=atile, hrow=hrow, qb=qb, k0=k0, kw=kw, ps=ps:
                         e.matmul(
                             ps[:, 0:kw],
                             q_v[hrow:hrow + 64, atile, qb * 128:(qb + 1) * 128],
                             k_v[hrow:hrow + 64, atile, k0:k0 + kw],
                             start=True, stop=True),
                         after=[ld.at(ld_all), vv.at(), sc.at()], inc=mm)
                    P.do("vector",
                         lambda e, k0=k0, kw=kw, ps=ps: e.tensor_scalar(
                             out=scores[:, k0:k0 + kw], in0=ps[:, 0:kw], scalar1=0.125,
                             scalar2=None, op0=ALU.mult),
                         after=[mm.at()], inc=vv)
                P.strict = {"vector": [vv, sc], "scalar": [sc, vv]}
                P.do("vector", lambda e, qb=qb: e.tensor_tensor(
                    out=scores[:, qb * 128:(qb + 1) * 128],
                    in0=scores[:, qb * 128:(qb + 1) * 128],
                    in1=cmask[:], op=ALU.add),
                     after=[gp.at(gi_const)], inc=vv)
                P.do("vector", lambda e, nkb=nkb: e.tensor_reduce(
                    out=neg_c, in_=scores[:, 0:nkb * 128], axis=AX.X, op=ALU.max,
                    negate=True), inc=vv)
                P.do("scalar", lambda e, nkb=nkb: e.activation(
                    out=pbuf[:, 0:nkb * 128], in_=scores[:, 0:nkb * 128], func=AF.Exp,
                    bias=neg_c, accum_out=rsum_c), inc=sc)
                P.do("vector", lambda e: e.reciprocal(out=rinv_c, in_=rsum_c), inc=vv)
                P.do("vector", lambda e, nkb=nkb: e.tensor_scalar(
                    out=pbuf[:, 0:nkb * 128], in0=pbuf[:, 0:nkb * 128], scalar1=rinv_c,
                    scalar2=None, op0=ALU.mult), inc=vv)
                P.strict = {}
                pb_vv = vv.val
                for kb in range(nkb):
                    sl2 = kb % 2
                    P.do("tensor",
                         lambda e, kb=kb, sl2=sl2: e.transpose(
                             psT[:, sl2 * 128:(sl2 + 1) * 128],
                             pbuf[:, kb * 128:(kb + 1) * 128], ident_bf[:]),
                         after=[vv.at(pb_vv), vv.at()], inc=mm)
                    P.do("vector",
                         lambda e, sl2=sl2: e.tensor_copy(
                             out=ptb[:, sl2 * 128:(sl2 + 1) * 128],
                             in_=psT[:, sl2 * 128:(sl2 + 1) * 128]),
                         after=[mm.at()], inc=vv)
                    P.do("tensor",
                         lambda e, kb=kb, sl2=sl2, h=h, nkb=nkb: e.matmul(
                             psctx[:], v_v[:, kb, h * 64:(h + 1) * 64],
                             ptb[:, sl2 * 128:(sl2 + 1) * 128],
                             start=(kb == 0), stop=(kb == nkb - 1)),
                         after=[vv.at(), sc.at()], inc=mm)
                P.do("scalar",
                     lambda e, atile=atile, hrow=hrow, qb=qb: e.copy(
                         out=c_v[hrow:hrow + 64, atile, qb * 128:(qb + 1) * 128],
                         in_=psctx[:]),
                     after=[mm.at()], inc=sc)
        for a in range(4):
            P.dma("sync", ctx_out[a * 128:(a + 1) * 128, :], c_v[:, a, :],
                  after=[sc.at(), vv.at()], inc=st)
        P.wait("gpsimd", [st.at(), ld.at(), gp.at()])
        P.realize()
    return nc


NE1 = 31

def build_phase_e(debug=False):
    nc = bass.Bass("TRN2", target_bir_lowering=False, debug=False)
    x3_in = nc.dram_tensor("x3_in", [TA, DM], F32, kind="ExternalInput").ap()
    xn_in = nc.dram_tensor("xn_in", [TA, DM], F32, kind="ExternalInput").ap()
    ctx_in = nc.dram_tensor("ctx_in", [DM, TA], BF16, kind="ExternalInput").ap()
    owT = nc.dram_tensor("owT", [DM, DM], F32, kind="ExternalInput").ap()
    ob_in = nc.dram_tensor("ob_in", [1, DM], F32, kind="ExternalInput").ap()
    gamma2 = nc.dram_tensor("gamma2", [1, DM], F32, kind="ExternalInput").ap()
    shw1 = nc.dram_tensor("shw1", [DM, DM], F32, kind="ExternalInput").ap()
    shb1_rs = nc.dram_tensor("shb1_rs", [128, 8], F32, kind="ExternalInput").ap()
    shw2 = nc.dram_tensor("shw2", [DM, DM], F32, kind="ExternalInput").ap()
    shb2 = nc.dram_tensor("shb2", [1, DM], F32, kind="ExternalInput").ap()
    rw = nc.dram_tensor("rw", [DM, NE1], F32, kind="ExternalInput").ap()
    rb = nc.dram_tensor("rb", [1, NE1], F32, kind="ExternalInput").ap()
    rew1 = nc.dram_tensor("rew1", [NE1, DM, DM], F32, kind="ExternalInput").ap()
    reb1T = nc.dram_tensor("reb1T", [DM, NE1], F32, kind="ExternalInput").ap()
    rew2 = nc.dram_tensor("rew2", [NE1, DM, DM], F32, kind="ExternalInput").ap()
    reb2 = nc.dram_tensor("reb2", [NE1, DM], F32, kind="ExternalInput").ap()
    out_f = nc.dram_tensor("out_f", [TA, DM], F32, kind="ExternalOutput").ap()
    knd = "ExternalOutput" if debug else "Internal"
    ao_hbm = nc.dram_tensor("ao_hbm", [TA, DM], F32, kind=knd).ap()
    an_hbm = nc.dram_tensor("an_hbm", [TA, DM], F32, kind=knd).ap()
    dbgE = nc.dram_tensor("dbgE", [128, 1024], F32, kind=knd).ap()

    from contextlib import ExitStack
    with ExitStack() as _stk:
        ec = _stk.enter_context
        anf = ec(nc.sbuf_tensor("anfE", [128, 8 * TA], BF16))
        acc = ec(nc.sbuf_tensor("accE", [128, 16 * DM], F32))
        hf = ec(nc.sbuf_tensor("hfE", [128, 8 * TA], BF16))
        wbuf = ec(nc.sbuf_tensor("wbufE", [128, 24576], BF16))
        ctxf = ec(nc.sbuf_tensor("ctxfE", [128, 8 * 128], BF16))
        gates = ec(nc.sbuf_tensor("gatesE", [128, 16 * NE1], F32))
        gatesb = ec(nc.sbuf_tensor("gatesbE", [128, NE1], BF16))
        gatesT = ec(nc.sbuf_tensor("gatesTE", [31, 16 * 128], BF16))
        rw_sb = ec(nc.sbuf_tensor("rw_sbE", [128, 8 * NE1], F32))
        reb2_sb = ec(nc.sbuf_tensor("reb2E", [31, DM], BF16))
        b1c = ec(nc.sbuf_tensor("b1cE", [128, 8 + 8 * NE1], F32))
        biasr = ec(nc.sbuf_tensor("biasrE", [1, 3 * DM + NE1], BF16))
        ones_r = ec(nc.sbuf_tensor("ones_rE", [1, 128], BF16))
        ident_bf = ec(nc.sbuf_tensor("identE", [128, 128], BF16))
        ident32 = ec(nc.sbuf_tensor("ident32E", [128, 128], F32))

        scr2 = ec(nc.sbuf_tensor("scr2E", [128, 64], F32))
        smE = ec(nc.sbuf_tensor("smE", [128, 32], F32))
        psq = ec(nc.psum_tensor("psqE", [128, 512], F32))
        psq2 = ec(nc.psum_tensor("psq2E", [128, 512], F32))
        pst = ec(nc.psum_tensor("pstE", [128, DM], BF16))
        pst32 = ec(nc.psum_tensor("pst32E", [128, DM], F32))
        psr = ec(nc.psum_tensor("psrE", [128, 128], F32))
        s_ld = ec(nc.semaphore("s_ld")); s_gp = ec(nc.semaphore("s_gp"))
        s_mm = ec(nc.semaphore("s_mm")); s_sc = ec(nc.semaphore("s_sc"))
        s_vv = ec(nc.semaphore("s_vv")); s_st = ec(nc.semaphore("s_st"))
        P = Prog(nc)
        ld = Ctr(s_ld); gp = Ctr(s_gp); mm = Ctr(s_mm)
        sc = Ctr(s_sc); vv = Ctr(s_vv); st = Ctr(s_st)

        anf_v = anf[:].rearrange("p (h t) -> p h t", h=8)
        ant32 = hf[:, 0:2048].bitcast(F32)
        gamb = hf[:, 2048:4096].bitcast(F32)
        h2 = hf[:, 4096:6144].bitcast(F32)
        h1 = hf[:, 6144:8192].bitcast(F32)
        hb = hf[:, 8192:9216]
        hf_v = hf[:].rearrange("p (h t) -> p h t", h=8)
        wv = [wbuf[:, 0:8192].rearrange("p (h o) -> p h o", h=8),
              wbuf[:, 8192:16384].rearrange("p (h o) -> p h o", h=8)]
        ctx_v = ctxf[:].rearrange("p (h t) -> p h t", h=8)
        rw_v = rw_sb[:].rearrange("p (h e) -> p h e", h=8)
        shb1_c = b1c[:, 0:8]
        reb1_c = b1c[:, 8:8 + 8 * NE1].rearrange("p (it e) -> p it e", it=8)
        rs_c = smE[:, 0:1]; ssq_c = smE[:, 1:2]; mean_c = smE[:, 2:3]
        var_c = smE[:, 3:4]; rstd_c = smE[:, 4:5]; msq_c = smE[:, 5:6]
        eps_c = smE[:, 6:7]; mx_c = smE[:, 8:16]
        rmax_c = smE[:, 16:17]; rsum_c = smE[:, 17:18]; rinv_c = smE[:, 18:19]

        P.do("gpsimd", lambda e: e.memset(ident_bf[:], 0.0), inc=gp, n=1)
        P.do("gpsimd", lambda e: e.affine_select(
            out=ident_bf[:], in_=ident_bf[:], compare_op=ALU.not_equal, fill=1.0,
            base=0, pattern=[[-1, 128]], channel_multiplier=1), inc=gp, n=1)
        P.do("gpsimd", lambda e: e.memset(ident32[:], 0.0), inc=gp, n=1)
        P.do("gpsimd", lambda e: e.affine_select(
            out=ident32[:], in_=ident32[:], compare_op=ALU.not_equal, fill=1.0,
            base=0, pattern=[[-1, 128]], channel_multiplier=1), inc=gp, n=1)
        P.do("gpsimd", lambda e: e.memset(ones_r[:], 1.0), inc=gp, n=1)
        gi_const = gp.val
        P.do("vector", lambda e: e.memset(eps_c, 1e-5), inc=vv)
        P.dma("sync", gamb[:], gamma2.broadcast_to([128, DM]), inc=ld)
        P.dma("sync", shb1_c, shb1_rs, inc=ld)
        P.dma("sync", reb1_c, reb1T.rearrange("(it p) e -> p it e", p=128), inc=ld)
        P.dma("sync", rw_v, rw.rearrange("(h p) e -> p h e", p=128), inc=ld)
        P.dma("gpsimd", reb2_sb[:], reb2, inc=gp)
        P.dma("gpsimd", biasr[0:1, 0:DM], ob_in, inc=gp)
        P.dma("gpsimd", biasr[0:1, DM:2 * DM], shb2, inc=gp)
        P.dma("gpsimd", biasr[0:1, 2 * DM:2 * DM + NE1], rb, inc=gp)
        consts_ld = ld.val; consts_gp = gp.val

        P.dma("gpsimd", wv[0], owT.rearrange("(a p) o -> p a o", p=128), inc=gp)
        ow_ld = gp.val
        for tt in range(16):
            r0, r1 = tt * 128, (tt + 1) * 128
            P.dma("gpsimd", ctx_v,
                  ctx_in.rearrange("(h p) t -> p h t", p=128)[:, :, r0:r1],
                  after=[mm.at()], inc=gp)
            ctx_ld = gp.val
            P.dma("sync", h2, xn_in[r0:r1, :], after=[vv.at(), sc.at()], inc=ld)
            xn_ld = ld.val
            for os_ in range(2):
                for ht in range(8):
                    P.do("tensor",
                         lambda e, os_=os_, ht=ht: e.matmul(
                             psq[:], ctx_v[:, ht, :],
                             wv[0][:, ht, os_ * 512:(os_ + 1) * 512],
                             start=(ht == 0), stop=False),
                         after=[gp.at(ctx_ld), gp.at(ow_ld), sc.at(), vv.at()]
                         if ht == 0 else (), inc=None)
                P.do("tensor",
                     lambda e, os_=os_: e.matmul(
                         psq[:], ones_r[:], biasr[0:1, os_ * 512:(os_ + 1) * 512],
                         start=False, stop=True), after=[gp.at(consts_gp)], inc=mm)
                P.do("scalar",
                     lambda e, os_=os_: e.copy(out=h1[:, os_ * 512:(os_ + 1) * 512],
                                               in_=psq[:]),
                     after=[mm.at()], inc=sc)
            P.strict = {"vector": [vv, sc], "scalar": [sc, vv]}
            P.do("vector", lambda e: e.tensor_tensor(out=h1[:], in0=h1[:], in1=h2[:],
                                                     op=ALU.add),
                 after=[ld.at(xn_ld)], inc=vv)
            ao_vv = vv.val
            P.dma("sync", ao_hbm[r0:r1, :], h1[:], after=[vv.at(ao_vv)], inc=st)
            P.do("scalar", lambda e: e.activation(out=h2, in_=h1[:], func=AF.Square,
                                                  accum_out=ssq_c), inc=sc)
            P.do("scalar", lambda e: e.activation(out=rstd_c, in_=ssq_c, func=AF.Ln,
                                                  scale=1.0 / DM, bias=eps_c), inc=sc)
            P.do("scalar", lambda e: e.activation(out=rstd_c, in_=rstd_c, func=AF.Exp,
                                                  scale=-0.5), inc=sc)
            P.do("vector", lambda e: e.tensor_scalar(out=h1[:], in0=h1[:], scalar1=rstd_c,
                                                     scalar2=None, op0=ALU.mult), inc=vv)
            P.do("vector", lambda e: e.tensor_tensor(out=h1[:], in0=h1[:], in1=gamb[:],
                                                     op=ALU.mult),
                 after=[ld.at(consts_ld)], inc=vv)
            P.strict = {}
            an_vv = vv.val
            P.dma("sync", an_hbm[r0:r1, :], h1[:], after=[vv.at(an_vv)], inc=st)
            P.do("vector", lambda e: e.tensor_copy(out=hb[:], in_=h1[:]),
                 after=[vv.at(an_vv)], inc=vv)
            for ht in range(8):
                P.do("tensor",
                     lambda e, ht=ht: e.transpose(
                         pst[:, ht * 128:(ht + 1) * 128],
                         hb[:, ht * 128:(ht + 1) * 128], ident_bf[:]),
                     after=[vv.at(), gp.at(gi_const), sc.at()] if ht == 0 else (),
                     inc=mm if ht == 7 else None)
            P.do("vector",
                 lambda e, tt=tt: e.tensor_copy(
                     out=anf_v[:, :, tt * 128:(tt + 1) * 128],
                     in_=pst[:].rearrange("p (h t) -> p h t", h=8)),
                 after=[mm.at()], inc=vv)
            for ht in range(8):
                P.do("tensor",
                     lambda e, ht=ht: e.transpose(
                         pst32[:, ht * 128:(ht + 1) * 128],
                         h1[:, ht * 128:(ht + 1) * 128], ident32[:]),
                     after=[vv.at(), sc.at()] if ht == 0 else (),
                     inc=mm if ht == 7 else None)
            P.do("vector", lambda e: e.tensor_copy(out=ant32[:], in_=pst32[:]),
                 after=[mm.at()], inc=vv)
            t32_vv = vv.val
            for ht in range(8):
                P.do("tensor",
                     lambda e, ht=ht: e.matmul(
                         psr[:, 0:NE1], ant32[:, ht * 128:(ht + 1) * 128],
                         rw_v[:, ht, :], start=(ht == 0), stop=False),
                     after=[vv.at(t32_vv), ld.at(consts_ld), sc.at()] if ht == 0 else (),
                     inc=None)
            P.do("tensor",
                 lambda e: e.matmul(psr[:, 0:NE1], ones_r[:],
                                    biasr[0:1, 2 * DM:2 * DM + NE1],
                                    start=False, stop=True), inc=mm)
            P.strict = {"vector": [vv, sc], "scalar": [sc, vv]}
            P.do("vector", lambda e: e.tensor_reduce(
                out=rmax_c, in_=psr[:, 0:NE1], axis=AX.X, op=ALU.max, negate=True),
                 after=[mm.at()], inc=vv)
            P.do("scalar", lambda e: e.activation(
                out=scr2[:, 0:NE1], in_=psr[:, 0:NE1], func=AF.Exp, bias=rmax_c,
                accum_out=rsum_c), inc=sc)
            P.do("vector", lambda e: e.reciprocal(out=rinv_c, in_=rsum_c), inc=vv)
            P.do("vector", lambda e: e.tensor_scalar(
                out=scr2[:, 0:NE1], in0=scr2[:, 0:NE1], scalar1=rinv_c, scalar2=None,
                op0=ALU.mult), inc=vv)
            P.do("vector", lambda e: e.max(out=mx_c, in_=scr2[:, 0:NE1]), inc=vv)
            P.do("vector", lambda e: e.memset(mx_c[:, 3:8], -1.0), inc=vv)
            P.do("vector", lambda e: e.match_replace(
                out=scr2[:, 32:63], in_to_replace=mx_c, in_values=scr2[:, 0:NE1],
                imm_value=0.0), inc=vv)
            P.do("vector", lambda e, tt=tt: e.tensor_tensor(
                out=gates[:, tt * NE1:(tt + 1) * NE1], in0=scr2[:, 0:NE1],
                in1=scr2[:, 32:63], op=ALU.subtract), inc=vv)
            P.do("vector", lambda e, tt=tt: e.tensor_copy(
                out=gatesb[:], in_=gates[:, tt * NE1:(tt + 1) * NE1]), inc=vv)
            P.strict = {}
            gb_vv = vv.val
            P.do("tensor",
                 lambda e: e.transpose(pst[0:31, 0:128], gatesb[:], ident_bf[:]),
                 after=[vv.at(gb_vv), vv.at(), sc.at()], inc=mm)
            P.do("vector",
                 lambda e, tt=tt: e.tensor_copy(
                     out=gatesT[0:31, tt * 128:(tt + 1) * 128],
                     in_=pst[0:31, 0:128]),
                 after=[mm.at()], inc=vv)
        prep_vv = vv.val; prep_sc = sc.val

        P.dma("gpsimd", wv[0], shw1.rearrange("(a p) o -> p a o", p=128),
              after=[mm.at()], inc=gp)
        sh1_ld = gp.val
        for it in range(8):
            for tsl in range(4):
                for ht in range(8):
                    P.do("tensor",
                         lambda e, it=it, tsl=tsl, ht=ht: e.matmul(
                             psq[:], wv[0][:, ht, it * 128:(it + 1) * 128],
                             anf_v[:, ht, tsl * 512:(tsl + 1) * 512],
                             start=(ht == 0), stop=(ht == 7)),
                         after=[gp.at(sh1_ld), vv.at(prep_vv), sc.at()]
                         if ht == 0 else (), inc=mm if ht == 7 else None)
                P.do("scalar",
                     lambda e, it=it, tsl=tsl: e.activation(
                         out=hf[:, it * TA + tsl * 512: it * TA + (tsl + 1) * 512],
                         in_=psq[:], func=AF.Gelu, bias=shb1_c[:, it:it + 1]),
                     after=[mm.at(), ld.at(consts_ld)], inc=sc)
        P.dma("gpsimd", wv[1], shw2.rearrange("(a p) o -> p a o", p=128),
              after=[mm.at()], inc=gp)
        sh2_ld = gp.val
        for tt in range(16):
            psx = (psq, psq2)[tt % 2]
            for hs in range(2):
                for it in range(8):
                    P.do("tensor",
                         lambda e, tt=tt, hs=hs, it=it, psx=psx: e.matmul(
                             psx[:], hf[:, it * TA + tt * 128: it * TA + (tt + 1) * 128],
                             wv[1][:, it, hs * 512:(hs + 1) * 512],
                             start=(it == 0), stop=False),
                         after=[gp.at(sh2_ld), sc.at(), vv.at()] if it == 0 else (),
                         inc=None)
                P.do("tensor",
                     lambda e, hs=hs, psx=psx: e.matmul(
                         psx[:], ones_r[:], biasr[0:1, DM + hs * 512:DM + (hs + 1) * 512],
                         start=False, stop=True), inc=mm)
                P.do("vector",
                     lambda e, tt=tt, hs=hs, psx=psx: e.tensor_copy(
                         out=acc[:, tt * DM + hs * 512: tt * DM + (hs + 1) * 512],
                         in_=psx[:]),
                     after=[mm.at()], inc=vv)
        shared_vv = vv.val

        w1load_gp = []
        wcons_mm = [0, 0]
        def load_rexpert_w1(e_i, slot):
            deps = [mm.at(wcons_mm[slot])] if wcons_mm[slot] else [mm.at()]
            P.dma("gpsimd",
                  wbuf[:, slot * 8192:(slot + 1) * 8192].rearrange(
                      "p (a o) -> p a o", a=8),
                  rew1[e_i].rearrange("(a p) o -> p a o", p=128), after=deps, inc=gp)
            w1load_gp.append(gp.val)
        rw1_v = [wbuf[:, s * 8192:(s + 1) * 8192].rearrange("p (h o) -> p h o", h=8)
                 for s in range(2)]
        rw2_v = wbuf[:, 16384:24576].rearrange("p (h o) -> p h o", h=8)
        load_rexpert_w1(0, 0)
        load_rexpert_w1(1, 1)
        w2cons_mm = 0
        for e_i in range(NE1):
            slot = e_i % 2
            wg = w1load_gp[e_i]
            P.dma("gpsimd",
                  wbuf[:, 16384:24576].rearrange("p (a o) -> p a o", a=8),
                  rew2[e_i].rearrange("(a p) o -> p a o", p=128),
                  after=[mm.at(w2cons_mm)] if w2cons_mm else [mm.at()], inc=gp)
            w2g = gp.val
            for it in range(8):
                for tsl in range(4):
                    for ht in range(8):
                        P.do("tensor",
                             lambda e, it=it, tsl=tsl, ht=ht, s=slot: e.matmul(
                                 psq[:], rw1_v[s][:, ht, it * 128:(it + 1) * 128],
                                 anf_v[:, ht, tsl * 512:(tsl + 1) * 512],
                                 start=(ht == 0), stop=(ht == 7)),
                             after=([gp.at(wg), vv.at(shared_vv), sc.at()]
                                    if (it == 0 and tsl == 0 and ht == 0) else
                                    ([sc.at()] if ht == 0 else ())),
                             inc=mm if ht == 7 else None)
                    P.do("scalar",
                         lambda e, it=it, tsl=tsl, e_i=e_i: e.activation(
                             out=hf[:, it * TA + tsl * 512: it * TA + (tsl + 1) * 512],
                             in_=psq[:], func=AF.Gelu,
                             bias=reb1_c[:, it, e_i:e_i + 1]),
                         after=[mm.at(), mm.at(w2cons_mm)], inc=sc)
            hf_sc = sc.val
            for tt in range(16):
                psx = (psq2, pst32)[tt % 2]
                for hs in range(2):
                    for it in range(8):
                        P.do("tensor",
                             lambda e, tt=tt, hs=hs, it=it, psx=psx: e.matmul(
                                 psx[:, 0:512],
                                 hf[:, it * TA + tt * 128: it * TA + (tt + 1) * 128],
                                 rw2_v[:, it, hs * 512:(hs + 1) * 512],
                                 start=(it == 0), stop=(it == 7)),
                             after=[sc.at(hf_sc), vv.at(), gp.at(w2g)]
                             if it == 0 else (), inc=mm if it == 7 else None)
                    P.do("vector",
                         lambda e, tt=tt, hs=hs, e_i=e_i, psx=psx: e.scalar_tensor_tensor(
                             out=acc[:, tt * DM + hs * 512: tt * DM + (hs + 1) * 512],
                             in0=psx[:, 0:512],
                             scalar=gates[:, tt * NE1 + e_i: tt * NE1 + e_i + 1],
                             in1=acc[:, tt * DM + hs * 512: tt * DM + (hs + 1) * 512],
                             op0=ALU.mult, op1=ALU.add),
                         after=[mm.at()], inc=vv)
            w2cons_mm = mm.val
            if e_i + 2 < NE1:
                wcons_mm[slot] = mm.val
                load_rexpert_w1(e_i + 2, slot)
        routed_vv = vv.val

        for tt in range(16):
            P.do("tensor",
                 lambda e, tt=tt: e.matmul(
                     psq[:], gatesT[0:31, tt * 128:(tt + 1) * 128], reb2_sb[0:31, 0:512],
                     start=True, stop=True),
                 after=[gp.at(consts_gp), vv.at(), sc.at()], inc=mm)
            P.do("tensor",
                 lambda e, tt=tt: e.matmul(
                     psq2[:], gatesT[0:31, tt * 128:(tt + 1) * 128],
                     reb2_sb[0:31, 512:1024], start=True, stop=True), inc=mm)
            P.do("vector",
                 lambda e, tt=tt: e.tensor_tensor(
                     out=acc[:, tt * DM: tt * DM + 512],
                     in0=acc[:, tt * DM: tt * DM + 512], in1=psq[:], op=ALU.add),
                 after=[mm.at()], inc=vv)
            P.do("vector",
                 lambda e, tt=tt: e.tensor_tensor(
                     out=acc[:, tt * DM + 512: tt * DM + 1024],
                     in0=acc[:, tt * DM + 512: tt * DM + 1024], in1=psq2[:], op=ALU.add),
                 inc=vv)
        moe_vv = vv.val

        for tt in range(16):
            r0, r1 = tt * 128, (tt + 1) * 128
            P.dma("sync", h1[:], x3_in[r0:r1, :], after=[vv.at(), sc.at()], inc=ld)
            P.dma("sync", h2, an_hbm[r0:r1, :], after=[st.at(), mm.at()], inc=ld)
            h_ld = ld.val
            P.strict = {"vector": [vv, sc], "scalar": [sc, vv]}
            P.do("vector", lambda e: e.tensor_tensor(out=h1[:], in0=h1[:], in1=h2[:],
                                                     op=ALU.add),
                 after=[ld.at(h_ld), vv.at(moe_vv)], inc=vv)
            P.dma("sync", h2, ao_hbm[r0:r1, :], after=[vv.at()], inc=ld)
            P.do("vector", lambda e: e.tensor_tensor(out=h1[:], in0=h1[:], in1=h2[:],
                                                     op=ALU.add),
                 after=[ld.at()], inc=vv)
            P.do("vector", lambda e, tt=tt: e.tensor_tensor(
                out=h1[:], in0=h1[:], in1=acc[:, tt * DM:(tt + 1) * DM], op=ALU.add),
                 inc=vv)
            P.do("vector", lambda e: e.reduce_sum(out=rs_c, in_=h1[:], axis=AX.X), inc=vv)
            P.do("scalar", lambda e: e.activation(out=h2, in_=h1[:], func=AF.Square,
                                                  accum_out=ssq_c), inc=sc)
            P.do("vector", lambda e: e.tensor_scalar_mul(mean_c, rs_c, 1.0 / DM), inc=vv)
            P.do("vector", lambda e: e.tensor_mul(msq_c, mean_c, mean_c), inc=vv)
            P.do("vector", lambda e: e.tensor_scalar_mul(var_c, ssq_c, 1.0 / DM), inc=vv)
            P.do("vector", lambda e: e.tensor_sub(var_c, var_c, msq_c), inc=vv)
            P.do("scalar", lambda e: e.activation(out=rstd_c, in_=var_c, func=AF.Ln,
                                                  bias=eps_c), inc=sc)
            P.do("scalar", lambda e: e.activation(out=rstd_c, in_=rstd_c, func=AF.Exp,
                                                  scale=-0.5), inc=sc)
            P.do("vector", lambda e: e.tensor_scalar(out=h1[:], in0=h1[:], scalar1=mean_c,
                                                     scalar2=rstd_c, op0=ALU.subtract,
                                                     op1=ALU.mult), inc=vv)
            P.strict = {}
            P.dma("sync", out_f[r0:r1, :], h1[:], after=[vv.at()], inc=st)
        P.wait("gpsimd", [st.at(), ld.at(), gp.at()])
        P.realize()
    return nc


def prep_phase_c_inputs(inputs, x3_list):
    g = lambda k: np.asarray(inputs["L1_ffn_" + k])
    shared = {
        "qwT": np.ascontiguousarray(g("q_w")), "kwT": np.ascontiguousarray(g("k_w")),
        "vwT": np.ascontiguousarray(g("v_w")),
        "biases": np.ascontiguousarray(np.stack([g("q_b"), g("k_b"), g("v_b")])),
        "gamma1": np.ascontiguousarray(g("gamma1").reshape(1, DM)),
        "theta": np.ascontiguousarray(g("theta").reshape(1, 32)),
    }
    maps = []
    for c in range(8):
        half = c % 2
        pos = (half * TA + np.arange(16)[None, :] * 128
               + np.arange(128)[:, None]).astype(np.float32)
        m = dict(shared)
        m["x_in"] = np.ascontiguousarray(np.asarray(x3_list[c]))
        m["posv"] = np.ascontiguousarray(pos)
        maps.append(m)
    return maps

def run_phase_c(inputs, x3_list):
    nc = build_phase_c()
    res = run_bass_kernel_spmd(nc, prep_phase_c_inputs(inputs, x3_list),
                               core_ids=list(range(8)))
    return res.results

def prep_phase_d_inputs(c_res):
    maps = []
    for c in range(8):
        b, hg = c // 2, c % 2
        rsl = slice(hg * 512, (hg + 1) * 512)
        qn = np.concatenate([np.asarray(c_res[2 * b]["qn_out"]),
                             np.asarray(c_res[2 * b + 1]["qn_out"])], axis=1)
        kn = np.concatenate([np.asarray(c_res[2 * b]["kn_out"]),
                             np.asarray(c_res[2 * b + 1]["kn_out"])], axis=1)
        vn = np.concatenate([np.asarray(c_res[2 * b]["vn_out"]),
                             np.asarray(c_res[2 * b + 1]["vn_out"])], axis=0)
        maps.append({
            "qn": np.ascontiguousarray(qn[rsl]),
            "kn": np.ascontiguousarray(kn[rsl]),
            "vn": np.ascontiguousarray(vn[:, rsl]),
        })
    return maps

def run_phase_d(c_res):
    nc = build_phase_d()
    res = run_bass_kernel_spmd(nc, prep_phase_d_inputs(c_res),
                               core_ids=list(range(8)))
    return [r["ctx_out"] for r in res.results]

def prep_phase_e_inputs(inputs, x3_list, c_res, ctx_list):
    g = lambda k: np.asarray(inputs["L1_ffn_" + k])
    shared = {
        "owT": np.ascontiguousarray(g("o_w")),
        "ob_in": np.ascontiguousarray(g("o_b").reshape(1, DM)),
        "gamma2": np.ascontiguousarray(g("gamma2").reshape(1, DM)),
        "shw1": np.ascontiguousarray(g("sh_w1")),
        "shb1_rs": np.ascontiguousarray(g("sh_b1").reshape(8, 128).T),
        "shw2": np.ascontiguousarray(g("sh_w2")),
        "shb2": np.ascontiguousarray(g("sh_b2").reshape(1, DM)),
        "rw": np.ascontiguousarray(g("router_w")),
        "rb": np.ascontiguousarray(g("router_b").reshape(1, NE1)),
        "rew1": np.ascontiguousarray(g("re_w1")),
        "reb1T": np.ascontiguousarray(g("re_b1").T),
        "rew2": np.ascontiguousarray(g("re_w2")),
        "reb2": np.ascontiguousarray(g("re_b2")),
    }
    maps = []
    for c in range(8):
        b, half = c // 2, c % 2
        ctx_b = np.concatenate([np.asarray(ctx_list[2 * b]),
                                np.asarray(ctx_list[2 * b + 1])], axis=0)
        m = dict(shared)
        m["x3_in"] = np.ascontiguousarray(np.asarray(x3_list[c]))
        m["xn_in"] = np.ascontiguousarray(np.asarray(c_res[c]["xn_out"]))
        m["ctx_in"] = np.ascontiguousarray(ctx_b[:, half * TA:(half + 1) * TA])
        maps.append(m)
    return maps

def run_phase_e(inputs, x3_list, c_res, ctx_list, debug=False):
    nc = build_phase_e(debug=debug)
    res = run_bass_kernel_spmd(nc, prep_phase_e_inputs(inputs, x3_list, c_res, ctx_list),
                               core_ids=list(range(8)))
    if debug:
        return res.results
    return [r["out_f"] for r in res.results]


def kernel(**inputs):
    m_outs = run_phase_m(inputs)
    x2_list = run_phase_a(inputs, m_outs)
    x3_list = run_phase_b(inputs, x2_list)
    c_res = run_phase_c(inputs, x3_list)
    ctx_list = run_phase_d(c_res)
    out_list = run_phase_e(inputs, x3_list, c_res, ctx_list)
    out = np.concatenate([np.asarray(o) for o in out_list], axis=0)
    return out.reshape(4, 4096, 1024).astype(np.float32)


# version 02
# speedup vs baseline: 1.0495x; 1.0495x over previous
import numpy as np
import ml_dtypes
import concourse.bass as bass
import concourse.mybir as mybir
from concourse.bass_utils import run_bass_kernel_spmd

F32 = mybir.dt.float32
BF16 = mybir.dt.bfloat16
AF = mybir.ActivationFunctionType
ALU = mybir.AluOpType
AX = mybir.AxisListType

BF = ml_dtypes.bfloat16


class Ctr:
    __slots__ = ("sem", "val")
    def __init__(self, sem):
        self.sem = sem
        self.val = 0
    def at(self, v=None):
        return (self, self.val if v is None else v)

class Prog:
    ENGINES = ("sync", "tensor", "vector", "scalar", "gpsimd")
    def __init__(self, nc):
        self.nc = nc
        self.ops = {e: [] for e in self.ENGINES}
        self.strict = {}
    def do(self, eng, fn, after=(), inc=None, n=1):
        waits = []
        sctr = self.strict.get(eng)
        if sctr is not None:
            for c in (sctr if isinstance(sctr, (list, tuple)) else [sctr]):
                if c.val > 0:
                    waits.append((c.sem, c.val))
        for a in after:
            ctr, v = a if isinstance(a, tuple) else (a, a.val)
            if v > 0:
                waits.append((ctr.sem, v))
        self.ops[eng].append((waits, fn, inc.sem if inc is not None else None, n))
        if inc is not None:
            inc.val += n
        return inc.val if inc is not None else None
    def wait(self, eng, after):
        self.do(eng, None, after=after)
    def dma(self, q, out, in_, after=(), inc=None):
        return self.do(q, lambda e, o=out, i=in_: e.dma_start(out=o, in_=i),
                       after=after, inc=inc, n=16)
    def realize(self):
        with self.nc.Block() as blk:
            for e in self.ENGINES:
                ops = self.ops[e]
                if not ops:
                    continue
                dec = getattr(blk, e)
                def body(eng, ops_=ops):
                    lastw = {}
                    for waits, fn, sem, n in ops_:
                        for (s, v) in waits:
                            k = id(s)
                            if lastw.get(k, -1) < v:
                                eng.wait_ge(s, v)
                                lastw[k] = v
                        if fn is None:
                            continue
                        inst = fn(eng)
                        if sem is not None:
                            inst.then_inc(sem, n)
                dec(body)


DM = 1024
DI = 2048
NST = 16
DTR = 64
T = 4096
TCH = 512
NCH = T // TCH

def declare_m_io(nc, debug=False):
    io = {}
    io["xT"] = nc.dram_tensor("xT", [DM, T], F32, kind="ExternalInput").ap()
    io["in_wT"] = nc.dram_tensor("in_wT", [DM, 2 * DI], F32, kind="ExternalInput").ap()
    io["xp_wT"] = nc.dram_tensor("xp_wT", [DI, 96], F32, kind="ExternalInput").ap()
    io["dtp_wT"] = nc.dram_tensor("dtp_wT", [DTR, DI], F32, kind="ExternalInput").ap()
    io["out_wT"] = nc.dram_tensor("out_wT", [DI, DM], F32, kind="ExternalInput").ap()
    io["conv_w_rs"] = nc.dram_tensor("conv_w_rs", [128, 64], F32, kind="ExternalInput").ap()
    io["conv_b_rs"] = nc.dram_tensor("conv_b_rs", [128, 16], F32, kind="ExternalInput").ap()
    io["dtp_b_rs"] = nc.dram_tensor("dtp_b_rs", [128, 16], F32, kind="ExternalInput").ap()
    io["A_log_rs"] = nc.dram_tensor("A_log_rs", [128, 256], F32, kind="ExternalInput").ap()
    io["Dvec_rs"] = nc.dram_tensor("Dvec_rs", [128, 16], F32, kind="ExternalInput").ap()
    io["m_out"] = nc.dram_tensor("m_out", [T, DM], F32, kind="ExternalOutput").ap()
    knd = "ExternalOutput" if debug else "Internal"
    io["xc_hbm"] = nc.dram_tensor("xc_hbm", [DI, T], F32, kind=knd).ap()
    io["sz_hbm"] = nc.dram_tensor("sz_hbm", [DI, T], F32, kind=knd).ap()
    io["bc_hbm"] = nc.dram_tensor("bc_hbm", [32, T], F32, kind=knd).ap()
    io["dbg_hbm"] = nc.dram_tensor("dbg_hbm", [128, 4 * TCH], F32, kind=knd).ap()
    return io

def emit_phase_m(nc, P, ec, io, pfx="m_", debug=False):
    xT = io["xT"]; in_wT = io["in_wT"]; xp_wT = io["xp_wT"]; dtp_wT = io["dtp_wT"]
    out_wT = io["out_wT"]; conv_w_rs = io["conv_w_rs"]; conv_b_rs = io["conv_b_rs"]
    dtp_b_rs = io["dtp_b_rs"]; A_log_rs = io["A_log_rs"]; Dvec_rs = io["Dvec_rs"]
    m_out = io["m_out"]; xc_hbm = io["xc_hbm"]; sz_hbm = io["sz_hbm"]
    bc_hbm = io["bc_hbm"]; dbg_hbm = io["dbg_hbm"]
    arena_a = ec(nc.sbuf_tensor(pfx + "arena_a", [128, 65536], BF16))
    arena_b = ec(nc.sbuf_tensor(pfx + "arena_b", [128, 16384], F32))
    xp_sb = ec(nc.sbuf_tensor(pfx + "xp_sb", [128, 16 * 96], BF16))
    dtpw_sb = ec(nc.sbuf_tensor(pfx + "dtpw_sb", [64, DI], BF16))
    smalls = ec(nc.sbuf_tensor(pfx + "smalls", [128, 672], F32))
    ps0 = ec(nc.psum_tensor(pfx + "ps0", [128, 512], F32))
    ps1 = ec(nc.psum_tensor(pfx + "ps1", [128, 512], F32))
    psx = ec(nc.psum_tensor(pfx + "psx", [128, 512], F32))
    psd = ec(nc.psum_tensor(pfx + "psd", [128, 512], F32))
    pso0 = ec(nc.psum_tensor(pfx + "pso0", [128, 512], F32))
    pso1 = ec(nc.psum_tensor(pfx + "pso1", [128, 512], F32))
    s_ld = ec(nc.semaphore(pfx + "s_ld"))
    s_gp = ec(nc.semaphore(pfx + "s_gp"))
    s_mm = ec(nc.semaphore(pfx + "s_mm"))
    s_sc = ec(nc.semaphore(pfx + "s_sc"))
    s_vv = ec(nc.semaphore(pfx + "s_vv"))
    s_st = ec(nc.semaphore(pfx + "s_st"))
    ld = Ctr(s_ld); gp = Ctr(s_gp); mm = Ctr(s_mm)
    sc = Ctr(s_sc); vv = Ctr(s_vv); st = Ctr(s_st)

    aa = arena_a
    xT_sb = aa[:, 0:32768].rearrange("p (h t) -> p h t", h=8)
    inw_sb = aa[:, 32768:65536].rearrange("p (h c) -> p h c", h=8)
    ab = arena_b
    xi_buf = ab[:, 0:4100]
    cv_buf = ab[:, 4100:8196]
    xdbl = ab[:, 8196:12292]
    xc_bf = ab[:, 12292:14340].bitcast(BF16)
    dt_bf = ab[:, 12292:14340].bitcast(BF16)[0:64, :]

    convw_c = smalls[:, 0:64]
    convb_c = smalls[:, 64:80]
    dtpb_c = smalls[:, 80:96]
    Alog_c = smalls[:, 96:352]
    A_c = smalls[:, 352:608]
    Dvec_c = smalls[:, 608:624]

    for h in range(8):
        P.dma("gpsimd", xT_sb[:, h, :], xT[h * 128:(h + 1) * 128, :], inc=gp)
        P.dma("gpsimd", inw_sb[:, h, :], in_wT[h * 128:(h + 1) * 128, :], inc=gp)
    P.dma("gpsimd", xp_sb[:].rearrange("p (a c) -> p a c", a=16),
          xp_wT.rearrange("(a p) c -> p a c", p=128), inc=gp)
    P.dma("gpsimd", dtpw_sb[:], dtp_wT, inc=gp)
    P.dma("sync", convw_c, conv_w_rs, inc=ld)
    P.dma("sync", convb_c, conv_b_rs, inc=ld)
    P.dma("sync", dtpb_c, dtp_b_rs, inc=ld)
    P.dma("sync", Alog_c, A_log_rs, inc=ld)
    P.dma("sync", Dvec_c, Dvec_rs, inc=ld)
    ld_smalls = ld.val
    P.do("scalar", lambda e: e.activation(out=A_c, in_=Alog_c, func=AF.Exp),
         after=[ld.at(ld_smalls)], inc=sc)
    P.do("vector", lambda e: e.tensor_scalar_mul(A_c, A_c, -1.0),
         after=[sc.at()], inc=vv)

    gp_stage0 = gp.val

    conv_done_vv = 0
    xp_read_mm = 0
    last_xc_store = 0
    for ct in range(32):
        is_z = ct >= 16
        dtile = ct % 16
        if ct == 0:
            P.do("vector", lambda e: e.memset(xi_buf[:, 0:3], 0.0), inc=vv)
            P.do("vector", lambda e: e.memset(xi_buf[:, 4099:4100], 0.0), inc=vv)
        evac_sc_vals = []
        for ts in range(8):
            ps = (ps0, ps1)[ts % 2]
            deps = [gp.at(gp_stage0)]
            if ts >= 2:
                deps.append(sc.at(evac_sc_vals[ts - 2]))
            if not is_z and ts == 0 and conv_done_vv:
                deps.append(vv.at(conv_done_vv))
            for h in range(8):
                P.do("tensor",
                     lambda e, ps=ps, h=h, ct=ct, ts=ts: e.matmul(
                         ps[:], inw_sb[:, h, ct * 128:(ct + 1) * 128],
                         xT_sb[:, h, ts * 512:(ts + 1) * 512],
                         start=(h == 0), stop=(h == 7)),
                     after=deps if h == 0 else (), inc=mm if h == 7 else None)
            evac_deps = [mm.at(), st.at(last_xc_store)]
            if is_z and ts == 0:
                evac_deps.append(vv.at(conv_done_vv))
            if is_z:
                P.do("scalar",
                     lambda e, ps=ps, ts=ts: e.activation(
                         out=cv_buf[:, ts * 512:(ts + 1) * 512], in_=ps[:], func=AF.Silu),
                     after=evac_deps, inc=sc)
            else:
                P.do("scalar",
                     lambda e, ps=ps, ts=ts: e.copy(
                         out=xi_buf[:, 3 + ts * 512:3 + (ts + 1) * 512], in_=ps[:]),
                     after=evac_deps, inc=sc)
            evac_sc_vals.append(sc.val)
        if is_z:
            P.dma("sync", sz_hbm[dtile * 128:(dtile + 1) * 128, :], cv_buf[:, 0:4096],
                  after=[sc.at()], inc=st)
            last_xc_store = st.val
        else:
            P.do("vector",
                 lambda e, d=dtile: e.tensor_scalar(
                     out=cv_buf[:, 0:4096], in0=xi_buf[:, 0:4096],
                     scalar1=convw_c[:, d * 4:d * 4 + 1], scalar2=None, op0=ALU.mult),
                 after=[sc.at(), ld.at(ld_smalls)], inc=vv)
            for k in range(1, 4):
                P.do("vector",
                     lambda e, d=dtile, k=k: e.scalar_tensor_tensor(
                         out=cv_buf[:, 0:4096], in0=xi_buf[:, k:k + 4096],
                         scalar=convw_c[:, d * 4 + k:d * 4 + k + 1],
                         in1=cv_buf[:, 0:4096], op0=ALU.mult, op1=ALU.add),
                     inc=vv)
            P.do("scalar",
                 lambda e, d=dtile: e.activation(
                     out=xi_buf[:, 3:4099], in_=cv_buf[:, 0:4096], func=AF.Silu,
                     bias=convb_c[:, d:d + 1]),
                 after=[vv.at()], inc=sc)
            sc_silu = sc.val
            P.dma("sync", xc_hbm[dtile * 128:(dtile + 1) * 128, :], xi_buf[:, 3:4099],
                  after=[sc.at(sc_silu)], inc=st)
            last_xc_store = st.val
            P.do("vector",
                 lambda e: e.tensor_copy(out=xc_bf[:, :], in_=xi_buf[:, 3:4099]),
                 after=[sc.at(sc_silu), mm.at(xp_read_mm)], inc=vv)
            vv_cast = vv.val
            for ts in range(8):
                P.do("tensor",
                     lambda e, d=dtile, ts=ts: e.matmul(
                         psx[0:96, :], xp_sb[:, d * 96:(d + 1) * 96],
                         xc_bf[:, ts * 512:(ts + 1) * 512], start=True, stop=True),
                     after=[vv.at(vv_cast)] + ([vv.at()] if ts > 0 or dtile > 0 else []),
                     inc=mm)
                if dtile == 0:
                    P.do("vector",
                         lambda e, ts=ts: e.tensor_copy(
                             out=xdbl[0:96, ts * 512:(ts + 1) * 512], in_=psx[0:96, :]),
                         after=[mm.at()], inc=vv)
                else:
                    P.do("vector",
                         lambda e, ts=ts: e.tensor_tensor(
                             out=xdbl[0:96, ts * 512:(ts + 1) * 512],
                             in0=xdbl[0:96, ts * 512:(ts + 1) * 512],
                             in1=psx[0:96, :], op=ALU.add),
                         after=[mm.at()], inc=vv)
            xp_read_mm = mm.val
            conv_done_vv = vv.val

    P.dma("sync", bc_hbm, xdbl[64:96, :], after=[vv.at()], inc=st)
    bc_stored_st = st.val
    P.do("vector", lambda e: e.tensor_copy(out=dt_bf[:, :], in_=xdbl[0:64, :]),
         after=[vv.at(), mm.at(xp_read_mm)], inc=vv)
    dt_cast_vv = vv.val

    aaf = arena_a[:].bitcast(F32)
    bc_all = aaf[:, 0:16384].rearrange("p (j t) -> p j t", j=32)
    outw_sb = arena_a[:, 32768:49152].rearrange("p (d m) -> p d m", d=16)
    g_buf = arena_a[:, 49152:57344].rearrange("p (d t) -> p d t", d=16)
    dl_buf = ab[:, 0:512]
    w_buf = ab[:, 512:1024]
    dA_buf = ab[:, 1024:1536]
    r_buf = ab[:, 1536:2048]
    H_buf = ab[:, 2048:2560]
    y_buf = ab[:, 2560:3072]
    tm_buf = ab[:, 3072:3584]
    vb_buf = ab[:, 3584:4096]
    ab_buf = ab[:, 4096:4608]
    hcar = ab[:, 4608:4864]
    xc_c = [ab[:, 4864 + i * 512: 5376 + i * 512] for i in range(2)]
    sz_c = [ab[:, 5888 + i * 512: 6400 + i * 512] for i in range(2)]
    mo_buf = [ab[:, 6912 + i * 1024: 7936 + i * 1024] for i in range(2)]

    all_stage1_st = st.val
    for d in range(16):
        P.dma("gpsimd", outw_sb[:, d, :], out_wT[d * 128:(d + 1) * 128, :],
              after=[st.at(all_stage1_st)], inc=gp)
    gp_outw = gp.val
    P.do("vector", lambda e: e.memset(hcar, 0.0),
         after=[st.at(all_stage1_st)], inc=vv)

    xc_dma_vals = {}
    sz_dma_vals = {}
    stream_war = {}
    sp_done_vv = 0
    def emit_stream_loads(c, dtile, slot):
        deps = [st.at(all_stage1_st)]
        P.dma("sync", xc_c[slot], xc_hbm[dtile * 128:(dtile + 1) * 128,
                                         c * TCH:(c + 1) * TCH],
              after=deps + stream_war.get(("xc", slot), []), inc=ld)
        xc_dma_vals[(c, dtile)] = ld.val
        P.dma("sync", sz_c[slot], sz_hbm[dtile * 128:(dtile + 1) * 128,
                                         c * TCH:(c + 1) * TCH],
              after=deps + stream_war.get(("sz", slot), []), inc=ld)
        sz_dma_vals[(c, dtile)] = ld.val

    pending = []
    for c in range(NCH):
        pair_idx = c * 16
        bc_deps = [st.at(all_stage1_st)]
        if c > 0:
            bc_deps.append(vv.at())
        for j in range(32):
            P.dma("gpsimd", bc_all[:, j, :],
                  bc_hbm[j:j + 1, c * TCH:(c + 1) * TCH].broadcast_to([128, TCH]),
                  after=bc_deps if j == 0 else (), inc=gp)
        gp_bc = gp.val

        for dtile in range(16):
            slot = dtile % 2
            emit_stream_loads(c, dtile, slot)
            xcv = xc_dma_vals[(c, dtile)]
            szv = sz_dma_vals[(c, dtile)]
            P.do("tensor",
                 lambda e, d=dtile, c=c: e.matmul(
                     psd[:], dtpw_sb[:, d * 128:(d + 1) * 128],
                     dt_bf[:, c * TCH:(c + 1) * TCH], start=True, stop=True),
                 after=[vv.at(dt_cast_vv), sc.at()], inc=mm)
            P.do("scalar",
                 lambda e, d=dtile: e.activation(
                     out=vb_buf, in_=psd[:], func=AF.Identity,
                     bias=dtpb_c[:, d:d + 1]),
                 after=[mm.at(), st.at(all_stage1_st), vv.at(sp_done_vv)], inc=sc)
            P.do("scalar", lambda e: e.activation(out=ab_buf, in_=vb_buf, func=AF.Abs),
                 after=[sc.at()], inc=sc)
            P.do("scalar", lambda e: e.activation(out=ab_buf, in_=ab_buf, func=AF.Exp,
                                                  scale=-1.0), inc=sc)
            P.do("scalar", lambda e: e.activation(out=ab_buf, in_=ab_buf, func=AF.Ln,
                                                  bias=1.0), inc=sc)
            sc_sp = sc.val
            P.do("vector",
                 lambda e: e.tensor_scalar(out=dl_buf, in0=vb_buf, scalar1=0.0,
                                           scalar2=None, op0=ALU.max),
                 after=[sc.at(sc_sp)], inc=vv)
            P.do("vector",
                 lambda e: e.tensor_tensor(out=dl_buf, in0=dl_buf, in1=ab_buf,
                                           op=ALU.add), inc=vv)
            sp_done_vv = vv.val
            P.do("vector",
                 lambda e, s=slot: e.tensor_tensor(out=w_buf, in0=dl_buf,
                                                   in1=xc_c[s], op=ALU.mult),
                 after=[ld.at(xcv)], inc=vv)
            for n in range(16):
                col = dtile * 16 + n
                P.do("scalar",
                     lambda e, col=col: e.activation(
                         out=dA_buf, in_=dl_buf, func=AF.Exp,
                         scale=A_c[:, col:col + 1]),
                     after=[vv.at()], inc=sc)
                P.do("vector",
                     lambda e, n=n: e.tensor_tensor(
                         out=r_buf, in0=w_buf, in1=bc_all[:, n, :], op=ALU.mult),
                     after=[gp.at(gp_bc), sc.at()], inc=vv)
                P.do("vector",
                     lambda e, col=col: e.tensor_tensor_scan(
                         out=H_buf, data0=dA_buf, data1=r_buf,
                         initial=hcar[:, col:col + 1], op0=ALU.mult, op1=ALU.add),
                     inc=vv)
                P.do("vector",
                     lambda e, col=col: e.tensor_copy(
                         out=hcar[:, col:col + 1], in_=H_buf[:, TCH - 1:TCH]),
                     after=[vv.at()], inc=vv)
                if n == 0:
                    P.do("vector",
                         lambda e, n=n: e.tensor_tensor(
                             out=y_buf, in0=H_buf, in1=bc_all[:, 16 + n, :],
                             op=ALU.mult), inc=vv)
                else:
                    P.do("vector",
                         lambda e, n=n: e.tensor_tensor(
                             out=tm_buf, in0=H_buf, in1=bc_all[:, 16 + n, :],
                             op=ALU.mult), inc=vv)
                    P.do("vector",
                         lambda e: e.tensor_tensor(out=y_buf, in0=y_buf, in1=tm_buf,
                                                   op=ALU.add), inc=vv)
            if debug and c == 0 and dtile == 0:
                for i, buf in enumerate([dl_buf, w_buf, H_buf, y_buf]):
                    P.dma("sync", dbg_hbm[:, i * TCH:(i + 1) * TCH], buf,
                          after=[vv.at()], inc=st)
                P.wait("vector", [st.at()])
            P.do("vector",
                 lambda e, d=dtile, s=slot: e.scalar_tensor_tensor(
                     out=tm_buf, in0=xc_c[s], scalar=Dvec_c[:, d:d + 1],
                     in1=y_buf, op0=ALU.mult, op1=ALU.add), inc=vv)
            P.do("vector",
                 lambda e, d=dtile, s=slot: e.tensor_tensor(
                     out=g_buf[:, d, :], in0=tm_buf, in1=sz_c[s], op=ALU.mult),
                 after=[ld.at(szv), mm.at()], inc=vv)
            stream_war[("xc", slot)] = [vv.at(vv.val)]
            stream_war[("sz", slot)] = [vv.at(vv.val)]
        vv_chunk = vv.val

        for tsub in range(4):
            ob = mo_buf[tsub % 2]
            pso = (pso0, pso1)[tsub % 2]
            for ms in range(2):
                for d in range(16):
                    P.do("tensor",
                         lambda e, d=d, ts=tsub, ms=ms, pso=pso: e.matmul(
                             pso[:], g_buf[:, d, ts * 128:(ts + 1) * 128],
                             outw_sb[:, d, ms * 512:(ms + 1) * 512],
                             start=(d == 0), stop=(d == 15)),
                         after=([vv.at(vv_chunk), gp.at(gp_outw), sc.at()]
                                if d == 0 else ()),
                         inc=mm if d == 15 else None)
                P.do("scalar",
                     lambda e, ms=ms, ob=ob, pso=pso: e.copy(
                         out=ob[:, ms * 512:(ms + 1) * 512], in_=pso[:]),
                     after=[mm.at(), st.at(st.val)], inc=sc)
            P.dma("sync", m_out[c * TCH + tsub * 128: c * TCH + (tsub + 1) * 128, :],
                  ob, after=[sc.at()], inc=st)

    return dict(ld=ld, gp=gp, mm=mm, sc=sc, vv=vv, st=st)

def build_phase_m(debug=False):
    nc = bass.Bass("TRN2", target_bir_lowering=False, debug=False)
    io = declare_m_io(nc, debug)
    from contextlib import ExitStack
    with ExitStack() as _stk:
        P = Prog(nc)
        C = emit_phase_m(nc, P, _stk.enter_context, io, "m_", debug)
        P.wait("gpsimd", [C["st"].at(), C["ld"].at(), C["gp"].at()])
        P.realize()
    return nc


def prep_phase_m_inputs(inputs):
    x = inputs["L0_attn_x"]
    maps = []
    for core in range(8):
        b, d = core // 2, core % 2
        sfx = "1" if d == 0 else "2"
        xb = x[b] if d == 0 else x[b][::-1]
        g = lambda k: np.asarray(inputs["L0_attn_" + k + sfx])
        m = {
            "xT": np.ascontiguousarray(xb.T),
            "in_wT": np.ascontiguousarray(g("in_w").T),
            "xp_wT": np.ascontiguousarray(g("xp_w").T),
            "dtp_wT": np.ascontiguousarray(g("dtp_w").T),
            "out_wT": np.ascontiguousarray(g("out_w").T),
            "conv_w_rs": np.ascontiguousarray(
                g("conv_w").reshape(16, 128, 4).transpose(1, 0, 2).reshape(128, 64)),
            "conv_b_rs": np.ascontiguousarray(g("conv_b").reshape(16, 128).T),
            "dtp_b_rs": np.ascontiguousarray(g("dtp_b").reshape(16, 128).T),
            "A_log_rs": np.ascontiguousarray(
                g("A_log").reshape(16, 128, 16).transpose(1, 0, 2).reshape(128, 256)),
            "Dvec_rs": np.ascontiguousarray(g("Dvec").reshape(16, 128).T),
        }
        maps.append(m)
    return maps

def run_phase_m(inputs, debug=False):
    nc = build_phase_m(debug=debug)
    in_maps = prep_phase_m_inputs(inputs)
    res = run_bass_kernel_spmd(nc, in_maps, core_ids=list(range(8)))
    if debug:
        return res.results
    return [r["m_out"] for r in res.results]


NE0 = 32
I0 = 768
TA = 2048
NTT = TA // 128

def declare_a_io(nc, debug=False):
    io = {}
    io["x_in"] = nc.dram_tensor("x_in", [TA, DM], F32, kind="ExternalInput").ap()
    io["mf_in"] = nc.dram_tensor("mf_in", [TA, DM], F32, kind="ExternalInput").ap()
    io["mr_in"] = nc.dram_tensor("mr_in", [TA, DM], F32, kind="ExternalInput").ap()
    io["gate_wT"] = nc.dram_tensor("gate_wT", [DM, NE0], F32, kind="ExternalInput").ap()
    io["e_bias"] = nc.dram_tensor("e_bias", [1, NE0], F32, kind="ExternalInput").ap()
    io["w1_h"] = nc.dram_tensor("w1", [NE0, DM, I0], F32, kind="ExternalInput").ap()
    io["w3_h"] = nc.dram_tensor("w3", [NE0, DM, I0], F32, kind="ExternalInput").ap()
    io["w2_h"] = nc.dram_tensor("w2", [NE0, I0, DM], F32, kind="ExternalInput").ap()
    io["x2_out"] = nc.dram_tensor("x2_out", [TA, DM], F32, kind="ExternalOutput").ap()
    knd = "ExternalOutput" if debug else "Internal"
    io["xn_hbm"] = nc.dram_tensor("xn_hbm", [TA, DM], F32, kind=knd).ap()
    io["dbgA"] = nc.dram_tensor("dbgA", [128, 1024], F32, kind=knd).ap()
    return io

def emit_phase_a(nc, P, ec, io, pfx="a_", debug=False):
    x_in = io["x_in"]
    mf_in = io["mf_in"]
    mr_in = io["mr_in"]
    gate_wT = io["gate_wT"]
    e_bias = io["e_bias"]
    w1_h = io["w1_h"]
    w3_h = io["w3_h"]
    w2_h = io["w2_h"]
    x2_out = io["x2_out"]
    xn_hbm = io["xn_hbm"]
    dbgA = io["dbgA"]
    xnf = ec(nc.sbuf_tensor(pfx + "xnf", [128, 8 * TA], BF16))
    acc = ec(nc.sbuf_tensor(pfx + "acc", [128, NTT * DM], F32))
    a_bf = ec(nc.sbuf_tensor(pfx + "a_bf", [128, 6 * TA], BF16))
    wbuf = ec(nc.sbuf_tensor(pfx + "wbuf", [128, 2 * 18432], BF16))
    gate_sb = ec(nc.sbuf_tensor(pfx + "gate_sb", [128, 8 * NE0], F32))
    ident32 = ec(nc.sbuf_tensor(pfx + "ident32", [128, 128], F32))
    comb_all = ec(nc.sbuf_tensor(pfx + "comb_all", [128, NTT * NE0 + NE0], F32))
    comb = comb_all[:, 0:NTT * NE0]
    h1 = ec(nc.sbuf_tensor(pfx + "h1", [128, DM], F32))
    scr2 = ec(nc.sbuf_tensor(pfx + "scr2", [128, 64], F32))
    xnb = ec(nc.sbuf_tensor(pfx + "xnb", [128, DM], BF16))
    sg = ec(nc.sbuf_tensor(pfx + "sg", [128, 512], F32))
    sm = ec(nc.sbuf_tensor(pfx + "sm", [128, 64], F32))
    ebias_sb = comb_all[:, NTT * NE0:NTT * NE0 + NE0]
    ident_bf = ec(nc.sbuf_tensor(pfx + "ident_bf", [128, 128], BF16))
    pst = ec(nc.psum_tensor(pfx + "pst", [128, DM], BF16))
    pst32 = ec(nc.psum_tensor(pfx + "pst32", [128, DM], F32))
    psg = ec(nc.psum_tensor(pfx + "psg", [128, 512], F32))
    psu = ec(nc.psum_tensor(pfx + "psu", [128, 512], F32))
    pso0 = ec(nc.psum_tensor(pfx + "pso0", [128, 512], F32))
    pso1 = ec(nc.psum_tensor(pfx + "pso1", [128, 512], F32))
    s_ld = ec(nc.semaphore(pfx + "s_ld")); s_gp = ec(nc.semaphore(pfx + "s_gp"))
    s_mm = ec(nc.semaphore(pfx + "s_mm")); s_sc = ec(nc.semaphore(pfx + "s_sc"))
    s_vv = ec(nc.semaphore(pfx + "s_vv")); s_st = ec(nc.semaphore(pfx + "s_st"))
    s_gi = ec(nc.semaphore(pfx + "s_gi"))
    ld = Ctr(s_ld); gp = Ctr(s_gp); mm = Ctr(s_mm)
    sc = Ctr(s_sc); vv = Ctr(s_vv); st = Ctr(s_st)
    gi = Ctr(s_gi)

    route_done_vv = 0
    psr = psg[:, 0:NE0]
    xnf_v = xnf[:].rearrange("p (h t) -> p h t", h=8)
    xnt32 = a_bf[:, 0:2048].bitcast(F32)
    wv = [wbuf[:, i * 18432:(i + 1) * 18432] for i in range(2)]
    w1_v = [w[:, 0:6144].rearrange("p (h i) -> p h i", h=8) for w in wv]
    w3_v = [w[:, 6144:12288].rearrange("p (h i) -> p h i", h=8) for w in wv]
    w2_v = [w[:, 12288:18432].rearrange("p (i h) -> p i h", i=6) for w in wv]
    gate_v = gate_sb[:].rearrange("p (h e) -> p h e", h=8)
    rs_c = sm[:, 0:1]; ssq_c = sm[:, 1:2]; mean_c = sm[:, 2:3]
    var_c = sm[:, 3:4]; rstd_c = sm[:, 4:5]; msq_c = sm[:, 5:6]
    mx_c = sm[:, 8:16]; rinv_c = sm[:, 16:17]; rsum_c = sm[:, 17:18]
    eps_c = sm[:, 18:19]
    scr_c = sm[:, 32:64]

    P.dma("sync", ebias_sb, e_bias.broadcast_to([128, NE0]), inc=ld)
    P.dma("sync", gate_v, gate_wT.rearrange("(a p) e -> p a e", p=128), inc=ld)
    P.do("vector", lambda e: e.memset(eps_c, 1e-5), inc=vv)
    P.do("gpsimd", lambda e: e.memset(ident_bf[:], 0.0), inc=gi)
    P.do("gpsimd", lambda e: e.affine_select(
        out=ident_bf[:], in_=ident_bf[:], compare_op=ALU.not_equal, fill=1.0,
        base=0, pattern=[[-1, 128]], channel_multiplier=1), inc=gi)
    P.do("gpsimd", lambda e: e.memset(ident32[:], 0.0), inc=gi)
    P.do("gpsimd", lambda e: e.affine_select(
        out=ident32[:], in_=ident32[:], compare_op=ALU.not_equal, fill=1.0,
        base=0, pattern=[[-1, 128]], channel_multiplier=1), inc=gi)
    ld_init = ld.val; gp_init = gp.val

    P.strict = {"vector": [vv, sc], "scalar": [sc, vv]}
    blend = "sel01" in io
    if blend:
        sel_sb = ec(nc.sbuf_tensor(pfx + "sel", [128, 2], F32))
        P.dma("sync", sel_sb[:], io["sel01"], inc=ld)
        mf2_in = io["mf2_in"]; mr2_in = io["mr2_in"]
    for tt in range(NTT):
        r0, r1 = tt * 128, (tt + 1) * 128
        ldep = [vv.at(), sc.at(), mm.at()]
        P.dma("sync", h1[:], x_in[r0:r1, :], after=ldep, inc=ld)
        P.dma("sync", xnt32, mf_in[r0:r1, :], after=ldep, inc=ld)
        v_ld = ld.val
        if blend:
            P.do("vector", lambda e: e.scalar_tensor_tensor(
                out=h1[:], in0=xnt32, scalar=sel_sb[:, 0:1], in1=h1[:],
                op0=ALU.mult, op1=ALU.add),
                 after=[ld.at(v_ld)], inc=vv)
            P.dma("sync", xnt32, mf2_in[r0:r1, :], after=[vv.at()], inc=ld)
            P.do("vector", lambda e: e.scalar_tensor_tensor(
                out=h1[:], in0=xnt32, scalar=sel_sb[:, 1:2], in1=h1[:],
                op0=ALU.mult, op1=ALU.add),
                 after=[ld.at()], inc=vv)
            P.dma("sync", xnt32, mr_in[r0:r1, :], after=[vv.at()], inc=ld)
            P.do("vector", lambda e: e.scalar_tensor_tensor(
                out=h1[:], in0=xnt32, scalar=sel_sb[:, 0:1], in1=h1[:],
                op0=ALU.mult, op1=ALU.add),
                 after=[ld.at()], inc=vv)
            P.dma("sync", xnt32, mr2_in[r0:r1, :], after=[vv.at()], inc=ld)
            P.do("vector", lambda e: e.scalar_tensor_tensor(
                out=h1[:], in0=xnt32, scalar=sel_sb[:, 1:2], in1=h1[:],
                op0=ALU.mult, op1=ALU.add),
                 after=[ld.at()], inc=vv)
        else:
            P.do("vector", lambda e: e.tensor_tensor(out=h1[:], in0=h1[:], in1=xnt32,
                                                     op=ALU.add),
                 after=[ld.at(v_ld)], inc=vv)
            P.dma("sync", xnt32, mr_in[r0:r1, :], after=[vv.at()], inc=ld)
            P.do("vector", lambda e: e.tensor_tensor(out=h1[:], in0=h1[:], in1=xnt32,
                                                     op=ALU.add),
                 after=[ld.at()], inc=vv)
        P.do("vector", lambda e: e.reduce_sum(out=rs_c, in_=h1[:], axis=AX.X), inc=vv)
        P.do("scalar", lambda e: e.activation(out=pst32[:], in_=h1[:], func=AF.Square,
                                              accum_out=ssq_c),
             after=[vv.at()], inc=sc)
        P.do("vector", lambda e: e.tensor_scalar_mul(mean_c, rs_c, 1.0 / DM),
             after=[vv.at()], inc=vv)
        P.do("vector", lambda e: e.tensor_mul(msq_c, mean_c, mean_c),
             after=[vv.at()], inc=vv)
        P.do("vector", lambda e: e.tensor_scalar_mul(var_c, ssq_c, 1.0 / DM),
             after=[sc.at(), vv.at()], inc=vv)
        P.do("vector", lambda e: e.tensor_sub(var_c, var_c, msq_c),
             after=[vv.at()], inc=vv)
        P.do("scalar", lambda e: e.activation(out=rstd_c, in_=var_c, func=AF.Ln,
                                              bias=eps_c),
             after=[vv.at()], inc=sc)
        P.do("scalar", lambda e: e.activation(out=rstd_c, in_=rstd_c, func=AF.Exp,
                                              scale=-0.5), inc=sc)
        P.do("vector", lambda e: e.tensor_scalar(out=h1[:], in0=h1[:], scalar1=mean_c,
                                                 scalar2=rstd_c, op0=ALU.subtract,
                                                 op1=ALU.mult),
             after=[sc.at()], inc=vv)
        xn_done = vv.val
        if debug and tt == 0:
            P.dma("sync", dbgA[:, 0:32], sm[:, 0:32], after=[vv.at()], inc=st)
            P.wait("vector", [st.at()])
            P.wait("scalar", [st.at()])
        P.dma("sync", xn_hbm[r0:r1, :], h1[:], after=[vv.at(xn_done)], inc=st)
        P.do("vector", lambda e: e.tensor_copy(out=xnb[:], in_=h1[:]),
             after=[vv.at(xn_done), mm.at()], inc=vv)
        cast_done = vv.val
        for ht in range(8):
            P.do("tensor",
                 lambda e, ht=ht: e.transpose(
                     pst[:, ht * 128:(ht + 1) * 128],
                     xnb[:, ht * 128:(ht + 1) * 128], ident_bf[:]),
                 after=[vv.at(cast_done), sc.at(), gi.at(4)] if ht == 0 else (),
                 inc=mm if ht == 7 else None)
        P.do("vector",
             lambda e, tt=tt: e.tensor_copy(
                 out=xnf_v[:, :, tt * 128:(tt + 1) * 128],
                 in_=pst[:].rearrange("p (h t) -> p h t", h=8)),
             after=[mm.at()], inc=vv)
        tr_done = vv.val
        for ht in range(8):
            P.do("tensor",
                 lambda e, ht=ht: e.transpose(
                     pst32[:, ht * 128:(ht + 1) * 128],
                     h1[:, ht * 128:(ht + 1) * 128], ident32[:]),
                 after=[vv.at(xn_done), sc.at()] if ht == 0 else (),
                 inc=mm if ht == 7 else None)
        P.do("vector", lambda e: e.tensor_copy(out=xnt32[:], in_=pst32[:]),
             after=[mm.at()], inc=vv)
        tr32_done = vv.val
        for ht in range(8):
            P.do("tensor",
                 lambda e, ht=ht, tt=tt: e.matmul(
                     psr, xnt32[:, ht * 128:(ht + 1) * 128],
                     gate_v[:, ht, :], start=(ht == 0), stop=(ht == 7)),
                 after=[vv.at(tr32_done), ld.at(ld_init), sc.at()] if ht == 0 else (),
                 inc=mm if ht == 7 else None)
        P.do("scalar", lambda e: e.activation(out=scr_c, in_=psr, func=AF.Sigmoid),
             after=[mm.at(), vv.at(route_done_vv)], inc=sc)
        P.do("vector", lambda e: e.tensor_tensor(out=scr2[:, 0:32], in0=scr_c,
                                                 in1=ebias_sb, op=ALU.add),
             after=[sc.at(), ld.at(ld_init)], inc=vv)
        P.do("vector", lambda e: e.max(out=mx_c, in_=scr2[:, 0:32]), inc=vv)
        P.do("vector", lambda e: e.memset(mx_c[:, 4:8], -1e30),
             after=[vv.at()], inc=vv)
        P.do("vector", lambda e: e.match_replace(out=scr2[:, 32:64], in_to_replace=mx_c,
                                                 in_values=scr2[:, 0:32],
                                                 imm_value=-1e30),
             after=[vv.at()], inc=vv)
        P.do("vector", lambda e: e.tensor_single_scalar(out=scr2[:, 32:64],
                                                        in_=scr2[:, 32:64],
                                                        scalar=-1e30,
                                                        op=ALU.is_equal),
             after=[vv.at()], inc=vv)
        P.do("vector", lambda e: e.tensor_tensor(out=scr2[:, 32:64], in0=scr2[:, 32:64],
                                                 in1=scr_c, op=ALU.mult), inc=vv)
        P.do("vector", lambda e: e.reduce_sum(out=rsum_c, in_=scr2[:, 32:64],
                                              axis=AX.X), inc=vv)
        P.do("vector", lambda e: e.reciprocal(out=rinv_c, in_=rsum_c),
             after=[vv.at()], inc=vv)
        P.do("vector",
             lambda e, tt=tt: e.tensor_scalar(
                 out=comb[:, tt * NE0:(tt + 1) * NE0], in0=scr2[:, 32:64],
                 scalar1=rinv_c, scalar2=None, op0=ALU.mult),
             after=[vv.at()], inc=vv)
        route_done_vv = vv.val
    P.strict = {}
    prep_done_vv = vv.val
    prep_done_sc = sc.val
    if debug:
        P.dma("sync", dbgA[:, 0:512], comb, after=[vv.at()], inc=st)


    wload_gp = []
    def load_expert(e_i, slot):
        deps = [mm.at(wcons_mm[slot])] if wcons_mm[slot] else []
        P.dma("gpsimd", w1_v[slot],
              w1_h[e_i].rearrange("(a p) i -> p a i", p=128), after=deps, inc=gp)
        P.dma("gpsimd", w3_v[slot],
              w3_h[e_i].rearrange("(a p) i -> p a i", p=128), inc=gp)
        P.dma("gpsimd", w2_v[slot],
              w2_h[e_i].rearrange("(a p) h -> p a h", p=128), inc=gp)
        wload_gp.append(gp.val)

    wcons_mm = [0, 0]
    load_expert(0, 0)
    load_expert(1, 1)
    a_written_vv = 0
    silu_war_vv = 0
    for e_i in range(NE0):
        slot = e_i % 2
        wg = wload_gp[e_i]
        for it in range(6):
            for tsl in range(4):
                for ht in range(8):
                    P.do("tensor",
                         lambda e, it=it, tsl=tsl, ht=ht, s=slot: e.matmul(
                             psg[:], w1_v[s][:, ht, it * 128:(it + 1) * 128],
                             xnf_v[:, ht, tsl * 512:(tsl + 1) * 512],
                             start=(ht == 0), stop=(ht == 7)),
                         after=([gp.at(wg), vv.at(prep_done_vv), sc.at()]
                                if (it == 0 and tsl == 0 and ht == 0) else
                                ([sc.at()] if ht == 0 else ())),
                         inc=mm if ht == 7 else None)
                for ht in range(8):
                    P.do("tensor",
                         lambda e, it=it, tsl=tsl, ht=ht, s=slot: e.matmul(
                             psu[:], w3_v[s][:, ht, it * 128:(it + 1) * 128],
                             xnf_v[:, ht, tsl * 512:(tsl + 1) * 512],
                             start=(ht == 0), stop=(ht == 7)),
                         after=[vv.at()] if ht == 0 else (),
                         inc=mm if ht == 7 else None)
                P.do("scalar",
                     lambda e: e.activation(out=sg[:], in_=psg[:], func=AF.Silu),
                     after=[mm.at(), vv.at(silu_war_vv)], inc=sc)
                P.do("vector",
                     lambda e, it=it, tsl=tsl: e.tensor_tensor(
                         out=a_bf[:, it * TA + tsl * 512: it * TA + (tsl + 1) * 512],
                         in0=sg[:], in1=psu[:], op=ALU.mult),
                     after=[sc.at(), mm.at(w2cons_mm if e_i > 0 else 0)], inc=vv)
                silu_war_vv = vv.val
        a_written_vv = vv.val
        for tt in range(NTT):
            pso = (pso0, pso1)[tt % 2]
            for hs in range(2):
                for it in range(6):
                    P.do("tensor",
                         lambda e, tt=tt, hs=hs, it=it, s=slot, pso=pso: e.matmul(
                             pso[:], a_bf[:, it * TA + tt * 128: it * TA + (tt + 1) * 128],
                             w2_v[s][:, it, hs * 512:(hs + 1) * 512],
                             start=(it == 0), stop=(it == 5)),
                         after=[vv.at()] if it == 0 else (),
                         inc=mm if it == 5 else None)
                if e_i == 0:
                    P.do("vector",
                         lambda e, tt=tt, hs=hs, pso=pso: e.tensor_scalar(
                             out=acc[:, tt * DM + hs * 512: tt * DM + (hs + 1) * 512],
                             in0=pso[:], scalar1=comb[:, tt * NE0:tt * NE0 + 1],
                             scalar2=None, op0=ALU.mult),
                         after=[mm.at()], inc=vv)
                else:
                    P.do("vector",
                         lambda e, tt=tt, hs=hs, pso=pso, e_i=e_i: e.scalar_tensor_tensor(
                             out=acc[:, tt * DM + hs * 512: tt * DM + (hs + 1) * 512],
                             in0=pso[:], scalar=comb[:, tt * NE0 + e_i:tt * NE0 + e_i + 1],
                             in1=acc[:, tt * DM + hs * 512: tt * DM + (hs + 1) * 512],
                             op0=ALU.mult, op1=ALU.add),
                         after=[mm.at()], inc=vv)
        w2cons_mm = mm.val
        if e_i + 2 < NE0:
            wcons_mm[slot] = mm.val
            load_expert(e_i + 2, slot)
    moe_done_vv = vv.val
    if debug:
        P.dma("sync", dbgA[:, 512:1024], acc[:, 0:512], after=[vv.at()], inc=st)
        P.wait("vector", [st.at()])

    P.strict = {"vector": [vv, sc], "scalar": [sc, vv]}
    for tt in range(NTT):
        r0, r1 = tt * 128, (tt + 1) * 128
        P.dma("sync", h1[:], xn_hbm[r0:r1, :],
              after=[vv.at(), sc.at(), st.at()], inc=ld)
        P.do("vector",
             lambda e, tt=tt: e.tensor_tensor(
                 out=h1[:], in0=h1[:], in1=acc[:, tt * DM:(tt + 1) * DM], op=ALU.add),
             after=[ld.at(), vv.at(moe_done_vv)], inc=vv)
        P.do("vector", lambda e: e.reduce_sum(out=rs_c, in_=h1[:], axis=AX.X), inc=vv)
        P.do("scalar", lambda e: e.activation(out=pst32[:], in_=h1[:], func=AF.Square,
                                              accum_out=ssq_c),
             after=[vv.at()], inc=sc)
        P.do("vector", lambda e: e.tensor_scalar_mul(mean_c, rs_c, 1.0 / DM),
             after=[vv.at()], inc=vv)
        P.do("vector", lambda e: e.tensor_mul(msq_c, mean_c, mean_c),
             after=[vv.at()], inc=vv)
        P.do("vector", lambda e: e.tensor_scalar_mul(var_c, ssq_c, 1.0 / DM),
             after=[sc.at(), vv.at()], inc=vv)
        P.do("vector", lambda e: e.tensor_sub(var_c, var_c, msq_c),
             after=[vv.at()], inc=vv)
        P.do("scalar", lambda e: e.activation(out=rstd_c, in_=var_c, func=AF.Ln,
                                              bias=eps_c),
             after=[vv.at()], inc=sc)
        P.do("scalar", lambda e: e.activation(out=rstd_c, in_=rstd_c, func=AF.Exp,
                                              scale=-0.5), inc=sc)
        P.do("vector", lambda e: e.tensor_scalar(out=h1[:], in0=h1[:], scalar1=mean_c,
                                                 scalar2=rstd_c, op0=ALU.subtract,
                                                 op1=ALU.mult),
             after=[sc.at()], inc=vv)
        P.dma("sync", x2_out[r0:r1, :], h1[:], after=[vv.at()], inc=st)

    return dict(ld=ld, gp=gp, mm=mm, sc=sc, vv=vv, st=st)

def build_phase_a(debug=False):
    nc = bass.Bass("TRN2", target_bir_lowering=False, debug=False)
    io = declare_a_io(nc, debug)
    from contextlib import ExitStack
    with ExitStack() as _stk:
        P = Prog(nc)
        C = emit_phase_a(nc, P, _stk.enter_context, io, "a_", debug)
        P.wait("gpsimd", [C["st"].at(), C["ld"].at(), C["gp"].at()])
        P.realize()
    return nc

def prep_phase_a_inputs(inputs, m_outs):
    x = np.asarray(inputs["L0_attn_x"]).reshape(16384, DM)
    gate_wT = np.ascontiguousarray(np.asarray(inputs["L0_ffn_gate_w"]).T)
    e_bias = np.asarray(inputs["L0_ffn_e_bias"]).reshape(1, NE0)
    w1 = np.ascontiguousarray(inputs["L0_ffn_w1"])
    w3 = np.ascontiguousarray(inputs["L0_ffn_w3"])
    w2 = np.ascontiguousarray(inputs["L0_ffn_w2"])
    maps = []
    for c in range(8):
        b, half = c // 2, c % 2
        sl = slice(half * TA, (half + 1) * TA)
        maps.append({
            "x_in": np.ascontiguousarray(x[c * TA:(c + 1) * TA]),
            "mf_in": np.ascontiguousarray(np.asarray(m_outs[2 * b])[sl]),
            "mr_in": np.ascontiguousarray(np.asarray(m_outs[2 * b + 1])[sl]),
            "gate_wT": gate_wT, "e_bias": e_bias,
            "w1": w1, "w3": w3, "w2": w2,
        })
    return maps

def run_phase_a(inputs, m_outs, debug=False):
    nc = build_phase_a(debug=debug)
    in_maps = prep_phase_a_inputs(inputs, m_outs)
    res = run_bass_kernel_spmd(nc, in_maps, core_ids=list(range(8)))
    if debug:
        return res.results
    return [r["x2_out"] for r in res.results]


NH = 16
HD = 64
NTB = 17

def declare_b_io(nc, debug=False):
    io = {}
    io["x_in"] = nc.dram_tensor("x_in", [TA, DM], F32, kind="ExternalInput").ap()
    io["halo_in"] = nc.dram_tensor("halo_in", [128, DM], F32, kind="ExternalInput").ap()
    io["wqT"] = nc.dram_tensor("wqT", [DM, DM], F32, kind="ExternalInput").ap()
    io["wkT"] = nc.dram_tensor("wkT", [DM, DM], F32, kind="ExternalInput").ap()
    io["wvT"] = nc.dram_tensor("wvT", [DM, DM], F32, kind="ExternalInput").ap()
    io["woT"] = nc.dram_tensor("woT", [DM, DM], F32, kind="ExternalInput").ap()
    io["biases"] = nc.dram_tensor("biases", [4, DM], F32, kind="ExternalInput").ap()
    io["posv"] = nc.dram_tensor("posv", [128, NTB], F32, kind="ExternalInput").ap()
    io["masks_in"] = nc.dram_tensor("masks_in", [128, 512], F32, kind="ExternalInput").ap()
    io["x3_out"] = nc.dram_tensor("x3_out", [TA, DM], F32, kind="ExternalOutput").ap()
    knd = "ExternalOutput" if debug else "Internal"
    io["dbgB"] = nc.dram_tensor("dbgB", [128, 2048], F32, kind=knd).ap()
    return io

def emit_phase_b(nc, P, ec, io, pfx="b_", debug=False):
    x_in = io["x_in"]
    halo_in = io["halo_in"]
    wqT = io["wqT"]
    wkT = io["wkT"]
    wvT = io["wvT"]
    woT = io["woT"]
    biases = io["biases"]
    posv = io["posv"]
    masks_in = io["masks_in"]
    x3_out = io["x3_out"]
    dbgB = io["dbgB"]
    x2f = ec(nc.sbuf_tensor(pfx + "x2f", [128, 8 * NTB * 128], BF16))
    qf = ec(nc.sbuf_tensor(pfx + "qf", [128, 8 * NTB * 128], BF16))
    kf = ec(nc.sbuf_tensor(pfx + "kf", [128, 8 * NTB * 128], BF16))
    vtok = ec(nc.sbuf_tensor(pfx + "vtok", [128, NTB * DM], BF16))
    wbuf = ec(nc.sbuf_tensor(pfx + "wbuf", [128, 8 * DM], BF16))
    biasr = ec(nc.sbuf_tensor(pfx + "biasr", [1, 4 * DM], BF16))
    ones_r = ec(nc.sbuf_tensor(pfx + "ones_r", [1, 128], BF16))
    ident_bf = ec(nc.sbuf_tensor(pfx + "identb", [128, 128], BF16))
    masks = ec(nc.sbuf_tensor(pfx + "masks", [128, 512], F32))
    trig = ec(nc.sbuf_tensor(pfx + "trig", [128, 256], F32))
    posv_sb = ec(nc.sbuf_tensor(pfx + "posv_sb", [128, NTB], F32))
    qs = ec(nc.sbuf_tensor(pfx + "qs", [128, DM], F32))
    qs2 = ec(nc.sbuf_tensor(pfx + "qs2", [128, DM], F32))
    tmp5 = ec(nc.sbuf_tensor(pfx + "tmp5", [128, 512], F32))
    qsb = ec(nc.sbuf_tensor(pfx + "qsb", [128, DM], BF16))
    scb = ec(nc.sbuf_tensor(pfx + "scb", [128, 256], F32))
    pb = ec(nc.sbuf_tensor(pfx + "pb", [128, 256], BF16))
    ptb = ec(nc.sbuf_tensor(pfx + "ptb", [128, 256], BF16))
    smB = ec(nc.sbuf_tensor(pfx + "smB", [128, 16], F32))
    resid = ec(nc.sbuf_tensor(pfx + "resid", [128, DM], F32))
    psq = ec(nc.psum_tensor(pfx + "psq", [128, 512], F32))
    pstB = ec(nc.psum_tensor(pfx + "pstB", [128, DM], BF16))
    pssc = ec(nc.psum_tensor(pfx + "pssc", [128, 256], F32))
    psT = ec(nc.psum_tensor(pfx + "psT", [128, 256], BF16))
    psctx = ec(nc.psum_tensor(pfx + "psctx", [64, 128], F32))
    pso = ec(nc.psum_tensor(pfx + "psoB", [128, 512], F32))
    s_ld = ec(nc.semaphore(pfx + "s_ld")); s_gp = ec(nc.semaphore(pfx + "s_gp"))
    s_mm = ec(nc.semaphore(pfx + "s_mm")); s_sc = ec(nc.semaphore(pfx + "s_sc"))
    s_vv = ec(nc.semaphore(pfx + "s_vv")); s_st = ec(nc.semaphore(pfx + "s_st"))
    s_gi = ec(nc.semaphore(pfx + "s_gi"))
    ld = Ctr(s_ld); gp = Ctr(s_gp); mm = Ctr(s_mm)
    sc = Ctr(s_sc); vv = Ctr(s_vv); st = Ctr(s_st); gi = Ctr(s_gi)

    x2f_v = x2f[:].rearrange("p (h t) -> p h t", h=8)
    qf_v = qf[:].rearrange("p (h t) -> p h t", h=8)
    kf_v = kf[:].rearrange("p (h t) -> p h t", h=8)
    vt_v = vtok[:].rearrange("p (t o) -> p t o", t=NTB)
    w_v = wbuf[:].rearrange("p (h o) -> p h o", h=8)
    mask_f = masks[:, 0:256]
    mask_n = masks[:, 256:512]
    ifr_c = trig[:, 0:32]
    ang_c = trig[:, 32:64]
    cos_c = trig[:, 64:96]
    sin_c = trig[:, 96:128]
    kk_c = trig[:, 128:160]
    rmax_c = smB[:, 0:1]; rsum_c = smB[:, 1:2]; rinv_c = smB[:, 2:3]

    M_MAGIC = 12582912.0
    INV2PI = 0.15915493667125702
    CW1 = 6.28125
    CW2 = float(np.float32(6.283185307179586 - 6.28125))
    HALFPI = float(np.float32(np.pi / 2))

    P.do("gpsimd", lambda e: e.memset(ident_bf[:], 0.0), inc=gi)
    P.do("gpsimd", lambda e: e.affine_select(
        out=ident_bf[:], in_=ident_bf[:], compare_op=ALU.not_equal, fill=1.0,
        base=0, pattern=[[-1, 128]], channel_multiplier=1), inc=gi)
    P.do("gpsimd", lambda e: e.memset(ones_r[:], 1.0), inc=gi)
    P.dma("sync", masks[:], masks_in, inc=ld)
    ifr_i32 = trig[:, 160:192].bitcast(mybir.dt.int32)
    P.do("gpsimd", lambda e: e.iota(ifr_i32, pattern=[[1, 32]], base=0,
                                    channel_multiplier=0), inc=gi)
    gi_const = gi.val
    P.do("vector", lambda e: e.tensor_copy(out=ifr_c, in_=ifr_i32),
         after=[gi.at(gi_const)], inc=vv)
    P.do("scalar", lambda e: e.activation(out=ifr_c, in_=ifr_c, func=AF.Exp,
                                          scale=-float(np.log(10000.0) / 32.0)),
         after=[vv.at()], inc=sc)
    ifr_done_sc = sc.val
    P.dma("sync", posv_sb[:], posv, inc=ld)
    P.dma("gpsimd", biasr[:].rearrange("p (a o) -> p a o", a=4),
          biases.rearrange("a o -> a o")[None, :, :] if False
          else biases.rearrange("(x a) o -> x a o", x=1), inc=gp)
    posv_ld = ld.val

    for tt in range(NTB):
        src = halo_in if tt == 0 else x_in[(tt - 1) * 128: tt * 128, :]
        P.dma("gpsimd", qsb[:], src, after=[mm.at()], inc=gp)
        for ht in range(8):
            P.do("tensor",
                 lambda e, ht=ht: e.transpose(
                     pstB[:, ht * 128:(ht + 1) * 128],
                     qsb[:, ht * 128:(ht + 1) * 128], ident_bf[:]),
                 after=[gp.at(), gi.at(gi_const), vv.at()] if ht == 0 else (),
                 inc=mm if ht == 7 else None)
        P.do("vector",
             lambda e, tt=tt: e.tensor_copy(
                 out=x2f_v[:, :, tt * 128:(tt + 1) * 128],
                 in_=pstB[:].rearrange("p (h t) -> p h t", h=8)),
             after=[mm.at()], inc=vv)
    x2f_done_vv = vv.val

    for w_i, wT in enumerate([wqT, wkT, wvT]):
        P.dma("gpsimd", w_v, wT.rearrange("(a p) o -> p a o", p=128),
              after=[mm.at()], inc=gp)
        wld = gp.val
        for tt in range(NTB):
            for os_ in range(2):
                dep0 = [gp.at(wld), vv.at(x2f_done_vv), sc.at(), vv.at()]
                for ht in range(8):
                    P.do("tensor",
                         lambda e, tt=tt, os_=os_, ht=ht: e.matmul(
                             psq[:], x2f_v[:, ht, tt * 128:(tt + 1) * 128],
                             w_v[:, ht, os_ * 512:(os_ + 1) * 512],
                             start=(ht == 0), stop=False),
                         after=dep0 if ht == 0 else (), inc=None)
                P.do("tensor",
                     lambda e, w_i=w_i, os_=os_: e.matmul(
                         psq[:], ones_r[:],
                         biasr[0:1, w_i * DM + os_ * 512: w_i * DM + (os_ + 1) * 512],
                         start=False, stop=True),
                     after=[gp.at()], inc=mm)
                if w_i == 2:
                    P.do("scalar",
                         lambda e, tt=tt, os_=os_: e.copy(
                             out=vt_v[:, tt, os_ * 512:(os_ + 1) * 512], in_=psq[:]),
                         after=[mm.at()], inc=sc)
                else:
                    P.do("scalar",
                         lambda e, os_=os_: e.copy(
                             out=qs[:, os_ * 512:(os_ + 1) * 512], in_=psq[:]),
                         after=[mm.at()], inc=sc)
            if w_i == 2:
                continue
            P.strict = {"vector": [vv, sc], "scalar": [sc, vv]}
            for trig_i in range(2):
                P.do("vector", lambda e, tt=tt: e.tensor_scalar(
                    out=ang_c, in0=ifr_c, scalar1=posv_sb[:, tt:tt + 1],
                    scalar2=None, op0=ALU.mult),
                     after=[sc.at(ifr_done_sc), ld.at(posv_ld)], inc=vv)
                if trig_i == 1:
                    P.do("vector", lambda e: e.tensor_scalar(
                        out=ang_c, in0=ang_c, scalar1=HALFPI, scalar2=None,
                        op0=ALU.add), inc=vv)
                P.do("vector", lambda e: e.tensor_scalar(
                    out=kk_c, in0=ang_c, scalar1=INV2PI, scalar2=M_MAGIC,
                    op0=ALU.mult, op1=ALU.add), inc=vv)
                P.do("vector", lambda e: e.tensor_scalar(
                    out=kk_c, in0=kk_c, scalar1=M_MAGIC, scalar2=None,
                    op0=ALU.subtract), inc=vv)
                P.do("vector", lambda e: e.scalar_tensor_tensor(
                    out=ang_c, in0=kk_c, scalar=-CW1, in1=ang_c,
                    op0=ALU.mult, op1=ALU.add), inc=vv)
                P.do("vector", lambda e: e.scalar_tensor_tensor(
                    out=ang_c, in0=kk_c, scalar=-CW2, in1=ang_c,
                    op0=ALU.mult, op1=ALU.add), inc=vv)
                P.do("scalar", lambda e, trig_i=trig_i: e.activation(
                    out=(sin_c if trig_i == 0 else cos_c), in_=ang_c,
                    func=AF.Sin), inc=sc)
            q1 = qs[:].rearrange("p (h d) -> p h d", h=NH)[:, :, 0:32]
            q2 = qs[:].rearrange("p (h d) -> p h d", h=NH)[:, :, 32:64]
            o1 = qs2[:].rearrange("p (h d) -> p h d", h=NH)[:, :, 0:32]
            o2 = qs2[:].rearrange("p (h d) -> p h d", h=NH)[:, :, 32:64]
            tm = tmp5[:].rearrange("p (h d) -> p h d", h=NH, d=32)
            cosb = cos_c.rearrange("p (a d) -> p a d", a=1).broadcast_to([128, NH, 32])
            sinb = sin_c.rearrange("p (a d) -> p a d", a=1).broadcast_to([128, NH, 32])
            P.do("vector", lambda e, q1=q1, o1=o1, cosb=cosb: e.tensor_tensor(
                out=o1, in0=q1, in1=cosb, op=ALU.mult), after=[sc.at()], inc=vv)
            P.do("vector", lambda e, q2=q2, tm=tm, sinb=sinb: e.tensor_tensor(
                out=tm, in0=q2, in1=sinb, op=ALU.mult), inc=vv)
            P.do("vector", lambda e, o1=o1, tm=tm: e.tensor_tensor(
                out=o1, in0=o1, in1=tm, op=ALU.subtract), inc=vv)
            P.do("vector", lambda e, q2=q2, o2=o2, cosb=cosb: e.tensor_tensor(
                out=o2, in0=q2, in1=cosb, op=ALU.mult), inc=vv)
            P.do("vector", lambda e, q1=q1, tm=tm, sinb=sinb: e.tensor_tensor(
                out=tm, in0=q1, in1=sinb, op=ALU.mult), inc=vv)
            P.do("vector", lambda e, o2=o2, tm=tm: e.tensor_tensor(
                out=o2, in0=o2, in1=tm, op=ALU.add), inc=vv)
            P.do("vector", lambda e: e.tensor_copy(out=qsb[:], in_=qs2[:]), inc=vv)
            P.strict = {}
            cast_vv = vv.val
            dst = qf_v if w_i == 0 else kf_v
            for ht in range(8):
                P.do("tensor",
                     lambda e, ht=ht: e.transpose(
                         pstB[:, ht * 128:(ht + 1) * 128],
                         qsb[:, ht * 128:(ht + 1) * 128], ident_bf[:]),
                     after=[vv.at(cast_vv), sc.at(), vv.at()] if ht == 0 else (),
                     inc=mm if ht == 7 else None)
            P.do("vector",
                 lambda e, tt=tt, dst=dst: e.tensor_copy(
                     out=dst[:, :, tt * 128:(tt + 1) * 128],
                     in_=pstB[:].rearrange("p (h t) -> p h t", h=8)),
                 after=[mm.at()], inc=vv)
    qkv_done_vv = vv.val
    qkv_done_sc = sc.val
    if debug:
        P.do("vector", lambda e: e.tensor_copy(out=qs[:], in_=vt_v[:, 1, :]),
             after=[vv.at(), sc.at()], inc=vv)
        P.dma("sync", dbgB[:, 0:1024], qs[:], after=[vv.at()], inc=st)
        P.do("vector", lambda e: e.tensor_copy(
            out=qs2[:].rearrange("p (h t) -> p h t", h=8),
            in_=qf_v[:, :, 128:256]),
             after=[st.at()], inc=vv)
        P.dma("sync", dbgB[:, 1024:2048], qs2[:], after=[vv.at()], inc=st)
        P.wait("vector", [st.at()])

    cf_v = x2f_v
    for qb in range(16):
        t_self = qb + 1
        t_prev = qb
        for h in range(NH):
            hrow = (h % 2) * 64
            htile = h // 2
            dep0 = [vv.at(qkv_done_vv), vv.at(), sc.at(), ld.at()]
            P.do("tensor",
                 lambda e, hrow=hrow, htile=htile, t_prev=t_prev, qb=qb: e.matmul(
                     pssc[:, 0:128],
                     qf_v[hrow:hrow + 64, htile, (qb + 1) * 128:(qb + 2) * 128],
                     kf_v[hrow:hrow + 64, htile, t_prev * 128:(t_prev + 1) * 128],
                     start=True, stop=True),
                 after=dep0, inc=None)
            P.do("tensor",
                 lambda e, hrow=hrow, htile=htile, t_self=t_self, qb=qb: e.matmul(
                     pssc[:, 128:256],
                     qf_v[hrow:hrow + 64, htile, (qb + 1) * 128:(qb + 2) * 128],
                     kf_v[hrow:hrow + 64, htile, t_self * 128:(t_self + 1) * 128],
                     start=True, stop=True),
                 inc=mm)
            mk_ = mask_f if (qb == 0) else mask_n
            P.strict = {"vector": [vv, sc], "scalar": [sc, vv]}
            P.do("vector", lambda e, mk_=mk_: e.scalar_tensor_tensor(
                out=scb[:], in0=pssc[:], scalar=0.125, in1=mk_,
                op0=ALU.mult, op1=ALU.add),
                 after=[mm.at()], inc=vv)
            P.do("vector", lambda e: e.tensor_reduce(
                out=rmax_c, in_=scb[:], axis=AX.X, op=ALU.max, negate=True), inc=vv)
            P.do("scalar", lambda e: e.activation(
                out=scb[:], in_=scb[:], func=AF.Exp, bias=rmax_c,
                accum_out=rsum_c), after=[vv.at()], inc=sc)
            P.do("vector", lambda e: e.reciprocal(out=rinv_c, in_=rsum_c),
                 after=[sc.at()], inc=vv)
            P.do("vector", lambda e: e.tensor_scalar(
                out=pb[:], in0=scb[:], scalar1=rinv_c, scalar2=None,
                op0=ALU.mult), inc=vv)
            P.strict = {}
            pb_vv = vv.val
            for blk in range(2):
                P.do("tensor",
                     lambda e, blk=blk: e.transpose(
                         psT[:, blk * 128:(blk + 1) * 128],
                         pb[:, blk * 128:(blk + 1) * 128], ident_bf[:]),
                     after=[vv.at(pb_vv), vv.at()] if blk == 0 else (),
                     inc=mm if blk == 1 else None)
            P.do("vector", lambda e: e.tensor_copy(out=ptb[:], in_=psT[:]),
                 after=[mm.at()], inc=vv)
            ptb_vv = vv.val
            for blk, tl in ((0, t_prev), (1, t_self)):
                P.do("tensor",
                     lambda e, blk=blk, tl=tl, h=h: e.matmul(
                         psctx[:], vt_v[:, tl, h * 64:(h + 1) * 64],
                         ptb[:, blk * 128:(blk + 1) * 128],
                         start=(blk == 0), stop=(blk == 1)),
                     after=[vv.at(ptb_vv), sc.at()] if blk == 0 else (),
                     inc=mm if blk == 1 else None)
            P.do("scalar",
                 lambda e, hrow=hrow, htile=htile, qb=qb: e.copy(
                     out=cf_v[hrow:hrow + 64, htile, (qb + 1) * 128:(qb + 2) * 128],
                     in_=psctx[:]),
                 after=[mm.at()], inc=sc)
    attn_done_sc = sc.val

    P.dma("gpsimd", w_v, woT.rearrange("(a p) o -> p a o", p=128),
          after=[mm.at()], inc=gp)
    wo_ld = gp.val
    rs_c = smB[:, 4:5]; ssq_c = smB[:, 5:6]; mean_c = smB[:, 6:7]
    var_c = smB[:, 7:8]; rstd_c = smB[:, 8:9]; msq_c = smB[:, 9:10]
    eps_c = smB[:, 10:11]
    P.do("vector", lambda e: e.memset(eps_c, 1e-5), inc=vv)
    for tt in range(16):
        r0, r1 = tt * 128, (tt + 1) * 128
        P.dma("sync", resid[:], x_in[r0:r1, :], after=[vv.at(), sc.at()], inc=ld)
        res_ld = ld.val
        for os_ in range(2):
            for ht in range(8):
                P.do("tensor",
                     lambda e, tt=tt, os_=os_, ht=ht: e.matmul(
                         pso[:], cf_v[:, ht, (tt + 1) * 128:(tt + 2) * 128],
                         w_v[:, ht, os_ * 512:(os_ + 1) * 512],
                         start=(ht == 0), stop=False),
                     after=[gp.at(wo_ld), sc.at(attn_done_sc), vv.at(), sc.at()]
                     if ht == 0 else (), inc=None)
            P.do("tensor",
                 lambda e, os_=os_: e.matmul(
                     pso[:], ones_r[:],
                     biasr[0:1, 3 * DM + os_ * 512: 3 * DM + (os_ + 1) * 512],
                     start=False, stop=True), inc=mm)
            P.do("scalar",
                 lambda e, os_=os_: e.copy(out=qs[:, os_ * 512:(os_ + 1) * 512],
                                           in_=pso[:]),
                 after=[mm.at()], inc=sc)
        P.strict = {"vector": [vv, sc], "scalar": [sc, vv]}
        P.do("vector", lambda e: e.tensor_tensor(out=qs[:], in0=qs[:], in1=resid[:],
                                                 op=ALU.add),
             after=[ld.at(res_ld), sc.at()], inc=vv)
        P.do("vector", lambda e: e.reduce_sum(out=rs_c, in_=qs[:], axis=AX.X), inc=vv)
        P.do("scalar", lambda e: e.activation(out=qs2[:],
                                              in_=qs[:], func=AF.Square,
                                              accum_out=ssq_c), inc=sc)
        P.do("vector", lambda e: e.tensor_scalar_mul(mean_c, rs_c, 1.0 / DM), inc=vv)
        P.do("vector", lambda e: e.tensor_mul(msq_c, mean_c, mean_c), inc=vv)
        P.do("vector", lambda e: e.tensor_scalar_mul(var_c, ssq_c, 1.0 / DM),
             after=[sc.at()], inc=vv)
        P.do("vector", lambda e: e.tensor_sub(var_c, var_c, msq_c), inc=vv)
        P.do("scalar", lambda e: e.activation(out=rstd_c, in_=var_c, func=AF.Ln,
                                              bias=eps_c), inc=sc)
        P.do("scalar", lambda e: e.activation(out=rstd_c, in_=rstd_c, func=AF.Exp,
                                              scale=-0.5), inc=sc)
        P.do("vector", lambda e: e.tensor_scalar(out=qs[:], in0=qs[:], scalar1=mean_c,
                                                 scalar2=rstd_c, op0=ALU.subtract,
                                                 op1=ALU.mult),
             after=[sc.at()], inc=vv)
        P.strict = {}
        P.dma("sync", x3_out[r0:r1, :], qs[:], after=[vv.at()], inc=st)
    return dict(ld=ld, gp=gp, mm=mm, sc=sc, vv=vv, st=st)

def build_phase_b(debug=False):
    nc = bass.Bass("TRN2", target_bir_lowering=False, debug=False)
    io = declare_b_io(nc, debug)
    from contextlib import ExitStack
    with ExitStack() as _stk:
        P = Prog(nc)
        C = emit_phase_b(nc, P, _stk.enter_context, io, "b_", debug)
        P.wait("gpsimd", [C["st"].at(), C["ld"].at(), C["gp"].at()])
        P.realize()
    return nc

def prep_phase_b_inputs(inputs, x2_list):
    x2 = np.concatenate([np.asarray(a) for a in x2_list], axis=0)
    g = lambda k: np.asarray(inputs["L1_attn_" + k])
    shared = {
        "wqT": np.ascontiguousarray(g("Wq").T), "wkT": np.ascontiguousarray(g("Wk").T),
        "wvT": np.ascontiguousarray(g("Wv").T), "woT": np.ascontiguousarray(g("Wo").T),
        "biases": np.ascontiguousarray(np.stack([g("bq"), g("bk"), g("bv"), g("bo")])),
    }
    qi = np.arange(128)[:, None]
    kj = np.arange(256)[None, :]
    band = (kj >= qi) & (kj <= qi + 128)
    mask_n = np.where(band, 0.0, -1e30).astype(np.float32)
    mask_f = np.where(band & (kj >= 128), 0.0, -1e30).astype(np.float32)
    maps = []
    for c in range(8):
        b, half = c // 2, c % 2
        base = half * TA
        pos = (base - 128 + np.arange(NTB)[None, :] * 128
               + np.arange(128)[:, None]).astype(np.float32)
        halo = (x2[c * TA - 128: c * TA] if half == 1
                else np.zeros((128, DM), np.float32))
        m = dict(shared)
        m["x_in"] = np.ascontiguousarray(x2[c * TA:(c + 1) * TA])
        m["halo_in"] = np.ascontiguousarray(halo)
        m["posv"] = np.ascontiguousarray(pos)
        m["masks_in"] = np.ascontiguousarray(
            np.concatenate([mask_f if half == 0 else mask_n, mask_n], axis=1))
        maps.append(m)
    return maps

def run_phase_b(inputs, x2_list, debug=False):
    nc = build_phase_b(debug=debug)
    in_maps = prep_phase_b_inputs(inputs, x2_list)
    res = run_bass_kernel_spmd(nc, in_maps, core_ids=list(range(8)))
    return [r["x3_out"] for r in res.results]


def declare_c_io(nc, debug=False):
    io = {}
    io["x_in"] = nc.dram_tensor("x_in", [TA, DM], F32, kind="ExternalInput").ap()
    io["qwT"] = nc.dram_tensor("qwT", [DM, DM], F32, kind="ExternalInput").ap()
    io["kwT"] = nc.dram_tensor("kwT", [DM, DM], F32, kind="ExternalInput").ap()
    io["vwT"] = nc.dram_tensor("vwT", [DM, DM], F32, kind="ExternalInput").ap()
    io["biases"] = nc.dram_tensor("biases", [3, DM], F32, kind="ExternalInput").ap()
    io["gamma1"] = nc.dram_tensor("gamma1", [1, DM], F32, kind="ExternalInput").ap()
    io["theta"] = nc.dram_tensor("theta", [1, 32], F32, kind="ExternalInput").ap()
    io["posv"] = nc.dram_tensor("posv", [128, 16], F32, kind="ExternalInput").ap()
    io["qn_out"] = nc.dram_tensor("qn_out", [DM, TA], BF16, kind="ExternalOutput").ap()
    io["kn_out"] = nc.dram_tensor("kn_out", [DM, TA], BF16, kind="ExternalOutput").ap()
    io["vn_out"] = nc.dram_tensor("vn_out", [TA, DM], BF16, kind="ExternalOutput").ap()
    io["vn_out3"] = io["vn_out"].rearrange("t (r c) -> t r c", r=2)
    io["xn_out"] = nc.dram_tensor("xn_out", [TA, DM], F32, kind="ExternalOutput").ap()
    return io

def emit_phase_c(nc, P, ec, io, pfx="c_", debug=False):
    x_in = io["x_in"]
    qwT = io["qwT"]
    kwT = io["kwT"]
    vwT = io["vwT"]
    biases = io["biases"]
    gamma1 = io["gamma1"]
    theta = io["theta"]
    posv = io["posv"]
    qn_out = io["qn_out"]
    kn_out = io["kn_out"]
    vn_out3 = io["vn_out3"]
    xn_out = io["xn_out"]
    xnf = ec(nc.sbuf_tensor(pfx + "xnfC", [128, 8 * TA], BF16))
    wbuf = ec(nc.sbuf_tensor(pfx + "wbufC", [128, 8 * DM], BF16))
    outbuf = ec(nc.sbuf_tensor(pfx + "outbufC", [128, 16 * DM], BF16))
    biasr = ec(nc.sbuf_tensor(pfx + "biasrC", [1, 3 * DM], BF16))
    ones_r = ec(nc.sbuf_tensor(pfx + "ones_rC", [1, 128], BF16))
    ident_bf = ec(nc.sbuf_tensor(pfx + "identC", [128, 128], BF16))
    gam_b = ec(nc.sbuf_tensor(pfx + "gam_bC", [128, DM], F32))
    trig = ec(nc.sbuf_tensor(pfx + "trigC", [128, 160], F32))
    posv_sb = ec(nc.sbuf_tensor(pfx + "posvC", [128, 16], F32))
    qs = ec(nc.sbuf_tensor(pfx + "qsC", [128, DM], F32))
    qs2 = ec(nc.sbuf_tensor(pfx + "qs2C", [128, DM], F32))
    tmp5 = ec(nc.sbuf_tensor(pfx + "tmp5C", [128, 512], F32))
    qsb = ec(nc.sbuf_tensor(pfx + "qsbC", [128, DM], BF16))
    smC = ec(nc.sbuf_tensor(pfx + "smC", [128, 16], F32))
    psq = ec(nc.psum_tensor(pfx + "psqC", [128, 512], F32))
    pstC = ec(nc.psum_tensor(pfx + "pstC", [128, DM], BF16))
    s_ld = ec(nc.semaphore(pfx + "s_ld")); s_gp = ec(nc.semaphore(pfx + "s_gp"))
    s_mm = ec(nc.semaphore(pfx + "s_mm")); s_sc = ec(nc.semaphore(pfx + "s_sc"))
    s_vv = ec(nc.semaphore(pfx + "s_vv")); s_st = ec(nc.semaphore(pfx + "s_st"))
    ld = Ctr(s_ld); gp = Ctr(s_gp); mm = Ctr(s_mm)
    sc = Ctr(s_sc); vv = Ctr(s_vv); st = Ctr(s_st)

    xnf_v = xnf[:].rearrange("p (h t) -> p h t", h=8)
    w_v = wbuf[:].rearrange("p (h o) -> p h o", h=8)
    ob_v = outbuf[:].rearrange("p (t o) -> p t o", t=16)
    th_c = trig[:, 0:32]; ang_c = trig[:, 32:64]
    cos_c = trig[:, 64:96]; sin_c = trig[:, 96:128]; kk_c = trig[:, 128:160]
    rs_c = smC[:, 0:1]; ssq_c = smC[:, 1:2]; rstd_c = smC[:, 2:3]; eps_c = smC[:, 3:4]

    M_MAGIC = 12582912.0
    INV2PI = 0.15915493667125702
    CW1 = 6.28125
    CW2 = float(np.float32(6.283185307179586 - 6.28125))
    HALFPI = float(np.float32(np.pi / 2))

    P.do("gpsimd", lambda e: e.memset(ident_bf[:], 0.0), inc=gp, n=1)
    P.do("gpsimd", lambda e: e.affine_select(
        out=ident_bf[:], in_=ident_bf[:], compare_op=ALU.not_equal, fill=1.0,
        base=0, pattern=[[-1, 128]], channel_multiplier=1), inc=gp, n=1)
    P.do("gpsimd", lambda e: e.memset(ones_r[:], 1.0), inc=gp, n=1)
    gi_const = gp.val
    P.do("vector", lambda e: e.memset(eps_c, 1e-5), inc=vv)
    P.dma("sync", th_c, theta.broadcast_to([128, 32]), inc=ld)
    P.dma("sync", gam_b[:], gamma1.broadcast_to([128, DM]), inc=ld)
    P.dma("sync", posv_sb[:], posv, inc=ld)
    P.dma("gpsimd", biasr[:].rearrange("p (a o) -> p a o", a=3),
          biases.rearrange("(x a) o -> x a o", x=1), inc=gp)
    consts_ld = ld.val

    for tt in range(16):
        r0, r1 = tt * 128, (tt + 1) * 128
        P.dma("sync", qs[:], x_in[r0:r1, :], after=[vv.at(), sc.at(), mm.at()], inc=ld)
        P.strict = {"vector": [vv, sc], "scalar": [sc, vv]}
        P.do("scalar", lambda e: e.activation(out=qs2[:],
                                              in_=qs[:], func=AF.Square,
                                              accum_out=ssq_c),
             after=[ld.at()], inc=sc)
        P.do("scalar", lambda e: e.activation(out=rstd_c, in_=ssq_c, func=AF.Ln,
                                              scale=1.0 / DM, bias=eps_c), inc=sc)
        P.do("scalar", lambda e: e.activation(out=rstd_c, in_=rstd_c, func=AF.Exp,
                                              scale=-0.5), inc=sc)
        P.do("vector", lambda e: e.tensor_scalar(out=qs[:], in0=qs[:], scalar1=rstd_c,
                                                 scalar2=None, op0=ALU.mult), inc=vv)
        P.do("vector", lambda e: e.tensor_tensor(out=qs[:], in0=qs[:], in1=gam_b[:],
                                                 op=ALU.mult),
             after=[ld.at(consts_ld)], inc=vv)
        P.strict = {}
        xn_vv = vv.val
        P.dma("sync", xn_out[r0:r1, :], qs[:], after=[vv.at(xn_vv)], inc=st)
        P.do("vector", lambda e: e.tensor_copy(out=qsb[:], in_=qs[:]),
             after=[vv.at(xn_vv)], inc=vv)
        for ht in range(8):
            P.do("tensor",
                 lambda e, ht=ht: e.transpose(
                     pstC[:, ht * 128:(ht + 1) * 128],
                     qsb[:, ht * 128:(ht + 1) * 128], ident_bf[:]),
                 after=[vv.at(), gp.at(gi_const), sc.at()] if ht == 0 else (),
                 inc=mm if ht == 7 else None)
        P.do("vector",
             lambda e, tt=tt: e.tensor_copy(
                 out=xnf_v[:, :, tt * 128:(tt + 1) * 128],
                 in_=pstC[:].rearrange("p (h t) -> p h t", h=8)),
             after=[mm.at()], inc=vv)
    xnf_done_vv = vv.val

    for w_i, wT in enumerate([qwT, kwT, vwT]):
        P.dma("gpsimd", w_v, wT.rearrange("(a p) o -> p a o", p=128),
              after=[mm.at()], inc=gp)
        wld = gp.val
        for tt in range(16):
            for os_ in range(2):
                dep0 = [gp.at(wld), vv.at(xnf_done_vv), sc.at(), vv.at()]
                for ht in range(8):
                    P.do("tensor",
                         lambda e, tt=tt, os_=os_, ht=ht: e.matmul(
                             psq[:], xnf_v[:, ht, tt * 128:(tt + 1) * 128],
                             w_v[:, ht, os_ * 512:(os_ + 1) * 512],
                             start=(ht == 0), stop=False),
                         after=dep0 if ht == 0 else (), inc=None)
                P.do("tensor",
                     lambda e, w_i=w_i, os_=os_: e.matmul(
                         psq[:], ones_r[:],
                         biasr[0:1, w_i * DM + os_ * 512: w_i * DM + (os_ + 1) * 512],
                         start=False, stop=True),
                     after=[gp.at()], inc=mm)
                P.do("scalar",
                     lambda e, os_=os_: e.copy(out=qs[:, os_ * 512:(os_ + 1) * 512],
                                               in_=psq[:]),
                     after=[mm.at()], inc=sc)
            if w_i == 2:
                P.do("vector", lambda e, tt=tt: e.tensor_copy(out=ob_v[:, tt, :],
                                                              in_=qs[:]),
                     after=[sc.at()], inc=vv)
                continue
            P.strict = {"vector": [vv, sc], "scalar": [sc, vv]}
            for trig_i in range(2):
                P.do("vector", lambda e, tt=tt: e.tensor_scalar(
                    out=ang_c, in0=th_c, scalar1=posv_sb[:, tt:tt + 1],
                    scalar2=None, op0=ALU.mult),
                     after=[ld.at(consts_ld)], inc=vv)
                if trig_i == 1:
                    P.do("vector", lambda e: e.tensor_scalar(
                        out=ang_c, in0=ang_c, scalar1=HALFPI, scalar2=None,
                        op0=ALU.add), inc=vv)
                P.do("vector", lambda e: e.tensor_scalar(
                    out=kk_c, in0=ang_c, scalar1=INV2PI, scalar2=M_MAGIC,
                    op0=ALU.mult, op1=ALU.add), inc=vv)
                P.do("vector", lambda e: e.tensor_scalar(
                    out=kk_c, in0=kk_c, scalar1=M_MAGIC, scalar2=None,
                    op0=ALU.subtract), inc=vv)
                P.do("vector", lambda e: e.scalar_tensor_tensor(
                    out=ang_c, in0=kk_c, scalar=-CW1, in1=ang_c,
                    op0=ALU.mult, op1=ALU.add), inc=vv)
                P.do("vector", lambda e: e.scalar_tensor_tensor(
                    out=ang_c, in0=kk_c, scalar=-CW2, in1=ang_c,
                    op0=ALU.mult, op1=ALU.add), inc=vv)
                P.do("scalar", lambda e, trig_i=trig_i: e.activation(
                    out=(sin_c if trig_i == 0 else cos_c), in_=ang_c,
                    func=AF.Sin), inc=sc)
            qv = qs[:].rearrange("p (h d) -> p h d", h=NH)
            q1 = qv[:, :, 0:63:2] if False else qs[:].rearrange(
                "p (h d2 two) -> p h d2 two", h=NH, two=2)[:, :, :, 0]
            q2 = qs[:].rearrange("p (h d2 two) -> p h d2 two", h=NH, two=2)[:, :, :, 1]
            o1 = qs2[:].rearrange("p (h d) -> p h d", h=NH)[:, :, 0:32]
            o2 = qs2[:].rearrange("p (h d) -> p h d", h=NH)[:, :, 32:64]
            tm = tmp5[:].rearrange("p (h d) -> p h d", h=NH, d=32)
            cosb = cos_c.rearrange("p (a d) -> p a d", a=1).broadcast_to([128, NH, 32])
            sinb = sin_c.rearrange("p (a d) -> p a d", a=1).broadcast_to([128, NH, 32])
            P.do("vector", lambda e, q1=q1, o1=o1, cosb=cosb: e.tensor_tensor(
                out=o1, in0=q1, in1=cosb, op=ALU.mult), inc=vv)
            P.do("vector", lambda e, q2=q2, tm=tm, sinb=sinb: e.tensor_tensor(
                out=tm, in0=q2, in1=sinb, op=ALU.mult), inc=vv)
            P.do("vector", lambda e, o1=o1, tm=tm: e.tensor_tensor(
                out=o1, in0=o1, in1=tm, op=ALU.subtract), inc=vv)
            P.do("vector", lambda e, q2=q2, o2=o2, cosb=cosb: e.tensor_tensor(
                out=o2, in0=q2, in1=cosb, op=ALU.mult), inc=vv)
            P.do("vector", lambda e, q1=q1, tm=tm, sinb=sinb: e.tensor_tensor(
                out=tm, in0=q1, in1=sinb, op=ALU.mult), inc=vv)
            P.do("vector", lambda e, o2=o2, tm=tm: e.tensor_tensor(
                out=o2, in0=o2, in1=tm, op=ALU.add), inc=vv)
            P.do("vector", lambda e: e.tensor_copy(out=qsb[:], in_=qs2[:]), inc=vv)
            P.strict = {}
            cast_vv = vv.val
            for ht in range(8):
                P.do("tensor",
                     lambda e, ht=ht: e.transpose(
                         pstC[:, ht * 128:(ht + 1) * 128],
                         qsb[:, ht * 128:(ht + 1) * 128], ident_bf[:]),
                     after=[vv.at(cast_vv), sc.at(), vv.at()] if ht == 0 else (),
                     inc=mm if ht == 7 else None)
            P.do("vector",
                 lambda e, tt=tt: e.tensor_copy(
                     out=ob_v[:, tt, :].rearrange("p (h t) -> p h t", h=8)
                     if False else ob_v[:, tt, :],
                     in_=pstC[:]),
                 after=[mm.at()], inc=vv)
        dst = (qn_out, kn_out, None)[w_i]
        if w_i < 2:
            P.dma("sync", dst.rearrange("(h p) (tt t) -> p tt h t", p=128, t=128),
                  ob_v[:].rearrange("p tt (h t) -> p tt h t", h=8),
                  after=[vv.at()], inc=st)
        else:
            P.dma("sync", vn_out3.rearrange("(tt p) r c -> p tt r c", p=128),
                  ob_v[:].rearrange("p tt (r c) -> p tt r c", r=2),
                  after=[vv.at()], inc=st)
    return dict(ld=ld, gp=gp, mm=mm, sc=sc, vv=vv, st=st)

def build_phase_c(debug=False):
    nc = bass.Bass("TRN2", target_bir_lowering=False, debug=False)
    io = declare_c_io(nc, debug)
    from contextlib import ExitStack
    with ExitStack() as _stk:
        P = Prog(nc)
        C = emit_phase_c(nc, P, _stk.enter_context, io, "c_", debug)
        P.wait("gpsimd", [C["st"].at(), C["ld"].at(), C["gp"].at()])
        P.realize()
    return nc


def declare_d_io(nc, debug=False):
    io = {}
    io["qn"] = nc.dram_tensor("qn", [512, T], BF16, kind="ExternalInput").ap()
    io["kn"] = nc.dram_tensor("kn", [512, T], BF16, kind="ExternalInput").ap()
    io["vn"] = nc.dram_tensor("vn", [T, 512], BF16, kind="ExternalInput").ap()
    io["ctx_out"] = nc.dram_tensor("ctx_out", [512, T], BF16, kind="ExternalOutput").ap()
    io["qn3"] = io["qn"].rearrange("h (r t) -> h r t", r=2)
    io["kn3"] = io["kn"].rearrange("h (r t) -> h r t", r=2)
    io["ctx_out3"] = io["ctx_out"].rearrange("h (r t) -> h r t", r=2)
    return io

def emit_phase_d(nc, P, ec, io, pfx="d_", debug=False):
    qn3 = io.get("qn3"); kn3 = io.get("kn3"); vn = io.get("vn")
    ctx_out3 = io["ctx_out3"]
    qsb = ec(nc.sbuf_tensor(pfx + "qsbD", [128, 4 * T], BF16))
    ksb = ec(nc.sbuf_tensor(pfx + "ksbD", [128, 4 * T], BF16))
    vsb = ec(nc.sbuf_tensor(pfx + "vsbD", [128, 32 * 512], BF16))
    csb = ec(nc.sbuf_tensor(pfx + "csbD", [128, 4 * T], BF16))
    scores = ec(nc.sbuf_tensor(pfx + "scoresD", [128, T], F32))
    scores2 = ec(nc.sbuf_tensor(pfx + "scoresD2", [128, T], F32))
    pbuf = ec(nc.sbuf_tensor(pfx + "pbufD", [128, T], BF16))
    pbuf2 = ec(nc.sbuf_tensor(pfx + "pbufD2", [128, T], BF16))
    ptb = ec(nc.sbuf_tensor(pfx + "ptbD", [128, 256], BF16))
    cmask = ec(nc.sbuf_tensor(pfx + "cmaskD", [128, 128], F32))
    ident_bf = ec(nc.sbuf_tensor(pfx + "identD", [128, 128], BF16))
    smD = ec(nc.sbuf_tensor(pfx + "smD", [128, 16], F32))
    pss0 = ec(nc.psum_tensor(pfx + "pss0", [128, 512], F32))
    pss1 = ec(nc.psum_tensor(pfx + "pss1", [128, 512], F32))
    psT = ec(nc.psum_tensor(pfx + "psTD", [128, 256], BF16))
    psctx0 = ec(nc.psum_tensor(pfx + "psctxD0", [64, 128], F32))
    psctx1 = ec(nc.psum_tensor(pfx + "psctxD1", [64, 128], F32))
    s_ld = ec(nc.semaphore(pfx + "s_ld")); s_gp = ec(nc.semaphore(pfx + "s_gp"))
    s_mm = ec(nc.semaphore(pfx + "s_mm")); s_sc = ec(nc.semaphore(pfx + "s_sc"))
    s_vv = ec(nc.semaphore(pfx + "s_vv")); s_st = ec(nc.semaphore(pfx + "s_st"))
    ld = Ctr(s_ld); gp = Ctr(s_gp); mm = Ctr(s_mm)
    sc = Ctr(s_sc); vv = Ctr(s_vv); st = Ctr(s_st)

    q_v = qsb[:].rearrange("p (a t) -> p a t", a=4)
    k_v = ksb[:].rearrange("p (a t) -> p a t", a=4)
    v_v = vsb[:].rearrange("p (tt o) -> p tt o", tt=32)
    c_v = csb[:].rearrange("p (a t) -> p a t", a=4)
    rmax_c = smD[:, 0:1]; rsum_c = smD[:, 1:2]; rinv_c = smD[:, 2:3]
    neg_c = smD[:, 3:4]

    P.do("gpsimd", lambda e: e.memset(ident_bf[:], 0.0), inc=gp, n=1)
    P.do("gpsimd", lambda e: e.affine_select(
        out=ident_bf[:], in_=ident_bf[:], compare_op=ALU.not_equal, fill=1.0,
        base=0, pattern=[[-1, 128]], channel_multiplier=1), inc=gp, n=1)
    P.do("gpsimd", lambda e: e.memset(cmask[:], 0.0), inc=gp, n=1)
    P.do("gpsimd", lambda e: e.affine_select(
        out=cmask[:], in_=cmask[:], compare_op=ALU.is_ge, fill=-1e30,
        base=0, pattern=[[-1, 128]], channel_multiplier=1), inc=gp, n=1)
    gi_const = gp.val
    blend = "sel01" in io
    if blend:
        sel_sb = ec(nc.sbuf_tensor(pfx + "sel", [128, 2], F32))
        P.dma("sync", sel_sb[:], io["sel01"], inc=ld)
        qg5 = io["qg"].rearrange("(r g a p) t -> p r g a t", r=2, g=2, a=4)
        kg5 = io["kg"].rearrange("(r g a p) t -> p r g a t", r=2, g=2, a=4)
        vg5 = io["vg"].rearrange("(r g tt p) o -> p r g tt o", r=2, g=2, tt=16)
        scr_q = csb[:, 0:2048]
        sel_ld = ld.val
        for arr, g5, dstv in (("q", qg5, q_v), ("k", kg5, k_v)):
            for a in range(4):
                for r in range(2):
                    P.dma("sync", dstv[:, a, r * TA:(r + 1) * TA],
                          g5[:, r, 0, a, :], inc=ld)
                    P.dma("sync", scr_q, g5[:, r, 1, a, :],
                          after=[vv.at()], inc=ld)
                    v_ld = ld.val
                    P.do("vector", lambda e, dv=dstv[:, a, r * TA:(r + 1) * TA]:
                         e.tensor_scalar(out=dv, in0=dv, scalar1=sel_sb[:, 0:1],
                                         scalar2=None, op0=ALU.mult),
                         after=[ld.at(v_ld)], inc=vv)
                    P.do("vector", lambda e, dv=dstv[:, a, r * TA:(r + 1) * TA]:
                         e.scalar_tensor_tensor(out=dv, in0=scr_q,
                                                scalar=sel_sb[:, 1:2], in1=dv,
                                                op0=ALU.mult, op1=ALU.add),
                         after=[vv.at()], inc=vv)
        vsb4 = vsb[:].rearrange("p (r tt o) -> p r tt o", r=2, tt=16)
        P.dma("sync", vsb4, vg5[:, :, 0, :, :], after=[vv.at()], inc=ld)
        P.dma("sync", csb[:, 0:16384].rearrange("p (r tt o) -> p r tt o", r=2, tt=16),
              vg5[:, :, 1, :, :], inc=ld)
        v_ld = ld.val
        P.do("vector", lambda e: e.tensor_scalar(out=vsb[:], in0=vsb[:],
                                                 scalar1=sel_sb[:, 0:1], scalar2=None,
                                                 op0=ALU.mult),
             after=[ld.at(v_ld)], inc=vv)
        P.do("vector", lambda e: e.scalar_tensor_tensor(
            out=vsb[:], in0=csb[:, 0:16384], scalar=sel_sb[:, 1:2], in1=vsb[:],
            op0=ALU.mult, op1=ALU.add),
             after=[vv.at()], inc=vv)
    else:
        for a in range(4):
            for r in range(2):
                P.dma("sync", q_v[:, a, r * TA:(r + 1) * TA],
                      qn3[a * 128:(a + 1) * 128, r, :], inc=ld)
                P.dma("sync", k_v[:, a, r * TA:(r + 1) * TA],
                      kn3[a * 128:(a + 1) * 128, r, :], inc=ld)
        P.dma("sync", vsb[:].rearrange("p (tt o) -> p tt o", tt=32),
              vn.rearrange("(tt p) o -> p tt o", p=128), inc=ld)
    ld_all = ld.val
    blend_done_vv = vv.val

    iters = [(h, qb) for h in range(8) for qb in range(32)]
    NIT = len(iters)
    sc_half = [scores[:, 0:2048], None]
    stats = [dict() for _ in range(NIT)]

    def s_scores(i):
        h, qb = iters[i]
        atile, hrow = h // 2, (h % 2) * 64
        nkb = qb + 1
        nslab = (nkb + 3) // 4
        scb_i = (scores, scores2)[i % 2]
        for sl in range(nslab):
            k0 = sl * 512
            kw = min(512, nkb * 128 - k0)
            bank = (i * 37 + sl) % 2
            ps = (pss0, pss1)[bank]
            deps = [ld.at(ld_all), vv.at(blend_done_vv)]
            if psum_free_vv[bank]:
                deps.append(vv.at(psum_free_vv[bank]))
            P.do("tensor",
                 lambda e, atile=atile, hrow=hrow, qb=qb, k0=k0, kw=kw, ps=ps:
                 e.matmul(
                     ps[:, 0:kw],
                     q_v[hrow:hrow + 64, atile, qb * 128:(qb + 1) * 128],
                     k_v[hrow:hrow + 64, atile, k0:k0 + kw],
                     start=True, stop=True),
                 after=deps, inc=mm)
            mmv = mm.val
            P.do("vector",
                 lambda e, k0=k0, kw=kw, ps=ps, scb_i=scb_i: e.tensor_scalar(
                     out=scb_i[:, k0:k0 + kw], in0=ps[:, 0:kw], scalar1=0.125,
                     scalar2=None, op0=ALU.mult),
                 after=[mm.at(mmv), sc.at(exp_done_sc[i % 2])], inc=vv)
            psum_free_vv[bank] = vv.val
        P.do("vector", lambda e, qb=qb, scb_i=scb_i: e.tensor_tensor(
            out=scb_i[:, qb * 128:(qb + 1) * 128],
            in0=scb_i[:, qb * 128:(qb + 1) * 128],
            in1=cmask[:], op=ALU.add),
             after=[gp.at(gi_const)], inc=vv)
        neg_i = smD[:, 4 * (i % 2):4 * (i % 2) + 1]
        P.do("vector", lambda e, nkb=nkb, scb_i=scb_i, neg_i=neg_i: e.tensor_reduce(
            out=neg_i, in_=scb_i[:, 0:nkb * 128], axis=AX.X, op=ALU.max,
            negate=True), inc=vv)
        stats[i]["rmax_vv"] = vv.val

    def s_softmax(i):
        h, qb = iters[i]
        nkb = qb + 1
        scb_i = (scores, scores2)[i % 2]
        pb_i = (pbuf, pbuf2)[i % 2]
        neg_i = smD[:, 4 * (i % 2):4 * (i % 2) + 1]
        rsum_i = smD[:, 4 * (i % 2) + 1:4 * (i % 2) + 2]
        rinv_i = smD[:, 4 * (i % 2) + 2:4 * (i % 2) + 3]
        P.do("scalar", lambda e, nkb=nkb, scb_i=scb_i, pb_i=pb_i, neg_i=neg_i,
             rsum_i=rsum_i: e.activation(
                 out=pb_i[:, 0:nkb * 128], in_=scb_i[:, 0:nkb * 128], func=AF.Exp,
                 bias=neg_i, accum_out=rsum_i),
             after=[vv.at(stats[i]["rmax_vv"]),
                    vv.at(stats[i - 2]["norm_vv"] if i >= 2 else 0),
                    sc.at(ctx_evac_sc[i % 2])], inc=sc)
        exp_done_sc[i % 2] = sc.val
        exp_sc = sc.val
        P.do("vector", lambda e, rsum_i=rsum_i, rinv_i=rinv_i: e.reciprocal(
            out=rinv_i, in_=rsum_i), after=[sc.at(exp_sc)], inc=vv)
        rec_vv = vv.val
        P.do("vector", lambda e, nkb=nkb, pb_i=pb_i, rinv_i=rinv_i: e.tensor_scalar(
            out=pb_i[:, 0:nkb * 128], in0=pb_i[:, 0:nkb * 128], scalar1=rinv_i,
            scalar2=None, op0=ALU.mult), after=[vv.at(rec_vv)], inc=vv)
        stats[i]["norm_vv"] = vv.val

    def s_ctx(i):
        h, qb = iters[i]
        atile, hrow = h // 2, (h % 2) * 64
        nkb = qb + 1
        pb_i = (pbuf, pbuf2)[i % 2]
        psc = (psctx0, psctx1)[i % 2]
        for kb in range(nkb):
            sl2 = kb % 2
            tdeps = [vv.at(stats[i]["norm_vv"])]
            if ptb_free_vv[sl2]:
                tdeps.append(vv.at(ptb_free_vv[sl2]))
            P.do("tensor",
                 lambda e, kb=kb, sl2=sl2, pb_i=pb_i: e.transpose(
                     psT[:, sl2 * 128:(sl2 + 1) * 128],
                     pb_i[:, kb * 128:(kb + 1) * 128], ident_bf[:]),
                 after=tdeps, inc=mm)
            tr_mm = mm.val
            P.do("vector",
                 lambda e, sl2=sl2: e.tensor_copy(
                     out=ptb[:, sl2 * 128:(sl2 + 1) * 128],
                     in_=psT[:, sl2 * 128:(sl2 + 1) * 128]),
                 after=[mm.at(tr_mm)], inc=vv)
            ptb_vv = vv.val
            P.do("tensor",
                 lambda e, kb=kb, sl2=sl2, h=h, nkb=nkb, psc=psc: e.matmul(
                     psc[:], v_v[:, kb, h * 64:(h + 1) * 64],
                     ptb[:, sl2 * 128:(sl2 + 1) * 128],
                     start=(kb == 0), stop=(kb == nkb - 1)),
                 after=[vv.at(ptb_vv)] + ([sc.at(ctx_evac_sc[i % 2])]
                                          if (kb == 0 and ctx_evac_sc[i % 2]) else []),
                 inc=mm)
            ptb_free_vv[sl2] = 0
        ctx_mm = mm.val
        P.do("scalar",
             lambda e, atile=atile, hrow=hrow, qb=qb, psc=psc: e.copy(
                 out=c_v[hrow:hrow + 64, atile, qb * 128:(qb + 1) * 128],
                 in_=psc[:]),
             after=[mm.at(ctx_mm)], inc=sc)
        ctx_evac_sc[i % 2] = sc.val

    psum_free_vv = [0, 0]
    ptb_free_vv = [0, 0]
    ctx_evac_sc = [0, 0]
    exp_done_sc = [0, 0]
    s_scores(0)
    s_softmax(0)
    s_scores(1)
    for i in range(NIT):
        if i + 2 < NIT:
            s_scores(i + 2)
        if i + 1 < NIT:
            s_softmax(i + 1)
        s_ctx(i)

    for a in range(4):
        for r in range(2):
            P.dma("sync", ctx_out3[a * 128:(a + 1) * 128, r, :],
                  c_v[:, a, r * TA:(r + 1) * TA],
                  after=[sc.at(), vv.at()], inc=st)
    return dict(ld=ld, gp=gp, mm=mm, sc=sc, vv=vv, st=st)

def build_phase_d(debug=False):
    nc = bass.Bass("TRN2", target_bir_lowering=False, debug=False)
    io = declare_d_io(nc, debug)
    from contextlib import ExitStack
    with ExitStack() as _stk:
        P = Prog(nc)
        C = emit_phase_d(nc, P, _stk.enter_context, io, "d_", debug)
        P.wait("gpsimd", [C["st"].at(), C["ld"].at(), C["gp"].at()])
        P.realize()
    return nc


NE1 = 31

def declare_e_io(nc, debug=False):
    io = {}
    io["x3_in"] = nc.dram_tensor("x3_in", [TA, DM], F32, kind="ExternalInput").ap()
    io["xn_in"] = nc.dram_tensor("xn_in", [TA, DM], F32, kind="ExternalInput").ap()
    io["ctx_in"] = nc.dram_tensor("ctx_in", [DM, TA], BF16, kind="ExternalInput").ap()
    io["owT"] = nc.dram_tensor("owT", [DM, DM], F32, kind="ExternalInput").ap()
    io["ob_in"] = nc.dram_tensor("ob_in", [1, DM], F32, kind="ExternalInput").ap()
    io["gamma2"] = nc.dram_tensor("gamma2", [1, DM], F32, kind="ExternalInput").ap()
    io["shw1"] = nc.dram_tensor("shw1", [DM, DM], F32, kind="ExternalInput").ap()
    io["shb1_rs"] = nc.dram_tensor("shb1_rs", [128, 8], F32, kind="ExternalInput").ap()
    io["shw2"] = nc.dram_tensor("shw2", [DM, DM], F32, kind="ExternalInput").ap()
    io["shb2"] = nc.dram_tensor("shb2", [1, DM], F32, kind="ExternalInput").ap()
    io["rw"] = nc.dram_tensor("rw", [DM, NE1], F32, kind="ExternalInput").ap()
    io["rb"] = nc.dram_tensor("rb", [1, NE1], F32, kind="ExternalInput").ap()
    io["rew1"] = nc.dram_tensor("rew1", [NE1, DM, DM], F32, kind="ExternalInput").ap()
    io["reb1T"] = nc.dram_tensor("reb1T", [DM, NE1], F32, kind="ExternalInput").ap()
    io["rew2"] = nc.dram_tensor("rew2", [NE1, DM, DM], F32, kind="ExternalInput").ap()
    io["reb2"] = nc.dram_tensor("reb2", [NE1, DM], F32, kind="ExternalInput").ap()
    io["out_f"] = nc.dram_tensor("out_f", [TA, DM], F32, kind="ExternalOutput").ap()
    knd = "ExternalOutput" if debug else "Internal"
    io["ao_hbm"] = nc.dram_tensor("ao_hbm", [TA, DM], F32, kind=knd).ap()
    io["an_hbm"] = nc.dram_tensor("an_hbm", [TA, DM], F32, kind=knd).ap()
    io["dbgE"] = nc.dram_tensor("dbgE", [128, 1024], F32, kind=knd).ap()
    return io

def emit_phase_e(nc, P, ec, io, pfx="e_", debug=False):
    x3_in = io["x3_in"]
    xn_in = io["xn_in"]
    ctx_in = io.get("ctx_in")
    owT = io["owT"]
    ob_in = io["ob_in"]
    gamma2 = io["gamma2"]
    shw1 = io["shw1"]
    shb1_rs = io["shb1_rs"]
    shw2 = io["shw2"]
    shb2 = io["shb2"]
    rw = io["rw"]
    rb = io["rb"]
    rew1 = io["rew1"]
    reb1T = io["reb1T"]
    rew2 = io["rew2"]
    reb2 = io["reb2"]
    out_f = io["out_f"]
    ao_hbm = io["ao_hbm"]
    an_hbm = io["an_hbm"]
    dbgE = io["dbgE"]
    anf = ec(nc.sbuf_tensor(pfx + "anfE", [128, 8 * TA], BF16))
    acc = ec(nc.sbuf_tensor(pfx + "accE", [128, 16 * DM], F32))
    hf = ec(nc.sbuf_tensor(pfx + "hfE", [128, 8 * TA], BF16))
    wbuf = ec(nc.sbuf_tensor(pfx + "wbufE", [128, 24576], BF16))
    ctxf = ec(nc.sbuf_tensor(pfx + "ctxfE", [128, 8 * 128], BF16))
    gates = ec(nc.sbuf_tensor(pfx + "gatesE", [128, 16 * NE1], F32))
    gatesb = ec(nc.sbuf_tensor(pfx + "gatesbE", [128, NE1], BF16))
    gatesT = ec(nc.sbuf_tensor(pfx + "gatesTE", [31, 16 * 128], BF16))
    rw_sb = ec(nc.sbuf_tensor(pfx + "rw_sbE", [128, 8 * NE1], F32))
    reb2_sb = ec(nc.sbuf_tensor(pfx + "reb2E", [31, DM], BF16))
    b1c = ec(nc.sbuf_tensor(pfx + "b1cE", [128, 8 + 8 * NE1], F32))
    biasr = ec(nc.sbuf_tensor(pfx + "biasrE", [1, 3 * DM + NE1], BF16))
    ones_r = ec(nc.sbuf_tensor(pfx + "ones_rE", [1, 128], BF16))
    ident_bf = ec(nc.sbuf_tensor(pfx + "identE", [128, 128], BF16))
    ident32 = ec(nc.sbuf_tensor(pfx + "ident32E", [128, 128], F32))

    scr2 = ec(nc.sbuf_tensor(pfx + "scr2E", [128, 64], F32))
    smE = ec(nc.sbuf_tensor(pfx + "smE", [128, 32], F32))
    psq = ec(nc.psum_tensor(pfx + "psqE", [128, 512], F32))
    psq2 = ec(nc.psum_tensor(pfx + "psq2E", [128, 512], F32))
    pst = ec(nc.psum_tensor(pfx + "pstE", [128, DM], BF16))
    pst32 = ec(nc.psum_tensor(pfx + "pst32E", [128, DM], F32))
    psr = ec(nc.psum_tensor(pfx + "psrE", [128, 128], F32))
    s_ld = ec(nc.semaphore(pfx + "s_ld")); s_gp = ec(nc.semaphore(pfx + "s_gp"))
    s_mm = ec(nc.semaphore(pfx + "s_mm")); s_sc = ec(nc.semaphore(pfx + "s_sc"))
    s_vv = ec(nc.semaphore(pfx + "s_vv")); s_st = ec(nc.semaphore(pfx + "s_st"))
    ld = Ctr(s_ld); gp = Ctr(s_gp); mm = Ctr(s_mm)
    sc = Ctr(s_sc); vv = Ctr(s_vv); st = Ctr(s_st)

    eblend = "sel01" in io
    anf_v = anf[:].rearrange("p (h t) -> p h t", h=8)
    ant32 = hf[:, 0:2048].bitcast(F32)
    gamb = hf[:, 2048:4096].bitcast(F32)
    h2 = hf[:, 4096:6144].bitcast(F32)
    h1 = hf[:, 6144:8192].bitcast(F32)
    hb = hf[:, 8192:9216]
    hf_v = hf[:].rearrange("p (h t) -> p h t", h=8)
    wv = [wbuf[:, 0:8192].rearrange("p (h o) -> p h o", h=8),
          wbuf[:, 8192:16384].rearrange("p (h o) -> p h o", h=8)]
    ctx_v = ctxf[:].rearrange("p (h t) -> p h t", h=8)
    rw_v = rw_sb[:].rearrange("p (h e) -> p h e", h=8)
    shb1_c = b1c[:, 0:8]
    reb1_c = b1c[:, 8:8 + 8 * NE1].rearrange("p (it e) -> p it e", it=8)
    rs_c = smE[:, 0:1]; ssq_c = smE[:, 1:2]; mean_c = smE[:, 2:3]
    var_c = smE[:, 3:4]; rstd_c = smE[:, 4:5]; msq_c = smE[:, 5:6]
    eps_c = smE[:, 6:7]; mx_c = smE[:, 8:16]
    rmax_c = smE[:, 16:17]; rsum_c = smE[:, 17:18]; rinv_c = smE[:, 18:19]

    P.do("gpsimd", lambda e: e.memset(ident_bf[:], 0.0), inc=gp, n=1)
    P.do("gpsimd", lambda e: e.affine_select(
        out=ident_bf[:], in_=ident_bf[:], compare_op=ALU.not_equal, fill=1.0,
        base=0, pattern=[[-1, 128]], channel_multiplier=1), inc=gp, n=1)
    P.do("gpsimd", lambda e: e.memset(ident32[:], 0.0), inc=gp, n=1)
    P.do("gpsimd", lambda e: e.affine_select(
        out=ident32[:], in_=ident32[:], compare_op=ALU.not_equal, fill=1.0,
        base=0, pattern=[[-1, 128]], channel_multiplier=1), inc=gp, n=1)
    P.do("gpsimd", lambda e: e.memset(ones_r[:], 1.0), inc=gp, n=1)
    gi_const = gp.val
    P.do("vector", lambda e: e.memset(eps_c, 1e-5), inc=vv)
    P.dma("sync", gamb[:], gamma2.broadcast_to([128, DM]), inc=ld)
    P.dma("sync", shb1_c, shb1_rs, inc=ld)
    P.dma("sync", reb1_c, reb1T.rearrange("(it p) e -> p it e", p=128), inc=ld)
    P.dma("sync", rw_v, rw.rearrange("(h p) e -> p h e", p=128), inc=ld)
    P.dma("gpsimd", reb2_sb[:], reb2, inc=gp)
    P.dma("gpsimd", biasr[0:1, 0:DM], ob_in, inc=gp)
    P.dma("gpsimd", biasr[0:1, DM:2 * DM], shb2, inc=gp)
    P.dma("gpsimd", biasr[0:1, 2 * DM:2 * DM + NE1], rb, inc=gp)
    consts_ld = ld.val; consts_gp = gp.val

    sel_ldE = 0
    if eblend:
        sel_sb = ec(nc.sbuf_tensor(pfx + "sel", [128, 2], F32))
        P.dma("sync", sel_sb[:], io["sel01"], inc=ld)
        sel_ldE = ld.val
        ctxg5 = io["ctxg"].rearrange("(g f q p) t -> p g f q t", g=2, f=2, q=4)
        ctxsc = hf[:, 9216:10240]
    P.dma("gpsimd", wv[0], owT.rearrange("(a p) o -> p a o", p=128), inc=gp)
    ow_ld = gp.val
    for tt in range(16):
        r0, r1 = tt * 128, (tt + 1) * 128
        if eblend:
            P.dma("gpsimd", ctx_v.rearrange("p (g q) t -> p g q t", g=2),
                  ctxg5[:, :, 0, :, r0:r1],
                  after=[mm.at()], inc=gp)
            P.dma("gpsimd", ctxsc.rearrange("p (g q t) -> p g q t", g=2, q=4),
                  ctxg5[:, :, 1, :, r0:r1],
                  inc=gp)
            ctx_gp = gp.val
            P.do("vector", lambda e: e.tensor_scalar(
                out=ctxf[:], in0=ctxf[:], scalar1=sel_sb[:, 0:1], scalar2=None,
                op0=ALU.mult),
                 after=[gp.at(ctx_gp), ld.at(sel_ldE)], inc=vv)
            P.do("vector", lambda e: e.scalar_tensor_tensor(
                out=ctxf[:], in0=ctxsc, scalar=sel_sb[:, 1:2], in1=ctxf[:],
                op0=ALU.mult, op1=ALU.add),
                 after=[vv.at()], inc=vv)
        else:
            P.dma("gpsimd", ctx_v,
                  ctx_in.rearrange("(h p) t -> p h t", p=128)[:, :, r0:r1],
                  after=[mm.at()], inc=gp)
        ctx_ld = gp.val
        ctx_blend_vv = vv.val
        P.dma("sync", h2, xn_in[r0:r1, :], after=[vv.at(), sc.at()], inc=ld)
        xn_ld = ld.val
        for os_ in range(2):
            for ht in range(8):
                P.do("tensor",
                     lambda e, os_=os_, ht=ht: e.matmul(
                         psq[:], ctx_v[:, ht, :],
                         wv[0][:, ht, os_ * 512:(os_ + 1) * 512],
                         start=(ht == 0), stop=False),
                     after=[gp.at(ctx_ld), gp.at(ow_ld), sc.at(), vv.at()]
                     if ht == 0 else (), inc=None)
            P.do("tensor",
                 lambda e, os_=os_: e.matmul(
                     psq[:], ones_r[:], biasr[0:1, os_ * 512:(os_ + 1) * 512],
                     start=False, stop=True), after=[gp.at(consts_gp)], inc=mm)
            P.do("scalar",
                 lambda e, os_=os_: e.copy(out=h1[:, os_ * 512:(os_ + 1) * 512],
                                           in_=psq[:]),
                 after=[mm.at()], inc=sc)
        P.strict = {"vector": [vv, sc], "scalar": [sc, vv]}
        P.do("vector", lambda e: e.tensor_tensor(out=h1[:], in0=h1[:], in1=h2[:],
                                                 op=ALU.add),
             after=[ld.at(xn_ld)], inc=vv)
        ao_vv = vv.val
        P.dma("sync", ao_hbm[r0:r1, :], h1[:], after=[vv.at(ao_vv)], inc=st)
        P.do("scalar", lambda e: e.activation(out=h2, in_=h1[:], func=AF.Square,
                                              accum_out=ssq_c), inc=sc)
        P.do("scalar", lambda e: e.activation(out=rstd_c, in_=ssq_c, func=AF.Ln,
                                              scale=1.0 / DM, bias=eps_c), inc=sc)
        P.do("scalar", lambda e: e.activation(out=rstd_c, in_=rstd_c, func=AF.Exp,
                                              scale=-0.5), inc=sc)
        P.do("vector", lambda e: e.tensor_scalar(out=h1[:], in0=h1[:], scalar1=rstd_c,
                                                 scalar2=None, op0=ALU.mult), inc=vv)
        P.do("vector", lambda e: e.tensor_tensor(out=h1[:], in0=h1[:], in1=gamb[:],
                                                 op=ALU.mult),
             after=[ld.at(consts_ld)], inc=vv)
        P.strict = {}
        an_vv = vv.val
        P.dma("sync", an_hbm[r0:r1, :], h1[:], after=[vv.at(an_vv)], inc=st)
        P.do("vector", lambda e: e.tensor_copy(out=hb[:], in_=h1[:]),
             after=[vv.at(an_vv)], inc=vv)
        for ht in range(8):
            P.do("tensor",
                 lambda e, ht=ht: e.transpose(
                     pst[:, ht * 128:(ht + 1) * 128],
                     hb[:, ht * 128:(ht + 1) * 128], ident_bf[:]),
                 after=[vv.at(), gp.at(gi_const), sc.at()] if ht == 0 else (),
                 inc=mm if ht == 7 else None)
        P.do("vector",
             lambda e, tt=tt: e.tensor_copy(
                 out=anf_v[:, :, tt * 128:(tt + 1) * 128],
                 in_=pst[:].rearrange("p (h t) -> p h t", h=8)),
             after=[mm.at()], inc=vv)
        for ht in range(8):
            P.do("tensor",
                 lambda e, ht=ht: e.transpose(
                     pst32[:, ht * 128:(ht + 1) * 128],
                     h1[:, ht * 128:(ht + 1) * 128], ident32[:]),
                 after=[vv.at(), sc.at()] if ht == 0 else (),
                 inc=mm if ht == 7 else None)
        P.do("vector", lambda e: e.tensor_copy(out=ant32[:], in_=pst32[:]),
             after=[mm.at()], inc=vv)
        t32_vv = vv.val
        for ht in range(8):
            P.do("tensor",
                 lambda e, ht=ht: e.matmul(
                     psr[:, 0:NE1], ant32[:, ht * 128:(ht + 1) * 128],
                     rw_v[:, ht, :], start=(ht == 0), stop=False),
                 after=[vv.at(t32_vv), ld.at(consts_ld), sc.at()] if ht == 0 else (),
                 inc=None)
        P.do("tensor",
             lambda e: e.matmul(psr[:, 0:NE1], ones_r[:],
                                biasr[0:1, 2 * DM:2 * DM + NE1],
                                start=False, stop=True), inc=mm)
        P.strict = {"vector": [vv, sc], "scalar": [sc, vv]}
        P.do("vector", lambda e: e.tensor_reduce(
            out=rmax_c, in_=psr[:, 0:NE1], axis=AX.X, op=ALU.max, negate=True),
             after=[mm.at()], inc=vv)
        P.do("scalar", lambda e: e.activation(
            out=scr2[:, 0:NE1], in_=psr[:, 0:NE1], func=AF.Exp, bias=rmax_c,
            accum_out=rsum_c), inc=sc)
        P.do("vector", lambda e: e.reciprocal(out=rinv_c, in_=rsum_c), inc=vv)
        P.do("vector", lambda e: e.tensor_scalar(
            out=scr2[:, 0:NE1], in0=scr2[:, 0:NE1], scalar1=rinv_c, scalar2=None,
            op0=ALU.mult), inc=vv)
        P.do("vector", lambda e: e.max(out=mx_c, in_=scr2[:, 0:NE1]), inc=vv)
        P.do("vector", lambda e: e.memset(mx_c[:, 3:8], -1.0), inc=vv)
        P.do("vector", lambda e: e.match_replace(
            out=scr2[:, 32:63], in_to_replace=mx_c, in_values=scr2[:, 0:NE1],
            imm_value=0.0), inc=vv)
        P.do("vector", lambda e, tt=tt: e.tensor_tensor(
            out=gates[:, tt * NE1:(tt + 1) * NE1], in0=scr2[:, 0:NE1],
            in1=scr2[:, 32:63], op=ALU.subtract), inc=vv)
        P.do("vector", lambda e, tt=tt: e.tensor_copy(
            out=gatesb[:], in_=gates[:, tt * NE1:(tt + 1) * NE1]), inc=vv)
        P.strict = {}
        gb_vv = vv.val
        P.do("tensor",
             lambda e: e.transpose(pst[0:31, 0:128], gatesb[:], ident_bf[:]),
             after=[vv.at(gb_vv), vv.at(), sc.at()], inc=mm)
        P.do("vector",
             lambda e, tt=tt: e.tensor_copy(
                 out=gatesT[0:31, tt * 128:(tt + 1) * 128],
                 in_=pst[0:31, 0:128]),
             after=[mm.at()], inc=vv)
    prep_vv = vv.val; prep_sc = sc.val

    P.dma("gpsimd", wv[0], shw1.rearrange("(a p) o -> p a o", p=128),
          after=[mm.at()], inc=gp)
    sh1_ld = gp.val
    for it in range(8):
        for tsl in range(4):
            for ht in range(8):
                P.do("tensor",
                     lambda e, it=it, tsl=tsl, ht=ht: e.matmul(
                         psq[:], wv[0][:, ht, it * 128:(it + 1) * 128],
                         anf_v[:, ht, tsl * 512:(tsl + 1) * 512],
                         start=(ht == 0), stop=(ht == 7)),
                     after=[gp.at(sh1_ld), vv.at(prep_vv), sc.at()]
                     if ht == 0 else (), inc=mm if ht == 7 else None)
            P.do("scalar",
                 lambda e, it=it, tsl=tsl: e.activation(
                     out=hf[:, it * TA + tsl * 512: it * TA + (tsl + 1) * 512],
                     in_=psq[:], func=AF.Gelu, bias=shb1_c[:, it:it + 1]),
                 after=[mm.at(), ld.at(consts_ld)], inc=sc)
    P.dma("gpsimd", wv[1], shw2.rearrange("(a p) o -> p a o", p=128),
          after=[mm.at()], inc=gp)
    sh2_ld = gp.val
    for tt in range(16):
        psx = (psq, psq2)[tt % 2]
        for hs in range(2):
            for it in range(8):
                P.do("tensor",
                     lambda e, tt=tt, hs=hs, it=it, psx=psx: e.matmul(
                         psx[:], hf[:, it * TA + tt * 128: it * TA + (tt + 1) * 128],
                         wv[1][:, it, hs * 512:(hs + 1) * 512],
                         start=(it == 0), stop=False),
                     after=[gp.at(sh2_ld), sc.at(), vv.at()] if it == 0 else (),
                     inc=None)
            P.do("tensor",
                 lambda e, hs=hs, psx=psx: e.matmul(
                     psx[:], ones_r[:], biasr[0:1, DM + hs * 512:DM + (hs + 1) * 512],
                     start=False, stop=True), inc=mm)
            P.do("vector",
                 lambda e, tt=tt, hs=hs, psx=psx: e.tensor_copy(
                     out=acc[:, tt * DM + hs * 512: tt * DM + (hs + 1) * 512],
                     in_=psx[:]),
                 after=[mm.at()], inc=vv)
    shared_vv = vv.val

    w1load_gp = []
    wcons_mm = [0, 0]
    def load_rexpert_w1(e_i, slot):
        deps = [mm.at(wcons_mm[slot])] if wcons_mm[slot] else [mm.at()]
        P.dma("gpsimd",
              wbuf[:, slot * 8192:(slot + 1) * 8192].rearrange(
                  "p (a o) -> p a o", a=8),
              rew1[e_i].rearrange("(a p) o -> p a o", p=128), after=deps, inc=gp)
        w1load_gp.append(gp.val)
    rw1_v = [wbuf[:, s * 8192:(s + 1) * 8192].rearrange("p (h o) -> p h o", h=8)
             for s in range(2)]
    rw2_v = wbuf[:, 16384:24576].rearrange("p (h o) -> p h o", h=8)
    load_rexpert_w1(0, 0)
    load_rexpert_w1(1, 1)
    w2cons_mm = 0
    for e_i in range(NE1):
        slot = e_i % 2
        wg = w1load_gp[e_i]
        P.dma("gpsimd",
              wbuf[:, 16384:24576].rearrange("p (a o) -> p a o", a=8),
              rew2[e_i].rearrange("(a p) o -> p a o", p=128),
              after=[mm.at(w2cons_mm)] if w2cons_mm else [mm.at()], inc=gp)
        w2g = gp.val
        for it in range(8):
            for tsl in range(4):
                for ht in range(8):
                    P.do("tensor",
                         lambda e, it=it, tsl=tsl, ht=ht, s=slot: e.matmul(
                             psq[:], rw1_v[s][:, ht, it * 128:(it + 1) * 128],
                             anf_v[:, ht, tsl * 512:(tsl + 1) * 512],
                             start=(ht == 0), stop=(ht == 7)),
                         after=([gp.at(wg), vv.at(shared_vv), sc.at()]
                                if (it == 0 and tsl == 0 and ht == 0) else
                                ([sc.at()] if ht == 0 else ())),
                         inc=mm if ht == 7 else None)
                P.do("scalar",
                     lambda e, it=it, tsl=tsl, e_i=e_i: e.activation(
                         out=hf[:, it * TA + tsl * 512: it * TA + (tsl + 1) * 512],
                         in_=psq[:], func=AF.Gelu,
                         bias=reb1_c[:, it, e_i:e_i + 1]),
                     after=[mm.at(), mm.at(w2cons_mm)], inc=sc)
        hf_sc = sc.val
        for tt in range(16):
            psx = (psq2, pst32)[tt % 2]
            for hs in range(2):
                for it in range(8):
                    P.do("tensor",
                         lambda e, tt=tt, hs=hs, it=it, psx=psx: e.matmul(
                             psx[:, 0:512],
                             hf[:, it * TA + tt * 128: it * TA + (tt + 1) * 128],
                             rw2_v[:, it, hs * 512:(hs + 1) * 512],
                             start=(it == 0), stop=(it == 7)),
                         after=[sc.at(hf_sc), vv.at(), gp.at(w2g)]
                         if it == 0 else (), inc=mm if it == 7 else None)
                P.do("vector",
                     lambda e, tt=tt, hs=hs, e_i=e_i, psx=psx: e.scalar_tensor_tensor(
                         out=acc[:, tt * DM + hs * 512: tt * DM + (hs + 1) * 512],
                         in0=psx[:, 0:512],
                         scalar=gates[:, tt * NE1 + e_i: tt * NE1 + e_i + 1],
                         in1=acc[:, tt * DM + hs * 512: tt * DM + (hs + 1) * 512],
                         op0=ALU.mult, op1=ALU.add),
                     after=[mm.at()], inc=vv)
        w2cons_mm = mm.val
        if e_i + 2 < NE1:
            wcons_mm[slot] = mm.val
            load_rexpert_w1(e_i + 2, slot)
    routed_vv = vv.val

    for tt in range(16):
        P.do("tensor",
             lambda e, tt=tt: e.matmul(
                 psq[:], gatesT[0:31, tt * 128:(tt + 1) * 128], reb2_sb[0:31, 0:512],
                 start=True, stop=True),
             after=[gp.at(consts_gp), vv.at(), sc.at()], inc=mm)
        P.do("tensor",
             lambda e, tt=tt: e.matmul(
                 psq2[:], gatesT[0:31, tt * 128:(tt + 1) * 128],
                 reb2_sb[0:31, 512:1024], start=True, stop=True), inc=mm)
        P.do("vector",
             lambda e, tt=tt: e.tensor_tensor(
                 out=acc[:, tt * DM: tt * DM + 512],
                 in0=acc[:, tt * DM: tt * DM + 512], in1=psq[:], op=ALU.add),
             after=[mm.at()], inc=vv)
        P.do("vector",
             lambda e, tt=tt: e.tensor_tensor(
                 out=acc[:, tt * DM + 512: tt * DM + 1024],
                 in0=acc[:, tt * DM + 512: tt * DM + 1024], in1=psq2[:], op=ALU.add),
             inc=vv)
    moe_vv = vv.val

    for tt in range(16):
        r0, r1 = tt * 128, (tt + 1) * 128
        P.dma("sync", h1[:], x3_in[r0:r1, :], after=[vv.at(), sc.at()], inc=ld)
        P.dma("sync", h2, an_hbm[r0:r1, :], after=[st.at(), mm.at()], inc=ld)
        h_ld = ld.val
        P.strict = {"vector": [vv, sc], "scalar": [sc, vv]}
        P.do("vector", lambda e: e.tensor_tensor(out=h1[:], in0=h1[:], in1=h2[:],
                                                 op=ALU.add),
             after=[ld.at(h_ld), vv.at(moe_vv)], inc=vv)
        P.dma("sync", h2, ao_hbm[r0:r1, :], after=[vv.at()], inc=ld)
        P.do("vector", lambda e: e.tensor_tensor(out=h1[:], in0=h1[:], in1=h2[:],
                                                 op=ALU.add),
             after=[ld.at()], inc=vv)
        P.do("vector", lambda e, tt=tt: e.tensor_tensor(
            out=h1[:], in0=h1[:], in1=acc[:, tt * DM:(tt + 1) * DM], op=ALU.add),
             inc=vv)
        P.do("vector", lambda e: e.reduce_sum(out=rs_c, in_=h1[:], axis=AX.X), inc=vv)
        P.do("scalar", lambda e: e.activation(out=h2, in_=h1[:], func=AF.Square,
                                              accum_out=ssq_c), inc=sc)
        P.do("vector", lambda e: e.tensor_scalar_mul(mean_c, rs_c, 1.0 / DM), inc=vv)
        P.do("vector", lambda e: e.tensor_mul(msq_c, mean_c, mean_c), inc=vv)
        P.do("vector", lambda e: e.tensor_scalar_mul(var_c, ssq_c, 1.0 / DM), inc=vv)
        P.do("vector", lambda e: e.tensor_sub(var_c, var_c, msq_c), inc=vv)
        P.do("scalar", lambda e: e.activation(out=rstd_c, in_=var_c, func=AF.Ln,
                                              bias=eps_c), inc=sc)
        P.do("scalar", lambda e: e.activation(out=rstd_c, in_=rstd_c, func=AF.Exp,
                                              scale=-0.5), inc=sc)
        P.do("vector", lambda e: e.tensor_scalar(out=h1[:], in0=h1[:], scalar1=mean_c,
                                                 scalar2=rstd_c, op0=ALU.subtract,
                                                 op1=ALU.mult), inc=vv)
        P.strict = {}
        P.dma("sync", out_f[r0:r1, :], h1[:], after=[vv.at()], inc=st)
    return dict(ld=ld, gp=gp, mm=mm, sc=sc, vv=vv, st=st)

def build_phase_e(debug=False):
    nc = bass.Bass("TRN2", target_bir_lowering=False, debug=False)
    io = declare_e_io(nc, debug)
    from contextlib import ExitStack
    with ExitStack() as _stk:
        P = Prog(nc)
        C = emit_phase_e(nc, P, _stk.enter_context, io, "e_", debug)
        P.wait("gpsimd", [C["st"].at(), C["ld"].at(), C["gp"].at()])
        P.realize()
    return nc


def prep_phase_c_inputs(inputs, x3_list):
    g = lambda k: np.asarray(inputs["L1_ffn_" + k])
    shared = {
        "qwT": np.ascontiguousarray(g("q_w")), "kwT": np.ascontiguousarray(g("k_w")),
        "vwT": np.ascontiguousarray(g("v_w")),
        "biases": np.ascontiguousarray(np.stack([g("q_b"), g("k_b"), g("v_b")])),
        "gamma1": np.ascontiguousarray(g("gamma1").reshape(1, DM)),
        "theta": np.ascontiguousarray(g("theta").reshape(1, 32)),
    }
    maps = []
    for c in range(8):
        half = c % 2
        pos = (half * TA + np.arange(16)[None, :] * 128
               + np.arange(128)[:, None]).astype(np.float32)
        m = dict(shared)
        m["x_in"] = np.ascontiguousarray(np.asarray(x3_list[c]))
        m["posv"] = np.ascontiguousarray(pos)
        maps.append(m)
    return maps

def run_phase_c(inputs, x3_list):
    nc = build_phase_c()
    res = run_bass_kernel_spmd(nc, prep_phase_c_inputs(inputs, x3_list),
                               core_ids=list(range(8)))
    return res.results

def prep_phase_d_inputs(c_res):
    maps = []
    for c in range(8):
        b, hg = c // 2, c % 2
        rsl = slice(hg * 512, (hg + 1) * 512)
        qn = np.concatenate([np.asarray(c_res[2 * b]["qn_out"]),
                             np.asarray(c_res[2 * b + 1]["qn_out"])], axis=1)
        kn = np.concatenate([np.asarray(c_res[2 * b]["kn_out"]),
                             np.asarray(c_res[2 * b + 1]["kn_out"])], axis=1)
        vn = np.concatenate([np.asarray(c_res[2 * b]["vn_out"]),
                             np.asarray(c_res[2 * b + 1]["vn_out"])], axis=0)
        maps.append({
            "qn": np.ascontiguousarray(qn[rsl]),
            "kn": np.ascontiguousarray(kn[rsl]),
            "vn": np.ascontiguousarray(vn[:, rsl]),
        })
    return maps

def run_phase_d(c_res):
    nc = build_phase_d()
    res = run_bass_kernel_spmd(nc, prep_phase_d_inputs(c_res),
                               core_ids=list(range(8)))
    return [r["ctx_out"] for r in res.results]

def prep_phase_e_inputs(inputs, x3_list, c_res, ctx_list):
    g = lambda k: np.asarray(inputs["L1_ffn_" + k])
    shared = {
        "owT": np.ascontiguousarray(g("o_w")),
        "ob_in": np.ascontiguousarray(g("o_b").reshape(1, DM)),
        "gamma2": np.ascontiguousarray(g("gamma2").reshape(1, DM)),
        "shw1": np.ascontiguousarray(g("sh_w1")),
        "shb1_rs": np.ascontiguousarray(g("sh_b1").reshape(8, 128).T),
        "shw2": np.ascontiguousarray(g("sh_w2")),
        "shb2": np.ascontiguousarray(g("sh_b2").reshape(1, DM)),
        "rw": np.ascontiguousarray(g("router_w")),
        "rb": np.ascontiguousarray(g("router_b").reshape(1, NE1)),
        "rew1": np.ascontiguousarray(g("re_w1")),
        "reb1T": np.ascontiguousarray(g("re_b1").T),
        "rew2": np.ascontiguousarray(g("re_w2")),
        "reb2": np.ascontiguousarray(g("re_b2")),
    }
    maps = []
    for c in range(8):
        b, half = c // 2, c % 2
        ctx_b = np.concatenate([np.asarray(ctx_list[2 * b]),
                                np.asarray(ctx_list[2 * b + 1])], axis=0)
        m = dict(shared)
        m["x3_in"] = np.ascontiguousarray(np.asarray(x3_list[c]))
        m["xn_in"] = np.ascontiguousarray(np.asarray(c_res[c]["xn_out"]))
        m["ctx_in"] = np.ascontiguousarray(ctx_b[:, half * TA:(half + 1) * TA])
        maps.append(m)
    return maps

def run_phase_e(inputs, x3_list, c_res, ctx_list, debug=False):
    nc = build_phase_e(debug=debug)
    res = run_bass_kernel_spmd(nc, prep_phase_e_inputs(inputs, x3_list, c_res, ctx_list),
                               core_ids=list(range(8)))
    if debug:
        return res.results
    return [r["out_f"] for r in res.results]


CC_GROUPS = [[0, 1], [2, 3], [4, 5], [6, 7]]

def build_fused():
    nc = bass.Bass("TRN2", target_bir_lowering=False, debug=False)
    from contextlib import ExitStack

    iom = {}
    iom["xT"] = nc.dram_tensor("m_xT", [DM, T], F32, kind="ExternalInput").ap()
    iom["in_wT"] = nc.dram_tensor("m_in_wT", [DM, 2 * DI], F32, kind="ExternalInput").ap()
    iom["xp_wT"] = nc.dram_tensor("m_xp_wT", [DI, 96], F32, kind="ExternalInput").ap()
    iom["dtp_wT"] = nc.dram_tensor("m_dtp_wT", [DTR, DI], F32, kind="ExternalInput").ap()
    iom["out_wT"] = nc.dram_tensor("m_out_wT", [DI, DM], F32, kind="ExternalInput").ap()
    iom["conv_w_rs"] = nc.dram_tensor("m_conv_w_rs", [128, 64], F32, kind="ExternalInput").ap()
    iom["conv_b_rs"] = nc.dram_tensor("m_conv_b_rs", [128, 16], F32, kind="ExternalInput").ap()
    iom["dtp_b_rs"] = nc.dram_tensor("m_dtp_b_rs", [128, 16], F32, kind="ExternalInput").ap()
    iom["A_log_rs"] = nc.dram_tensor("m_A_log_rs", [128, 256], F32, kind="ExternalInput").ap()
    iom["Dvec_rs"] = nc.dram_tensor("m_Dvec_rs", [128, 16], F32, kind="ExternalInput").ap()
    iom["m_out"] = nc.dram_tensor("m_m_out", [T, DM], F32, kind="Internal").ap()
    iom["xc_hbm"] = nc.dram_tensor("m_xc_hbm", [DI, T], F32, kind="Internal").ap()
    iom["sz_hbm"] = nc.dram_tensor("m_sz_hbm", [DI, T], F32, kind="Internal").ap()
    iom["bc_hbm"] = nc.dram_tensor("m_bc_hbm", [32, T], F32, kind="Internal").ap()
    iom["dbg_hbm"] = nc.dram_tensor("m_dbg", [128, 4 * TCH], F32, kind="Internal").ap()
    gout_m = nc.dram_tensor("m_gout", [2 * T, DM], F32, kind="Internal").ap()

    ioa = {}
    ioa["x_in"] = nc.dram_tensor("a_x_in", [TA, DM], F32, kind="ExternalInput").ap()
    ioa["gate_wT"] = nc.dram_tensor("a_gate_wT", [DM, NE0], F32, kind="ExternalInput").ap()
    ioa["e_bias"] = nc.dram_tensor("a_e_bias", [1, NE0], F32, kind="ExternalInput").ap()
    ioa["w1_h"] = nc.dram_tensor("a_w1", [NE0, DM, I0], F32, kind="ExternalInput").ap()
    ioa["w3_h"] = nc.dram_tensor("a_w3", [NE0, DM, I0], F32, kind="ExternalInput").ap()
    ioa["w2_h"] = nc.dram_tensor("a_w2", [NE0, I0, DM], F32, kind="ExternalInput").ap()
    ioa["x2_out"] = nc.dram_tensor("a_x2", [TA, DM], F32, kind="Internal").ap()
    ioa["xn_hbm"] = nc.dram_tensor("a_xn", [TA, DM], F32, kind="Internal").ap()
    ioa["dbgA"] = nc.dram_tensor("a_dbg", [128, 1024], F32, kind="Internal").ap()
    ioa["mf_in"] = gout_m[0:TA, :]
    ioa["mf2_in"] = gout_m[TA:2 * TA, :]
    ioa["mr_in"] = gout_m[T:T + TA, :]
    ioa["mr2_in"] = gout_m[T + TA:2 * T, :]
    sel01 = nc.dram_tensor("sel01", [128, 2], F32, kind="ExternalInput").ap()
    ioa["sel01"] = sel01
    halo_src = nc.dram_tensor("b_halo_src", [128, DM], F32, kind="Internal").ap()
    halo_recv = nc.dram_tensor("b_halo_recv", [256, DM], F32, kind="Internal").ap()

    iob = {}
    iob["x_in"] = ioa["x2_out"]
    iob["halo_in"] = halo_recv[0:128, :]
    iob["wqT"] = nc.dram_tensor("b_wqT", [DM, DM], F32, kind="ExternalInput").ap()
    iob["wkT"] = nc.dram_tensor("b_wkT", [DM, DM], F32, kind="ExternalInput").ap()
    iob["wvT"] = nc.dram_tensor("b_wvT", [DM, DM], F32, kind="ExternalInput").ap()
    iob["woT"] = nc.dram_tensor("b_woT", [DM, DM], F32, kind="ExternalInput").ap()
    iob["biases"] = nc.dram_tensor("b_biases", [4, DM], F32, kind="ExternalInput").ap()
    iob["posv"] = nc.dram_tensor("b_posv", [128, NTB], F32, kind="ExternalInput").ap()
    iob["masks_in"] = nc.dram_tensor("b_masks", [128, 512], F32, kind="ExternalInput").ap()
    iob["x3_out"] = nc.dram_tensor("b_x3", [TA, DM], F32, kind="Internal").ap()
    iob["dbgB"] = nc.dram_tensor("b_dbg", [128, 2048], F32, kind="Internal").ap()

    ioc = {}
    ioc["x_in"] = iob["x3_out"]
    ioc["qwT"] = nc.dram_tensor("c_qwT", [DM, DM], F32, kind="ExternalInput").ap()
    ioc["kwT"] = nc.dram_tensor("c_kwT", [DM, DM], F32, kind="ExternalInput").ap()
    ioc["vwT"] = nc.dram_tensor("c_vwT", [DM, DM], F32, kind="ExternalInput").ap()
    ioc["biases"] = nc.dram_tensor("c_biases", [3, DM], F32, kind="ExternalInput").ap()
    ioc["gamma1"] = nc.dram_tensor("c_gamma1", [1, DM], F32, kind="ExternalInput").ap()
    ioc["theta"] = nc.dram_tensor("c_theta", [1, 32], F32, kind="ExternalInput").ap()
    ioc["posv"] = nc.dram_tensor("c_posv", [128, 16], F32, kind="ExternalInput").ap()
    qn_src = nc.dram_tensor("c_qn_src", [DM, TA], BF16, kind="Internal").ap()
    kn_src = nc.dram_tensor("c_kn_src", [DM, TA], BF16, kind="Internal").ap()
    v_src = nc.dram_tensor("c_v_src", [2 * TA, 512], BF16, kind="Internal").ap()
    ioc["qn_out"] = qn_src
    ioc["kn_out"] = kn_src
    ioc["vn_out3"] = v_src.rearrange("(r t) c -> t r c", r=2)
    ioc["xn_out"] = nc.dram_tensor("c_xn", [TA, DM], F32, kind="Internal").ap()
    gout_q = nc.dram_tensor("d_gout_q", [2 * DM, TA], BF16, kind="Internal").ap()
    gout_k = nc.dram_tensor("d_gout_k", [2 * DM, TA], BF16, kind="Internal").ap()
    gout_v = nc.dram_tensor("d_gout_v", [4 * TA, 512], BF16, kind="Internal").ap()

    iod = {}
    iod["qg"] = gout_q
    iod["kg"] = gout_k
    iod["vg"] = gout_v
    iod["sel01"] = sel01
    ctx_src = nc.dram_tensor("d_ctx_src", [DM, TA], BF16, kind="Internal").ap()
    iod["ctx_out3"] = ctx_src.rearrange("(r h) t -> h r t", r=2)
    gout_c = nc.dram_tensor("e_gout_c", [2 * DM, TA], BF16, kind="Internal").ap()

    ioe = {}
    ioe["x3_in"] = iob["x3_out"]
    ioe["xn_in"] = ioc["xn_out"]
    ioe["ctxg"] = gout_c
    ioe["sel01"] = sel01
    ioe["owT"] = nc.dram_tensor("e_owT", [DM, DM], F32, kind="ExternalInput").ap()
    ioe["ob_in"] = nc.dram_tensor("e_ob", [1, DM], F32, kind="ExternalInput").ap()
    ioe["gamma2"] = nc.dram_tensor("e_gamma2", [1, DM], F32, kind="ExternalInput").ap()
    ioe["shw1"] = nc.dram_tensor("e_shw1", [DM, DM], F32, kind="ExternalInput").ap()
    ioe["shb1_rs"] = nc.dram_tensor("e_shb1", [128, 8], F32, kind="ExternalInput").ap()
    ioe["shw2"] = nc.dram_tensor("e_shw2", [DM, DM], F32, kind="ExternalInput").ap()
    ioe["shb2"] = nc.dram_tensor("e_shb2", [1, DM], F32, kind="ExternalInput").ap()
    ioe["rw"] = nc.dram_tensor("e_rw", [DM, NE1], F32, kind="ExternalInput").ap()
    ioe["rb"] = nc.dram_tensor("e_rb", [1, NE1], F32, kind="ExternalInput").ap()
    ioe["rew1"] = nc.dram_tensor("e_rew1", [NE1, DM, DM], F32, kind="ExternalInput").ap()
    ioe["reb1T"] = nc.dram_tensor("e_reb1T", [DM, NE1], F32, kind="ExternalInput").ap()
    ioe["rew2"] = nc.dram_tensor("e_rew2", [NE1, DM, DM], F32, kind="ExternalInput").ap()
    ioe["reb2"] = nc.dram_tensor("e_reb2", [NE1, DM], F32, kind="ExternalInput").ap()
    ioe["out_f"] = nc.dram_tensor("out_f", [TA, DM], F32, kind="ExternalOutput").ap()
    ioe["ao_hbm"] = nc.dram_tensor("e_ao", [TA, DM], F32, kind="Internal").ap()
    ioe["an_hbm"] = nc.dram_tensor("e_an", [TA, DM], F32, kind="Internal").ap()
    ioe["dbgE"] = nc.dram_tensor("e_dbg", [128, 1024], F32, kind="Internal").ap()

    with ExitStack() as s0:
        P = Prog(nc)
        s_cc = s0.enter_context(nc.semaphore("fz_cc"))
        cc = Ctr(s_cc)

        def barrier(C):
            P.strict = {}
            for eng in Prog.ENGINES:
                P.wait(eng, [c.at() for c in C.values()] + [cc.at()])

        def gather(src, dst, after):
            P.do("gpsimd",
                 lambda e, src=src, dst=dst: e.collective_compute(
                     "AllGather", ALU.bypass, CC_GROUPS, [src], [dst]),
                 after=after, inc=cc, n=16)

        with ExitStack() as s1:
            Cm = emit_phase_m(nc, P, s1.enter_context, iom, "mS_")
        gather(iom["m_out"], gout_m, [Cm["st"].at()])
        barrier(Cm)

        with ExitStack() as s2:
            Ca = emit_phase_a(nc, P, s2.enter_context, ioa, "aS_")
        P.dma("sync", halo_src[:], ioa["x2_out"][TA - 128:TA, :],
              after=[Ca["st"].at()], inc=Ca["st"])
        gather(halo_src, halo_recv, [Ca["st"].at()])
        barrier(Ca)

        with ExitStack() as s3:
            Cb = emit_phase_b(nc, P, s3.enter_context, iob, "bS_")
        barrier(Cb)

        with ExitStack() as s4:
            Cc = emit_phase_c(nc, P, s4.enter_context, ioc, "cS_")
        gather(qn_src, gout_q, [Cc["st"].at()])
        gather(kn_src, gout_k, [Cc["st"].at()])
        gather(v_src, gout_v, [Cc["st"].at()])
        barrier(Cc)

        with ExitStack() as s5:
            Cd = emit_phase_d(nc, P, s5.enter_context, iod, "dS_")
        gather(ctx_src, gout_c, [Cd["st"].at()])
        barrier(Cd)

        with ExitStack() as s6:
            Ce = emit_phase_e(nc, P, s6.enter_context, ioe, "eS_")
        P.wait("gpsimd", [Ce["st"].at(), Ce["ld"].at(), Ce["gp"].at(), cc.at()])
        P.realize()
    return nc

def prep_fused_inputs(inputs):
    mm_ = prep_phase_m_inputs(inputs)
    x = np.asarray(inputs["L0_attn_x"]).reshape(16384, DM)
    ga = lambda k: np.asarray(inputs["L0_ffn_" + k])
    gb = lambda k: np.asarray(inputs["L1_attn_" + k])
    gc = lambda k: np.asarray(inputs["L1_ffn_" + k])
    qi = np.arange(128)[:, None]; kj = np.arange(256)[None, :]
    band = (kj >= qi) & (kj <= qi + 128)
    mask_n = np.where(band, 0.0, -1e30).astype(np.float32)
    mask_f = np.where(band & (kj >= 128), 0.0, -1e30).astype(np.float32)
    shared = {
        "a_gate_wT": np.ascontiguousarray(ga("gate_w").T),
        "a_e_bias": ga("e_bias").reshape(1, NE0),
        "a_w1": np.ascontiguousarray(ga("w1")), "a_w3": np.ascontiguousarray(ga("w3")),
        "a_w2": np.ascontiguousarray(ga("w2")),
        "b_wqT": np.ascontiguousarray(gb("Wq").T), "b_wkT": np.ascontiguousarray(gb("Wk").T),
        "b_wvT": np.ascontiguousarray(gb("Wv").T), "b_woT": np.ascontiguousarray(gb("Wo").T),
        "b_biases": np.ascontiguousarray(
            np.stack([gb("bq"), gb("bk"), gb("bv"), gb("bo")])),
        "c_qwT": np.ascontiguousarray(gc("q_w")), "c_kwT": np.ascontiguousarray(gc("k_w")),
        "c_vwT": np.ascontiguousarray(gc("v_w")),
        "c_biases": np.ascontiguousarray(np.stack([gc("q_b"), gc("k_b"), gc("v_b")])),
        "c_gamma1": np.ascontiguousarray(gc("gamma1").reshape(1, DM)),
        "c_theta": np.ascontiguousarray(gc("theta").reshape(1, 32)),
        "e_owT": np.ascontiguousarray(gc("o_w")),
        "e_ob": np.ascontiguousarray(gc("o_b").reshape(1, DM)),
        "e_gamma2": np.ascontiguousarray(gc("gamma2").reshape(1, DM)),
        "e_shw1": np.ascontiguousarray(gc("sh_w1")),
        "e_shb1": np.ascontiguousarray(gc("sh_b1").reshape(8, 128).T),
        "e_shw2": np.ascontiguousarray(gc("sh_w2")),
        "e_shb2": np.ascontiguousarray(gc("sh_b2").reshape(1, DM)),
        "e_rw": np.ascontiguousarray(gc("router_w")),
        "e_rb": np.ascontiguousarray(gc("router_b").reshape(1, NE1)),
        "e_rew1": np.ascontiguousarray(gc("re_w1")),
        "e_reb1T": np.ascontiguousarray(gc("re_b1").T),
        "e_rew2": np.ascontiguousarray(gc("re_w2")),
        "e_reb2": np.ascontiguousarray(gc("re_b2")),
    }
    maps = []
    for c in range(8):
        half = c % 2
        m = dict(shared)
        for k, v in mm_[c].items():
            m["m_" + k] = v
        m["a_x_in"] = np.ascontiguousarray(x[c * TA:(c + 1) * TA])
        sel = np.zeros((128, 2), np.float32); sel[:, half] = 1.0
        m["sel01"] = sel
        pos_b = (half * TA - 128 + np.arange(NTB)[None, :] * 128
                 + np.arange(128)[:, None]).astype(np.float32)
        m["b_posv"] = np.ascontiguousarray(pos_b)
        m["b_masks"] = np.ascontiguousarray(
            np.concatenate([mask_f if half == 0 else mask_n, mask_n], axis=1))
        pos_c = (half * TA + np.arange(16)[None, :] * 128
                 + np.arange(128)[:, None]).astype(np.float32)
        m["c_posv"] = np.ascontiguousarray(pos_c)
        maps.append(m)
    return maps

FUSED = False

def kernel(**inputs):
    if FUSED:
        nc = build_fused()
        in_maps = prep_fused_inputs(inputs)
        res = run_bass_kernel_spmd(nc, in_maps, core_ids=list(range(8)))
        out = np.concatenate([np.asarray(r["out_f"]) for r in res.results], axis=0)
        return out.reshape(4, 4096, 1024).astype(np.float32)
    m_outs = run_phase_m(inputs)
    x2_list = run_phase_a(inputs, m_outs)
    x3_list = run_phase_b(inputs, x2_list)
    c_res = run_phase_c(inputs, x3_list)
    ctx_list = run_phase_d(c_res)
    out_list = run_phase_e(inputs, x3_list, c_res, ctx_list)
    out = np.concatenate([np.asarray(o) for o in out_list], axis=0)
    return out.reshape(4, 4096, 1024).astype(np.float32)


# version 05
# speedup vs baseline: 1.2043x; 1.1475x over previous
import numpy as np
import ml_dtypes
import concourse.bass as bass
import concourse.mybir as mybir
from concourse.bass_utils import run_bass_kernel_spmd

F32 = mybir.dt.float32
BF16 = mybir.dt.bfloat16
AF = mybir.ActivationFunctionType
ALU = mybir.AluOpType
AX = mybir.AxisListType

BF = ml_dtypes.bfloat16


class Ctr:
    __slots__ = ("sem", "val")
    def __init__(self, sem):
        self.sem = sem
        self.val = 0
    def at(self, v=None):
        return (self, self.val if v is None else v)

class Prog:
    ENGINES = ("sync", "tensor", "vector", "scalar", "gpsimd")
    def __init__(self, nc):
        self.nc = nc
        self.ops = {e: [] for e in self.ENGINES}
        self.strict = {}
    def do(self, eng, fn, after=(), inc=None, n=1):
        waits = []
        sctr = self.strict.get(eng)
        if sctr is not None:
            for c in (sctr if isinstance(sctr, (list, tuple)) else [sctr]):
                if c.val > 0:
                    waits.append((c.sem, c.val))
        for a in after:
            ctr, v = a if isinstance(a, tuple) else (a, a.val)
            if v > 0:
                waits.append((ctr.sem, v))
        self.ops[eng].append((waits, fn, inc.sem if inc is not None else None, n))
        if inc is not None:
            inc.val += n
        return inc.val if inc is not None else None
    def wait(self, eng, after):
        self.do(eng, None, after=after)
    def dma(self, q, out, in_, after=(), inc=None):
        return self.do(q, lambda e, o=out, i=in_: e.dma_start(out=o, in_=i),
                       after=after, inc=inc, n=16)
    def realize(self):
        with self.nc.Block() as blk:
            for e in self.ENGINES:
                ops = self.ops[e]
                if not ops:
                    continue
                dec = getattr(blk, e)
                def body(eng, ops_=ops):
                    lastw = {}
                    for waits, fn, sem, n in ops_:
                        for (s, v) in waits:
                            k = id(s)
                            if lastw.get(k, -1) < v:
                                eng.wait_ge(s, v)
                                lastw[k] = v
                        if fn is None:
                            continue
                        inst = fn(eng)
                        if sem is not None:
                            inst.then_inc(sem, n)
                dec(body)


DM = 1024
DI = 2048
NST = 16
DTR = 64
T = 4096
TCH = 512
NCH = T // TCH

def declare_m_io(nc, debug=False):
    io = {}
    io["xT"] = nc.dram_tensor("xT", [DM, T], F32, kind="ExternalInput").ap()
    io["in_wT"] = nc.dram_tensor("in_wT", [DM, 2 * DI], F32, kind="ExternalInput").ap()
    io["xp_wT"] = nc.dram_tensor("xp_wT", [DI, 96], F32, kind="ExternalInput").ap()
    io["dtp_wT"] = nc.dram_tensor("dtp_wT", [DTR, DI], F32, kind="ExternalInput").ap()
    io["out_wT"] = nc.dram_tensor("out_wT", [DI, DM], F32, kind="ExternalInput").ap()
    io["conv_w_rs"] = nc.dram_tensor("conv_w_rs", [128, 64], F32, kind="ExternalInput").ap()
    io["conv_b_rs"] = nc.dram_tensor("conv_b_rs", [128, 16], F32, kind="ExternalInput").ap()
    io["dtp_b_rs"] = nc.dram_tensor("dtp_b_rs", [128, 16], F32, kind="ExternalInput").ap()
    io["A_log_rs"] = nc.dram_tensor("A_log_rs", [128, 256], F32, kind="ExternalInput").ap()
    io["Dvec_rs"] = nc.dram_tensor("Dvec_rs", [128, 16], F32, kind="ExternalInput").ap()
    io["m_out"] = nc.dram_tensor("m_out", [T, DM], F32, kind="ExternalOutput").ap()
    knd = "ExternalOutput" if debug else "Internal"
    io["xc_hbm"] = nc.dram_tensor("xc_hbm", [DI, T], F32, kind=knd).ap()
    io["sz_hbm"] = nc.dram_tensor("sz_hbm", [DI, T], F32, kind=knd).ap()
    io["bc_hbm"] = nc.dram_tensor("bc_hbm", [32, T], F32, kind=knd).ap()
    io["dbg_hbm"] = nc.dram_tensor("dbg_hbm", [128, 4 * TCH], F32, kind=knd).ap()
    return io

def emit_phase_m(nc, P, ec, io, pfx="m_", debug=False):
    xT = io["xT"]; in_wT = io["in_wT"]; xp_wT = io["xp_wT"]; dtp_wT = io["dtp_wT"]
    out_wT = io["out_wT"]; conv_w_rs = io["conv_w_rs"]; conv_b_rs = io["conv_b_rs"]
    dtp_b_rs = io["dtp_b_rs"]; A_log_rs = io["A_log_rs"]; Dvec_rs = io["Dvec_rs"]
    m_out = io["m_out"]; xc_hbm = io["xc_hbm"]; sz_hbm = io["sz_hbm"]
    bc_hbm = io["bc_hbm"]; dbg_hbm = io["dbg_hbm"]
    arena_a = ec(nc.sbuf_tensor(pfx + "arena_a", [128, 65536], BF16))
    arena_b = ec(nc.sbuf_tensor(pfx + "arena_b", [128, 16384], F32))
    xp_sb = ec(nc.sbuf_tensor(pfx + "xp_sb", [128, 16 * 96], BF16))
    dtpw_sb = ec(nc.sbuf_tensor(pfx + "dtpw_sb", [64, DI], BF16))
    smalls = ec(nc.sbuf_tensor(pfx + "smalls", [128, 672], F32))
    ps0 = ec(nc.psum_tensor(pfx + "ps0", [128, 512], F32))
    ps1 = ec(nc.psum_tensor(pfx + "ps1", [128, 512], F32))
    psx = ec(nc.psum_tensor(pfx + "psx", [128, 512], F32))
    psd = ec(nc.psum_tensor(pfx + "psd", [128, 512], F32))
    pso0 = ec(nc.psum_tensor(pfx + "pso0", [128, 512], F32))
    pso1 = ec(nc.psum_tensor(pfx + "pso1", [128, 512], F32))
    s_ld = ec(nc.semaphore(pfx + "s_ld"))
    s_gp = ec(nc.semaphore(pfx + "s_gp"))
    s_mm = ec(nc.semaphore(pfx + "s_mm"))
    s_sc = ec(nc.semaphore(pfx + "s_sc"))
    s_vv = ec(nc.semaphore(pfx + "s_vv"))
    s_st = ec(nc.semaphore(pfx + "s_st"))
    ld = Ctr(s_ld); gp = Ctr(s_gp); mm = Ctr(s_mm)
    sc = Ctr(s_sc); vv = Ctr(s_vv); st = Ctr(s_st)

    aa = arena_a
    xT_sb = aa[:, 0:32768].rearrange("p (h t) -> p h t", h=8)
    inw_sb = aa[:, 32768:65536].rearrange("p (h c) -> p h c", h=8)
    ab = arena_b
    xi_buf = ab[:, 0:4100]
    cv_buf = ab[:, 4100:8196]
    xdbl = ab[:, 8196:12292]
    xc_bf = ab[:, 12292:14340].bitcast(BF16)
    dt_bf = ab[:, 12292:14340].bitcast(BF16)[0:64, :]

    convw_c = smalls[:, 0:64]
    convb_c = smalls[:, 64:80]
    dtpb_c = smalls[:, 80:96]
    Alog_c = smalls[:, 96:352]
    A_c = smalls[:, 352:608]
    Dvec_c = smalls[:, 608:624]

    for h in range(8):
        P.dma("gpsimd", xT_sb[:, h, :], xT[h * 128:(h + 1) * 128, :], inc=gp)
        P.dma("gpsimd", inw_sb[:, h, :], in_wT[h * 128:(h + 1) * 128, :], inc=gp)
    P.dma("gpsimd", xp_sb[:].rearrange("p (a c) -> p a c", a=16),
          xp_wT.rearrange("(a p) c -> p a c", p=128), inc=gp)
    P.dma("gpsimd", dtpw_sb[:], dtp_wT, inc=gp)
    P.dma("sync", convw_c, conv_w_rs, inc=ld)
    P.dma("sync", convb_c, conv_b_rs, inc=ld)
    P.dma("sync", dtpb_c, dtp_b_rs, inc=ld)
    P.dma("sync", Alog_c, A_log_rs, inc=ld)
    P.dma("sync", Dvec_c, Dvec_rs, inc=ld)
    ld_smalls = ld.val
    P.do("scalar", lambda e: e.activation(out=A_c, in_=Alog_c, func=AF.Exp),
         after=[ld.at(ld_smalls)], inc=sc)
    P.do("vector", lambda e: e.tensor_scalar_mul(A_c, A_c, -1.0),
         after=[sc.at()], inc=vv)

    gp_stage0 = gp.val

    conv_done_vv = 0
    xp_read_mm = 0
    last_xc_store = 0
    for ct in range(32):
        is_z = ct >= 16
        dtile = ct % 16
        if ct == 0:
            P.do("vector", lambda e: e.memset(xi_buf[:, 0:3], 0.0), inc=vv)
            P.do("vector", lambda e: e.memset(xi_buf[:, 4099:4100], 0.0), inc=vv)
        evac_sc_vals = []
        for ts in range(8):
            ps = (ps0, ps1)[ts % 2]
            deps = [gp.at(gp_stage0)]
            if ts >= 2:
                deps.append(sc.at(evac_sc_vals[ts - 2]))
            if not is_z and ts == 0 and conv_done_vv:
                deps.append(vv.at(conv_done_vv))
            for h in range(8):
                P.do("tensor",
                     lambda e, ps=ps, h=h, ct=ct, ts=ts: e.matmul(
                         ps[:], inw_sb[:, h, ct * 128:(ct + 1) * 128],
                         xT_sb[:, h, ts * 512:(ts + 1) * 512],
                         start=(h == 0), stop=(h == 7)),
                     after=deps if h == 0 else (), inc=mm if h == 7 else None)
            evac_deps = [mm.at(), st.at(last_xc_store)]
            if is_z and ts == 0:
                evac_deps.append(vv.at(conv_done_vv))
            if is_z:
                P.do("scalar",
                     lambda e, ps=ps, ts=ts: e.activation(
                         out=cv_buf[:, ts * 512:(ts + 1) * 512], in_=ps[:], func=AF.Silu),
                     after=evac_deps, inc=sc)
            else:
                P.do("scalar",
                     lambda e, ps=ps, ts=ts: e.copy(
                         out=xi_buf[:, 3 + ts * 512:3 + (ts + 1) * 512], in_=ps[:]),
                     after=evac_deps, inc=sc)
            evac_sc_vals.append(sc.val)
        if is_z:
            P.dma("sync", sz_hbm[dtile * 128:(dtile + 1) * 128, :], cv_buf[:, 0:4096],
                  after=[sc.at()], inc=st)
            last_xc_store = st.val
        else:
            P.do("vector",
                 lambda e, d=dtile: e.tensor_scalar(
                     out=cv_buf[:, 0:4096], in0=xi_buf[:, 0:4096],
                     scalar1=convw_c[:, d * 4:d * 4 + 1], scalar2=None, op0=ALU.mult),
                 after=[sc.at(), ld.at(ld_smalls)], inc=vv)
            for k in range(1, 4):
                P.do("vector",
                     lambda e, d=dtile, k=k: e.scalar_tensor_tensor(
                         out=cv_buf[:, 0:4096], in0=xi_buf[:, k:k + 4096],
                         scalar=convw_c[:, d * 4 + k:d * 4 + k + 1],
                         in1=cv_buf[:, 0:4096], op0=ALU.mult, op1=ALU.add),
                     inc=vv)
            P.do("scalar",
                 lambda e, d=dtile: e.activation(
                     out=xi_buf[:, 3:4099], in_=cv_buf[:, 0:4096], func=AF.Silu,
                     bias=convb_c[:, d:d + 1]),
                 after=[vv.at()], inc=sc)
            sc_silu = sc.val
            P.dma("sync", xc_hbm[dtile * 128:(dtile + 1) * 128, :], xi_buf[:, 3:4099],
                  after=[sc.at(sc_silu)], inc=st)
            last_xc_store = st.val
            P.do("vector",
                 lambda e: e.tensor_copy(out=xc_bf[:, :], in_=xi_buf[:, 3:4099]),
                 after=[sc.at(sc_silu), mm.at(xp_read_mm)], inc=vv)
            vv_cast = vv.val
            for ts in range(8):
                P.do("tensor",
                     lambda e, d=dtile, ts=ts: e.matmul(
                         psx[0:96, :], xp_sb[:, d * 96:(d + 1) * 96],
                         xc_bf[:, ts * 512:(ts + 1) * 512], start=True, stop=True),
                     after=[vv.at(vv_cast)] + ([vv.at()] if ts > 0 or dtile > 0 else []),
                     inc=mm)
                if dtile == 0:
                    P.do("vector",
                         lambda e, ts=ts: e.tensor_copy(
                             out=xdbl[0:96, ts * 512:(ts + 1) * 512], in_=psx[0:96, :]),
                         after=[mm.at()], inc=vv)
                else:
                    P.do("vector",
                         lambda e, ts=ts: e.tensor_tensor(
                             out=xdbl[0:96, ts * 512:(ts + 1) * 512],
                             in0=xdbl[0:96, ts * 512:(ts + 1) * 512],
                             in1=psx[0:96, :], op=ALU.add),
                         after=[mm.at()], inc=vv)
            xp_read_mm = mm.val
            conv_done_vv = vv.val

    P.dma("sync", bc_hbm, xdbl[64:96, :], after=[vv.at()], inc=st)
    bc_stored_st = st.val
    P.do("vector", lambda e: e.tensor_copy(out=dt_bf[:, :], in_=xdbl[0:64, :]),
         after=[vv.at(), mm.at(xp_read_mm)], inc=vv)
    dt_cast_vv = vv.val

    aaf = arena_a[:].bitcast(F32)
    bc_all = aaf[:, 0:16384].rearrange("p (j t) -> p j t", j=32)
    outw_sb = arena_a[:, 32768:49152].rearrange("p (d m) -> p d m", d=16)
    g_buf = arena_a[:, 49152:57344].rearrange("p (d t) -> p d t", d=16)
    dl_buf = ab[:, 0:512]
    w_buf = ab[:, 512:1024]
    dA_buf = ab[:, 1024:1536]
    r_buf = ab[:, 1536:2048]
    H_buf = ab[:, 2048:2560]
    y_buf = ab[:, 2560:3072]
    tm_buf = ab[:, 3072:3584]
    dA_buf2 = ab[:, 7936:8448]
    vb_buf = ab[:, 3584:4096]
    ab_buf = ab[:, 4096:4608]
    hcar = ab[:, 4608:4864]
    xc_c = [ab[:, 4864 + i * 512: 5376 + i * 512] for i in range(2)]
    sz_c = [ab[:, 5888 + i * 512: 6400 + i * 512] for i in range(2)]
    mo_buf = [ab[:, 6912 + i * 1024: 7936 + i * 1024] for i in range(2)]

    all_stage1_st = st.val
    for d in range(16):
        P.dma("gpsimd", outw_sb[:, d, :], out_wT[d * 128:(d + 1) * 128, :],
              after=[st.at(all_stage1_st)], inc=gp)
    gp_outw = gp.val
    P.do("vector", lambda e: e.memset(hcar, 0.0),
         after=[st.at(all_stage1_st)], inc=vv)

    xc_dma_vals = {}
    sz_dma_vals = {}
    stream_war = {}
    sp_done_vv = 0
    def emit_stream_loads(c, dtile, slot):
        deps = [st.at(all_stage1_st)]
        P.dma("sync", xc_c[slot], xc_hbm[dtile * 128:(dtile + 1) * 128,
                                         c * TCH:(c + 1) * TCH],
              after=deps + stream_war.get(("xc", slot), []), inc=ld)
        xc_dma_vals[(c, dtile)] = ld.val
        P.dma("sync", sz_c[slot], sz_hbm[dtile * 128:(dtile + 1) * 128,
                                         c * TCH:(c + 1) * TCH],
              after=deps + stream_war.get(("sz", slot), []), inc=ld)
        sz_dma_vals[(c, dtile)] = ld.val

    pending = []
    for c in range(NCH):
        pair_idx = c * 16
        bc_deps = [st.at(all_stage1_st)]
        if c > 0:
            bc_deps.append(vv.at())
        for j in range(32):
            P.dma("gpsimd", bc_all[:, j, :],
                  bc_hbm[j:j + 1, c * TCH:(c + 1) * TCH].broadcast_to([128, TCH]),
                  after=bc_deps if j == 0 else (), inc=gp)
        gp_bc = gp.val

        for dtile in range(16):
            slot = dtile % 2
            emit_stream_loads(c, dtile, slot)
            xcv = xc_dma_vals[(c, dtile)]
            szv = sz_dma_vals[(c, dtile)]
            P.do("tensor",
                 lambda e, d=dtile, c=c: e.matmul(
                     psd[:], dtpw_sb[:, d * 128:(d + 1) * 128],
                     dt_bf[:, c * TCH:(c + 1) * TCH], start=True, stop=True),
                 after=[vv.at(dt_cast_vv), sc.at()], inc=mm)
            P.do("scalar",
                 lambda e, d=dtile: e.activation(
                     out=vb_buf, in_=psd[:], func=AF.Identity,
                     bias=dtpb_c[:, d:d + 1]),
                 after=[mm.at(), st.at(all_stage1_st), vv.at(sp_done_vv)], inc=sc)
            P.do("scalar", lambda e: e.activation(out=ab_buf, in_=vb_buf, func=AF.Abs),
                 after=[sc.at()], inc=sc)
            P.do("scalar", lambda e: e.activation(out=ab_buf, in_=ab_buf, func=AF.Exp,
                                                  scale=-1.0), inc=sc)
            P.do("scalar", lambda e: e.activation(out=ab_buf, in_=ab_buf, func=AF.Ln,
                                                  bias=1.0), inc=sc)
            sc_sp = sc.val
            P.do("vector",
                 lambda e: e.tensor_scalar(out=dl_buf, in0=vb_buf, scalar1=0.0,
                                           scalar2=None, op0=ALU.max),
                 after=[sc.at(sc_sp)], inc=vv)
            P.do("vector",
                 lambda e: e.tensor_tensor(out=dl_buf, in0=dl_buf, in1=ab_buf,
                                           op=ALU.add), inc=vv)
            sp_done_vv = vv.val
            P.do("vector",
                 lambda e, s=slot: e.tensor_tensor(out=w_buf, in0=dl_buf,
                                                   in1=xc_c[s], op=ALU.mult),
                 after=[ld.at(xcv)], inc=vv)
            scan_vv = [0, 0]
            carry_sc = [0, 0]
            dA2 = [dA_buf, dA_buf2]
            for n in range(16):
                col = dtile * 16 + n
                sl = n % 2
                P.do("scalar",
                     lambda e, col=col, sl=sl: e.activation(
                         out=dA2[sl], in_=dl_buf, func=AF.Exp,
                         scale=A_c[:, col:col + 1]),
                     after=[vv.at(sp_done_vv), vv.at(scan_vv[sl])], inc=sc)
                dA_sc = sc.val
                P.do("vector",
                     lambda e, n=n: e.tensor_tensor(
                         out=r_buf, in0=w_buf, in1=bc_all[:, n, :], op=ALU.mult),
                     after=[gp.at(gp_bc), sc.at(dA_sc), sc.at(carry_sc[0]),
                            sc.at(carry_sc[1])], inc=vv)
                P.do("vector",
                     lambda e, col=col, sl=sl: e.tensor_tensor_scan(
                         out=H_buf, data0=dA2[sl], data1=r_buf,
                         initial=hcar[:, col:col + 1], op0=ALU.mult, op1=ALU.add),
                     inc=vv)
                scan_vv[sl] = vv.val
                P.do("scalar",
                     lambda e, col=col: e.copy(
                         out=hcar[:, col:col + 1], in_=H_buf[:, TCH - 1:TCH]),
                     after=[vv.at(scan_vv[sl])], inc=sc)
                carry_sc[0] = carry_sc[1]
                carry_sc[1] = sc.val
                if n == 0:
                    P.do("vector",
                         lambda e, n=n: e.tensor_tensor(
                             out=y_buf, in0=H_buf, in1=bc_all[:, 16 + n, :],
                             op=ALU.mult), inc=vv)
                else:
                    P.do("vector",
                         lambda e, n=n: e.tensor_tensor(
                             out=tm_buf, in0=H_buf, in1=bc_all[:, 16 + n, :],
                             op=ALU.mult), inc=vv)
                    P.do("vector",
                         lambda e: e.tensor_tensor(out=y_buf, in0=y_buf, in1=tm_buf,
                                                   op=ALU.add), inc=vv)
            if debug and c == 0 and dtile == 0:
                for i, buf in enumerate([dl_buf, w_buf, H_buf, y_buf]):
                    P.dma("sync", dbg_hbm[:, i * TCH:(i + 1) * TCH], buf,
                          after=[vv.at()], inc=st)
                P.wait("vector", [st.at()])
            P.do("vector",
                 lambda e, d=dtile, s=slot: e.scalar_tensor_tensor(
                     out=tm_buf, in0=xc_c[s], scalar=Dvec_c[:, d:d + 1],
                     in1=y_buf, op0=ALU.mult, op1=ALU.add), inc=vv)
            P.do("vector",
                 lambda e, d=dtile, s=slot: e.tensor_tensor(
                     out=g_buf[:, d, :], in0=tm_buf, in1=sz_c[s], op=ALU.mult),
                 after=[ld.at(szv), mm.at()], inc=vv)
            stream_war[("xc", slot)] = [vv.at(vv.val)]
            stream_war[("sz", slot)] = [vv.at(vv.val)]
        vv_chunk = vv.val

        for tsub in range(4):
            ob = mo_buf[tsub % 2]
            pso = (pso0, pso1)[tsub % 2]
            for ms in range(2):
                for d in range(16):
                    P.do("tensor",
                         lambda e, d=d, ts=tsub, ms=ms, pso=pso: e.matmul(
                             pso[:], g_buf[:, d, ts * 128:(ts + 1) * 128],
                             outw_sb[:, d, ms * 512:(ms + 1) * 512],
                             start=(d == 0), stop=(d == 15)),
                         after=([vv.at(vv_chunk), gp.at(gp_outw), sc.at()]
                                if d == 0 else ()),
                         inc=mm if d == 15 else None)
                P.do("scalar",
                     lambda e, ms=ms, ob=ob, pso=pso: e.copy(
                         out=ob[:, ms * 512:(ms + 1) * 512], in_=pso[:]),
                     after=[mm.at(), st.at(st.val)], inc=sc)
            P.dma("sync", m_out[c * TCH + tsub * 128: c * TCH + (tsub + 1) * 128, :],
                  ob, after=[sc.at()], inc=st)

    return dict(ld=ld, gp=gp, mm=mm, sc=sc, vv=vv, st=st)

def build_phase_m(debug=False):
    nc = bass.Bass("TRN2", target_bir_lowering=False, debug=False)
    io = declare_m_io(nc, debug)
    from contextlib import ExitStack
    with ExitStack() as _stk:
        P = Prog(nc)
        C = emit_phase_m(nc, P, _stk.enter_context, io, "m_", debug)
        P.wait("gpsimd", [C["st"].at(), C["ld"].at(), C["gp"].at()])
        P.realize()
    return nc


def prep_phase_m_inputs(inputs):
    x = inputs["L0_attn_x"]
    maps = []
    for core in range(8):
        b, d = core // 2, core % 2
        sfx = "1" if d == 0 else "2"
        xb = x[b] if d == 0 else x[b][::-1]
        g = lambda k: np.asarray(inputs["L0_attn_" + k + sfx])
        m = {
            "xT": np.ascontiguousarray(xb.T),
            "in_wT": np.ascontiguousarray(g("in_w").T),
            "xp_wT": np.ascontiguousarray(g("xp_w").T),
            "dtp_wT": np.ascontiguousarray(g("dtp_w").T),
            "out_wT": np.ascontiguousarray(g("out_w").T),
            "conv_w_rs": np.ascontiguousarray(
                g("conv_w").reshape(16, 128, 4).transpose(1, 0, 2).reshape(128, 64)),
            "conv_b_rs": np.ascontiguousarray(g("conv_b").reshape(16, 128).T),
            "dtp_b_rs": np.ascontiguousarray(g("dtp_b").reshape(16, 128).T),
            "A_log_rs": np.ascontiguousarray(
                g("A_log").reshape(16, 128, 16).transpose(1, 0, 2).reshape(128, 256)),
            "Dvec_rs": np.ascontiguousarray(g("Dvec").reshape(16, 128).T),
        }
        maps.append(m)
    return maps

def run_phase_m(inputs, debug=False):
    nc = build_phase_m(debug=debug)
    in_maps = prep_phase_m_inputs(inputs)
    res = run_bass_kernel_spmd(nc, in_maps, core_ids=list(range(8)))
    if debug:
        return res.results
    return [r["m_out"] for r in res.results]


NE0 = 32
I0 = 768
TA = 2048
NTT = TA // 128

def declare_a_io(nc, debug=False):
    io = {}
    io["x_in"] = nc.dram_tensor("x_in", [TA, DM], F32, kind="ExternalInput").ap()
    io["mf_in"] = nc.dram_tensor("mf_in", [TA, DM], F32, kind="ExternalInput").ap()
    io["mr_in"] = nc.dram_tensor("mr_in", [TA, DM], F32, kind="ExternalInput").ap()
    io["gate_wT"] = nc.dram_tensor("gate_wT", [DM, NE0], F32, kind="ExternalInput").ap()
    io["e_bias"] = nc.dram_tensor("e_bias", [1, NE0], F32, kind="ExternalInput").ap()
    io["w1_h"] = nc.dram_tensor("w1", [NE0, DM, I0], F32, kind="ExternalInput").ap()
    io["w3_h"] = nc.dram_tensor("w3", [NE0, DM, I0], F32, kind="ExternalInput").ap()
    io["w2_h"] = nc.dram_tensor("w2", [NE0, I0, DM], F32, kind="ExternalInput").ap()
    io["x2_out"] = nc.dram_tensor("x2_out", [TA, DM], F32, kind="ExternalOutput").ap()
    knd = "ExternalOutput" if debug else "Internal"
    io["xn_hbm"] = nc.dram_tensor("xn_hbm", [TA, DM], F32, kind=knd).ap()
    io["dbgA"] = nc.dram_tensor("dbgA", [128, 1024], F32, kind=knd).ap()
    return io

def emit_phase_a(nc, P, ec, io, pfx="a_", debug=False):
    x_in = io["x_in"]
    mf_in = io["mf_in"]
    mr_in = io["mr_in"]
    gate_wT = io["gate_wT"]
    e_bias = io["e_bias"]
    w1_h = io["w1_h"]
    w3_h = io["w3_h"]
    w2_h = io["w2_h"]
    x2_out = io["x2_out"]
    xn_hbm = io["xn_hbm"]
    dbgA = io["dbgA"]
    xnf = ec(nc.sbuf_tensor(pfx + "xnf", [128, 8 * TA], BF16))
    acc = ec(nc.sbuf_tensor(pfx + "acc", [128, NTT * DM], F32))
    a_bf = ec(nc.sbuf_tensor(pfx + "a_bf", [128, 6 * TA], BF16))
    wbuf = ec(nc.sbuf_tensor(pfx + "wbuf", [128, 2 * 18432], BF16))
    gate_sb = ec(nc.sbuf_tensor(pfx + "gate_sb", [128, 8 * NE0], F32))
    ident32 = ec(nc.sbuf_tensor(pfx + "ident32", [128, 128], F32))
    comb_all = ec(nc.sbuf_tensor(pfx + "comb_all", [128, NTT * NE0 + NE0], F32))
    comb = comb_all[:, 0:NTT * NE0]
    h1 = ec(nc.sbuf_tensor(pfx + "h1", [128, DM], F32))
    scr2 = ec(nc.sbuf_tensor(pfx + "scr2", [128, 64], F32))
    xnb = ec(nc.sbuf_tensor(pfx + "xnb", [128, DM], BF16))
    sg = ec(nc.sbuf_tensor(pfx + "sg", [128, 512], F32))
    sg2 = ec(nc.sbuf_tensor(pfx + "sg2", [128, 512], F32))
    sm = ec(nc.sbuf_tensor(pfx + "sm", [128, 64], F32))
    ebias_sb = comb_all[:, NTT * NE0:NTT * NE0 + NE0]
    ident_bf = ec(nc.sbuf_tensor(pfx + "ident_bf", [128, 128], BF16))
    pst = ec(nc.psum_tensor(pfx + "pst", [128, DM], BF16))
    pst32 = ec(nc.psum_tensor(pfx + "pst32", [128, DM], F32))
    psg = ec(nc.psum_tensor(pfx + "psg", [128, 512], F32))
    psu = ec(nc.psum_tensor(pfx + "psu", [128, 512], F32))
    pso0 = ec(nc.psum_tensor(pfx + "pso0", [128, 512], F32))
    pso1 = ec(nc.psum_tensor(pfx + "pso1", [128, 512], F32))
    s_ld = ec(nc.semaphore(pfx + "s_ld")); s_gp = ec(nc.semaphore(pfx + "s_gp"))
    s_mm = ec(nc.semaphore(pfx + "s_mm")); s_sc = ec(nc.semaphore(pfx + "s_sc"))
    s_vv = ec(nc.semaphore(pfx + "s_vv")); s_st = ec(nc.semaphore(pfx + "s_st"))
    s_gi = ec(nc.semaphore(pfx + "s_gi"))
    ld = Ctr(s_ld); gp = Ctr(s_gp); mm = Ctr(s_mm)
    sc = Ctr(s_sc); vv = Ctr(s_vv); st = Ctr(s_st)
    gi = Ctr(s_gi)

    route_done_vv = 0
    psr = psg[:, 0:NE0]
    xnf_v = xnf[:].rearrange("p (h t) -> p h t", h=8)
    xnt32 = a_bf[:, 0:2048].bitcast(F32)
    wv = [wbuf[:, i * 18432:(i + 1) * 18432] for i in range(2)]
    w1_v = [w[:, 0:6144].rearrange("p (h i) -> p h i", h=8) for w in wv]
    w3_v = [w[:, 6144:12288].rearrange("p (h i) -> p h i", h=8) for w in wv]
    w2_v = [w[:, 12288:18432].rearrange("p (i h) -> p i h", i=6) for w in wv]
    gate_v = gate_sb[:].rearrange("p (h e) -> p h e", h=8)
    rs_c = sm[:, 0:1]; ssq_c = sm[:, 1:2]; mean_c = sm[:, 2:3]
    var_c = sm[:, 3:4]; rstd_c = sm[:, 4:5]; msq_c = sm[:, 5:6]
    mx_c = sm[:, 8:16]; rinv_c = sm[:, 16:17]; rsum_c = sm[:, 17:18]
    eps_c = sm[:, 18:19]
    scr_c = sm[:, 32:64]

    P.dma("sync", ebias_sb, e_bias.broadcast_to([128, NE0]), inc=ld)
    P.dma("sync", gate_v, gate_wT.rearrange("(a p) e -> p a e", p=128), inc=ld)
    P.do("vector", lambda e: e.memset(eps_c, 1e-5), inc=vv)
    P.do("gpsimd", lambda e: e.memset(ident_bf[:], 0.0), inc=gi)
    P.do("gpsimd", lambda e: e.affine_select(
        out=ident_bf[:], in_=ident_bf[:], compare_op=ALU.not_equal, fill=1.0,
        base=0, pattern=[[-1, 128]], channel_multiplier=1), inc=gi)
    P.do("gpsimd", lambda e: e.memset(ident32[:], 0.0), inc=gi)
    P.do("gpsimd", lambda e: e.affine_select(
        out=ident32[:], in_=ident32[:], compare_op=ALU.not_equal, fill=1.0,
        base=0, pattern=[[-1, 128]], channel_multiplier=1), inc=gi)
    ld_init = ld.val; gp_init = gp.val

    P.strict = {"vector": [vv, sc], "scalar": [sc, vv]}
    blend = "sel01" in io
    if blend:
        sel_sb = ec(nc.sbuf_tensor(pfx + "sel", [128, 2], F32))
        P.dma("sync", sel_sb[:], io["sel01"], inc=ld)
        mf2_in = io["mf2_in"]; mr2_in = io["mr2_in"]
    for tt in range(NTT):
        r0, r1 = tt * 128, (tt + 1) * 128
        ldep = [vv.at(), sc.at(), mm.at()]
        P.dma("sync", h1[:], x_in[r0:r1, :], after=ldep, inc=ld)
        P.dma("sync", xnt32, mf_in[r0:r1, :], after=ldep, inc=ld)
        v_ld = ld.val
        if blend:
            P.do("vector", lambda e: e.scalar_tensor_tensor(
                out=h1[:], in0=xnt32, scalar=sel_sb[:, 0:1], in1=h1[:],
                op0=ALU.mult, op1=ALU.add),
                 after=[ld.at(v_ld)], inc=vv)
            P.dma("sync", xnt32, mf2_in[r0:r1, :], after=[vv.at()], inc=ld)
            P.do("vector", lambda e: e.scalar_tensor_tensor(
                out=h1[:], in0=xnt32, scalar=sel_sb[:, 1:2], in1=h1[:],
                op0=ALU.mult, op1=ALU.add),
                 after=[ld.at()], inc=vv)
            P.dma("sync", xnt32, mr_in[r0:r1, :], after=[vv.at()], inc=ld)
            P.do("vector", lambda e: e.scalar_tensor_tensor(
                out=h1[:], in0=xnt32, scalar=sel_sb[:, 0:1], in1=h1[:],
                op0=ALU.mult, op1=ALU.add),
                 after=[ld.at()], inc=vv)
            P.dma("sync", xnt32, mr2_in[r0:r1, :], after=[vv.at()], inc=ld)
            P.do("vector", lambda e: e.scalar_tensor_tensor(
                out=h1[:], in0=xnt32, scalar=sel_sb[:, 1:2], in1=h1[:],
                op0=ALU.mult, op1=ALU.add),
                 after=[ld.at()], inc=vv)
        else:
            P.do("vector", lambda e: e.tensor_tensor(out=h1[:], in0=h1[:], in1=xnt32,
                                                     op=ALU.add),
                 after=[ld.at(v_ld)], inc=vv)
            P.dma("sync", xnt32, mr_in[r0:r1, :], after=[vv.at()], inc=ld)
            P.do("vector", lambda e: e.tensor_tensor(out=h1[:], in0=h1[:], in1=xnt32,
                                                     op=ALU.add),
                 after=[ld.at()], inc=vv)
        P.do("vector", lambda e: e.reduce_sum(out=rs_c, in_=h1[:], axis=AX.X), inc=vv)
        P.do("scalar", lambda e: e.activation(out=pst32[:], in_=h1[:], func=AF.Square,
                                              accum_out=ssq_c),
             after=[vv.at()], inc=sc)
        P.do("vector", lambda e: e.tensor_scalar_mul(mean_c, rs_c, 1.0 / DM),
             after=[vv.at()], inc=vv)
        P.do("vector", lambda e: e.tensor_mul(msq_c, mean_c, mean_c),
             after=[vv.at()], inc=vv)
        P.do("vector", lambda e: e.tensor_scalar_mul(var_c, ssq_c, 1.0 / DM),
             after=[sc.at(), vv.at()], inc=vv)
        P.do("vector", lambda e: e.tensor_sub(var_c, var_c, msq_c),
             after=[vv.at()], inc=vv)
        P.do("scalar", lambda e: e.activation(out=rstd_c, in_=var_c, func=AF.Ln,
                                              bias=eps_c),
             after=[vv.at()], inc=sc)
        P.do("scalar", lambda e: e.activation(out=rstd_c, in_=rstd_c, func=AF.Exp,
                                              scale=-0.5), inc=sc)
        P.do("vector", lambda e: e.tensor_scalar(out=h1[:], in0=h1[:], scalar1=mean_c,
                                                 scalar2=rstd_c, op0=ALU.subtract,
                                                 op1=ALU.mult),
             after=[sc.at()], inc=vv)
        xn_done = vv.val
        if debug and tt == 0:
            P.dma("sync", dbgA[:, 0:32], sm[:, 0:32], after=[vv.at()], inc=st)
            P.wait("vector", [st.at()])
            P.wait("scalar", [st.at()])
        P.dma("sync", xn_hbm[r0:r1, :], h1[:], after=[vv.at(xn_done)], inc=st)
        P.do("vector", lambda e: e.tensor_copy(out=xnb[:], in_=h1[:]),
             after=[vv.at(xn_done), mm.at()], inc=vv)
        cast_done = vv.val
        for ht in range(8):
            P.do("tensor",
                 lambda e, ht=ht: e.transpose(
                     pst[:, ht * 128:(ht + 1) * 128],
                     xnb[:, ht * 128:(ht + 1) * 128], ident_bf[:]),
                 after=[vv.at(cast_done), sc.at(), gi.at(4)] if ht == 0 else (),
                 inc=mm if ht == 7 else None)
        P.do("vector",
             lambda e, tt=tt: e.tensor_copy(
                 out=xnf_v[:, :, tt * 128:(tt + 1) * 128],
                 in_=pst[:].rearrange("p (h t) -> p h t", h=8)),
             after=[mm.at()], inc=vv)
        tr_done = vv.val
        for ht in range(8):
            P.do("tensor",
                 lambda e, ht=ht: e.transpose(
                     pst32[:, ht * 128:(ht + 1) * 128],
                     h1[:, ht * 128:(ht + 1) * 128], ident32[:]),
                 after=[vv.at(xn_done), sc.at()] if ht == 0 else (),
                 inc=mm if ht == 7 else None)
        P.do("vector", lambda e: e.tensor_copy(out=xnt32[:], in_=pst32[:]),
             after=[mm.at()], inc=vv)
        tr32_done = vv.val
        for ht in range(8):
            P.do("tensor",
                 lambda e, ht=ht, tt=tt: e.matmul(
                     psr, xnt32[:, ht * 128:(ht + 1) * 128],
                     gate_v[:, ht, :], start=(ht == 0), stop=(ht == 7)),
                 after=[vv.at(tr32_done), ld.at(ld_init), sc.at()] if ht == 0 else (),
                 inc=mm if ht == 7 else None)
        P.do("scalar", lambda e: e.activation(out=scr_c, in_=psr, func=AF.Sigmoid),
             after=[mm.at(), vv.at(route_done_vv)], inc=sc)
        P.do("vector", lambda e: e.tensor_tensor(out=scr2[:, 0:32], in0=scr_c,
                                                 in1=ebias_sb, op=ALU.add),
             after=[sc.at(), ld.at(ld_init)], inc=vv)
        P.do("vector", lambda e: e.max(out=mx_c, in_=scr2[:, 0:32]), inc=vv)
        P.do("vector", lambda e: e.memset(mx_c[:, 4:8], -1e30),
             after=[vv.at()], inc=vv)
        P.do("vector", lambda e: e.match_replace(out=scr2[:, 32:64], in_to_replace=mx_c,
                                                 in_values=scr2[:, 0:32],
                                                 imm_value=-1e30),
             after=[vv.at()], inc=vv)
        P.do("vector", lambda e: e.tensor_single_scalar(out=scr2[:, 32:64],
                                                        in_=scr2[:, 32:64],
                                                        scalar=-1e30,
                                                        op=ALU.is_equal),
             after=[vv.at()], inc=vv)
        P.do("vector", lambda e: e.tensor_tensor(out=scr2[:, 32:64], in0=scr2[:, 32:64],
                                                 in1=scr_c, op=ALU.mult), inc=vv)
        P.do("vector", lambda e: e.reduce_sum(out=rsum_c, in_=scr2[:, 32:64],
                                              axis=AX.X), inc=vv)
        P.do("vector", lambda e: e.reciprocal(out=rinv_c, in_=rsum_c),
             after=[vv.at()], inc=vv)
        P.do("vector",
             lambda e, tt=tt: e.tensor_scalar(
                 out=comb[:, tt * NE0:(tt + 1) * NE0], in0=scr2[:, 32:64],
                 scalar1=rinv_c, scalar2=None, op0=ALU.mult),
             after=[vv.at()], inc=vv)
        route_done_vv = vv.val
    P.strict = {}
    prep_done_vv = vv.val
    prep_done_sc = sc.val
    if debug:
        P.dma("sync", dbgA[:, 0:512], comb, after=[vv.at()], inc=st)


    wload_gp = []
    def load_expert(e_i, slot):
        deps = [mm.at(wcons_mm[slot])] if wcons_mm[slot] else []
        P.dma("gpsimd", w1_v[slot],
              w1_h[e_i].rearrange("(a p) i -> p a i", p=128), after=deps, inc=gp)
        P.dma("gpsimd", w3_v[slot],
              w3_h[e_i].rearrange("(a p) i -> p a i", p=128), inc=gp)
        P.dma("gpsimd", w2_v[slot],
              w2_h[e_i].rearrange("(a p) h -> p a h", p=128), inc=gp)
        wload_gp.append(gp.val)

    wcons_mm = [0, 0]
    load_expert(0, 0)
    load_expert(1, 1)
    a_written_vv = 0
    psg_l = [psg[:], pst32[:, 0:512]]
    psu_l = [psu[:], pst32[:, 512:1024]]
    sg_l = [sg[:, 0:512], sg2[:, 0:512]]
    silu_sc = [0, 0]
    au_vv = [prep_done_vv, prep_done_vv]
    for e_i in range(NE0):
        slot = e_i % 2
        wg = wload_gp[e_i]
        for it in range(6):
            for tsl in range(4):
                bk = (it * 4 + tsl) % 2
                psg_b = psg_l[bk]; psu_b = psu_l[bk]
                sg_b = sg_l[bk]
                gdeps = [sc.at(silu_sc[bk]), vv.at(au_vv[bk])]
                if it == 0 and tsl == 0:
                    gdeps += [gp.at(wg), vv.at(prep_done_vv), sc.at()]
                for ht in range(8):
                    P.do("tensor",
                         lambda e, it=it, tsl=tsl, ht=ht, s=slot, psg_b=psg_b: e.matmul(
                             psg_b, w1_v[s][:, ht, it * 128:(it + 1) * 128],
                             xnf_v[:, ht, tsl * 512:(tsl + 1) * 512],
                             start=(ht == 0), stop=(ht == 7)),
                         after=gdeps if ht == 0 else (),
                         inc=mm if ht == 7 else None)
                for ht in range(8):
                    P.do("tensor",
                         lambda e, it=it, tsl=tsl, ht=ht, s=slot, psu_b=psu_b: e.matmul(
                             psu_b, w3_v[s][:, ht, it * 128:(it + 1) * 128],
                             xnf_v[:, ht, tsl * 512:(tsl + 1) * 512],
                             start=(ht == 0), stop=(ht == 7)),
                         after=[vv.at(au_vv[bk])] if ht == 0 else (),
                         inc=mm if ht == 7 else None)
                mm_gu = mm.val
                P.do("scalar",
                     lambda e, psg_b=psg_b, sg_b=sg_b: e.activation(
                         out=sg_b, in_=psg_b, func=AF.Silu),
                     after=[mm.at(mm_gu), vv.at(au_vv[bk])], inc=sc)
                silu_sc[bk] = sc.val
                P.do("vector",
                     lambda e, it=it, tsl=tsl, sg_b=sg_b, psu_b=psu_b: e.tensor_tensor(
                         out=a_bf[:, it * TA + tsl * 512: it * TA + (tsl + 1) * 512],
                         in0=sg_b, in1=psu_b, op=ALU.mult),
                     after=[sc.at(silu_sc[bk]), mm.at(w2cons_mm if e_i > 0 else 0)],
                     inc=vv)
                au_vv[bk] = vv.val
        a_written_vv = vv.val
        for tt in range(NTT):
            pso = (pso0, pso1)[tt % 2]
            for hs in range(2):
                for it in range(6):
                    P.do("tensor",
                         lambda e, tt=tt, hs=hs, it=it, s=slot, pso=pso: e.matmul(
                             pso[:], a_bf[:, it * TA + tt * 128: it * TA + (tt + 1) * 128],
                             w2_v[s][:, it, hs * 512:(hs + 1) * 512],
                             start=(it == 0), stop=(it == 5)),
                         after=[vv.at()] if it == 0 else (),
                         inc=mm if it == 5 else None)
                if e_i == 0:
                    P.do("vector",
                         lambda e, tt=tt, hs=hs, pso=pso: e.tensor_scalar(
                             out=acc[:, tt * DM + hs * 512: tt * DM + (hs + 1) * 512],
                             in0=pso[:], scalar1=comb[:, tt * NE0:tt * NE0 + 1],
                             scalar2=None, op0=ALU.mult),
                         after=[mm.at()], inc=vv)
                else:
                    P.do("vector",
                         lambda e, tt=tt, hs=hs, pso=pso, e_i=e_i: e.scalar_tensor_tensor(
                             out=acc[:, tt * DM + hs * 512: tt * DM + (hs + 1) * 512],
                             in0=pso[:], scalar=comb[:, tt * NE0 + e_i:tt * NE0 + e_i + 1],
                             in1=acc[:, tt * DM + hs * 512: tt * DM + (hs + 1) * 512],
                             op0=ALU.mult, op1=ALU.add),
                         after=[mm.at()], inc=vv)
        w2cons_mm = mm.val
        if e_i + 2 < NE0:
            wcons_mm[slot] = mm.val
            load_expert(e_i + 2, slot)
    moe_done_vv = vv.val
    if debug:
        P.dma("sync", dbgA[:, 512:1024], acc[:, 0:512], after=[vv.at()], inc=st)
        P.wait("vector", [st.at()])

    P.strict = {"vector": [vv, sc], "scalar": [sc, vv]}
    for tt in range(NTT):
        r0, r1 = tt * 128, (tt + 1) * 128
        P.dma("sync", h1[:], xn_hbm[r0:r1, :],
              after=[vv.at(), sc.at(), st.at()], inc=ld)
        P.do("vector",
             lambda e, tt=tt: e.tensor_tensor(
                 out=h1[:], in0=h1[:], in1=acc[:, tt * DM:(tt + 1) * DM], op=ALU.add),
             after=[ld.at(), vv.at(moe_done_vv)], inc=vv)
        P.do("vector", lambda e: e.reduce_sum(out=rs_c, in_=h1[:], axis=AX.X), inc=vv)
        P.do("scalar", lambda e: e.activation(out=pst32[:], in_=h1[:], func=AF.Square,
                                              accum_out=ssq_c),
             after=[vv.at()], inc=sc)
        P.do("vector", lambda e: e.tensor_scalar_mul(mean_c, rs_c, 1.0 / DM),
             after=[vv.at()], inc=vv)
        P.do("vector", lambda e: e.tensor_mul(msq_c, mean_c, mean_c),
             after=[vv.at()], inc=vv)
        P.do("vector", lambda e: e.tensor_scalar_mul(var_c, ssq_c, 1.0 / DM),
             after=[sc.at(), vv.at()], inc=vv)
        P.do("vector", lambda e: e.tensor_sub(var_c, var_c, msq_c),
             after=[vv.at()], inc=vv)
        P.do("scalar", lambda e: e.activation(out=rstd_c, in_=var_c, func=AF.Ln,
                                              bias=eps_c),
             after=[vv.at()], inc=sc)
        P.do("scalar", lambda e: e.activation(out=rstd_c, in_=rstd_c, func=AF.Exp,
                                              scale=-0.5), inc=sc)
        P.do("vector", lambda e: e.tensor_scalar(out=h1[:], in0=h1[:], scalar1=mean_c,
                                                 scalar2=rstd_c, op0=ALU.subtract,
                                                 op1=ALU.mult),
             after=[sc.at()], inc=vv)
        P.dma("sync", x2_out[r0:r1, :], h1[:], after=[vv.at()], inc=st)

    return dict(ld=ld, gp=gp, mm=mm, sc=sc, vv=vv, st=st)

def build_phase_a(debug=False):
    nc = bass.Bass("TRN2", target_bir_lowering=False, debug=False)
    io = declare_a_io(nc, debug)
    from contextlib import ExitStack
    with ExitStack() as _stk:
        P = Prog(nc)
        C = emit_phase_a(nc, P, _stk.enter_context, io, "a_", debug)
        P.wait("gpsimd", [C["st"].at(), C["ld"].at(), C["gp"].at()])
        P.realize()
    return nc

def prep_phase_a_inputs(inputs, m_outs):
    x = np.asarray(inputs["L0_attn_x"]).reshape(16384, DM)
    gate_wT = np.ascontiguousarray(np.asarray(inputs["L0_ffn_gate_w"]).T)
    e_bias = np.asarray(inputs["L0_ffn_e_bias"]).reshape(1, NE0)
    w1 = np.ascontiguousarray(inputs["L0_ffn_w1"])
    w3 = np.ascontiguousarray(inputs["L0_ffn_w3"])
    w2 = np.ascontiguousarray(inputs["L0_ffn_w2"])
    maps = []
    for c in range(8):
        b, half = c // 2, c % 2
        sl = slice(half * TA, (half + 1) * TA)
        maps.append({
            "x_in": np.ascontiguousarray(x[c * TA:(c + 1) * TA]),
            "mf_in": np.ascontiguousarray(np.asarray(m_outs[2 * b])[sl]),
            "mr_in": np.ascontiguousarray(np.asarray(m_outs[2 * b + 1])[sl]),
            "gate_wT": gate_wT, "e_bias": e_bias,
            "w1": w1, "w3": w3, "w2": w2,
        })
    return maps

def run_phase_a(inputs, m_outs, debug=False):
    nc = build_phase_a(debug=debug)
    in_maps = prep_phase_a_inputs(inputs, m_outs)
    res = run_bass_kernel_spmd(nc, in_maps, core_ids=list(range(8)))
    if debug:
        return res.results
    return [r["x2_out"] for r in res.results]


NH = 16
HD = 64
NTB = 17

def declare_b_io(nc, debug=False):
    io = {}
    io["x_in"] = nc.dram_tensor("x_in", [TA, DM], F32, kind="ExternalInput").ap()
    io["halo_in"] = nc.dram_tensor("halo_in", [128, DM], F32, kind="ExternalInput").ap()
    io["wqT"] = nc.dram_tensor("wqT", [DM, DM], F32, kind="ExternalInput").ap()
    io["wkT"] = nc.dram_tensor("wkT", [DM, DM], F32, kind="ExternalInput").ap()
    io["wvT"] = nc.dram_tensor("wvT", [DM, DM], F32, kind="ExternalInput").ap()
    io["woT"] = nc.dram_tensor("woT", [DM, DM], F32, kind="ExternalInput").ap()
    io["biases"] = nc.dram_tensor("biases", [4, DM], F32, kind="ExternalInput").ap()
    io["posv"] = nc.dram_tensor("posv", [128, NTB], F32, kind="ExternalInput").ap()
    io["masks_in"] = nc.dram_tensor("masks_in", [128, 512], F32, kind="ExternalInput").ap()
    io["x3_out"] = nc.dram_tensor("x3_out", [TA, DM], F32, kind="ExternalOutput").ap()
    knd = "ExternalOutput" if debug else "Internal"
    io["dbgB"] = nc.dram_tensor("dbgB", [128, 2048], F32, kind=knd).ap()
    return io

def emit_phase_b(nc, P, ec, io, pfx="b_", debug=False):
    x_in = io["x_in"]
    halo_in = io["halo_in"]
    wqT = io["wqT"]
    wkT = io["wkT"]
    wvT = io["wvT"]
    woT = io["woT"]
    biases = io["biases"]
    posv = io["posv"]
    masks_in = io["masks_in"]
    x3_out = io["x3_out"]
    dbgB = io["dbgB"]
    x2f = ec(nc.sbuf_tensor(pfx + "x2f", [128, 8 * NTB * 128], BF16))
    qf = ec(nc.sbuf_tensor(pfx + "qf", [128, 8 * NTB * 128], BF16))
    kf = ec(nc.sbuf_tensor(pfx + "kf", [128, 8 * NTB * 128], BF16))
    vtok = ec(nc.sbuf_tensor(pfx + "vtok", [128, NTB * DM], BF16))
    wbuf = ec(nc.sbuf_tensor(pfx + "wbuf", [128, 8 * DM], BF16))
    biasr = ec(nc.sbuf_tensor(pfx + "biasr", [1, 4 * DM], BF16))
    ones_r = ec(nc.sbuf_tensor(pfx + "ones_r", [1, 128], BF16))
    ident_bf = ec(nc.sbuf_tensor(pfx + "identb", [128, 128], BF16))
    masks = ec(nc.sbuf_tensor(pfx + "masks", [128, 512], F32))
    trig = ec(nc.sbuf_tensor(pfx + "trig", [128, 256], F32))
    posv_sb = ec(nc.sbuf_tensor(pfx + "posv_sb", [128, NTB], F32))
    qs = ec(nc.sbuf_tensor(pfx + "qs", [128, DM], F32))
    qs2 = ec(nc.sbuf_tensor(pfx + "qs2", [128, DM], F32))
    tmp5 = ec(nc.sbuf_tensor(pfx + "tmp5", [128, 512], F32))
    qsb = ec(nc.sbuf_tensor(pfx + "qsb", [128, DM], BF16))
    scb = ec(nc.sbuf_tensor(pfx + "scb", [128, 256], F32))
    pb = ec(nc.sbuf_tensor(pfx + "pb", [128, 256], BF16))
    ptb = ec(nc.sbuf_tensor(pfx + "ptb", [128, 256], BF16))
    smB = ec(nc.sbuf_tensor(pfx + "smB", [128, 16], F32))
    resid = ec(nc.sbuf_tensor(pfx + "resid", [128, DM], F32))
    psq = ec(nc.psum_tensor(pfx + "psq", [128, 512], F32))
    pstB = ec(nc.psum_tensor(pfx + "pstB", [128, DM], BF16))
    pssc = ec(nc.psum_tensor(pfx + "pssc", [128, 256], F32))
    psT = ec(nc.psum_tensor(pfx + "psT", [128, 256], BF16))
    psctx = ec(nc.psum_tensor(pfx + "psctx", [64, 128], F32))
    pso = ec(nc.psum_tensor(pfx + "psoB", [128, 512], F32))
    s_ld = ec(nc.semaphore(pfx + "s_ld")); s_gp = ec(nc.semaphore(pfx + "s_gp"))
    s_mm = ec(nc.semaphore(pfx + "s_mm")); s_sc = ec(nc.semaphore(pfx + "s_sc"))
    s_vv = ec(nc.semaphore(pfx + "s_vv")); s_st = ec(nc.semaphore(pfx + "s_st"))
    s_gi = ec(nc.semaphore(pfx + "s_gi"))
    ld = Ctr(s_ld); gp = Ctr(s_gp); mm = Ctr(s_mm)
    sc = Ctr(s_sc); vv = Ctr(s_vv); st = Ctr(s_st); gi = Ctr(s_gi)

    x2f_v = x2f[:].rearrange("p (h t) -> p h t", h=8)
    qf_v = qf[:].rearrange("p (h t) -> p h t", h=8)
    kf_v = kf[:].rearrange("p (h t) -> p h t", h=8)
    vt_v = vtok[:].rearrange("p (t o) -> p t o", t=NTB)
    w_v = wbuf[:].rearrange("p (h o) -> p h o", h=8)
    mask_f = masks[:, 0:256]
    mask_n = masks[:, 256:512]
    ifr_c = trig[:, 0:32]
    ang_c = trig[:, 32:64]
    cos_c = trig[:, 64:96]
    sin_c = trig[:, 96:128]
    kk_c = trig[:, 128:160]
    rmax_c = smB[:, 0:1]; rsum_c = smB[:, 1:2]; rinv_c = smB[:, 2:3]

    M_MAGIC = 12582912.0
    INV2PI = 0.15915493667125702
    CW1 = 6.28125
    CW2 = float(np.float32(6.283185307179586 - 6.28125))
    HALFPI = float(np.float32(np.pi / 2))

    P.do("gpsimd", lambda e: e.memset(ident_bf[:], 0.0), inc=gi)
    P.do("gpsimd", lambda e: e.affine_select(
        out=ident_bf[:], in_=ident_bf[:], compare_op=ALU.not_equal, fill=1.0,
        base=0, pattern=[[-1, 128]], channel_multiplier=1), inc=gi)
    P.do("gpsimd", lambda e: e.memset(ones_r[:], 1.0), inc=gi)
    P.dma("sync", masks[:], masks_in, inc=ld)
    ifr_i32 = trig[:, 160:192].bitcast(mybir.dt.int32)
    P.do("gpsimd", lambda e: e.iota(ifr_i32, pattern=[[1, 32]], base=0,
                                    channel_multiplier=0), inc=gi)
    gi_const = gi.val
    P.do("vector", lambda e: e.tensor_copy(out=ifr_c, in_=ifr_i32),
         after=[gi.at(gi_const)], inc=vv)
    P.do("scalar", lambda e: e.activation(out=ifr_c, in_=ifr_c, func=AF.Exp,
                                          scale=-float(np.log(10000.0) / 32.0)),
         after=[vv.at()], inc=sc)
    ifr_done_sc = sc.val
    P.dma("sync", posv_sb[:], posv, inc=ld)
    P.dma("gpsimd", biasr[:].rearrange("p (a o) -> p a o", a=4),
          biases.rearrange("a o -> a o")[None, :, :] if False
          else biases.rearrange("(x a) o -> x a o", x=1), inc=gp)
    posv_ld = ld.val

    for tt in range(NTB):
        src = halo_in if tt == 0 else x_in[(tt - 1) * 128: tt * 128, :]
        P.dma("gpsimd", qsb[:], src, after=[mm.at()], inc=gp)
        for ht in range(8):
            P.do("tensor",
                 lambda e, ht=ht: e.transpose(
                     pstB[:, ht * 128:(ht + 1) * 128],
                     qsb[:, ht * 128:(ht + 1) * 128], ident_bf[:]),
                 after=[gp.at(), gi.at(gi_const), vv.at()] if ht == 0 else (),
                 inc=mm if ht == 7 else None)
        P.do("vector",
             lambda e, tt=tt: e.tensor_copy(
                 out=x2f_v[:, :, tt * 128:(tt + 1) * 128],
                 in_=pstB[:].rearrange("p (h t) -> p h t", h=8)),
             after=[mm.at()], inc=vv)
    x2f_done_vv = vv.val

    for w_i, wT in enumerate([wqT, wkT, wvT]):
        P.dma("gpsimd", w_v, wT.rearrange("(a p) o -> p a o", p=128),
              after=[mm.at()], inc=gp)
        wld = gp.val
        for tt in range(NTB):
            for os_ in range(2):
                dep0 = [gp.at(wld), vv.at(x2f_done_vv), sc.at(), vv.at()]
                for ht in range(8):
                    P.do("tensor",
                         lambda e, tt=tt, os_=os_, ht=ht: e.matmul(
                             psq[:], x2f_v[:, ht, tt * 128:(tt + 1) * 128],
                             w_v[:, ht, os_ * 512:(os_ + 1) * 512],
                             start=(ht == 0), stop=False),
                         after=dep0 if ht == 0 else (), inc=None)
                P.do("tensor",
                     lambda e, w_i=w_i, os_=os_: e.matmul(
                         psq[:], ones_r[:],
                         biasr[0:1, w_i * DM + os_ * 512: w_i * DM + (os_ + 1) * 512],
                         start=False, stop=True),
                     after=[gp.at()], inc=mm)
                if w_i == 2:
                    P.do("scalar",
                         lambda e, tt=tt, os_=os_: e.copy(
                             out=vt_v[:, tt, os_ * 512:(os_ + 1) * 512], in_=psq[:]),
                         after=[mm.at()], inc=sc)
                else:
                    P.do("scalar",
                         lambda e, os_=os_: e.copy(
                             out=qs[:, os_ * 512:(os_ + 1) * 512], in_=psq[:]),
                         after=[mm.at()], inc=sc)
            if w_i == 2:
                continue
            P.strict = {"vector": [vv, sc], "scalar": [sc, vv]}
            for trig_i in range(2):
                P.do("vector", lambda e, tt=tt: e.tensor_scalar(
                    out=ang_c, in0=ifr_c, scalar1=posv_sb[:, tt:tt + 1],
                    scalar2=None, op0=ALU.mult),
                     after=[sc.at(ifr_done_sc), ld.at(posv_ld)], inc=vv)
                if trig_i == 1:
                    P.do("vector", lambda e: e.tensor_scalar(
                        out=ang_c, in0=ang_c, scalar1=HALFPI, scalar2=None,
                        op0=ALU.add), inc=vv)
                P.do("vector", lambda e: e.tensor_scalar(
                    out=kk_c, in0=ang_c, scalar1=INV2PI, scalar2=M_MAGIC,
                    op0=ALU.mult, op1=ALU.add), inc=vv)
                P.do("vector", lambda e: e.tensor_scalar(
                    out=kk_c, in0=kk_c, scalar1=M_MAGIC, scalar2=None,
                    op0=ALU.subtract), inc=vv)
                P.do("vector", lambda e: e.scalar_tensor_tensor(
                    out=ang_c, in0=kk_c, scalar=-CW1, in1=ang_c,
                    op0=ALU.mult, op1=ALU.add), inc=vv)
                P.do("vector", lambda e: e.scalar_tensor_tensor(
                    out=ang_c, in0=kk_c, scalar=-CW2, in1=ang_c,
                    op0=ALU.mult, op1=ALU.add), inc=vv)
                P.do("scalar", lambda e, trig_i=trig_i: e.activation(
                    out=(sin_c if trig_i == 0 else cos_c), in_=ang_c,
                    func=AF.Sin), inc=sc)
            q1 = qs[:].rearrange("p (h d) -> p h d", h=NH)[:, :, 0:32]
            q2 = qs[:].rearrange("p (h d) -> p h d", h=NH)[:, :, 32:64]
            o1 = qs2[:].rearrange("p (h d) -> p h d", h=NH)[:, :, 0:32]
            o2 = qs2[:].rearrange("p (h d) -> p h d", h=NH)[:, :, 32:64]
            tm = tmp5[:].rearrange("p (h d) -> p h d", h=NH, d=32)
            cosb = cos_c.rearrange("p (a d) -> p a d", a=1).broadcast_to([128, NH, 32])
            sinb = sin_c.rearrange("p (a d) -> p a d", a=1).broadcast_to([128, NH, 32])
            P.do("vector", lambda e, q1=q1, o1=o1, cosb=cosb: e.tensor_tensor(
                out=o1, in0=q1, in1=cosb, op=ALU.mult), after=[sc.at()], inc=vv)
            P.do("vector", lambda e, q2=q2, tm=tm, sinb=sinb: e.tensor_tensor(
                out=tm, in0=q2, in1=sinb, op=ALU.mult), inc=vv)
            P.do("vector", lambda e, o1=o1, tm=tm: e.tensor_tensor(
                out=o1, in0=o1, in1=tm, op=ALU.subtract), inc=vv)
            P.do("vector", lambda e, q2=q2, o2=o2, cosb=cosb: e.tensor_tensor(
                out=o2, in0=q2, in1=cosb, op=ALU.mult), inc=vv)
            P.do("vector", lambda e, q1=q1, tm=tm, sinb=sinb: e.tensor_tensor(
                out=tm, in0=q1, in1=sinb, op=ALU.mult), inc=vv)
            P.do("vector", lambda e, o2=o2, tm=tm: e.tensor_tensor(
                out=o2, in0=o2, in1=tm, op=ALU.add), inc=vv)
            P.do("vector", lambda e: e.tensor_copy(out=qsb[:], in_=qs2[:]), inc=vv)
            P.strict = {}
            cast_vv = vv.val
            dst = qf_v if w_i == 0 else kf_v
            for ht in range(8):
                P.do("tensor",
                     lambda e, ht=ht: e.transpose(
                         pstB[:, ht * 128:(ht + 1) * 128],
                         qsb[:, ht * 128:(ht + 1) * 128], ident_bf[:]),
                     after=[vv.at(cast_vv), sc.at(), vv.at()] if ht == 0 else (),
                     inc=mm if ht == 7 else None)
            P.do("vector",
                 lambda e, tt=tt, dst=dst: e.tensor_copy(
                     out=dst[:, :, tt * 128:(tt + 1) * 128],
                     in_=pstB[:].rearrange("p (h t) -> p h t", h=8)),
                 after=[mm.at()], inc=vv)
    qkv_done_vv = vv.val
    qkv_done_sc = sc.val
    if debug:
        P.do("vector", lambda e: e.tensor_copy(out=qs[:], in_=vt_v[:, 1, :]),
             after=[vv.at(), sc.at()], inc=vv)
        P.dma("sync", dbgB[:, 0:1024], qs[:], after=[vv.at()], inc=st)
        P.do("vector", lambda e: e.tensor_copy(
            out=qs2[:].rearrange("p (h t) -> p h t", h=8),
            in_=qf_v[:, :, 128:256]),
             after=[st.at()], inc=vv)
        P.dma("sync", dbgB[:, 1024:2048], qs2[:], after=[vv.at()], inc=st)
        P.wait("vector", [st.at()])

    cf_v = x2f_v
    for qb in range(16):
        t_self = qb + 1
        t_prev = qb
        for h in range(NH):
            hrow = (h % 2) * 64
            htile = h // 2
            dep0 = [vv.at(qkv_done_vv), vv.at(), sc.at(), ld.at()]
            P.do("tensor",
                 lambda e, hrow=hrow, htile=htile, t_prev=t_prev, qb=qb: e.matmul(
                     pssc[:, 0:128],
                     qf_v[hrow:hrow + 64, htile, (qb + 1) * 128:(qb + 2) * 128],
                     kf_v[hrow:hrow + 64, htile, t_prev * 128:(t_prev + 1) * 128],
                     start=True, stop=True),
                 after=dep0, inc=None)
            P.do("tensor",
                 lambda e, hrow=hrow, htile=htile, t_self=t_self, qb=qb: e.matmul(
                     pssc[:, 128:256],
                     qf_v[hrow:hrow + 64, htile, (qb + 1) * 128:(qb + 2) * 128],
                     kf_v[hrow:hrow + 64, htile, t_self * 128:(t_self + 1) * 128],
                     start=True, stop=True),
                 inc=mm)
            mk_ = mask_f if (qb == 0) else mask_n
            P.strict = {"vector": [vv, sc], "scalar": [sc, vv]}
            P.do("vector", lambda e, mk_=mk_: e.scalar_tensor_tensor(
                out=scb[:], in0=pssc[:], scalar=0.125, in1=mk_,
                op0=ALU.mult, op1=ALU.add),
                 after=[mm.at()], inc=vv)
            P.do("vector", lambda e: e.tensor_reduce(
                out=rmax_c, in_=scb[:], axis=AX.X, op=ALU.max, negate=True), inc=vv)
            P.do("scalar", lambda e: e.activation(
                out=scb[:], in_=scb[:], func=AF.Exp, bias=rmax_c,
                accum_out=rsum_c), after=[vv.at()], inc=sc)
            P.do("vector", lambda e: e.reciprocal(out=rinv_c, in_=rsum_c),
                 after=[sc.at()], inc=vv)
            P.do("vector", lambda e: e.tensor_scalar(
                out=pb[:], in0=scb[:], scalar1=rinv_c, scalar2=None,
                op0=ALU.mult), inc=vv)
            P.strict = {}
            pb_vv = vv.val
            for blk in range(2):
                P.do("tensor",
                     lambda e, blk=blk: e.transpose(
                         psT[:, blk * 128:(blk + 1) * 128],
                         pb[:, blk * 128:(blk + 1) * 128], ident_bf[:]),
                     after=[vv.at(pb_vv), vv.at()] if blk == 0 else (),
                     inc=mm if blk == 1 else None)
            P.do("vector", lambda e: e.tensor_copy(out=ptb[:], in_=psT[:]),
                 after=[mm.at()], inc=vv)
            ptb_vv = vv.val
            for blk, tl in ((0, t_prev), (1, t_self)):
                P.do("tensor",
                     lambda e, blk=blk, tl=tl, h=h: e.matmul(
                         psctx[:], vt_v[:, tl, h * 64:(h + 1) * 64],
                         ptb[:, blk * 128:(blk + 1) * 128],
                         start=(blk == 0), stop=(blk == 1)),
                     after=[vv.at(ptb_vv), sc.at()] if blk == 0 else (),
                     inc=mm if blk == 1 else None)
            P.do("scalar",
                 lambda e, hrow=hrow, htile=htile, qb=qb: e.copy(
                     out=cf_v[hrow:hrow + 64, htile, (qb + 1) * 128:(qb + 2) * 128],
                     in_=psctx[:]),
                 after=[mm.at()], inc=sc)
    attn_done_sc = sc.val

    P.dma("gpsimd", w_v, woT.rearrange("(a p) o -> p a o", p=128),
          after=[mm.at()], inc=gp)
    wo_ld = gp.val
    rs_c = smB[:, 4:5]; ssq_c = smB[:, 5:6]; mean_c = smB[:, 6:7]
    var_c = smB[:, 7:8]; rstd_c = smB[:, 8:9]; msq_c = smB[:, 9:10]
    eps_c = smB[:, 10:11]
    P.do("vector", lambda e: e.memset(eps_c, 1e-5), inc=vv)
    for tt in range(16):
        r0, r1 = tt * 128, (tt + 1) * 128
        P.dma("sync", resid[:], x_in[r0:r1, :], after=[vv.at(), sc.at()], inc=ld)
        res_ld = ld.val
        for os_ in range(2):
            for ht in range(8):
                P.do("tensor",
                     lambda e, tt=tt, os_=os_, ht=ht: e.matmul(
                         pso[:], cf_v[:, ht, (tt + 1) * 128:(tt + 2) * 128],
                         w_v[:, ht, os_ * 512:(os_ + 1) * 512],
                         start=(ht == 0), stop=False),
                     after=[gp.at(wo_ld), sc.at(attn_done_sc), vv.at(), sc.at()]
                     if ht == 0 else (), inc=None)
            P.do("tensor",
                 lambda e, os_=os_: e.matmul(
                     pso[:], ones_r[:],
                     biasr[0:1, 3 * DM + os_ * 512: 3 * DM + (os_ + 1) * 512],
                     start=False, stop=True), inc=mm)
            P.do("scalar",
                 lambda e, os_=os_: e.copy(out=qs[:, os_ * 512:(os_ + 1) * 512],
                                           in_=pso[:]),
                 after=[mm.at()], inc=sc)
        P.strict = {"vector": [vv, sc], "scalar": [sc, vv]}
        P.do("vector", lambda e: e.tensor_tensor(out=qs[:], in0=qs[:], in1=resid[:],
                                                 op=ALU.add),
             after=[ld.at(res_ld), sc.at()], inc=vv)
        P.do("vector", lambda e: e.reduce_sum(out=rs_c, in_=qs[:], axis=AX.X), inc=vv)
        P.do("scalar", lambda e: e.activation(out=qs2[:],
                                              in_=qs[:], func=AF.Square,
                                              accum_out=ssq_c), inc=sc)
        P.do("vector", lambda e: e.tensor_scalar_mul(mean_c, rs_c, 1.0 / DM), inc=vv)
        P.do("vector", lambda e: e.tensor_mul(msq_c, mean_c, mean_c), inc=vv)
        P.do("vector", lambda e: e.tensor_scalar_mul(var_c, ssq_c, 1.0 / DM),
             after=[sc.at()], inc=vv)
        P.do("vector", lambda e: e.tensor_sub(var_c, var_c, msq_c), inc=vv)
        P.do("scalar", lambda e: e.activation(out=rstd_c, in_=var_c, func=AF.Ln,
                                              bias=eps_c), inc=sc)
        P.do("scalar", lambda e: e.activation(out=rstd_c, in_=rstd_c, func=AF.Exp,
                                              scale=-0.5), inc=sc)
        P.do("vector", lambda e: e.tensor_scalar(out=qs[:], in0=qs[:], scalar1=mean_c,
                                                 scalar2=rstd_c, op0=ALU.subtract,
                                                 op1=ALU.mult),
             after=[sc.at()], inc=vv)
        P.strict = {}
        P.dma("sync", x3_out[r0:r1, :], qs[:], after=[vv.at()], inc=st)
    return dict(ld=ld, gp=gp, mm=mm, sc=sc, vv=vv, st=st)

def build_phase_b(debug=False):
    nc = bass.Bass("TRN2", target_bir_lowering=False, debug=False)
    io = declare_b_io(nc, debug)
    from contextlib import ExitStack
    with ExitStack() as _stk:
        P = Prog(nc)
        C = emit_phase_b(nc, P, _stk.enter_context, io, "b_", debug)
        P.wait("gpsimd", [C["st"].at(), C["ld"].at(), C["gp"].at()])
        P.realize()
    return nc

def prep_phase_b_inputs(inputs, x2_list):
    x2 = np.concatenate([np.asarray(a) for a in x2_list], axis=0)
    g = lambda k: np.asarray(inputs["L1_attn_" + k])
    shared = {
        "wqT": np.ascontiguousarray(g("Wq").T), "wkT": np.ascontiguousarray(g("Wk").T),
        "wvT": np.ascontiguousarray(g("Wv").T), "woT": np.ascontiguousarray(g("Wo").T),
        "biases": np.ascontiguousarray(np.stack([g("bq"), g("bk"), g("bv"), g("bo")])),
    }
    qi = np.arange(128)[:, None]
    kj = np.arange(256)[None, :]
    band = (kj >= qi) & (kj <= qi + 128)
    mask_n = np.where(band, 0.0, -1e30).astype(np.float32)
    mask_f = np.where(band & (kj >= 128), 0.0, -1e30).astype(np.float32)
    maps = []
    for c in range(8):
        b, half = c // 2, c % 2
        base = half * TA
        pos = (base - 128 + np.arange(NTB)[None, :] * 128
               + np.arange(128)[:, None]).astype(np.float32)
        halo = (x2[c * TA - 128: c * TA] if half == 1
                else np.zeros((128, DM), np.float32))
        m = dict(shared)
        m["x_in"] = np.ascontiguousarray(x2[c * TA:(c + 1) * TA])
        m["halo_in"] = np.ascontiguousarray(halo)
        m["posv"] = np.ascontiguousarray(pos)
        m["masks_in"] = np.ascontiguousarray(
            np.concatenate([mask_f if half == 0 else mask_n, mask_n], axis=1))
        maps.append(m)
    return maps

def run_phase_b(inputs, x2_list, debug=False):
    nc = build_phase_b(debug=debug)
    in_maps = prep_phase_b_inputs(inputs, x2_list)
    res = run_bass_kernel_spmd(nc, in_maps, core_ids=list(range(8)))
    return [r["x3_out"] for r in res.results]


def declare_c_io(nc, debug=False):
    io = {}
    io["x_in"] = nc.dram_tensor("x_in", [TA, DM], F32, kind="ExternalInput").ap()
    io["qwT"] = nc.dram_tensor("qwT", [DM, DM], F32, kind="ExternalInput").ap()
    io["kwT"] = nc.dram_tensor("kwT", [DM, DM], F32, kind="ExternalInput").ap()
    io["vwT"] = nc.dram_tensor("vwT", [DM, DM], F32, kind="ExternalInput").ap()
    io["biases"] = nc.dram_tensor("biases", [3, DM], F32, kind="ExternalInput").ap()
    io["gamma1"] = nc.dram_tensor("gamma1", [1, DM], F32, kind="ExternalInput").ap()
    io["theta"] = nc.dram_tensor("theta", [1, 32], F32, kind="ExternalInput").ap()
    io["posv"] = nc.dram_tensor("posv", [128, 16], F32, kind="ExternalInput").ap()
    io["qn_out"] = nc.dram_tensor("qn_out", [DM, TA], BF16, kind="ExternalOutput").ap()
    io["kn_out"] = nc.dram_tensor("kn_out", [DM, TA], BF16, kind="ExternalOutput").ap()
    io["vn_out"] = nc.dram_tensor("vn_out", [TA, DM], BF16, kind="ExternalOutput").ap()
    io["vn_out3"] = io["vn_out"].rearrange("t (r c) -> t r c", r=2)
    io["xn_out"] = nc.dram_tensor("xn_out", [TA, DM], F32, kind="ExternalOutput").ap()
    return io

def emit_phase_c(nc, P, ec, io, pfx="c_", debug=False):
    x_in = io["x_in"]
    qwT = io["qwT"]
    kwT = io["kwT"]
    vwT = io["vwT"]
    biases = io["biases"]
    gamma1 = io["gamma1"]
    theta = io["theta"]
    posv = io["posv"]
    qn_out = io["qn_out"]
    kn_out = io["kn_out"]
    vn_out3 = io["vn_out3"]
    xn_out = io["xn_out"]
    xnf = ec(nc.sbuf_tensor(pfx + "xnfC", [128, 8 * TA], BF16))
    wbuf = ec(nc.sbuf_tensor(pfx + "wbufC", [128, 8 * DM], BF16))
    outbuf = ec(nc.sbuf_tensor(pfx + "outbufC", [128, 16 * DM], BF16))
    biasr = ec(nc.sbuf_tensor(pfx + "biasrC", [1, 3 * DM], BF16))
    ones_r = ec(nc.sbuf_tensor(pfx + "ones_rC", [1, 128], BF16))
    ident_bf = ec(nc.sbuf_tensor(pfx + "identC", [128, 128], BF16))
    gam_b = ec(nc.sbuf_tensor(pfx + "gam_bC", [128, DM], F32))
    trig = ec(nc.sbuf_tensor(pfx + "trigC", [128, 160], F32))
    posv_sb = ec(nc.sbuf_tensor(pfx + "posvC", [128, 16], F32))
    qs = ec(nc.sbuf_tensor(pfx + "qsC", [128, DM], F32))
    qs2 = ec(nc.sbuf_tensor(pfx + "qs2C", [128, DM], F32))
    tmp5 = ec(nc.sbuf_tensor(pfx + "tmp5C", [128, 512], F32))
    qsb = ec(nc.sbuf_tensor(pfx + "qsbC", [128, DM], BF16))
    smC = ec(nc.sbuf_tensor(pfx + "smC", [128, 16], F32))
    psq = ec(nc.psum_tensor(pfx + "psqC", [128, 512], F32))
    pstC = ec(nc.psum_tensor(pfx + "pstC", [128, DM], BF16))
    s_ld = ec(nc.semaphore(pfx + "s_ld")); s_gp = ec(nc.semaphore(pfx + "s_gp"))
    s_mm = ec(nc.semaphore(pfx + "s_mm")); s_sc = ec(nc.semaphore(pfx + "s_sc"))
    s_vv = ec(nc.semaphore(pfx + "s_vv")); s_st = ec(nc.semaphore(pfx + "s_st"))
    ld = Ctr(s_ld); gp = Ctr(s_gp); mm = Ctr(s_mm)
    sc = Ctr(s_sc); vv = Ctr(s_vv); st = Ctr(s_st)

    xnf_v = xnf[:].rearrange("p (h t) -> p h t", h=8)
    w_v = wbuf[:].rearrange("p (h o) -> p h o", h=8)
    ob_v = outbuf[:].rearrange("p (t o) -> p t o", t=16)
    th_c = trig[:, 0:32]; ang_c = trig[:, 32:64]
    cos_c = trig[:, 64:96]; sin_c = trig[:, 96:128]; kk_c = trig[:, 128:160]
    rs_c = smC[:, 0:1]; ssq_c = smC[:, 1:2]; rstd_c = smC[:, 2:3]; eps_c = smC[:, 3:4]

    M_MAGIC = 12582912.0
    INV2PI = 0.15915493667125702
    CW1 = 6.28125
    CW2 = float(np.float32(6.283185307179586 - 6.28125))
    HALFPI = float(np.float32(np.pi / 2))

    P.do("gpsimd", lambda e: e.memset(ident_bf[:], 0.0), inc=gp, n=1)
    P.do("gpsimd", lambda e: e.affine_select(
        out=ident_bf[:], in_=ident_bf[:], compare_op=ALU.not_equal, fill=1.0,
        base=0, pattern=[[-1, 128]], channel_multiplier=1), inc=gp, n=1)
    P.do("gpsimd", lambda e: e.memset(ones_r[:], 1.0), inc=gp, n=1)
    gi_const = gp.val
    P.do("vector", lambda e: e.memset(eps_c, 1e-5), inc=vv)
    P.dma("sync", th_c, theta.broadcast_to([128, 32]), inc=ld)
    P.dma("sync", gam_b[:], gamma1.broadcast_to([128, DM]), inc=ld)
    P.dma("sync", posv_sb[:], posv, inc=ld)
    P.dma("gpsimd", biasr[:].rearrange("p (a o) -> p a o", a=3),
          biases.rearrange("(x a) o -> x a o", x=1), inc=gp)
    consts_ld = ld.val

    for tt in range(16):
        r0, r1 = tt * 128, (tt + 1) * 128
        P.dma("sync", qs[:], x_in[r0:r1, :], after=[vv.at(), sc.at(), mm.at()], inc=ld)
        P.strict = {"vector": [vv, sc], "scalar": [sc, vv]}
        P.do("scalar", lambda e: e.activation(out=qs2[:],
                                              in_=qs[:], func=AF.Square,
                                              accum_out=ssq_c),
             after=[ld.at()], inc=sc)
        P.do("scalar", lambda e: e.activation(out=rstd_c, in_=ssq_c, func=AF.Ln,
                                              scale=1.0 / DM, bias=eps_c), inc=sc)
        P.do("scalar", lambda e: e.activation(out=rstd_c, in_=rstd_c, func=AF.Exp,
                                              scale=-0.5), inc=sc)
        P.do("vector", lambda e: e.tensor_scalar(out=qs[:], in0=qs[:], scalar1=rstd_c,
                                                 scalar2=None, op0=ALU.mult), inc=vv)
        P.do("vector", lambda e: e.tensor_tensor(out=qs[:], in0=qs[:], in1=gam_b[:],
                                                 op=ALU.mult),
             after=[ld.at(consts_ld)], inc=vv)
        P.strict = {}
        xn_vv = vv.val
        P.dma("sync", xn_out[r0:r1, :], qs[:], after=[vv.at(xn_vv)], inc=st)
        P.do("vector", lambda e: e.tensor_copy(out=qsb[:], in_=qs[:]),
             after=[vv.at(xn_vv)], inc=vv)
        for ht in range(8):
            P.do("tensor",
                 lambda e, ht=ht: e.transpose(
                     pstC[:, ht * 128:(ht + 1) * 128],
                     qsb[:, ht * 128:(ht + 1) * 128], ident_bf[:]),
                 after=[vv.at(), gp.at(gi_const), sc.at()] if ht == 0 else (),
                 inc=mm if ht == 7 else None)
        P.do("vector",
             lambda e, tt=tt: e.tensor_copy(
                 out=xnf_v[:, :, tt * 128:(tt + 1) * 128],
                 in_=pstC[:].rearrange("p (h t) -> p h t", h=8)),
             after=[mm.at()], inc=vv)
    xnf_done_vv = vv.val

    for w_i, wT in enumerate([qwT, kwT, vwT]):
        P.dma("gpsimd", w_v, wT.rearrange("(a p) o -> p a o", p=128),
              after=[mm.at()], inc=gp)
        wld = gp.val
        for tt in range(16):
            for os_ in range(2):
                dep0 = [gp.at(wld), vv.at(xnf_done_vv), sc.at(), vv.at()]
                for ht in range(8):
                    P.do("tensor",
                         lambda e, tt=tt, os_=os_, ht=ht: e.matmul(
                             psq[:], xnf_v[:, ht, tt * 128:(tt + 1) * 128],
                             w_v[:, ht, os_ * 512:(os_ + 1) * 512],
                             start=(ht == 0), stop=False),
                         after=dep0 if ht == 0 else (), inc=None)
                P.do("tensor",
                     lambda e, w_i=w_i, os_=os_: e.matmul(
                         psq[:], ones_r[:],
                         biasr[0:1, w_i * DM + os_ * 512: w_i * DM + (os_ + 1) * 512],
                         start=False, stop=True),
                     after=[gp.at()], inc=mm)
                P.do("scalar",
                     lambda e, os_=os_: e.copy(out=qs[:, os_ * 512:(os_ + 1) * 512],
                                               in_=psq[:]),
                     after=[mm.at()], inc=sc)
            if w_i == 2:
                P.do("vector", lambda e, tt=tt: e.tensor_copy(out=ob_v[:, tt, :],
                                                              in_=qs[:]),
                     after=[sc.at()], inc=vv)
                continue
            P.strict = {"vector": [vv, sc], "scalar": [sc, vv]}
            for trig_i in range(2):
                P.do("vector", lambda e, tt=tt: e.tensor_scalar(
                    out=ang_c, in0=th_c, scalar1=posv_sb[:, tt:tt + 1],
                    scalar2=None, op0=ALU.mult),
                     after=[ld.at(consts_ld)], inc=vv)
                if trig_i == 1:
                    P.do("vector", lambda e: e.tensor_scalar(
                        out=ang_c, in0=ang_c, scalar1=HALFPI, scalar2=None,
                        op0=ALU.add), inc=vv)
                P.do("vector", lambda e: e.tensor_scalar(
                    out=kk_c, in0=ang_c, scalar1=INV2PI, scalar2=M_MAGIC,
                    op0=ALU.mult, op1=ALU.add), inc=vv)
                P.do("vector", lambda e: e.tensor_scalar(
                    out=kk_c, in0=kk_c, scalar1=M_MAGIC, scalar2=None,
                    op0=ALU.subtract), inc=vv)
                P.do("vector", lambda e: e.scalar_tensor_tensor(
                    out=ang_c, in0=kk_c, scalar=-CW1, in1=ang_c,
                    op0=ALU.mult, op1=ALU.add), inc=vv)
                P.do("vector", lambda e: e.scalar_tensor_tensor(
                    out=ang_c, in0=kk_c, scalar=-CW2, in1=ang_c,
                    op0=ALU.mult, op1=ALU.add), inc=vv)
                P.do("scalar", lambda e, trig_i=trig_i: e.activation(
                    out=(sin_c if trig_i == 0 else cos_c), in_=ang_c,
                    func=AF.Sin), inc=sc)
            qv = qs[:].rearrange("p (h d) -> p h d", h=NH)
            q1 = qv[:, :, 0:63:2] if False else qs[:].rearrange(
                "p (h d2 two) -> p h d2 two", h=NH, two=2)[:, :, :, 0]
            q2 = qs[:].rearrange("p (h d2 two) -> p h d2 two", h=NH, two=2)[:, :, :, 1]
            o1 = qs2[:].rearrange("p (h d) -> p h d", h=NH)[:, :, 0:32]
            o2 = qs2[:].rearrange("p (h d) -> p h d", h=NH)[:, :, 32:64]
            tm = tmp5[:].rearrange("p (h d) -> p h d", h=NH, d=32)
            cosb = cos_c.rearrange("p (a d) -> p a d", a=1).broadcast_to([128, NH, 32])
            sinb = sin_c.rearrange("p (a d) -> p a d", a=1).broadcast_to([128, NH, 32])
            P.do("vector", lambda e, q1=q1, o1=o1, cosb=cosb: e.tensor_tensor(
                out=o1, in0=q1, in1=cosb, op=ALU.mult), inc=vv)
            P.do("vector", lambda e, q2=q2, tm=tm, sinb=sinb: e.tensor_tensor(
                out=tm, in0=q2, in1=sinb, op=ALU.mult), inc=vv)
            P.do("vector", lambda e, o1=o1, tm=tm: e.tensor_tensor(
                out=o1, in0=o1, in1=tm, op=ALU.subtract), inc=vv)
            P.do("vector", lambda e, q2=q2, o2=o2, cosb=cosb: e.tensor_tensor(
                out=o2, in0=q2, in1=cosb, op=ALU.mult), inc=vv)
            P.do("vector", lambda e, q1=q1, tm=tm, sinb=sinb: e.tensor_tensor(
                out=tm, in0=q1, in1=sinb, op=ALU.mult), inc=vv)
            P.do("vector", lambda e, o2=o2, tm=tm: e.tensor_tensor(
                out=o2, in0=o2, in1=tm, op=ALU.add), inc=vv)
            P.do("vector", lambda e: e.tensor_copy(out=qsb[:], in_=qs2[:]), inc=vv)
            P.strict = {}
            cast_vv = vv.val
            for ht in range(8):
                P.do("tensor",
                     lambda e, ht=ht: e.transpose(
                         pstC[:, ht * 128:(ht + 1) * 128],
                         qsb[:, ht * 128:(ht + 1) * 128], ident_bf[:]),
                     after=[vv.at(cast_vv), sc.at(), vv.at()] if ht == 0 else (),
                     inc=mm if ht == 7 else None)
            P.do("vector",
                 lambda e, tt=tt: e.tensor_copy(
                     out=ob_v[:, tt, :].rearrange("p (h t) -> p h t", h=8)
                     if False else ob_v[:, tt, :],
                     in_=pstC[:]),
                 after=[mm.at()], inc=vv)
        dst = (qn_out, kn_out, None)[w_i]
        if w_i < 2:
            P.dma("sync", dst.rearrange("(h p) (tt t) -> p tt h t", p=128, t=128),
                  ob_v[:].rearrange("p tt (h t) -> p tt h t", h=8),
                  after=[vv.at()], inc=st)
        else:
            P.dma("sync", vn_out3.rearrange("(tt p) r c -> p tt r c", p=128),
                  ob_v[:].rearrange("p tt (r c) -> p tt r c", r=2),
                  after=[vv.at()], inc=st)
    return dict(ld=ld, gp=gp, mm=mm, sc=sc, vv=vv, st=st)

def build_phase_c(debug=False):
    nc = bass.Bass("TRN2", target_bir_lowering=False, debug=False)
    io = declare_c_io(nc, debug)
    from contextlib import ExitStack
    with ExitStack() as _stk:
        P = Prog(nc)
        C = emit_phase_c(nc, P, _stk.enter_context, io, "c_", debug)
        P.wait("gpsimd", [C["st"].at(), C["ld"].at(), C["gp"].at()])
        P.realize()
    return nc


def declare_d_io(nc, debug=False):
    io = {}
    io["qn"] = nc.dram_tensor("qn", [512, T], BF16, kind="ExternalInput").ap()
    io["kn"] = nc.dram_tensor("kn", [512, T], BF16, kind="ExternalInput").ap()
    io["vn"] = nc.dram_tensor("vn", [T, 512], BF16, kind="ExternalInput").ap()
    io["ctx_out"] = nc.dram_tensor("ctx_out", [512, T], BF16, kind="ExternalOutput").ap()
    io["qn3"] = io["qn"].rearrange("h (r t) -> h r t", r=2)
    io["kn3"] = io["kn"].rearrange("h (r t) -> h r t", r=2)
    io["ctx_out3"] = io["ctx_out"].rearrange("h (r t) -> h r t", r=2)
    return io

def emit_phase_d(nc, P, ec, io, pfx="d_", debug=False):
    qn3 = io.get("qn3"); kn3 = io.get("kn3"); vn = io.get("vn")
    ctx_out3 = io["ctx_out3"]
    qsb = ec(nc.sbuf_tensor(pfx + "qsbD", [128, 4 * T], BF16))
    ksb = ec(nc.sbuf_tensor(pfx + "ksbD", [128, 4 * T], BF16))
    vsb = ec(nc.sbuf_tensor(pfx + "vsbD", [128, 32 * 512], BF16))
    csb = ec(nc.sbuf_tensor(pfx + "csbD", [128, 4 * T], BF16))
    scores = ec(nc.sbuf_tensor(pfx + "scoresD", [128, T], F32))
    scores2 = ec(nc.sbuf_tensor(pfx + "scoresD2", [128, T], F32))
    pbuf = ec(nc.sbuf_tensor(pfx + "pbufD", [128, T], BF16))
    pbuf2 = ec(nc.sbuf_tensor(pfx + "pbufD2", [128, T], BF16))
    ptb = ec(nc.sbuf_tensor(pfx + "ptbD", [128, 256], BF16))
    cmask = ec(nc.sbuf_tensor(pfx + "cmaskD", [128, 128], F32))
    ident_bf = ec(nc.sbuf_tensor(pfx + "identD", [128, 128], BF16))
    smD = ec(nc.sbuf_tensor(pfx + "smD", [128, 16], F32))
    pss0 = ec(nc.psum_tensor(pfx + "pss0", [128, 512], F32))
    pss1 = ec(nc.psum_tensor(pfx + "pss1", [128, 512], F32))
    psT = ec(nc.psum_tensor(pfx + "psTD", [128, 256], BF16))
    psctx0 = ec(nc.psum_tensor(pfx + "psctxD0", [64, 128], F32))
    psctx1 = ec(nc.psum_tensor(pfx + "psctxD1", [64, 128], F32))
    s_ld = ec(nc.semaphore(pfx + "s_ld")); s_gp = ec(nc.semaphore(pfx + "s_gp"))
    s_mm = ec(nc.semaphore(pfx + "s_mm")); s_sc = ec(nc.semaphore(pfx + "s_sc"))
    s_vv = ec(nc.semaphore(pfx + "s_vv")); s_st = ec(nc.semaphore(pfx + "s_st"))
    ld = Ctr(s_ld); gp = Ctr(s_gp); mm = Ctr(s_mm)
    sc = Ctr(s_sc); vv = Ctr(s_vv); st = Ctr(s_st)

    q_v = qsb[:].rearrange("p (a t) -> p a t", a=4)
    k_v = ksb[:].rearrange("p (a t) -> p a t", a=4)
    v_v = vsb[:].rearrange("p (tt o) -> p tt o", tt=32)
    c_v = csb[:].rearrange("p (a t) -> p a t", a=4)
    rmax_c = smD[:, 0:1]; rsum_c = smD[:, 1:2]; rinv_c = smD[:, 2:3]
    neg_c = smD[:, 3:4]

    P.do("gpsimd", lambda e: e.memset(ident_bf[:], 0.0), inc=gp, n=1)
    P.do("gpsimd", lambda e: e.affine_select(
        out=ident_bf[:], in_=ident_bf[:], compare_op=ALU.not_equal, fill=1.0,
        base=0, pattern=[[-1, 128]], channel_multiplier=1), inc=gp, n=1)
    P.do("gpsimd", lambda e: e.memset(cmask[:], 0.0), inc=gp, n=1)
    P.do("gpsimd", lambda e: e.affine_select(
        out=cmask[:], in_=cmask[:], compare_op=ALU.is_ge, fill=-1e30,
        base=0, pattern=[[-1, 128]], channel_multiplier=1), inc=gp, n=1)
    gi_const = gp.val
    blend = "sel01" in io
    if blend:
        sel_sb = ec(nc.sbuf_tensor(pfx + "sel", [128, 2], F32))
        P.dma("sync", sel_sb[:], io["sel01"], inc=ld)
        qg5 = io["qg"].rearrange("(r g a p) t -> p r g a t", r=2, g=2, a=4)
        kg5 = io["kg"].rearrange("(r g a p) t -> p r g a t", r=2, g=2, a=4)
        vg5 = io["vg"].rearrange("(r g tt p) o -> p r g tt o", r=2, g=2, tt=16)
        scr_q = csb[:, 0:2048]
        sel_ld = ld.val
        for arr, g5, dstv in (("q", qg5, q_v), ("k", kg5, k_v)):
            for a in range(4):
                for r in range(2):
                    P.dma("sync", dstv[:, a, r * TA:(r + 1) * TA],
                          g5[:, r, 0, a, :], inc=ld)
                    P.dma("sync", scr_q, g5[:, r, 1, a, :],
                          after=[vv.at()], inc=ld)
                    v_ld = ld.val
                    P.do("vector", lambda e, dv=dstv[:, a, r * TA:(r + 1) * TA]:
                         e.tensor_scalar(out=dv, in0=dv, scalar1=sel_sb[:, 0:1],
                                         scalar2=None, op0=ALU.mult),
                         after=[ld.at(v_ld)], inc=vv)
                    P.do("vector", lambda e, dv=dstv[:, a, r * TA:(r + 1) * TA]:
                         e.scalar_tensor_tensor(out=dv, in0=scr_q,
                                                scalar=sel_sb[:, 1:2], in1=dv,
                                                op0=ALU.mult, op1=ALU.add),
                         after=[vv.at()], inc=vv)
        vsb4 = vsb[:].rearrange("p (r tt o) -> p r tt o", r=2, tt=16)
        P.dma("sync", vsb4, vg5[:, :, 0, :, :], after=[vv.at()], inc=ld)
        P.dma("sync", csb[:, 0:16384].rearrange("p (r tt o) -> p r tt o", r=2, tt=16),
              vg5[:, :, 1, :, :], inc=ld)
        v_ld = ld.val
        P.do("vector", lambda e: e.tensor_scalar(out=vsb[:], in0=vsb[:],
                                                 scalar1=sel_sb[:, 0:1], scalar2=None,
                                                 op0=ALU.mult),
             after=[ld.at(v_ld)], inc=vv)
        P.do("vector", lambda e: e.scalar_tensor_tensor(
            out=vsb[:], in0=csb[:, 0:16384], scalar=sel_sb[:, 1:2], in1=vsb[:],
            op0=ALU.mult, op1=ALU.add),
             after=[vv.at()], inc=vv)
    else:
        for a in range(4):
            for r in range(2):
                P.dma("sync", q_v[:, a, r * TA:(r + 1) * TA],
                      qn3[a * 128:(a + 1) * 128, r, :], inc=ld)
                P.dma("sync", k_v[:, a, r * TA:(r + 1) * TA],
                      kn3[a * 128:(a + 1) * 128, r, :], inc=ld)
        P.dma("sync", vsb[:].rearrange("p (tt o) -> p tt o", tt=32),
              vn.rearrange("(tt p) o -> p tt o", p=128), inc=ld)
    ld_all = ld.val
    blend_done_vv = vv.val

    iters = [(h, qb) for h in range(8) for qb in range(32)]
    NIT = len(iters)
    sc_half = [scores[:, 0:2048], None]
    stats = [dict() for _ in range(NIT)]

    def s_scores(i):
        h, qb = iters[i]
        atile, hrow = h // 2, (h % 2) * 64
        nkb = qb + 1
        nslab = (nkb + 3) // 4
        scb_i = (scores, scores2)[i % 2]
        for sl in range(nslab):
            k0 = sl * 512
            kw = min(512, nkb * 128 - k0)
            bank = (i * 37 + sl) % 2
            ps = (pss0, pss1)[bank]
            deps = [ld.at(ld_all), vv.at(blend_done_vv)]
            if psum_free_vv[bank]:
                deps.append(sc.at(psum_free_vv[bank][1]))
            P.do("tensor",
                 lambda e, atile=atile, hrow=hrow, qb=qb, k0=k0, kw=kw, ps=ps:
                 e.matmul(
                     ps[:, 0:kw],
                     q_v[hrow:hrow + 64, atile, qb * 128:(qb + 1) * 128],
                     k_v[hrow:hrow + 64, atile, k0:k0 + kw],
                     start=True, stop=True),
                 after=deps, inc=mm)
            mmv = mm.val
            P.do("scalar",
                 lambda e, k0=k0, kw=kw, ps=ps, scb_i=scb_i: e.activation(
                     out=scb_i[:, k0:k0 + kw], in_=ps[:, 0:kw], func=AF.Copy,
                     scale=0.125),
                 after=[mm.at(mmv), sc.at(exp_done_sc[i % 2])], inc=sc)
            psum_free_vv[bank] = ("sc", sc.val)
        P.do("vector", lambda e, qb=qb, scb_i=scb_i: e.tensor_tensor(
            out=scb_i[:, qb * 128:(qb + 1) * 128],
            in0=scb_i[:, qb * 128:(qb + 1) * 128],
            in1=cmask[:], op=ALU.add),
             after=[gp.at(gi_const), sc.at()], inc=vv)
        neg_i = smD[:, 4 * (i % 2):4 * (i % 2) + 1]
        P.do("vector", lambda e, nkb=nkb, scb_i=scb_i, neg_i=neg_i: e.tensor_reduce(
            out=neg_i, in_=scb_i[:, 0:nkb * 128], axis=AX.X, op=ALU.max,
            negate=True), inc=vv)
        stats[i]["rmax_vv"] = vv.val

    def s_softmax(i):
        h, qb = iters[i]
        nkb = qb + 1
        scb_i = (scores, scores2)[i % 2]
        pb_i = (pbuf, pbuf2)[i % 2]
        neg_i = smD[:, 4 * (i % 2):4 * (i % 2) + 1]
        rsum_i = smD[:, 4 * (i % 2) + 1:4 * (i % 2) + 2]
        rinv_i = smD[:, 4 * (i % 2) + 2:4 * (i % 2) + 3]
        P.do("scalar", lambda e, nkb=nkb, scb_i=scb_i, pb_i=pb_i, neg_i=neg_i,
             rsum_i=rsum_i: e.activation(
                 out=pb_i[:, 0:nkb * 128], in_=scb_i[:, 0:nkb * 128], func=AF.Exp,
                 bias=neg_i, accum_out=rsum_i),
             after=[vv.at(stats[i]["rmax_vv"]),
                    vv.at(stats[i - 2]["norm_vv"] if i >= 2 else 0),
                    sc.at(ctx_evac_sc[i % 2])], inc=sc)
        exp_done_sc[i % 2] = sc.val
        exp_sc = sc.val
        P.do("vector", lambda e, rsum_i=rsum_i, rinv_i=rinv_i: e.reciprocal(
            out=rinv_i, in_=rsum_i), after=[sc.at(exp_sc)], inc=vv)
        rec_vv = vv.val
        P.do("scalar", lambda e, nkb=nkb, pb_i=pb_i, rinv_i=rinv_i: e.activation(
            out=pb_i[:, 0:nkb * 128], in_=pb_i[:, 0:nkb * 128], func=AF.Copy,
            scale=rinv_i), after=[vv.at(rec_vv)], inc=sc)
        stats[i]["norm_sc"] = sc.val
        stats[i]["norm_vv"] = vv.val

    def s_ctx(i):
        h, qb = iters[i]
        atile, hrow = h // 2, (h % 2) * 64
        nkb = qb + 1
        pb_i = (pbuf, pbuf2)[i % 2]
        psc = (psctx0, psctx1)[i % 2]
        for kb in range(nkb):
            sl2 = kb % 2
            tdeps = [sc.at(stats[i]["norm_sc"])]
            if ptb_free_vv[sl2]:
                tdeps.append(vv.at(ptb_free_vv[sl2]))
            P.do("tensor",
                 lambda e, kb=kb, sl2=sl2, pb_i=pb_i: e.transpose(
                     psT[:, sl2 * 128:(sl2 + 1) * 128],
                     pb_i[:, kb * 128:(kb + 1) * 128], ident_bf[:]),
                 after=tdeps, inc=mm)
            tr_mm = mm.val
            P.do("vector",
                 lambda e, sl2=sl2: e.tensor_copy(
                     out=ptb[:, sl2 * 128:(sl2 + 1) * 128],
                     in_=psT[:, sl2 * 128:(sl2 + 1) * 128]),
                 after=[mm.at(tr_mm)], inc=vv)
            ptb_vv = vv.val
            P.do("tensor",
                 lambda e, kb=kb, sl2=sl2, h=h, nkb=nkb, psc=psc: e.matmul(
                     psc[:], v_v[:, kb, h * 64:(h + 1) * 64],
                     ptb[:, sl2 * 128:(sl2 + 1) * 128],
                     start=(kb == 0), stop=(kb == nkb - 1)),
                 after=[vv.at(ptb_vv)] + ([sc.at(ctx_evac_sc[i % 2])]
                                          if (kb == 0 and ctx_evac_sc[i % 2]) else []),
                 inc=mm)
            ptb_free_vv[sl2] = 0
        ctx_mm = mm.val
        P.do("scalar",
             lambda e, atile=atile, hrow=hrow, qb=qb, psc=psc: e.copy(
                 out=c_v[hrow:hrow + 64, atile, qb * 128:(qb + 1) * 128],
                 in_=psc[:]),
             after=[mm.at(ctx_mm)], inc=sc)
        ctx_evac_sc[i % 2] = sc.val

    psum_free_vv = [0, 0]
    ptb_free_vv = [0, 0]
    ctx_evac_sc = [0, 0]
    exp_done_sc = [0, 0]
    s_scores(0)
    s_softmax(0)
    s_scores(1)
    for i in range(NIT):
        if i + 2 < NIT:
            s_scores(i + 2)
        if i + 1 < NIT:
            s_softmax(i + 1)
        s_ctx(i)

    for a in range(4):
        for r in range(2):
            P.dma("sync", ctx_out3[a * 128:(a + 1) * 128, r, :],
                  c_v[:, a, r * TA:(r + 1) * TA],
                  after=[sc.at(), vv.at()], inc=st)
    return dict(ld=ld, gp=gp, mm=mm, sc=sc, vv=vv, st=st)

def build_phase_d(debug=False):
    nc = bass.Bass("TRN2", target_bir_lowering=False, debug=False)
    io = declare_d_io(nc, debug)
    from contextlib import ExitStack
    with ExitStack() as _stk:
        P = Prog(nc)
        C = emit_phase_d(nc, P, _stk.enter_context, io, "d_", debug)
        P.wait("gpsimd", [C["st"].at(), C["ld"].at(), C["gp"].at()])
        P.realize()
    return nc


NE1 = 31

def declare_e_io(nc, debug=False):
    io = {}
    io["x3_in"] = nc.dram_tensor("x3_in", [TA, DM], F32, kind="ExternalInput").ap()
    io["xn_in"] = nc.dram_tensor("xn_in", [TA, DM], F32, kind="ExternalInput").ap()
    io["ctx_in"] = nc.dram_tensor("ctx_in", [DM, TA], BF16, kind="ExternalInput").ap()
    io["owT"] = nc.dram_tensor("owT", [DM, DM], F32, kind="ExternalInput").ap()
    io["ob_in"] = nc.dram_tensor("ob_in", [1, DM], F32, kind="ExternalInput").ap()
    io["gamma2"] = nc.dram_tensor("gamma2", [1, DM], F32, kind="ExternalInput").ap()
    io["shw1"] = nc.dram_tensor("shw1", [DM, DM], F32, kind="ExternalInput").ap()
    io["shb1_rs"] = nc.dram_tensor("shb1_rs", [128, 8], F32, kind="ExternalInput").ap()
    io["shw2"] = nc.dram_tensor("shw2", [DM, DM], F32, kind="ExternalInput").ap()
    io["shb2"] = nc.dram_tensor("shb2", [1, DM], F32, kind="ExternalInput").ap()
    io["rw"] = nc.dram_tensor("rw", [DM, NE1], F32, kind="ExternalInput").ap()
    io["rb"] = nc.dram_tensor("rb", [1, NE1], F32, kind="ExternalInput").ap()
    io["rew1"] = nc.dram_tensor("rew1", [NE1, DM, DM], F32, kind="ExternalInput").ap()
    io["reb1T"] = nc.dram_tensor("reb1T", [DM, NE1], F32, kind="ExternalInput").ap()
    io["rew2"] = nc.dram_tensor("rew2", [NE1, DM, DM], F32, kind="ExternalInput").ap()
    io["reb2"] = nc.dram_tensor("reb2", [NE1, DM], F32, kind="ExternalInput").ap()
    io["out_f"] = nc.dram_tensor("out_f", [TA, DM], F32, kind="ExternalOutput").ap()
    knd = "ExternalOutput" if debug else "Internal"
    io["ao_hbm"] = nc.dram_tensor("ao_hbm", [TA, DM], F32, kind=knd).ap()
    io["an_hbm"] = nc.dram_tensor("an_hbm", [TA, DM], F32, kind=knd).ap()
    io["dbgE"] = nc.dram_tensor("dbgE", [128, 1024], F32, kind=knd).ap()
    return io

def emit_phase_e(nc, P, ec, io, pfx="e_", debug=False):
    x3_in = io["x3_in"]
    xn_in = io["xn_in"]
    ctx_in = io.get("ctx_in")
    owT = io["owT"]
    ob_in = io["ob_in"]
    gamma2 = io["gamma2"]
    shw1 = io["shw1"]
    shb1_rs = io["shb1_rs"]
    shw2 = io["shw2"]
    shb2 = io["shb2"]
    rw = io["rw"]
    rb = io["rb"]
    rew1 = io["rew1"]
    reb1T = io["reb1T"]
    rew2 = io["rew2"]
    reb2 = io["reb2"]
    out_f = io["out_f"]
    ao_hbm = io["ao_hbm"]
    an_hbm = io["an_hbm"]
    dbgE = io["dbgE"]
    anf = ec(nc.sbuf_tensor(pfx + "anfE", [128, 8 * TA], BF16))
    acc = ec(nc.sbuf_tensor(pfx + "accE", [128, 16 * DM], F32))
    hf = ec(nc.sbuf_tensor(pfx + "hfE", [128, 8 * TA], BF16))
    wbuf = ec(nc.sbuf_tensor(pfx + "wbufE", [128, 24576], BF16))
    ctxf = ec(nc.sbuf_tensor(pfx + "ctxfE", [128, 8 * 128], BF16))
    gates = ec(nc.sbuf_tensor(pfx + "gatesE", [128, 16 * NE1], F32))
    gatesb = ec(nc.sbuf_tensor(pfx + "gatesbE", [128, NE1], BF16))
    gatesT = ec(nc.sbuf_tensor(pfx + "gatesTE", [31, 16 * 128], BF16))
    rw_sb = ec(nc.sbuf_tensor(pfx + "rw_sbE", [128, 8 * NE1], F32))
    reb2_sb = ec(nc.sbuf_tensor(pfx + "reb2E", [31, DM], BF16))
    b1c = ec(nc.sbuf_tensor(pfx + "b1cE", [128, 8 + 8 * NE1], F32))
    biasr = ec(nc.sbuf_tensor(pfx + "biasrE", [1, 3 * DM + NE1], BF16))
    ones_r = ec(nc.sbuf_tensor(pfx + "ones_rE", [1, 128], BF16))
    ident_bf = ec(nc.sbuf_tensor(pfx + "identE", [128, 128], BF16))
    ident32 = ec(nc.sbuf_tensor(pfx + "ident32E", [128, 128], F32))

    scr2 = ec(nc.sbuf_tensor(pfx + "scr2E", [128, 64], F32))
    smE = ec(nc.sbuf_tensor(pfx + "smE", [128, 32], F32))
    psq = ec(nc.psum_tensor(pfx + "psqE", [128, 512], F32))
    psq2 = ec(nc.psum_tensor(pfx + "psq2E", [128, 512], F32))
    psq3 = ec(nc.psum_tensor(pfx + "psq3E", [128, 512], F32))
    pst = ec(nc.psum_tensor(pfx + "pstE", [128, DM], BF16))
    pst32 = ec(nc.psum_tensor(pfx + "pst32E", [128, DM], F32))
    psr = ec(nc.psum_tensor(pfx + "psrE", [128, 128], F32))
    s_ld = ec(nc.semaphore(pfx + "s_ld")); s_gp = ec(nc.semaphore(pfx + "s_gp"))
    s_mm = ec(nc.semaphore(pfx + "s_mm")); s_sc = ec(nc.semaphore(pfx + "s_sc"))
    s_vv = ec(nc.semaphore(pfx + "s_vv")); s_st = ec(nc.semaphore(pfx + "s_st"))
    ld = Ctr(s_ld); gp = Ctr(s_gp); mm = Ctr(s_mm)
    sc = Ctr(s_sc); vv = Ctr(s_vv); st = Ctr(s_st)

    eblend = "sel01" in io
    anf_v = anf[:].rearrange("p (h t) -> p h t", h=8)
    ant32 = hf[:, 0:2048].bitcast(F32)
    gamb = hf[:, 2048:4096].bitcast(F32)
    h2 = hf[:, 4096:6144].bitcast(F32)
    h1 = hf[:, 6144:8192].bitcast(F32)
    hb = hf[:, 8192:9216]
    hf_v = hf[:].rearrange("p (h t) -> p h t", h=8)
    wv = [wbuf[:, 0:8192].rearrange("p (h o) -> p h o", h=8),
          wbuf[:, 8192:16384].rearrange("p (h o) -> p h o", h=8)]
    ctx_v = ctxf[:].rearrange("p (h t) -> p h t", h=8)
    rw_v = rw_sb[:].rearrange("p (h e) -> p h e", h=8)
    shb1_c = b1c[:, 0:8]
    reb1_c = b1c[:, 8:8 + 8 * NE1].rearrange("p (it e) -> p it e", it=8)
    rs_c = smE[:, 0:1]; ssq_c = smE[:, 1:2]; mean_c = smE[:, 2:3]
    var_c = smE[:, 3:4]; rstd_c = smE[:, 4:5]; msq_c = smE[:, 5:6]
    eps_c = smE[:, 6:7]; mx_c = smE[:, 8:16]
    rmax_c = smE[:, 16:17]; rsum_c = smE[:, 17:18]; rinv_c = smE[:, 18:19]

    P.do("gpsimd", lambda e: e.memset(ident_bf[:], 0.0), inc=gp, n=1)
    P.do("gpsimd", lambda e: e.affine_select(
        out=ident_bf[:], in_=ident_bf[:], compare_op=ALU.not_equal, fill=1.0,
        base=0, pattern=[[-1, 128]], channel_multiplier=1), inc=gp, n=1)
    P.do("gpsimd", lambda e: e.memset(ident32[:], 0.0), inc=gp, n=1)
    P.do("gpsimd", lambda e: e.affine_select(
        out=ident32[:], in_=ident32[:], compare_op=ALU.not_equal, fill=1.0,
        base=0, pattern=[[-1, 128]], channel_multiplier=1), inc=gp, n=1)
    P.do("gpsimd", lambda e: e.memset(ones_r[:], 1.0), inc=gp, n=1)
    gi_const = gp.val
    P.do("vector", lambda e: e.memset(eps_c, 1e-5), inc=vv)
    P.dma("sync", gamb[:], gamma2.broadcast_to([128, DM]), inc=ld)
    P.dma("sync", shb1_c, shb1_rs, inc=ld)
    P.dma("sync", reb1_c, reb1T.rearrange("(it p) e -> p it e", p=128), inc=ld)
    P.dma("sync", rw_v, rw.rearrange("(h p) e -> p h e", p=128), inc=ld)
    P.dma("gpsimd", reb2_sb[:], reb2, inc=gp)
    P.dma("gpsimd", biasr[0:1, 0:DM], ob_in, inc=gp)
    P.dma("gpsimd", biasr[0:1, DM:2 * DM], shb2, inc=gp)
    P.dma("gpsimd", biasr[0:1, 2 * DM:2 * DM + NE1], rb, inc=gp)
    consts_ld = ld.val; consts_gp = gp.val

    sel_ldE = 0
    if eblend:
        sel_sb = ec(nc.sbuf_tensor(pfx + "sel", [128, 2], F32))
        P.dma("sync", sel_sb[:], io["sel01"], inc=ld)
        sel_ldE = ld.val
        ctxg5 = io["ctxg"].rearrange("(g f q p) t -> p g f q t", g=2, f=2, q=4)
        ctxsc = hf[:, 9216:10240]
    P.dma("gpsimd", wv[0], owT.rearrange("(a p) o -> p a o", p=128), inc=gp)
    ow_ld = gp.val
    for tt in range(16):
        r0, r1 = tt * 128, (tt + 1) * 128
        if eblend:
            P.dma("gpsimd", ctx_v.rearrange("p (g q) t -> p g q t", g=2),
                  ctxg5[:, :, 0, :, r0:r1],
                  after=[mm.at()], inc=gp)
            P.dma("gpsimd", ctxsc.rearrange("p (g q t) -> p g q t", g=2, q=4),
                  ctxg5[:, :, 1, :, r0:r1],
                  inc=gp)
            ctx_gp = gp.val
            P.do("vector", lambda e: e.tensor_scalar(
                out=ctxf[:], in0=ctxf[:], scalar1=sel_sb[:, 0:1], scalar2=None,
                op0=ALU.mult),
                 after=[gp.at(ctx_gp), ld.at(sel_ldE)], inc=vv)
            P.do("vector", lambda e: e.scalar_tensor_tensor(
                out=ctxf[:], in0=ctxsc, scalar=sel_sb[:, 1:2], in1=ctxf[:],
                op0=ALU.mult, op1=ALU.add),
                 after=[vv.at()], inc=vv)
        else:
            P.dma("gpsimd", ctx_v,
                  ctx_in.rearrange("(h p) t -> p h t", p=128)[:, :, r0:r1],
                  after=[mm.at()], inc=gp)
        ctx_ld = gp.val
        ctx_blend_vv = vv.val
        P.dma("sync", h2, xn_in[r0:r1, :], after=[vv.at(), sc.at()], inc=ld)
        xn_ld = ld.val
        for os_ in range(2):
            for ht in range(8):
                P.do("tensor",
                     lambda e, os_=os_, ht=ht: e.matmul(
                         psq[:], ctx_v[:, ht, :],
                         wv[0][:, ht, os_ * 512:(os_ + 1) * 512],
                         start=(ht == 0), stop=False),
                     after=[gp.at(ctx_ld), gp.at(ow_ld), sc.at(), vv.at()]
                     if ht == 0 else (), inc=None)
            P.do("tensor",
                 lambda e, os_=os_: e.matmul(
                     psq[:], ones_r[:], biasr[0:1, os_ * 512:(os_ + 1) * 512],
                     start=False, stop=True), after=[gp.at(consts_gp)], inc=mm)
            P.do("scalar",
                 lambda e, os_=os_: e.copy(out=h1[:, os_ * 512:(os_ + 1) * 512],
                                           in_=psq[:]),
                 after=[mm.at()], inc=sc)
        P.strict = {"vector": [vv, sc], "scalar": [sc, vv]}
        P.do("vector", lambda e: e.tensor_tensor(out=h1[:], in0=h1[:], in1=h2[:],
                                                 op=ALU.add),
             after=[ld.at(xn_ld)], inc=vv)
        ao_vv = vv.val
        P.dma("sync", ao_hbm[r0:r1, :], h1[:], after=[vv.at(ao_vv)], inc=st)
        P.do("scalar", lambda e: e.activation(out=h2, in_=h1[:], func=AF.Square,
                                              accum_out=ssq_c), inc=sc)
        P.do("scalar", lambda e: e.activation(out=rstd_c, in_=ssq_c, func=AF.Ln,
                                              scale=1.0 / DM, bias=eps_c), inc=sc)
        P.do("scalar", lambda e: e.activation(out=rstd_c, in_=rstd_c, func=AF.Exp,
                                              scale=-0.5), inc=sc)
        P.do("vector", lambda e: e.tensor_scalar(out=h1[:], in0=h1[:], scalar1=rstd_c,
                                                 scalar2=None, op0=ALU.mult), inc=vv)
        P.do("vector", lambda e: e.tensor_tensor(out=h1[:], in0=h1[:], in1=gamb[:],
                                                 op=ALU.mult),
             after=[ld.at(consts_ld)], inc=vv)
        P.strict = {}
        an_vv = vv.val
        P.dma("sync", an_hbm[r0:r1, :], h1[:], after=[vv.at(an_vv)], inc=st)
        P.do("vector", lambda e: e.tensor_copy(out=hb[:], in_=h1[:]),
             after=[vv.at(an_vv)], inc=vv)
        for ht in range(8):
            P.do("tensor",
                 lambda e, ht=ht: e.transpose(
                     pst[:, ht * 128:(ht + 1) * 128],
                     hb[:, ht * 128:(ht + 1) * 128], ident_bf[:]),
                 after=[vv.at(), gp.at(gi_const), sc.at()] if ht == 0 else (),
                 inc=mm if ht == 7 else None)
        P.do("vector",
             lambda e, tt=tt: e.tensor_copy(
                 out=anf_v[:, :, tt * 128:(tt + 1) * 128],
                 in_=pst[:].rearrange("p (h t) -> p h t", h=8)),
             after=[mm.at()], inc=vv)
        for ht in range(8):
            P.do("tensor",
                 lambda e, ht=ht: e.transpose(
                     pst32[:, ht * 128:(ht + 1) * 128],
                     h1[:, ht * 128:(ht + 1) * 128], ident32[:]),
                 after=[vv.at(), sc.at()] if ht == 0 else (),
                 inc=mm if ht == 7 else None)
        P.do("vector", lambda e: e.tensor_copy(out=ant32[:], in_=pst32[:]),
             after=[mm.at()], inc=vv)
        t32_vv = vv.val
        for ht in range(8):
            P.do("tensor",
                 lambda e, ht=ht: e.matmul(
                     psr[:, 0:NE1], ant32[:, ht * 128:(ht + 1) * 128],
                     rw_v[:, ht, :], start=(ht == 0), stop=False),
                 after=[vv.at(t32_vv), ld.at(consts_ld), sc.at()] if ht == 0 else (),
                 inc=None)
        P.do("tensor",
             lambda e: e.matmul(psr[:, 0:NE1], ones_r[:],
                                biasr[0:1, 2 * DM:2 * DM + NE1],
                                start=False, stop=True), inc=mm)
        P.strict = {"vector": [vv, sc], "scalar": [sc, vv]}
        P.do("vector", lambda e: e.tensor_reduce(
            out=rmax_c, in_=psr[:, 0:NE1], axis=AX.X, op=ALU.max, negate=True),
             after=[mm.at()], inc=vv)
        P.do("scalar", lambda e: e.activation(
            out=scr2[:, 0:NE1], in_=psr[:, 0:NE1], func=AF.Exp, bias=rmax_c,
            accum_out=rsum_c), inc=sc)
        P.do("vector", lambda e: e.reciprocal(out=rinv_c, in_=rsum_c), inc=vv)
        P.do("vector", lambda e: e.tensor_scalar(
            out=scr2[:, 0:NE1], in0=scr2[:, 0:NE1], scalar1=rinv_c, scalar2=None,
            op0=ALU.mult), inc=vv)
        P.do("vector", lambda e: e.max(out=mx_c, in_=scr2[:, 0:NE1]), inc=vv)
        P.do("vector", lambda e: e.memset(mx_c[:, 3:8], -1.0), inc=vv)
        P.do("vector", lambda e: e.match_replace(
            out=scr2[:, 32:63], in_to_replace=mx_c, in_values=scr2[:, 0:NE1],
            imm_value=0.0), inc=vv)
        P.do("vector", lambda e, tt=tt: e.tensor_tensor(
            out=gates[:, tt * NE1:(tt + 1) * NE1], in0=scr2[:, 0:NE1],
            in1=scr2[:, 32:63], op=ALU.subtract), inc=vv)
        P.do("vector", lambda e, tt=tt: e.tensor_copy(
            out=gatesb[:], in_=gates[:, tt * NE1:(tt + 1) * NE1]), inc=vv)
        P.strict = {}
        gb_vv = vv.val
        P.do("tensor",
             lambda e: e.transpose(pst[0:31, 0:128], gatesb[:], ident_bf[:]),
             after=[vv.at(gb_vv), vv.at(), sc.at()], inc=mm)
        P.do("vector",
             lambda e, tt=tt: e.tensor_copy(
                 out=gatesT[0:31, tt * 128:(tt + 1) * 128],
                 in_=pst[0:31, 0:128]),
             after=[mm.at()], inc=vv)
    prep_vv = vv.val; prep_sc = sc.val
    shared_vv = 0

    P.dma("gpsimd", wv[0], shw1.rearrange("(a p) o -> p a o", p=128),
          after=[mm.at()], inc=gp)
    sh1_ld = gp.val
    for it in range(8):
        for tsl in range(4):
            for ht in range(8):
                P.do("tensor",
                     lambda e, it=it, tsl=tsl, ht=ht: e.matmul(
                         psq[:], wv[0][:, ht, it * 128:(it + 1) * 128],
                         anf_v[:, ht, tsl * 512:(tsl + 1) * 512],
                         start=(ht == 0), stop=(ht == 7)),
                     after=[gp.at(sh1_ld), vv.at(prep_vv), sc.at()]
                     if ht == 0 else (), inc=mm if ht == 7 else None)
            P.do("scalar",
                 lambda e, it=it, tsl=tsl: e.activation(
                     out=hf[:, it * TA + tsl * 512: it * TA + (tsl + 1) * 512],
                     in_=psq[:], func=AF.Gelu, bias=shb1_c[:, it:it + 1]),
                 after=[mm.at(), ld.at(consts_ld)], inc=sc)
    P.dma("gpsimd", wv[1], shw2.rearrange("(a p) o -> p a o", p=128),
          after=[mm.at()], inc=gp)
    sh2_ld = gp.val
    for tt in range(16):
        psx = (psq, psq2)[tt % 2]
        for hs in range(2):
            for it in range(8):
                P.do("tensor",
                     lambda e, tt=tt, hs=hs, it=it, psx=psx: e.matmul(
                         psx[:], hf[:, it * TA + tt * 128: it * TA + (tt + 1) * 128],
                         wv[1][:, it, hs * 512:(hs + 1) * 512],
                         start=(it == 0), stop=False),
                     after=[gp.at(sh2_ld), sc.at(), vv.at()] if it == 0 else (),
                     inc=None)
            P.do("tensor",
                 lambda e, hs=hs, psx=psx: e.matmul(
                     psx[:], ones_r[:], biasr[0:1, DM + hs * 512:DM + (hs + 1) * 512],
                     start=False, stop=True), inc=mm)
            P.do("vector",
                 lambda e, tt=tt, hs=hs, psx=psx: e.tensor_copy(
                     out=acc[:, tt * DM + hs * 512: tt * DM + (hs + 1) * 512],
                     in_=psx[:]),
                 after=[mm.at()], inc=vv)
    shared_vv = vv.val

    w1load_gp = []
    wcons_mm = [0, 0]
    def load_rexpert_w1(e_i, slot):
        deps = [mm.at(wcons_mm[slot])] if wcons_mm[slot] else [mm.at()]
        P.dma("gpsimd",
              wbuf[:, slot * 8192:(slot + 1) * 8192].rearrange(
                  "p (a o) -> p a o", a=8),
              rew1[e_i].rearrange("(a p) o -> p a o", p=128), after=deps, inc=gp)
        w1load_gp.append(gp.val)
    rw1_v = [wbuf[:, s * 8192:(s + 1) * 8192].rearrange("p (h o) -> p h o", h=8)
             for s in range(2)]
    rw2_v = wbuf[:, 16384:24576].rearrange("p (h o) -> p h o", h=8)
    load_rexpert_w1(0, 0)
    load_rexpert_w1(1, 1)
    w2cons_mm = 0
    psq_l = [psq[:], psq3[:]]
    gelu_sc = [prep_sc, prep_sc]
    for e_i in range(NE1):
        slot = e_i % 2
        wg = w1load_gp[e_i]
        P.dma("gpsimd",
              wbuf[:, 16384:24576].rearrange("p (a o) -> p a o", a=8),
              rew2[e_i].rearrange("(a p) o -> p a o", p=128),
              after=[mm.at(w2cons_mm)] if w2cons_mm else [mm.at()], inc=gp)
        w2g = gp.val
        for it in range(8):
            for tsl in range(4):
                bk = (it * 4 + tsl) % 2
                psq_b = psq_l[bk]
                gdeps = [sc.at(gelu_sc[bk])]
                if it == 0 and tsl == 0:
                    gdeps += [gp.at(wg), vv.at(shared_vv), sc.at()]
                for ht in range(8):
                    P.do("tensor",
                         lambda e, it=it, tsl=tsl, ht=ht, s=slot, psq_b=psq_b: e.matmul(
                             psq_b, rw1_v[s][:, ht, it * 128:(it + 1) * 128],
                             anf_v[:, ht, tsl * 512:(tsl + 1) * 512],
                             start=(ht == 0), stop=(ht == 7)),
                         after=gdeps if ht == 0 else (),
                         inc=mm if ht == 7 else None)
                mm_g = mm.val
                P.do("scalar",
                     lambda e, it=it, tsl=tsl, e_i=e_i, psq_b=psq_b: e.activation(
                         out=hf[:, it * TA + tsl * 512: it * TA + (tsl + 1) * 512],
                         in_=psq_b, func=AF.Gelu,
                         bias=reb1_c[:, it, e_i:e_i + 1]),
                     after=[mm.at(mm_g), mm.at(w2cons_mm)], inc=sc)
                gelu_sc[bk] = sc.val
        hf_sc = sc.val
        for tt in range(16):
            psx = (psq2, pst32)[tt % 2]
            for hs in range(2):
                for it in range(8):
                    P.do("tensor",
                         lambda e, tt=tt, hs=hs, it=it, psx=psx: e.matmul(
                             psx[:, 0:512],
                             hf[:, it * TA + tt * 128: it * TA + (tt + 1) * 128],
                             rw2_v[:, it, hs * 512:(hs + 1) * 512],
                             start=(it == 0), stop=(it == 7)),
                         after=[sc.at(hf_sc), vv.at(), gp.at(w2g)]
                         if it == 0 else (), inc=mm if it == 7 else None)
                P.do("vector",
                     lambda e, tt=tt, hs=hs, e_i=e_i, psx=psx: e.scalar_tensor_tensor(
                         out=acc[:, tt * DM + hs * 512: tt * DM + (hs + 1) * 512],
                         in0=psx[:, 0:512],
                         scalar=gates[:, tt * NE1 + e_i: tt * NE1 + e_i + 1],
                         in1=acc[:, tt * DM + hs * 512: tt * DM + (hs + 1) * 512],
                         op0=ALU.mult, op1=ALU.add),
                     after=[mm.at()], inc=vv)
        w2cons_mm = mm.val
        if e_i + 2 < NE1:
            wcons_mm[slot] = mm.val
            load_rexpert_w1(e_i + 2, slot)
    routed_vv = vv.val

    for tt in range(16):
        P.do("tensor",
             lambda e, tt=tt: e.matmul(
                 psq[:], gatesT[0:31, tt * 128:(tt + 1) * 128], reb2_sb[0:31, 0:512],
                 start=True, stop=True),
             after=[gp.at(consts_gp), vv.at(), sc.at()], inc=mm)
        P.do("tensor",
             lambda e, tt=tt: e.matmul(
                 psq2[:], gatesT[0:31, tt * 128:(tt + 1) * 128],
                 reb2_sb[0:31, 512:1024], start=True, stop=True), inc=mm)
        P.do("vector",
             lambda e, tt=tt: e.tensor_tensor(
                 out=acc[:, tt * DM: tt * DM + 512],
                 in0=acc[:, tt * DM: tt * DM + 512], in1=psq[:], op=ALU.add),
             after=[mm.at()], inc=vv)
        P.do("vector",
             lambda e, tt=tt: e.tensor_tensor(
                 out=acc[:, tt * DM + 512: tt * DM + 1024],
                 in0=acc[:, tt * DM + 512: tt * DM + 1024], in1=psq2[:], op=ALU.add),
             inc=vv)
    moe_vv = vv.val

    for tt in range(16):
        r0, r1 = tt * 128, (tt + 1) * 128
        P.dma("sync", h1[:], x3_in[r0:r1, :], after=[vv.at(), sc.at()], inc=ld)
        P.dma("sync", h2, an_hbm[r0:r1, :], after=[st.at(), mm.at()], inc=ld)
        h_ld = ld.val
        P.strict = {"vector": [vv, sc], "scalar": [sc, vv]}
        P.do("vector", lambda e: e.tensor_tensor(out=h1[:], in0=h1[:], in1=h2[:],
                                                 op=ALU.add),
             after=[ld.at(h_ld), vv.at(moe_vv)], inc=vv)
        P.dma("sync", h2, ao_hbm[r0:r1, :], after=[vv.at()], inc=ld)
        P.do("vector", lambda e: e.tensor_tensor(out=h1[:], in0=h1[:], in1=h2[:],
                                                 op=ALU.add),
             after=[ld.at()], inc=vv)
        P.do("vector", lambda e, tt=tt: e.tensor_tensor(
            out=h1[:], in0=h1[:], in1=acc[:, tt * DM:(tt + 1) * DM], op=ALU.add),
             inc=vv)
        P.do("vector", lambda e: e.reduce_sum(out=rs_c, in_=h1[:], axis=AX.X), inc=vv)
        P.do("scalar", lambda e: e.activation(out=h2, in_=h1[:], func=AF.Square,
                                              accum_out=ssq_c), inc=sc)
        P.do("vector", lambda e: e.tensor_scalar_mul(mean_c, rs_c, 1.0 / DM), inc=vv)
        P.do("vector", lambda e: e.tensor_mul(msq_c, mean_c, mean_c), inc=vv)
        P.do("vector", lambda e: e.tensor_scalar_mul(var_c, ssq_c, 1.0 / DM), inc=vv)
        P.do("vector", lambda e: e.tensor_sub(var_c, var_c, msq_c), inc=vv)
        P.do("scalar", lambda e: e.activation(out=rstd_c, in_=var_c, func=AF.Ln,
                                              bias=eps_c), inc=sc)
        P.do("scalar", lambda e: e.activation(out=rstd_c, in_=rstd_c, func=AF.Exp,
                                              scale=-0.5), inc=sc)
        P.do("vector", lambda e: e.tensor_scalar(out=h1[:], in0=h1[:], scalar1=mean_c,
                                                 scalar2=rstd_c, op0=ALU.subtract,
                                                 op1=ALU.mult), inc=vv)
        P.strict = {}
        P.dma("sync", out_f[r0:r1, :], h1[:], after=[vv.at()], inc=st)
    return dict(ld=ld, gp=gp, mm=mm, sc=sc, vv=vv, st=st)

def build_phase_e(debug=False):
    nc = bass.Bass("TRN2", target_bir_lowering=False, debug=False)
    io = declare_e_io(nc, debug)
    from contextlib import ExitStack
    with ExitStack() as _stk:
        P = Prog(nc)
        C = emit_phase_e(nc, P, _stk.enter_context, io, "e_", debug)
        P.wait("gpsimd", [C["st"].at(), C["ld"].at(), C["gp"].at()])
        P.realize()
    return nc


def prep_phase_c_inputs(inputs, x3_list):
    g = lambda k: np.asarray(inputs["L1_ffn_" + k])
    shared = {
        "qwT": np.ascontiguousarray(g("q_w")), "kwT": np.ascontiguousarray(g("k_w")),
        "vwT": np.ascontiguousarray(g("v_w")),
        "biases": np.ascontiguousarray(np.stack([g("q_b"), g("k_b"), g("v_b")])),
        "gamma1": np.ascontiguousarray(g("gamma1").reshape(1, DM)),
        "theta": np.ascontiguousarray(g("theta").reshape(1, 32)),
    }
    maps = []
    for c in range(8):
        half = c % 2
        pos = (half * TA + np.arange(16)[None, :] * 128
               + np.arange(128)[:, None]).astype(np.float32)
        m = dict(shared)
        m["x_in"] = np.ascontiguousarray(np.asarray(x3_list[c]))
        m["posv"] = np.ascontiguousarray(pos)
        maps.append(m)
    return maps

def run_phase_c(inputs, x3_list):
    nc = build_phase_c()
    res = run_bass_kernel_spmd(nc, prep_phase_c_inputs(inputs, x3_list),
                               core_ids=list(range(8)))
    return res.results

def prep_phase_d_inputs(c_res):
    maps = []
    for c in range(8):
        b, hg = c // 2, c % 2
        rsl = slice(hg * 512, (hg + 1) * 512)
        qn = np.concatenate([np.asarray(c_res[2 * b]["qn_out"]),
                             np.asarray(c_res[2 * b + 1]["qn_out"])], axis=1)
        kn = np.concatenate([np.asarray(c_res[2 * b]["kn_out"]),
                             np.asarray(c_res[2 * b + 1]["kn_out"])], axis=1)
        vn = np.concatenate([np.asarray(c_res[2 * b]["vn_out"]),
                             np.asarray(c_res[2 * b + 1]["vn_out"])], axis=0)
        maps.append({
            "qn": np.ascontiguousarray(qn[rsl]),
            "kn": np.ascontiguousarray(kn[rsl]),
            "vn": np.ascontiguousarray(vn[:, rsl]),
        })
    return maps

def run_phase_d(c_res):
    nc = build_phase_d()
    res = run_bass_kernel_spmd(nc, prep_phase_d_inputs(c_res),
                               core_ids=list(range(8)))
    return [r["ctx_out"] for r in res.results]

def prep_phase_e_inputs(inputs, x3_list, c_res, ctx_list):
    g = lambda k: np.asarray(inputs["L1_ffn_" + k])
    shared = {
        "owT": np.ascontiguousarray(g("o_w")),
        "ob_in": np.ascontiguousarray(g("o_b").reshape(1, DM)),
        "gamma2": np.ascontiguousarray(g("gamma2").reshape(1, DM)),
        "shw1": np.ascontiguousarray(g("sh_w1")),
        "shb1_rs": np.ascontiguousarray(g("sh_b1").reshape(8, 128).T),
        "shw2": np.ascontiguousarray(g("sh_w2")),
        "shb2": np.ascontiguousarray(g("sh_b2").reshape(1, DM)),
        "rw": np.ascontiguousarray(g("router_w")),
        "rb": np.ascontiguousarray(g("router_b").reshape(1, NE1)),
        "rew1": np.ascontiguousarray(g("re_w1")),
        "reb1T": np.ascontiguousarray(g("re_b1").T),
        "rew2": np.ascontiguousarray(g("re_w2")),
        "reb2": np.ascontiguousarray(g("re_b2")),
    }
    maps = []
    for c in range(8):
        b, half = c // 2, c % 2
        ctx_b = np.concatenate([np.asarray(ctx_list[2 * b]),
                                np.asarray(ctx_list[2 * b + 1])], axis=0)
        m = dict(shared)
        m["x3_in"] = np.ascontiguousarray(np.asarray(x3_list[c]))
        m["xn_in"] = np.ascontiguousarray(np.asarray(c_res[c]["xn_out"]))
        m["ctx_in"] = np.ascontiguousarray(ctx_b[:, half * TA:(half + 1) * TA])
        maps.append(m)
    return maps

def run_phase_e(inputs, x3_list, c_res, ctx_list, debug=False):
    nc = build_phase_e(debug=debug)
    res = run_bass_kernel_spmd(nc, prep_phase_e_inputs(inputs, x3_list, c_res, ctx_list),
                               core_ids=list(range(8)))
    if debug:
        return res.results
    return [r["out_f"] for r in res.results]


CC_GROUPS = [[0, 1], [2, 3], [4, 5], [6, 7]]

def build_fused():
    nc = bass.Bass("TRN2", target_bir_lowering=False, debug=False)
    from contextlib import ExitStack

    iom = {}
    iom["xT"] = nc.dram_tensor("m_xT", [DM, T], F32, kind="ExternalInput").ap()
    iom["in_wT"] = nc.dram_tensor("m_in_wT", [DM, 2 * DI], F32, kind="ExternalInput").ap()
    iom["xp_wT"] = nc.dram_tensor("m_xp_wT", [DI, 96], F32, kind="ExternalInput").ap()
    iom["dtp_wT"] = nc.dram_tensor("m_dtp_wT", [DTR, DI], F32, kind="ExternalInput").ap()
    iom["out_wT"] = nc.dram_tensor("m_out_wT", [DI, DM], F32, kind="ExternalInput").ap()
    iom["conv_w_rs"] = nc.dram_tensor("m_conv_w_rs", [128, 64], F32, kind="ExternalInput").ap()
    iom["conv_b_rs"] = nc.dram_tensor("m_conv_b_rs", [128, 16], F32, kind="ExternalInput").ap()
    iom["dtp_b_rs"] = nc.dram_tensor("m_dtp_b_rs", [128, 16], F32, kind="ExternalInput").ap()
    iom["A_log_rs"] = nc.dram_tensor("m_A_log_rs", [128, 256], F32, kind="ExternalInput").ap()
    iom["Dvec_rs"] = nc.dram_tensor("m_Dvec_rs", [128, 16], F32, kind="ExternalInput").ap()
    iom["m_out"] = nc.dram_tensor("m_m_out", [T, DM], F32, kind="Internal").ap()
    iom["xc_hbm"] = nc.dram_tensor("m_xc_hbm", [DI, T], F32, kind="Internal").ap()
    iom["sz_hbm"] = nc.dram_tensor("m_sz_hbm", [DI, T], F32, kind="Internal").ap()
    iom["bc_hbm"] = nc.dram_tensor("m_bc_hbm", [32, T], F32, kind="Internal").ap()
    iom["dbg_hbm"] = nc.dram_tensor("m_dbg", [128, 4 * TCH], F32, kind="Internal").ap()
    gout_m = nc.dram_tensor("m_gout", [2 * T, DM], F32, kind="Internal").ap()

    ioa = {}
    ioa["x_in"] = nc.dram_tensor("a_x_in", [TA, DM], F32, kind="ExternalInput").ap()
    ioa["gate_wT"] = nc.dram_tensor("a_gate_wT", [DM, NE0], F32, kind="ExternalInput").ap()
    ioa["e_bias"] = nc.dram_tensor("a_e_bias", [1, NE0], F32, kind="ExternalInput").ap()
    ioa["w1_h"] = nc.dram_tensor("a_w1", [NE0, DM, I0], F32, kind="ExternalInput").ap()
    ioa["w3_h"] = nc.dram_tensor("a_w3", [NE0, DM, I0], F32, kind="ExternalInput").ap()
    ioa["w2_h"] = nc.dram_tensor("a_w2", [NE0, I0, DM], F32, kind="ExternalInput").ap()
    ioa["x2_out"] = nc.dram_tensor("a_x2", [TA, DM], F32, kind="Internal").ap()
    ioa["xn_hbm"] = nc.dram_tensor("a_xn", [TA, DM], F32, kind="Internal").ap()
    ioa["dbgA"] = nc.dram_tensor("a_dbg", [128, 1024], F32, kind="Internal").ap()
    ioa["mf_in"] = gout_m[0:TA, :]
    ioa["mf2_in"] = gout_m[TA:2 * TA, :]
    ioa["mr_in"] = gout_m[T:T + TA, :]
    ioa["mr2_in"] = gout_m[T + TA:2 * T, :]
    sel01 = nc.dram_tensor("sel01", [128, 2], F32, kind="ExternalInput").ap()
    ioa["sel01"] = sel01
    halo_src = nc.dram_tensor("b_halo_src", [128, DM], F32, kind="Internal").ap()
    halo_recv = nc.dram_tensor("b_halo_recv", [256, DM], F32, kind="Internal").ap()

    iob = {}
    iob["x_in"] = ioa["x2_out"]
    iob["halo_in"] = halo_recv[0:128, :]
    iob["wqT"] = nc.dram_tensor("b_wqT", [DM, DM], F32, kind="ExternalInput").ap()
    iob["wkT"] = nc.dram_tensor("b_wkT", [DM, DM], F32, kind="ExternalInput").ap()
    iob["wvT"] = nc.dram_tensor("b_wvT", [DM, DM], F32, kind="ExternalInput").ap()
    iob["woT"] = nc.dram_tensor("b_woT", [DM, DM], F32, kind="ExternalInput").ap()
    iob["biases"] = nc.dram_tensor("b_biases", [4, DM], F32, kind="ExternalInput").ap()
    iob["posv"] = nc.dram_tensor("b_posv", [128, NTB], F32, kind="ExternalInput").ap()
    iob["masks_in"] = nc.dram_tensor("b_masks", [128, 512], F32, kind="ExternalInput").ap()
    iob["x3_out"] = nc.dram_tensor("b_x3", [TA, DM], F32, kind="Internal").ap()
    iob["dbgB"] = nc.dram_tensor("b_dbg", [128, 2048], F32, kind="Internal").ap()

    ioc = {}
    ioc["x_in"] = iob["x3_out"]
    ioc["qwT"] = nc.dram_tensor("c_qwT", [DM, DM], F32, kind="ExternalInput").ap()
    ioc["kwT"] = nc.dram_tensor("c_kwT", [DM, DM], F32, kind="ExternalInput").ap()
    ioc["vwT"] = nc.dram_tensor("c_vwT", [DM, DM], F32, kind="ExternalInput").ap()
    ioc["biases"] = nc.dram_tensor("c_biases", [3, DM], F32, kind="ExternalInput").ap()
    ioc["gamma1"] = nc.dram_tensor("c_gamma1", [1, DM], F32, kind="ExternalInput").ap()
    ioc["theta"] = nc.dram_tensor("c_theta", [1, 32], F32, kind="ExternalInput").ap()
    ioc["posv"] = nc.dram_tensor("c_posv", [128, 16], F32, kind="ExternalInput").ap()
    qn_src = nc.dram_tensor("c_qn_src", [DM, TA], BF16, kind="Internal").ap()
    kn_src = nc.dram_tensor("c_kn_src", [DM, TA], BF16, kind="Internal").ap()
    v_src = nc.dram_tensor("c_v_src", [2 * TA, 512], BF16, kind="Internal").ap()
    ioc["qn_out"] = qn_src
    ioc["kn_out"] = kn_src
    ioc["vn_out3"] = v_src.rearrange("(r t) c -> t r c", r=2)
    ioc["xn_out"] = nc.dram_tensor("c_xn", [TA, DM], F32, kind="Internal").ap()
    gout_q = nc.dram_tensor("d_gout_q", [2 * DM, TA], BF16, kind="Internal").ap()
    gout_k = nc.dram_tensor("d_gout_k", [2 * DM, TA], BF16, kind="Internal").ap()
    gout_v = nc.dram_tensor("d_gout_v", [4 * TA, 512], BF16, kind="Internal").ap()

    iod = {}
    iod["qg"] = gout_q
    iod["kg"] = gout_k
    iod["vg"] = gout_v
    iod["sel01"] = sel01
    ctx_src = nc.dram_tensor("d_ctx_src", [DM, TA], BF16, kind="Internal").ap()
    iod["ctx_out3"] = ctx_src.rearrange("(r h) t -> h r t", r=2)
    gout_c = nc.dram_tensor("e_gout_c", [2 * DM, TA], BF16, kind="Internal").ap()

    ioe = {}
    ioe["x3_in"] = iob["x3_out"]
    ioe["xn_in"] = ioc["xn_out"]
    ioe["ctxg"] = gout_c
    ioe["sel01"] = sel01
    ioe["owT"] = nc.dram_tensor("e_owT", [DM, DM], F32, kind="ExternalInput").ap()
    ioe["ob_in"] = nc.dram_tensor("e_ob", [1, DM], F32, kind="ExternalInput").ap()
    ioe["gamma2"] = nc.dram_tensor("e_gamma2", [1, DM], F32, kind="ExternalInput").ap()
    ioe["shw1"] = nc.dram_tensor("e_shw1", [DM, DM], F32, kind="ExternalInput").ap()
    ioe["shb1_rs"] = nc.dram_tensor("e_shb1", [128, 8], F32, kind="ExternalInput").ap()
    ioe["shw2"] = nc.dram_tensor("e_shw2", [DM, DM], F32, kind="ExternalInput").ap()
    ioe["shb2"] = nc.dram_tensor("e_shb2", [1, DM], F32, kind="ExternalInput").ap()
    ioe["rw"] = nc.dram_tensor("e_rw", [DM, NE1], F32, kind="ExternalInput").ap()
    ioe["rb"] = nc.dram_tensor("e_rb", [1, NE1], F32, kind="ExternalInput").ap()
    ioe["rew1"] = nc.dram_tensor("e_rew1", [NE1, DM, DM], F32, kind="ExternalInput").ap()
    ioe["reb1T"] = nc.dram_tensor("e_reb1T", [DM, NE1], F32, kind="ExternalInput").ap()
    ioe["rew2"] = nc.dram_tensor("e_rew2", [NE1, DM, DM], F32, kind="ExternalInput").ap()
    ioe["reb2"] = nc.dram_tensor("e_reb2", [NE1, DM], F32, kind="ExternalInput").ap()
    ioe["out_f"] = nc.dram_tensor("out_f", [TA, DM], F32, kind="ExternalOutput").ap()
    ioe["ao_hbm"] = nc.dram_tensor("e_ao", [TA, DM], F32, kind="Internal").ap()
    ioe["an_hbm"] = nc.dram_tensor("e_an", [TA, DM], F32, kind="Internal").ap()
    ioe["dbgE"] = nc.dram_tensor("e_dbg", [128, 1024], F32, kind="Internal").ap()

    with ExitStack() as s0:
        P = Prog(nc)
        s_cc = s0.enter_context(nc.semaphore("fz_cc"))
        cc = Ctr(s_cc)

        def barrier(C):
            P.strict = {}
            for eng in Prog.ENGINES:
                P.wait(eng, [c.at() for c in C.values()] + [cc.at()])

        def gather(src, dst, after):
            P.do("gpsimd",
                 lambda e, src=src, dst=dst: e.collective_compute(
                     "AllGather", ALU.bypass, CC_GROUPS, [src], [dst]),
                 after=after, inc=cc, n=16)

        with ExitStack() as s1:
            Cm = emit_phase_m(nc, P, s1.enter_context, iom, "mS_")
        gather(iom["m_out"], gout_m, [Cm["st"].at()])
        barrier(Cm)

        with ExitStack() as s2:
            Ca = emit_phase_a(nc, P, s2.enter_context, ioa, "aS_")
        P.dma("sync", halo_src[:], ioa["x2_out"][TA - 128:TA, :],
              after=[Ca["st"].at()], inc=Ca["st"])
        gather(halo_src, halo_recv, [Ca["st"].at()])
        barrier(Ca)

        with ExitStack() as s3:
            Cb = emit_phase_b(nc, P, s3.enter_context, iob, "bS_")
        barrier(Cb)

        with ExitStack() as s4:
            Cc = emit_phase_c(nc, P, s4.enter_context, ioc, "cS_")
        gather(qn_src, gout_q, [Cc["st"].at()])
        gather(kn_src, gout_k, [Cc["st"].at()])
        gather(v_src, gout_v, [Cc["st"].at()])
        barrier(Cc)

        with ExitStack() as s5:
            Cd = emit_phase_d(nc, P, s5.enter_context, iod, "dS_")
        gather(ctx_src, gout_c, [Cd["st"].at()])
        barrier(Cd)

        with ExitStack() as s6:
            Ce = emit_phase_e(nc, P, s6.enter_context, ioe, "eS_")
        P.wait("gpsimd", [Ce["st"].at(), Ce["ld"].at(), Ce["gp"].at(), cc.at()])
        P.realize()
    return nc

def prep_fused_inputs(inputs):
    mm_ = prep_phase_m_inputs(inputs)
    x = np.asarray(inputs["L0_attn_x"]).reshape(16384, DM)
    ga = lambda k: np.asarray(inputs["L0_ffn_" + k])
    gb = lambda k: np.asarray(inputs["L1_attn_" + k])
    gc = lambda k: np.asarray(inputs["L1_ffn_" + k])
    qi = np.arange(128)[:, None]; kj = np.arange(256)[None, :]
    band = (kj >= qi) & (kj <= qi + 128)
    mask_n = np.where(band, 0.0, -1e30).astype(np.float32)
    mask_f = np.where(band & (kj >= 128), 0.0, -1e30).astype(np.float32)
    shared = {
        "a_gate_wT": np.ascontiguousarray(ga("gate_w").T),
        "a_e_bias": ga("e_bias").reshape(1, NE0),
        "a_w1": np.ascontiguousarray(ga("w1")), "a_w3": np.ascontiguousarray(ga("w3")),
        "a_w2": np.ascontiguousarray(ga("w2")),
        "b_wqT": np.ascontiguousarray(gb("Wq").T), "b_wkT": np.ascontiguousarray(gb("Wk").T),
        "b_wvT": np.ascontiguousarray(gb("Wv").T), "b_woT": np.ascontiguousarray(gb("Wo").T),
        "b_biases": np.ascontiguousarray(
            np.stack([gb("bq"), gb("bk"), gb("bv"), gb("bo")])),
        "c_qwT": np.ascontiguousarray(gc("q_w")), "c_kwT": np.ascontiguousarray(gc("k_w")),
        "c_vwT": np.ascontiguousarray(gc("v_w")),
        "c_biases": np.ascontiguousarray(np.stack([gc("q_b"), gc("k_b"), gc("v_b")])),
        "c_gamma1": np.ascontiguousarray(gc("gamma1").reshape(1, DM)),
        "c_theta": np.ascontiguousarray(gc("theta").reshape(1, 32)),
        "e_owT": np.ascontiguousarray(gc("o_w")),
        "e_ob": np.ascontiguousarray(gc("o_b").reshape(1, DM)),
        "e_gamma2": np.ascontiguousarray(gc("gamma2").reshape(1, DM)),
        "e_shw1": np.ascontiguousarray(gc("sh_w1")),
        "e_shb1": np.ascontiguousarray(gc("sh_b1").reshape(8, 128).T),
        "e_shw2": np.ascontiguousarray(gc("sh_w2")),
        "e_shb2": np.ascontiguousarray(gc("sh_b2").reshape(1, DM)),
        "e_rw": np.ascontiguousarray(gc("router_w")),
        "e_rb": np.ascontiguousarray(gc("router_b").reshape(1, NE1)),
        "e_rew1": np.ascontiguousarray(gc("re_w1")),
        "e_reb1T": np.ascontiguousarray(gc("re_b1").T),
        "e_rew2": np.ascontiguousarray(gc("re_w2")),
        "e_reb2": np.ascontiguousarray(gc("re_b2")),
    }
    maps = []
    for c in range(8):
        half = c % 2
        m = dict(shared)
        for k, v in mm_[c].items():
            m["m_" + k] = v
        m["a_x_in"] = np.ascontiguousarray(x[c * TA:(c + 1) * TA])
        sel = np.zeros((128, 2), np.float32); sel[:, half] = 1.0
        m["sel01"] = sel
        pos_b = (half * TA - 128 + np.arange(NTB)[None, :] * 128
                 + np.arange(128)[:, None]).astype(np.float32)
        m["b_posv"] = np.ascontiguousarray(pos_b)
        m["b_masks"] = np.ascontiguousarray(
            np.concatenate([mask_f if half == 0 else mask_n, mask_n], axis=1))
        pos_c = (half * TA + np.arange(16)[None, :] * 128
                 + np.arange(128)[:, None]).astype(np.float32)
        m["c_posv"] = np.ascontiguousarray(pos_c)
        maps.append(m)
    return maps

FUSED = False

def kernel(**inputs):
    if FUSED:
        nc = build_fused()
        in_maps = prep_fused_inputs(inputs)
        res = run_bass_kernel_spmd(nc, in_maps, core_ids=list(range(8)))
        out = np.concatenate([np.asarray(r["out_f"]) for r in res.results], axis=0)
        return out.reshape(4, 4096, 1024).astype(np.float32)
    m_outs = run_phase_m(inputs)
    x2_list = run_phase_a(inputs, m_outs)
    x3_list = run_phase_b(inputs, x2_list)
    c_res = run_phase_c(inputs, x3_list)
    ctx_list = run_phase_d(c_res)
    out_list = run_phase_e(inputs, x3_list, c_res, ctx_list)
    out = np.concatenate([np.asarray(o) for o in out_list], axis=0)
    return out.reshape(4, 4096, 1024).astype(np.float32)
